# version 09
# baseline (speedup 1.0000x reference)
import numpy as np
import ml_dtypes
from contextlib import ExitStack
import concourse.bass as bass
import concourse.mybir as mybir
from concourse.bass_utils import run_bass_kernel_spmd
from concourse.aot_env import allow_ici_all_to_all

F32 = mybir.dt.float32
BF16 = mybir.dt.bfloat16
U8 = mybir.dt.uint8
AF = mybir.ActivationFunctionType
ALU = mybir.AluOpType
AX = mybir.AxisListType
f32 = np.float32
BF = ml_dtypes.bfloat16

NCORES = 8
D = 2048
DC = 16
TOKS = 2048
TP = 512
NPASS = TOKS // TP
NSUB = TP // 128
H0 = 4096
H0C = H0 // 128
INTER = 8192
NH = 16
HD = 128
SEQ = 8192


def wtile(W):
    dout, din = W.shape
    t = W.reshape(dout // 128, 128, din // 128, 128)
    return np.ascontiguousarray(np.transpose(t, (3, 0, 2, 1)).reshape(128, -1))


def colchunks(v):
    return np.ascontiguousarray(np.asarray(v).reshape(-1, 128).T)


def x_fmaj(x_tok):
    t = x_tok.reshape(-1, DC, 128)
    return np.ascontiguousarray(np.transpose(t, (2, 1, 0)).reshape(128, -1))


def x_tok_of_fmaj(xf):
    t = xf.reshape(128, DC, -1)
    return np.ascontiguousarray(np.transpose(t, (2, 1, 0)).reshape(-1, D))


class Sem:
    def __init__(self, st, nc, name):
        self.h = st.enter_context(nc.semaphore(name))
        self.n = 0

    def inc(self, inst, k=1):
        inst.then_inc(self.h, k)
        self.n += k
        return inst


class Ctx:
    def __init__(self, nc, st):
        self.nc = nc
        self.st = st
        self.t = nc.tensor
        self.v = nc.vector
        self.s = nc.scalar
        self.g = nc.gpsimd
        self.sy = nc.sync
        self._sems = {}

    def sem(self, name):
        if name not in self._sems:
            self._sems[name] = Sem(self.st, self.nc, name)
        return self._sems[name]

    def sbuf(self, name, shape, dtype):
        return self.st.enter_context(self.nc.sbuf_tensor(name, shape, dtype))

    def psum(self, name, shape, dtype=F32):
        return self.st.enter_context(self.nc.psum_tensor(name, shape, dtype))


def emit_ln(c, vb, XF, XHB, extra_bcast_waits=()):
    XB = vb["XB"]; XQ = vb["XQ"]
    pS1 = vb["pS1"]; pS2 = vb["pS2"]
    onesbb = vb["onesbb"]; LROW = vb["LROW"]
    lnS = c.sem("lnS"); lnV = c.sem("lnV"); lnP = c.sem("lnP")

    c.s.wait_ge(lnV.h, lnV.n)
    for ci in range(DC):
        c.s.copy(XB[:, ci * TP:(ci + 1) * TP], XF[:, ci * TP:(ci + 1) * TP])
        inst = c.s.activation(XQ[:, ci * TP:(ci + 1) * TP],
                              XF[:, ci * TP:(ci + 1) * TP], AF.Square)
    lnS.inc(inst)
    c.t.wait_ge(lnS.h, lnS.n)
    for s_ in extra_bcast_waits:
        c.t.wait_ge(s_.h, s_.n)
    for ci in range(DC):
        c.t.matmul(pS1[:, :TP], onesbb[:, :], XB[:, ci * TP:(ci + 1) * TP],
                   start=(ci == 0), stop=(ci == DC - 1))
    for ci in range(DC):
        inst = c.t.matmul(pS2[:, :TP], onesbb[:, :], XQ[:, ci * TP:(ci + 1) * TP],
                          start=(ci == 0), stop=(ci == DC - 1))
    lnP.inc(inst)
    mean = LROW[:, 0 * TP:1 * TP]
    var = LROW[:, 1 * TP:2 * TP]
    rstd = LROW[:, 2 * TP:3 * TP]
    mr = LROW[:, 3 * TP:4 * TP]
    c.s.wait_ge(lnP.h, lnP.n)
    c.s.activation(mean, pS1[:, :TP], AF.Copy, scale=f32(1.0 / D))
    lnS.inc(c.s.activation(var, mean, AF.Square))
    c.v.wait_ge(lnS.h, lnS.n)
    lnV.inc(c.v.scalar_tensor_tensor(var, pS2[:, :TP], 1.0 / D, var,
                                     ALU.mult, ALU.subtract))
    c.s.wait_ge(lnV.h, lnV.n)
    c.s.activation(rstd, var, AF.Ln, bias=f32(1e-5))
    lnS.inc(c.s.activation(rstd, rstd, AF.Exp, scale=f32(-0.5)))
    c.v.wait_ge(lnS.h, lnS.n)
    c.v.wait_ge(c.sem("wfree").h, c.sem("wfree").n)
    lnV.inc(c.v.tensor_tensor(mr, mean, rstd, ALU.mult))
    for ci in range(DC):
        c.v.tensor_tensor(XB[:, ci * TP:(ci + 1) * TP],
                          XF[:, ci * TP:(ci + 1) * TP], rstd, ALU.mult)
        inst = c.v.tensor_tensor(XHB[:, ci * TP:(ci + 1) * TP],
                                 XB[:, ci * TP:(ci + 1) * TP], mr,
                                 ALU.subtract)
    lnV.inc(inst)
    return lnV


def emit_ln_stats(c, vb, xdram, p, parity, sbuf_xf=None, pre=None):
    XB2 = vb["XB2"]; XQr = vb["XQr"]
    pS1 = vb["pS1"]; pS2 = vb["pS2"]; onesbb = vb["onesbb"]; LROW = vb["LROW"]
    lnq = c.sem("lnq"); lnP = c.sem("lnP")
    lnS = c.sem("lnS"); lnV = c.sem("lnV")
    lnq0, lnP0 = lnq.n, lnP.n
    if sbuf_xf is None:
        lnc = c.sem("lnc")
        c.g.wait_ge(lnP.h, lnP.n)
        lnc.inc(c.g.dma_start(
            out=XB2[:, :].rearrange("pp (cc t) -> pp cc t", cc=DC),
            in_=xdram.rearrange("pp (cc t) -> pp cc t", cc=DC)
            [:, :, p * TP:(p + 1) * TP]), 16)
        rdy = (lnc, lnc.n)
    else:
        lncs = c.sem("lncs")
        c.s.wait_ge(lnP.h, lnP.n)
        if pre is not None:
            c.s.wait_ge(pre[0].h, pre[1])
        for ci in range(DC):
            inst = c.s.copy(XB2[:, ci * TP:(ci + 1) * TP],
                            sbuf_xf[:, ci * TP:(ci + 1) * TP])
        lncs.inc(inst)
        rdy = (lncs, lncs.n)
    for ci in range(DC):
        b = ci % 2
        if ci == 0:
            c.v.wait_ge(rdy[0].h, rdy[1])
            c.v.wait_ge(lnP.h, lnP0)
        if ci >= 2:
            c.v.wait_ge(lnP.h, lnP0 + 2 * (ci - 1))
        lnq.inc(c.v.tensor_tensor(XQr[:, b * TP:(b + 1) * TP],
                                  XB2[:, ci * TP:(ci + 1) * TP],
                                  XB2[:, ci * TP:(ci + 1) * TP], ALU.mult))
        c.t.wait_ge(lnq.h, lnq0 + ci + 1)
        if ci == 0:
            c.t.wait_ge(lnS.h, lnS.n)
        c.t.matmul(pS1[:, :TP], onesbb[:, :], XB2[:, ci * TP:(ci + 1) * TP],
                   start=(ci == 0), stop=(ci == DC - 1))
        inst = c.t.matmul(pS2[:, :TP], onesbb[:, :], XQr[:, b * TP:(b + 1) * TP],
                          start=(ci == 0), stop=(ci == DC - 1))
        lnP.inc(inst, 2)
    rstd = LROW[:, (2 * parity) * TP:(2 * parity + 1) * TP]
    mr = LROW[:, (2 * parity + 1) * TP:(2 * parity + 2) * TP]
    c.s.wait_ge(lnP.h, lnP.n)
    c.s.wait_ge(lnV.h, lnV.n)
    c.s.activation(mr, pS1[:, :TP], AF.Copy, scale=f32(1.0 / D))
    lnS.inc(c.s.activation(rstd, mr, AF.Square))
    c.v.wait_ge(lnS.h, lnS.n)
    lnV.inc(c.v.scalar_tensor_tensor(rstd, pS2[:, :TP], 1.0 / D, rstd,
                                     ALU.mult, ALU.subtract))
    c.s.wait_ge(lnV.h, lnV.n)
    c.s.activation(rstd, rstd, AF.Ln, bias=f32(1e-5))
    lnS.inc(c.s.activation(rstd, rstd, AF.Exp, scale=f32(-0.5)))
    c.v.wait_ge(lnS.h, lnS.n)
    lnV.inc(c.v.tensor_tensor(mr, mr, rstd, ALU.mult))
    return lnV


def emit_ln_norm(c, vb, xdram, p, parity, qe=None, xhb_free=None,
                 sbuf_xf=None, pre=None):
    XFR = vb["XFR"]; XFS = vb["XFS"]; XHB = vb["XHB"]; LROW = vb["LROW"]
    lnV = c.sem("lnV")
    rstd = LROW[:, (2 * parity) * TP:(2 * parity + 1) * TP]
    mr = LROW[:, (2 * parity + 1) * TP:(2 * parity + 2) * TP]
    lnV0 = lnV.n
    if sbuf_xf is None:
        lnf = c.sem("lnf" if qe is c.sy else
                    ("lnfs" if qe is c.s else "lnfg"))
        lnf0 = lnf.n
    for ci in range(DC):
        b = ci % 2
        src = XFR[:, b * TP:(b + 1) * TP]
        if sbuf_xf is None:
            if ci < 2:
                qe.wait_ge(lnV.h, lnV0)
            else:
                qe.wait_ge(lnV.h, lnV0 + 2 * (ci - 1))
            if ci > 0:
                qe.wait_ge(lnf.h, lnf0 + 16 * ci)
            lnf.inc(qe.dma_start(
                out=src,
                in_=xdram[:, ci * TOKS + p * TP:ci * TOKS + (p + 1) * TP]), 16)
            c.v.wait_ge(lnf.h, lnf0 + 16 * (ci + 1))
        if ci == 0:
            if xhb_free is not None:
                c.v.wait_ge(xhb_free[0].h, xhb_free[1])
            if pre is not None:
                c.v.wait_ge(pre[0].h, pre[1])
        if sbuf_xf is None:
            c.v.tensor_tensor(XFS[:, :], src, rstd, ALU.mult)
        else:
            c.v.tensor_tensor(XFS[:, :], sbuf_xf[:, ci * TP:(ci + 1) * TP],
                              rstd, ALU.mult)
        inst = c.v.tensor_tensor(XHB[:, ci * TP:(ci + 1) * TP], XFS[:, :], mr,
                                 ALU.subtract)
        lnV.inc(inst, 2)
    return lnV


def emit_wstream_mm(c, vb, w_dram, n_oc, cin, RHS, evac_fn, rhs_wait=None):
    wbufs = vb["wbufs"]; pAB = vb["pAB"]
    nb = len(wbufs)
    wld = c.sem("wld"); wfree = c.sem("wfree"); evd = c.sem("evd")
    if not hasattr(c, "wlast"):
        c.wlast = {}
    span = cin * 128
    wld0, wfree0, evd0 = wld.n, wfree.n, evd.n
    for oc in range(n_oc):
        buf = wbufs[oc % nb]
        pp = pAB[oc % 2]
        key = id(buf)
        if key in c.wlast:
            c.g.wait_ge(wfree.h, c.wlast[key])
        wld.inc(c.g.dma_start(out=buf[:, :span],
                              in_=w_dram[:, oc * span:(oc + 1) * span]), 16)
        c.t.wait_ge(wld.h, wld0 + 16 * (oc + 1))
        if oc < 2:
            c.t.wait_ge(evd.h, evd0)
            if rhs_wait is not None and oc == 0:
                c.t.wait_ge(rhs_wait[0].h, rhs_wait[1])
        else:
            c.t.wait_ge(evd.h, evd0 + oc - 1)
        for ci in range(cin):
            inst = c.t.matmul(pp[:, :TP], buf[:, ci * 128:(ci + 1) * 128],
                              RHS(ci), start=(ci == 0), stop=(ci == cin - 1))
        wfree.inc(inst)
        c.wlast[key] = wfree.n
        evac_fn(oc, pp, wfree, wfree0, evd)
    return wld, wfree, evd


def emit_ffn0_pass(c, vb, XF, XHB, lnV, w1, w2, w3, hook=None, xf_wait=None):
    H1B = vb["H1B"]; H2B = vb["H2B"]; vcat = vb["vcat"]; hb3 = vb["hb3"]
    TY = vb["TY"]
    B1C, B2C = vb["B1C"], vb["B2C"]
    evd = c.sem("evd")

    def evac_relu(OUT, bcol0):
        def f(oc, pp, wfree, wfree0, evd):
            c.s.wait_ge(wfree.h, wfree0 + oc + 1)
            inst = c.s.activation(OUT[:, oc * TP:(oc + 1) * TP], pp[:, :TP],
                                  AF.Relu, bias=vcat[:, bcol0 + oc:bcol0 + oc + 1])
            evd.inc(inst)
        return f

    def evac_res(oc, pp, wfree, wfree0, evd):
        c.v.wait_ge(wfree.h, wfree0 + oc + 1)
        if oc == 0 and xf_wait is not None:
            c.v.wait_ge(xf_wait[0].h, xf_wait[1])
        ty = TY[:, 0:TP]
        c.v.tensor_scalar(ty, pp[:, :TP], 0.5, hb3[:, oc:oc + 1],
                          ALU.mult, ALU.add)
        inst = c.v.tensor_tensor(XF[:, oc * TP:(oc + 1) * TP], ty,
                                 XF[:, oc * TP:(oc + 1) * TP], ALU.add)
        evd.inc(inst)

    emit_wstream_mm(c, vb, w1, H0C, DC, lambda ci: XHB[:, ci * TP:(ci + 1) * TP],
                    evac_relu(H1B, B1C), rhs_wait=(lnV, lnV.n))
    if hook is not None:
        hook()
    emit_wstream_mm(c, vb, w2, H0C, H0C, lambda ci: H1B[:, ci * TP:(ci + 1) * TP],
                    evac_relu(H2B, B2C))
    emit_wstream_mm(c, vb, w3, DC, H0C, lambda ci: H2B[:, ci * TP:(ci + 1) * TP],
                    evac_res)
    return evd


def mk_ffn0_vb(c):
    vb = {}
    vb["XB"] = c.sbuf("XB", [128, DC * TP], BF16)
    vb["XQ"] = c.sbuf("XQ", [128, DC * TP], BF16)
    vb["XHB"] = c.sbuf("XHB", [128, DC * TP], BF16)
    vb["H1B"] = c.sbuf("H1B", [128, H0C * TP], BF16)
    vb["H2B"] = c.sbuf("H2B", [128, H0C * TP], BF16)
    vb["wbufs"] = [c.sbuf("wb0", [128, H0C * 128], BF16),
                   c.sbuf("wb1", [128, H0C * 128], BF16),
                   c.sbuf("wb2", [128, H0C * 128], BF16)]
    vb["TY"] = c.sbuf("TY", [128, 2 * TP], F32)
    vb["SROW"] = c.sbuf("SROW", [1, 5 * TP], F32)
    vb["onesb"] = c.sbuf("onesb", [128, 1], BF16)
    vb["ones1f"] = c.sbuf("ones1f", [1, 128], F32)
    vb["onesbb"] = c.sbuf("onesbb", [128, 128], BF16)
    vb["LROW"] = c.sbuf("LROW", [128, 4 * TP], F32)
    vb["pS1"] = c.psum("pS1", [128, TP])
    vb["pS2"] = c.psum("pS2", [128, TP])
    vb["pR"] = c.psum("pR", [128, TP])
    vb["pM"] = c.psum("pM", [128, TP])
    vb["pAB"] = [c.psum("pA", [128, TP]), c.psum("pB", [128, TP])]
    return vb


def build_ffn0_launch(first):
    nc = bass.Bass("TRN2", target_bir_lowering=False)
    if first:
        xin = nc.dram_tensor("xin", [TOKS, D], F32, kind="ExternalInput").ap()
    else:
        xin = nc.dram_tensor("xin", [128, DC * TOKS], F32, kind="ExternalInput").ap()
    w1 = nc.dram_tensor("w1", [128, H0C * DC * 128], F32, kind="ExternalInput").ap()
    w2 = nc.dram_tensor("w2", [128, H0C * H0C * 128], F32, kind="ExternalInput").ap()
    w3 = nc.dram_tensor("w3", [128, DC * H0C * 128], F32, kind="ExternalInput").ap()
    vcat_in = nc.dram_tensor("vcat", [128, 80], F32, kind="ExternalInput").ap()
    identf = nc.dram_tensor("identf", [128, 128], F32, kind="ExternalInput").ap()
    xout = nc.dram_tensor("xout", [128, DC * TOKS], F32, kind="ExternalOutput").ap()

    with ExitStack() as st:
        c = Ctx(nc, st)
        vb = mk_ffn0_vb(c)
        vb["vcat"] = c.sbuf("vcat_sb", [128, 80], F32)
        vb["hb3"] = c.sbuf("hb3", [128, DC], F32)
        vb["B1C"], vb["B2C"], B3C = 0, 32, 64
        XF = c.sbuf("XF", [128, DC * TP], F32)
        if first:
            XT = c.sbuf("XT", [128, NSUB * D], F32)
            identsb = c.sbuf("identsb", [128, 128], F32)
            pT = [c.psum("pT0", [128, 128]), c.psum("pT1", [128, 128])]
        block = st.enter_context(nc.Block())

        @block.gpsimd
        def _(_g):
            cst = c.sem("cst")
            cst.inc(c.sy.dma_start(out=vb["vcat"][:, :], in_=vcat_in[:, :]), 16)
            if first:
                cst.inc(c.sy.dma_start(out=identsb[:, :], in_=identf[:, :]), 16)
            c.v.wait_ge(cst.h, cst.n)
            c.s.wait_ge(cst.h, cst.n)
            c.v.memset(vb["onesb"][:, :], 1.0)
            c.v.memset(vb["onesbb"][:, :], 1.0)
            c.v.memset(vb["ones1f"][:, :], 1.0)
            ini = c.v.tensor_scalar(vb["hb3"][:, :], vb["vcat"][:, B3C:B3C + DC],
                                    0.5, None, ALU.mult)
            c.sem("lnV").inc(ini)

            xld = c.sem("xld")
            xst = c.sem("xst")
            tpe = c.sem("tpe")
            tdv = c.sem("tdv")
            evd = c.sem("evd")
            for p in range(NPASS):
                if first:
                    c.sy.wait_ge(tpe.h, tpe.n)
                    xld.inc(c.sy.dma_start(
                        out=XT[:, :].rearrange("pp (s d) -> pp s d", s=NSUB),
                        in_=xin.rearrange("(s pp) d -> pp s d", pp=128)
                        [:, p * NSUB:(p + 1) * NSUB, :]), 16)
                    tpe0, tdv0 = tpe.n, tdv.n
                    c.t.wait_ge(xld.h, xld.n)
                    c.t.wait_ge(evd.h, evd.n)
                    c.t.wait_ge(xst.h, xst.n)
                    i = 0
                    for sub in range(NSUB):
                        for ci in range(DC):
                            if i >= 2:
                                c.t.wait_ge(tdv.h, tdv0 + i - 1)
                            tr = c.t.transpose(
                                pT[i % 2][:, :],
                                XT[:, sub * D + ci * 128:sub * D + (ci + 1) * 128],
                                identsb[:, :])
                            tpe.inc(tr)
                            c.v.wait_ge(tpe.h, tpe0 + i + 1)
                            cp = c.v.tensor_copy(
                                XF[:, ci * TP + sub * 128:ci * TP + (sub + 1) * 128],
                                pT[i % 2][:, :])
                            tdv.inc(cp)
                            i += 1
                    c.s.wait_ge(tdv.h, tdv.n)
                else:
                    c.sy.wait_ge(evd.h, evd.n)
                    c.sy.wait_ge(xst.h, xst.n)
                    xld.inc(c.sy.dma_start(
                        out=XF[:, :].rearrange("pp (cc t) -> pp cc t", cc=DC),
                        in_=xin.rearrange("pp (cc t) -> pp cc t", cc=DC)
                        [:, :, p * TP:(p + 1) * TP]), 16)
                    c.s.wait_ge(xld.h, xld.n)

                lnV = emit_ln(c, vb, XF, vb["XHB"])
                evd = emit_ffn0_pass(c, vb, XF, vb["XHB"], lnV, w1, w2, w3)
                c.sy.wait_ge(evd.h, evd.n)
                xst.inc(c.sy.dma_start(
                    out=xout.rearrange("pp (cc t) -> pp cc t", cc=DC)
                    [:, :, p * TP:(p + 1) * TP],
                    in_=XF[:, :].rearrange("pp (cc t) -> pp cc t", cc=DC)), 16)
            c.g.wait_ge(xst.h, xst.n)

    return nc



def emit_gla_a_subtile(c, vb, XHB, p, s, core_b, cut=99):
    si = p * NSUB + s
    WCB = vb["WCB"]; BBC = vb["BBC"]; MCUM = vb["MCUM"]; CHI = vb["CHI"]
    AMASK = vb["AMASK"]; IDB = vb["IDB"]; ZEROB = vb["ZEROB"]
    P384 = vb["P384"]; SC64 = vb["SC64"]; NLB = vb["NLB"]; DELTA = vb["DELTA"]
    NCC = vb["NCC"]
    EM = vb["EM"]; EP = vb["EP"]; QTM = vb["QTM"]; KTM = vb["KTM"]
    TKF = vb["TKF"]; VB = vb["VB"]; KTP = vb["KTP"]; QTall = vb["QTall"]
    USTG = vb["USTG"]; OISTG = vb["OISTG"]; PCST = vb["PCST"]
    pPP = vb["pPP"]; pU = vb["pU"]; pTQ = vb["pTQ"]
    pOXv = vb.get("pOX")
    if pOXv is not None:
        pNCa, pNCb = pOXv[:, 128:192], pOXv[0:64, 192:194]
    else:
        pNC = vb["pNC"]
        pNCa, pNCb = pNC[:, 0:64], pNC[0:64, 64:66]
    pA = vb["pR"]; pO = vb["pM"]
    st = si * 128

    prjP = c.sem("prjP"); prj = c.sem("prj"); scA = c.sem("scA")
    dvN = c.sem("dvN"); pePC = c.sem("pePC"); nclam = c.sem("nclam")
    scE = c.sem("scE"); dvQ = c.sem("dvQ"); peU = c.sem("peU")
    peT = c.sem("peT"); uev = c.sem("uev"); tqv = c.sem("tqv")
    peA = c.sem("peA"); sel = c.sem("sel"); peO = c.sem("peO")
    oev = c.sem("oev")

    c.t.wait_ge(c.sem("lnV").h, c.sem("lnV").n)
    c.t.wait_ge(prj.h, prj.n)
    if vb.get("pOX") is None:
        c.t.wait_ge(c.sem("otv").h, c.sem("otv").n)
    for ci in range(DC):
        inst = c.t.matmul(pPP[:, 0:384],
                          XHB[:, ci * TP + s * 128:ci * TP + (s + 1) * 128],
                          WCB[:, ci * 384:(ci + 1) * 384],
                          start=(ci == 0), stop=(ci == DC - 1))
    prjP.inc(inst)
    c.v.wait_ge(prjP.h, prjP.n)
    c.v.wait_ge(c.sem("scA").h, scA.n)
    prj.inc(c.v.tensor_tensor(P384[:, :], pPP[:, 0:384], BBC[:, :], ALU.add))
    if cut <= 1:
        return
    zb = P384[:, 256:320]; zd = P384[:, 320:384]
    t0 = SC64[:, 0:64]; t1 = SC64[:, 64:128]
    t2 = SC64[:, 128:192]; t3 = SC64[:, 192:256]
    
    c.s.wait_ge(prj.h, prj.n)
    c.s.wait_ge(dvN.h, dvN.n)
    c.s.activation(t0, zb, AF.Abs)
    c.s.activation(t0, t0, AF.Exp, scale=f32(-1.0))
    c.s.activation(t0, t0, AF.Ln, bias=f32(1.0))
    c.s.activation(t1, zb, AF.Relu, scale=f32(-1.0))
    c.s.activation(t2, zd, AF.Abs)
    c.s.activation(t2, t2, AF.Exp, scale=f32(-1.0))
    c.s.activation(t2, t2, AF.Ln, bias=f32(1.0))
    scA.inc(c.s.activation(t3, zd, AF.Relu))
    c.v.wait_ge(scA.h, scA.n)
    c.v.wait_ge(pePC.h, pePC.n)
    c.v.tensor_tensor(NLB[:, :], t1, t0, ALU.add)
    dvN.inc(c.v.tensor_tensor(DELTA[:, :], t3, t2, ALU.add))
    if cut <= 2:
        return
    c.t.wait_ge(dvN.h, dvN.n)
    c.t.wait_ge(nclam.h, nclam.n)
    if pOXv is not None:
        c.t.wait_ge(c.sem("otv").h, c.sem("otv").n)
    c.t.matmul(pNCa, MCUM[:, :], NLB[:, :], start=True, stop=True)
    pePC.inc(c.t.matmul(pNCb, NLB[:, :], CHI[:, :], start=True, stop=True))
    c.v.wait_ge(pePC.h, pePC.n)
    c.v.tensor_scalar(NCC[:, 0:64], pNCa, 85.0, None, ALU.min)
    nclam.inc(c.v.tensor_scalar(NCC[0:64, 64:66], pNCb, 85.0, None, ALU.min))
    c.s.wait_ge(nclam.h, nclam.n)
    c.s.activation(EM[:, :], NCC[:, 0:64], AF.Exp, scale=f32(-1.0))
    c.s.activation(EP[:, :], NCC[:, 0:64], AF.Exp)
    scE.inc(c.s.activation(NCC[0:64, 66:68], NCC[0:64, 64:66], AF.Exp, scale=f32(-1.0)))
    c.v.wait_ge(scE.h, scE.n)
    c.v.wait_ge(peU.h, peU.n)
    c.v.wait_ge(peT.h, peT.n)
    c.v.wait_ge(peO.h, peO.n)
    c.v.tensor_tensor(QTM[0:64, 0:64], P384[0:64, 0:64], EM[0:64, :], ALU.mult)
    c.v.tensor_tensor(QTM[64:128, 64:128], P384[64:128, 0:64], EM[64:128, :], ALU.mult)
    c.v.tensor_tensor(TKF[:, :], P384[:, 64:128], DELTA[:, :], ALU.mult)
    c.v.tensor_tensor(KTM[0:64, 0:64], TKF[0:64, :], EP[0:64, :], ALU.mult)
    c.v.tensor_tensor(KTM[64:128, 64:128], TKF[64:128, :], EP[64:128, :], ALU.mult)
    c.v.tensor_copy(VB[:, :], P384[:, 128:256])
    dvQ.inc(c.v.tensor_copy(PCST[:, 2 * si:2 * si + 2], NCC[0:64, 66:68]))
    if cut <= 3:
        return
    c.t.wait_ge(dvQ.h, dvQ.n)
    c.t.wait_ge(uev.h, uev.n)
    c.t.matmul(pU[:, 0:128], KTM[:, 0:64], VB[:, :], start=True, stop=True)
    peU.inc(c.t.matmul(pU[:, 128:256], KTM[:, 64:128], VB[:, :],
                       start=True, stop=True))
    c.t.wait_ge(tqv.h, tqv.n)
    c.t.transpose(pTQ[:, 0:128], QTM[:, :], IDB[:, :])
    peT.inc(c.t.transpose(pTQ[:, 128:256], KTM[:, :], IDB[:, :]))
    c.v.wait_ge(peU.h, peU.n)
    c.v.tensor_copy(USTG[:, (2 * si) * 128:(2 * si + 1) * 128], pU[:, 0:128])
    uev.inc(c.v.tensor_copy(USTG[:, (2 * si + 1) * 128:(2 * si + 2) * 128],
                            pU[:, 128:256]))
    c.v.wait_ge(peT.h, peT.n)
    c.v.wait_ge(peA.h, peA.n)
    c.v.tensor_copy(QTall[:, st:st + 128], pTQ[:, 0:128])
    tqv.inc(c.v.tensor_copy(KTP[:, :], pTQ[:, 128:256]))
    if cut <= 4:
        return
    c.t.wait_ge(tqv.h, tqv.n)
    c.t.wait_ge(sel.h, sel.n)
    peA.inc(c.t.matmul(pA[:, 0:128], KTP[:, :], QTall[:, st:st + 128],
                       start=True, stop=True))
    c.v.wait_ge(peA.h, peA.n)
    sel.inc(c.v.select(vb["AB"][:, :], AMASK[:, :], pA[:, 0:128], ZEROB[:, :]))
    if cut <= 5:
        return
    c.t.wait_ge(sel.h, sel.n)
    c.t.wait_ge(oev.h, oev.n)
    peO.inc(c.t.matmul(pO[:, 0:128], vb["AB"][:, :], VB[:, :],
                       start=True, stop=True))
    c.v.wait_ge(peO.h, peO.n)
    oev.inc(c.v.tensor_copy(vb["OIF"][:, :], pO[:, 0:128]))
    if cut <= 6:
        return
    peOT = c.sem("peOT"); otv = c.sem("otv")
    pOX = pOXv
    if pOX is None:
        pOX = pPP
        c.t.wait_ge(prj.h, prj.n)
    else:
        c.t.wait_ge(nclam.h, nclam.n)
    c.t.wait_ge(oev.h, oev.n)
    c.t.wait_ge(otv.h, otv.n)
    peOT.inc(c.t.transpose(pOX[:, 0:128], vb["OIF"][:, :], vb["IDF"][:, :]))
    c.v.wait_ge(peOT.h, peOT.n)
    otv.inc(c.v.tensor_copy(OISTG[:, si * 128:(si + 1) * 128], pOX[:, 0:128]))


def build_gla_a_launch(npass=NPASS, nsub=NSUB, cut=99):
    nc = bass.Bass("TRN2", target_bir_lowering=False)
    xin = nc.dram_tensor("xin", [128, DC * TOKS], F32, kind="ExternalInput").ap()
    wcat = nc.dram_tensor("wcat", [128, DC * 384], F32, kind="ExternalInput").ap()
    brow = nc.dram_tensor("brow", [1, 384], F32, kind="ExternalInput").ap()
    mcum = nc.dram_tensor("mcum", [128, 128], F32, kind="ExternalInput").ap()
    chi = nc.dram_tensor("chi", [128, 2], F32, kind="ExternalInput").ap()
    amask = nc.dram_tensor("amask", [128, 128], U8, kind="ExternalInput").ap()
    identb = nc.dram_tensor("identb", [128, 128], BF16, kind="ExternalInput").ap()
    identf = nc.dram_tensor("identf", [128, 128], F32, kind="ExternalInput").ap()
    uown = nc.dram_tensor("uown", [2048, 128], F32, kind="ExternalOutput").ap()
    pcown = nc.dram_tensor("pcown", [64, 32], F32, kind="ExternalOutput").ap()
    qtown = nc.dram_tensor("qtown", [128, TOKS], BF16, kind="ExternalOutput").ap()
    oiown = nc.dram_tensor("oiown", [128, TOKS], F32, kind="ExternalOutput").ap()

    with ExitStack() as st:
        c = Ctx(nc, st)
        vb = {}
        vb["XB"] = c.sbuf("XB", [128, DC * TP], BF16)
        vb["XQ"] = c.sbuf("XQ", [128, DC * TP], BF16)
        vb["XHB"] = c.sbuf("XHB", [128, DC * TP], BF16)
        vb["SROW"] = c.sbuf("SROW", [1, 5 * TP], F32)
        vb["onesb"] = c.sbuf("onesb", [128, 1], BF16)
        vb["ones1f"] = c.sbuf("ones1f", [1, 128], F32)
        vb["onesbb"] = c.sbuf("onesbb", [128, 128], BF16)
        vb["LROW"] = c.sbuf("LROW", [128, 4 * TP], F32)
        vb["WCB"] = c.sbuf("WCB", [128, DC * 384], BF16)
        vb["BBC"] = c.sbuf("BBC", [128, 384], F32)
        vb["MCUM"] = c.sbuf("MCUM", [128, 128], F32)
        vb["CHI"] = c.sbuf("CHI", [128, 2], F32)
        vb["AMASK"] = c.sbuf("AMASK", [128, 128], U8)
        vb["IDB"] = c.sbuf("IDB", [128, 128], BF16)
        vb["ZEROB"] = c.sbuf("ZEROB", [128, 128], BF16)
        vb["P384"] = c.sbuf("P384", [128, 384], F32)
        vb["SC64"] = c.sbuf("SC64", [128, 256], F32)
        vb["NCC"] = c.sbuf("NCC", [128, 68], F32)
        vb["NLB"] = c.sbuf("NLB", [128, 64], F32)
        vb["DELTA"] = c.sbuf("DELTA", [128, 64], F32)
        vb["EM"] = c.sbuf("EM", [128, 64], F32)
        vb["EP"] = c.sbuf("EP", [128, 64], F32)
        vb["QTM"] = c.sbuf("QTM", [128, 128], BF16)
        vb["KTM"] = c.sbuf("KTM", [128, 128], BF16)
        vb["TKF"] = c.sbuf("TKF", [128, 64], F32)
        vb["VB"] = c.sbuf("VB", [128, 128], BF16)
        vb["KTP"] = c.sbuf("KTP", [128, 128], BF16)
        vb["QTall"] = c.sbuf("QTall", [128, TOKS], BF16)
        vb["AB"] = c.sbuf("AB", [128, 128], BF16)
        vb["USTG"] = c.sbuf("USTG", [64, 32 * 128], F32)
        vb["OISTG"] = c.sbuf("OISTG", [128, DC * 128], F32)
        vb["OIF"] = c.sbuf("OIF", [128, 128], F32)
        vb["IDF"] = c.sbuf("IDF", [128, 128], F32)
        vb["PCST"] = c.sbuf("PCST", [64, 32], F32)
        XF = c.sbuf("XF", [128, DC * TP], F32)
        vb["pS1"] = c.psum("pS1", [128, TP])
        vb["pS2"] = c.psum("pS2", [128, TP])
        vb["pR"] = c.psum("pR", [128, TP])
        vb["pM"] = c.psum("pM", [128, TP])
        vb["pPP"] = c.psum("pPP", [128, 448])
        vb["pNC"] = c.psum("pNC", [128, 66])
        vb["pU"] = c.psum("pU", [64, 256])
        vb["pTQ"] = c.psum("pTQ", [128, 256], BF16)
        block = st.enter_context(nc.Block())

        @block.gpsimd
        def _(_g):
            cst = c.sem("cst")
            cst.inc(c.sy.dma_start(out=vb["MCUM"][:, :], in_=mcum[:, :]), 16)
            cst.inc(c.sy.dma_start(out=vb["CHI"][:, :], in_=chi[:, :]), 16)
            cst.inc(c.sy.dma_start(out=vb["AMASK"][:, :], in_=amask[:, :]), 16)
            cst.inc(c.sy.dma_start(out=vb["IDB"][:, :], in_=identb[:, :]), 16)
            cst.inc(c.sy.dma_start(out=vb["IDF"][:, :], in_=identf[:, :]), 16)
            cstg = c.sem("cstg")
            cstg.inc(c.g.dma_start(out=vb["WCB"][:, :], in_=wcat[:, :]), 16)
            brsb = vb["SROW"][0:1, 0:384]
            cst.inc(c.sy.dma_start(out=brsb, in_=brow[:, :]), 16)
            c.v.wait_ge(cst.h, cst.n)
            c.v.wait_ge(cstg.h, cstg.n)
            c.t.wait_ge(cstg.h, cstg.n)
            c.v.memset(vb["onesb"][:, :], 1.0)
            c.v.memset(vb["onesbb"][:, :], 1.0)
            c.v.memset(vb["ones1f"][:, :], 1.0)
            c.v.memset(vb["ZEROB"][:, :], 0.0)
            c.v.memset(vb["QTall"][:, :], 0.0)
            c.v.memset(vb["QTM"][:, :], 0.0)
            c.v.memset(vb["KTM"][:, :], 0.0)
            mz = c.v.memset(vb["KTP"][:, :], 0.0)
            c.sem("dvI").inc(mz)
            c.t.wait_ge(c.sem("dvI").h, c.sem("dvI").n)
            bb = c.t.matmul(vb["pR"][:, 0:384], vb["ones1f"][:, :], brsb,
                            start=True, stop=True)
            c.sem("lnP").inc(bb)
            c.v.wait_ge(c.sem("lnP").h, c.sem("lnP").n)
            ini = c.v.tensor_copy(vb["BBC"][:, :], vb["pR"][:, 0:384])
            c.sem("lnV").inc(ini)
            xld = c.sem("xld")
            for p in range(npass):
                c.sy.wait_ge(c.sem("lnV").h, c.sem("lnV").n)
                xld.inc(c.sy.dma_start(
                    out=XF[:, :].rearrange("pp (cc t) -> pp cc t", cc=DC),
                    in_=xin.rearrange("pp (cc t) -> pp cc t", cc=DC)
                    [:, :, p * TP:(p + 1) * TP]), 16)
                c.s.wait_ge(xld.h, xld.n)
                emit_ln(c, vb, XF, vb["XHB"],
                        extra_bcast_waits=(c.sem("sel"), c.sem("oev")))
                for s in range(nsub):
                    emit_gla_a_subtile(c, vb, vb["XHB"], p, s, 0, cut)
            fin = c.sem("fin")
            c.sy.wait_ge(c.sem("uev").h, c.sem("uev").n)
            c.sy.wait_ge(c.sem("oev").h, c.sem("oev").n)
            c.sy.wait_ge(c.sem("dvQ").h, c.sem("dvQ").n)
            c.sy.wait_ge(c.sem("tqv").h, c.sem("tqv").n)
            fin.inc(c.sy.dma_start(
                out=uown.rearrange("(j k) v -> k j v", k=64),
                in_=vb["USTG"][:, :].rearrange("k (j v) -> k j v", v=128)), 16)
            fin.inc(c.sy.dma_start(out=pcown[:, :], in_=vb["PCST"][:, :]), 16)
            fin.inc(c.sy.dma_start(out=qtown[:, :], in_=vb["QTall"][:, :]), 16)
            fin.inc(c.sy.dma_start(out=oiown[:, :], in_=vb["OISTG"][:, :]), 16)
            c.g.wait_ge(fin.h, fin.n)

    return nc




def build_gla_b_launch():
    nc = bass.Bass("TRN2", target_bir_lowering=False)
    xin = nc.dram_tensor("xin", [128, DC * TOKS], F32, kind="ExternalInput").ap()
    uin = nc.dram_tensor("uin", [128 * 128, 128], F32, kind="ExternalInput").ap()
    pcin = nc.dram_tensor("pcin", [128, 128], F32, kind="ExternalInput").ap()
    qtin = nc.dram_tensor("qtin", [128, TOKS], BF16, kind="ExternalInput").ap()
    oiin = nc.dram_tensor("oiin", [128, TOKS], F32, kind="ExternalInput").ap()
    wo = nc.dram_tensor("wo", [128, D], F32, kind="ExternalInput").ap()
    xout = nc.dram_tensor("xout", [128, DC * TOKS], F32, kind="ExternalOutput").ap()

    with ExitStack() as st:
        c = Ctx(nc, st)
        SCAN = c.sbuf("SCAN", [128, 129 * 128], F32)
        PCsb = c.sbuf("PCsb", [128, 128], F32)
        QTsb = c.sbuf("QTsb", [128, TOKS], F32)
        OIT = c.sbuf("OIT", [128, TOKS], F32)
        OTall = c.sbuf("OTall", [128, TOKS], BF16)
        WOB = c.sbuf("WOB", [128, D], BF16)
        XF = c.sbuf("XF", [128, DC * TP], F32)
        pOT = c.psum("pOT", [128, 64])
        pA = c.psum("pA", [128, TP])
        pB = c.psum("pB", [128, TP])
        block = st.enter_context(nc.Block())

        @block.gpsimd
        def _(_g):
            ld = c.sem("ld")
            ld.inc(c.sy.dma_start(
                out=SCAN[:, 128:].rearrange("k (j v) -> k j v", v=128),
                in_=uin.rearrange("(j k) v -> k j v", k=128)), 16)
            ld.inc(c.sy.dma_start(out=PCsb[:, :], in_=pcin[:, :]), 16)
            ld.inc(c.g.dma_start(out=QTsb[:, :], in_=qtin[:, :]), 16)
            ld.inc(c.sy.dma_start(out=OIT[:, :], in_=oiin[:, :]), 16)
            ld.inc(c.g.dma_start(out=WOB[:, :], in_=wo[:, :]), 16)
            dvI = c.sem("dvI")
            dvI.inc(c.v.memset(SCAN[:, 0:128], 0.0))
            c.v.wait_ge(ld.h, ld.n)
            scn = c.sem("scn")
            view = SCAN[:, 128:].rearrange("p (j v) -> p v j", v=128)
            for vi in range(128):
                inst = c.v.tensor_tensor_scan(view[:, vi, :], view[:, vi, :],
                                              PCsb[:, :], 0.0, ALU.add, ALU.mult)
            scn.inc(inst)

            xld = c.sem("xld"); xst = c.sem("xst")
            peI = c.sem("peI"); otv = c.sem("otv")
            peY = c.sem("peY"); yev = c.sem("yev")
            c.t.wait_ge(scn.h, scn.n)
            c.t.wait_ge(dvI.h, dvI.n)
            for p in range(NPASS):
                c.sy.wait_ge(yev.h, yev.n)
                c.sy.wait_ge(xst.h, xst.n)
                xld.inc(c.sy.dma_start(
                    out=XF[:, :].rearrange("pp (cc t) -> pp cc t", cc=DC),
                    in_=xin.rearrange("pp (cc t) -> pp cc t", cc=DC)
                    [:, :, p * TP:(p + 1) * TP]), 16)
                for s in range(NSUB):
                    si = p * NSUB + s
                    st_ = si * 128
                    for h in range(2):
                        jloc = 2 * si + h
                        slot = 96 + jloc
                        c.t.wait_ge(otv.h, otv.n)
                        mm = c.t.matmul(pOT[:, 0:64],
                                        SCAN[:, slot * 128:(slot + 1) * 128],
                                        QTsb[:, st_ + h * 64:st_ + (h + 1) * 64],
                                        start=True, stop=True)
                        peI.inc(mm)
                        c.v.wait_ge(peI.h, peI.n)
                        ot = c.v.tensor_tensor(
                            OTall[:, st_ + h * 64:st_ + (h + 1) * 64],
                            pOT[:, 0:64],
                            OIT[:, st_ + h * 64:st_ + (h + 1) * 64], ALU.add)
                        otv.inc(ot)
                c.t.wait_ge(otv.h, otv.n)
                c.t.wait_ge(xld.h, xld.n)
                for oc in range(DC):
                    pp = [pA, pB][oc % 2]
                    if oc >= 2:
                        c.t.wait_ge(yev.h, yev.n - (DC - oc) - 1 + 1
                                    if False else yev.n)
                    mm = c.t.matmul(pp[:, :TP], WOB[:, oc * 128:(oc + 1) * 128],
                                    OTall[:, p * TP:(p + 1) * TP],
                                    start=True, stop=True)
                    peY.inc(mm)
                    c.v.wait_ge(peY.h, peY.n)
                    yv = c.v.tensor_tensor(XF[:, oc * TP:(oc + 1) * TP],
                                           pp[:, :TP],
                                           XF[:, oc * TP:(oc + 1) * TP], ALU.add)
                    yev.inc(yv)
                c.sy.wait_ge(yev.h, yev.n)
                xst.inc(c.sy.dma_start(
                    out=xout.rearrange("pp (cc t) -> pp cc t", cc=DC)
                    [:, :, p * TP:(p + 1) * TP],
                    in_=XF[:, :].rearrange("pp (cc t) -> pp cc t", cc=DC)), 16)
            c.g.wait_ge(xst.h, xst.n)

    return nc


def emit_had_transform(c, vb, w_sl, out_sl, nch, wcols, hconst, su, wsrow, tag):
    ZA = vb["ZA"]; ZB = vb["ZB"]; WOUT = vb["WOUT"]
    pH = vb["pH"]; pW = vb["pW"]
    W = 512 if nch == 16 else 128
    njs = wcols // W
    ld = c.sem("hld"); pe = c.sem("hpe"); dv = c.sem("hdv"); so = c.sem("hso")
    for js in range(njs):
        c.sy.wait_ge(dv.h, dv.n)
        c.sy.wait_ge(so.h, so.n)
        ld.inc(c.sy.dma_start(
            out=ZA[:, 0:nch * W].rearrange("p (cc w) -> p cc w", cc=nch),
            in_=w_sl.rearrange("(cc p) j -> p cc j", p=128)
            [:, :, js * W:(js + 1) * W]), 16)
        c.t.wait_ge(ld.h, ld.n)
        c.t.wait_ge(dv.h, dv.n)
        dv0 = dv.n
        for cc in range(nch):
            pp = pH[cc % 2]
            if cc >= 2:
                c.t.wait_ge(dv.h, dv0 + cc - 1)
            inst = c.t.matmul(pp[:, 0:W], hconst[:, :],
                              ZA[:, cc * W:(cc + 1) * W], start=True, stop=True)
            pe.inc(inst)
            c.v.wait_ge(pe.h, pe.n)
            dv.inc(c.v.tensor_copy(ZB[:, cc * W:(cc + 1) * W], pp[:, 0:W]))
        bufs = [ZB, ZA]
        cur = 0
        d = 1
        while d < nch:
            src_, dst_ = bufs[cur], bufs[1 - cur]
            for cc in range(nch):
                partner = cc ^ d
                op = ALU.add if (cc & d) == 0 else ALU.subtract
                a = cc if (cc & d) == 0 else partner
                b = partner if (cc & d) == 0 else cc
                inst = c.v.tensor_tensor(dst_[:, cc * W:(cc + 1) * W],
                                         src_[:, a * W:(a + 1) * W],
                                         src_[:, b * W:(b + 1) * W], op)
            dv.inc(inst)
            cur = 1 - cur
            d *= 2
        fin = bufs[cur]
        c.t.wait_ge(dv.h, dv.n)
        wsmm = c.t.matmul(pW[:, 0:W], vb["ones1f"][:, :],
                          wsrow[0:1, js * W:(js + 1) * W], start=True, stop=True)
        pe.inc(wsmm)
        c.v.wait_ge(pe.h, pe.n)
        for cc in range(nch):
            inst = c.v.scalar_tensor_tensor(
                WOUT[:, cc * W:(cc + 1) * W], fin[:, cc * W:(cc + 1) * W],
                su[:, cc:cc + 1], pW[:, 0:W], ALU.mult, ALU.mult)
        dv.inc(inst)
        c.sy.wait_ge(dv.h, dv.n)
        ocl_per_js = W // 128
        outv = out_sl.rearrange("p (ocl cc m) -> p ocl cc m", cc=nch, m=128)
        winv = WOUT[:, 0:nch * W].rearrange("p (cc ocl m) -> p cc ocl m",
                                            ocl=ocl_per_js, m=128)
        for ol in range(ocl_per_js):
            so.inc(c.sy.dma_start(
                out=outv[:, js * ocl_per_js + ol, :, :],
                in_=winv[:, :, ol, :]), 16)
    return so


def build_prep_launch():
    nc = bass.Bass("TRN2", target_bir_lowering=False)
    wup = nc.dram_tensor("wup", [2048, 1024], F32, kind="ExternalInput").ap()
    wga = nc.dram_tensor("wga", [2048, 1024], F32, kind="ExternalInput").ap()
    wdo = nc.dram_tensor("wdo", [8192, 256], F32, kind="ExternalInput").ap()
    h128u = nc.dram_tensor("h128u", [128, 128], F32, kind="ExternalInput").ap()
    h128d = nc.dram_tensor("h128d", [128, 128], F32, kind="ExternalInput").ap()
    suug = nc.dram_tensor("suug", [128, 16], F32, kind="ExternalInput").ap()
    sudp = nc.dram_tensor("sudp", [128, 64], F32, kind="ExternalInput").ap()
    wsup = nc.dram_tensor("wsup", [1, 1024], F32, kind="ExternalInput").ap()
    wsga = nc.dram_tensor("wsga", [1, 1024], F32, kind="ExternalInput").ap()
    wsdo = nc.dram_tensor("wsdo", [1, 256], F32, kind="ExternalInput").ap()
    oup = nc.dram_tensor("oup", [128, 8 * 16 * 128], BF16, kind="ExternalOutput").ap()
    oga = nc.dram_tensor("oga", [128, 8 * 16 * 128], BF16, kind="ExternalOutput").ap()
    odo = nc.dram_tensor("odo", [128, 2 * 64 * 128], BF16, kind="ExternalOutput").ap()

    with ExitStack() as st:
        c = Ctx(nc, st)
        vb = {}
        vb["ZA"] = c.sbuf("ZA", [128, 8192], F32)
        vb["ZB"] = c.sbuf("ZB", [128, 8192], F32)
        vb["WOUT"] = c.sbuf("WOUT", [128, 8192], BF16)
        vb["ones1f"] = c.sbuf("ones1f", [1, 128], F32)
        HU = c.sbuf("HU", [128, 128], F32)
        HDN = c.sbuf("HDN", [128, 128], F32)
        SUU = c.sbuf("SUU", [128, 16], F32)
        SUD = c.sbuf("SUD", [128, 64], F32)
        WSU = c.sbuf("WSU", [1, 1024], F32)
        WSG = c.sbuf("WSG", [1, 1024], F32)
        WSD = c.sbuf("WSD", [1, 256], F32)
        vb["pH"] = [c.psum("pH0", [128, 512]), c.psum("pH1", [128, 512])]
        vb["pW"] = c.psum("pWb", [128, 512])
        block = st.enter_context(nc.Block())

        @block.gpsimd
        def _(_g):
            cst = c.sem("cst")
            for dst, srcap in [(HU, h128u), (HDN, h128d), (SUU, suug),
                               (SUD, sudp), (WSU, wsup), (WSG, wsga), (WSD, wsdo)]:
                cst.inc(c.sy.dma_start(out=dst[:, :], in_=srcap[:, :]), 16)
            c.v.wait_ge(cst.h, cst.n)
            dvi = c.sem("hdv")
            dvi.inc(c.v.memset(vb["ones1f"][:, :], 1.0))
            c.t.wait_ge(dvi.h, dvi.n)
            emit_had_transform(c, vb, wup, oup, 16, 1024, HU, SUU, WSU, "up")
            emit_had_transform(c, vb, wga, oga, 16, 1024, HU, SUU, WSG, "ga")
            so = emit_had_transform(c, vb, wdo, odo, 64, 256, HDN, SUD, WSD, "do")
            c.g.wait_ge(so.h, so.n)

    return nc


def build_l1ffn_launch():
    nc = bass.Bass("TRN2", target_bir_lowering=False)
    xin = nc.dram_tensor("xin", [128, DC * TOKS], F32, kind="ExternalInput").ap()
    wub = nc.dram_tensor("wub", [128, 64 * DC * 128], BF16, kind="ExternalInput").ap()
    wgb = nc.dram_tensor("wgb", [128, 64 * DC * 128], BF16, kind="ExternalInput").ap()
    wdb = nc.dram_tensor("wdb", [128, DC * 64 * 128], BF16, kind="ExternalInput").ap()
    xout = nc.dram_tensor("xout", [128, DC * TOKS], F32, kind="ExternalOutput").ap()
    IC = 64

    with ExitStack() as st:
        c = Ctx(nc, st)
        vb = {}
        vb["XB"] = c.sbuf("XB", [128, DC * TP], BF16)
        vb["XQ"] = c.sbuf("XQ", [128, DC * TP], BF16)
        vb["XHB"] = c.sbuf("XHB", [128, DC * TP], BF16)
        vb["SROW"] = c.sbuf("SROW", [1, 5 * TP], F32)
        vb["onesb"] = c.sbuf("onesb", [128, 1], BF16)
        vb["ones1f"] = c.sbuf("ones1f", [1, 128], F32)
        vb["onesbb"] = c.sbuf("onesbb", [128, 128], BF16)
        vb["LROW"] = c.sbuf("LROW", [128, 4 * TP], F32)
        XF = c.sbuf("XF", [128, DC * TP], F32)
        GB = c.sbuf("GB", [128, IC * TP], BF16)
        SGT = c.sbuf("SGT", [128, TP], BF16)
        wbu = [c.sbuf("wbu0", [128, DC * 128], BF16), c.sbuf("wbu1", [128, DC * 128], BF16)]
        wbg = [c.sbuf("wbg0", [128, DC * 128], BF16), c.sbuf("wbg1", [128, DC * 128], BF16)]
        wbd = [c.sbuf("wbd0", [128, IC * 128], BF16), c.sbuf("wbd1", [128, IC * 128], BF16)]
        vb["pS1"] = c.psum("pS1", [128, TP])
        vb["pS2"] = c.psum("pS2", [128, TP])
        vb["pR"] = c.psum("pR", [128, TP])
        vb["pM"] = c.psum("pM", [128, TP])
        pG1 = c.psum("pG1", [128, TP])
        pG2 = c.psum("pG2", [128, TP])
        pY = [c.psum("pY0", [128, TP]), c.psum("pY1", [128, TP])]
        block = st.enter_context(nc.Block())

        @block.gpsimd
        def _(_g):
            dvi = c.sem("lnV")
            c.v.memset(vb["onesb"][:, :], 1.0)
            c.v.memset(vb["onesbb"][:, :], 1.0)
            dvi.inc(c.v.memset(vb["ones1f"][:, :], 1.0))
            xld = c.sem("xld"); xst = c.sem("xst")
            wldu = c.sem("wldu"); wldg = c.sem("wldg"); wldd = c.sem("wldd")
            wfree = c.sem("wfree"); gvd = c.sem("gvd"); sil = c.sem("sil")
            yev = c.sem("yev")
            gpair = [(pG1, pG2), (vb["pR"], vb["pM"])]
            for p in range(NPASS):
                c.sy.wait_ge(yev.h, yev.n)
                c.sy.wait_ge(xst.h, xst.n)
                xld.inc(c.sy.dma_start(
                    out=XF[:, :].rearrange("pp (cc t) -> pp cc t", cc=DC),
                    in_=xin.rearrange("pp (cc t) -> pp cc t", cc=DC)
                    [:, :, p * TP:(p + 1) * TP]), 16)
                c.s.wait_ge(xld.h, xld.n)
                emit_ln(c, vb, XF, vb["XHB"], extra_bcast_waits=(gvd,))
                lnV = c.sem("lnV")
                wfree0, gvd0, sil0 = wfree.n, gvd.n, sil.n
                wldu0, wldg0 = wldu.n, wldg.n
                span = DC * 128
                for hc in range(IC):
                    bu = wbu[hc % 2]; bg = wbg[hc % 2]
                    if hc >= 2:
                        c.sy.wait_ge(wfree.h, wfree0 + hc - 1)
                        c.s.wait_ge(wfree.h, wfree0 + hc - 1)
                    wldu.inc(c.sy.dma_start(out=bu[:, :],
                                            in_=wub[:, hc * span:(hc + 1) * span]), 16)
                    wldg.inc(c.s.dma_start(out=bg[:, :],
                                           in_=wgb[:, hc * span:(hc + 1) * span]), 16)
                    ppu, ppg = gpair[hc % 2]
                    c.t.wait_ge(wldu.h, wldu0 + 16 * (hc + 1))
                    c.t.wait_ge(wldg.h, wldg0 + 16 * (hc + 1))
                    if hc == 0:
                        c.t.wait_ge(lnV.h, lnV.n)
                    if hc >= 2:
                        c.t.wait_ge(gvd.h, gvd0 + hc - 1)
                    elif p > 0 and hc < 2:
                        c.t.wait_ge(gvd.h, gvd0)
                    for ci in range(DC):
                        c.t.matmul(ppu[:, :TP], bu[:, ci * 128:(ci + 1) * 128],
                                   vb["XHB"][:, ci * TP:(ci + 1) * TP],
                                   start=(ci == 0), stop=(ci == DC - 1))
                    for ci in range(DC):
                        inst = c.t.matmul(ppg[:, :TP], bg[:, ci * 128:(ci + 1) * 128],
                                          vb["XHB"][:, ci * TP:(ci + 1) * TP],
                                          start=(ci == 0), stop=(ci == DC - 1))
                    wfree.inc(inst)
                    c.s.wait_ge(wfree.h, wfree0 + hc + 1)
                    c.s.wait_ge(gvd.h, gvd0 + hc)
                    sil.inc(c.s.activation(SGT[:, :], ppg[:, :TP], AF.Silu))
                    c.v.wait_ge(sil.h, sil0 + hc + 1)
                    gvd.inc(c.v.tensor_tensor(GB[:, hc * TP:(hc + 1) * TP],
                                              ppu[:, :TP], SGT[:, :], ALU.mult))
                wldd0, yev0 = wldd.n, yev.n
                dspan = IC * 128
                for dc in range(DC):
                    bd = wbd[dc % 2]
                    if dc >= 2:
                        c.sy.wait_ge(yev.h, yev0 + dc - 1)
                    wldd.inc(c.sy.dma_start(out=bd[:, :],
                                            in_=wdb[:, dc * dspan:(dc + 1) * dspan]), 16)
                    pp = pY[dc % 2]
                    c.t.wait_ge(wldd.h, wldd0 + 16 * (dc + 1))
                    if dc == 0:
                        c.t.wait_ge(gvd.h, gvd.n)
                    if dc >= 2:
                        c.t.wait_ge(yev.h, yev0 + dc - 1)
                    for fc in range(IC):
                        inst = c.t.matmul(pp[:, :TP], bd[:, fc * 128:(fc + 1) * 128],
                                          GB[:, fc * TP:(fc + 1) * TP],
                                          start=(fc == 0), stop=(fc == IC - 1))
                    wfree.inc(inst)
                    c.v.wait_ge(wfree.h, wfree.n)
                    yev.inc(c.v.tensor_tensor(XF[:, dc * TP:(dc + 1) * TP],
                                              pp[:, :TP],
                                              XF[:, dc * TP:(dc + 1) * TP], ALU.add))
                c.sy.wait_ge(yev.h, yev.n)
                xst.inc(c.sy.dma_start(
                    out=xout.rearrange("pp (cc t) -> pp cc t", cc=DC)
                    [:, :, p * TP:(p + 1) * TP],
                    in_=XF[:, :].rearrange("pp (cc t) -> pp cc t", cc=DC)), 16)
            c.g.wait_ge(xst.h, xst.n)

    return nc


def build_ln5_launch():
    nc = bass.Bass("TRN2", target_bir_lowering=False)
    xin = nc.dram_tensor("xin", [128, DC * TOKS], F32, kind="ExternalInput").ap()
    xh = nc.dram_tensor("xh", [128, DC * TOKS], BF16, kind="ExternalOutput").ap()
    with ExitStack() as st:
        c = Ctx(nc, st)
        vb = {}
        vb["XB"] = c.sbuf("XB", [128, DC * TP], BF16)
        vb["XQ"] = c.sbuf("XQ", [128, DC * TP], BF16)
        vb["XHB"] = c.sbuf("XHB", [128, DC * TP], BF16)
        vb["SROW"] = c.sbuf("SROW", [1, 5 * TP], F32)
        vb["onesb"] = c.sbuf("onesb", [128, 1], BF16)
        vb["ones1f"] = c.sbuf("ones1f", [1, 128], F32)
        vb["onesbb"] = c.sbuf("onesbb", [128, 128], BF16)
        vb["LROW"] = c.sbuf("LROW", [128, 4 * TP], F32)
        XF = c.sbuf("XF", [128, DC * TP], F32)
        vb["pS1"] = c.psum("pS1", [128, TP])
        vb["pS2"] = c.psum("pS2", [128, TP])
        vb["pR"] = c.psum("pR", [128, TP])
        vb["pM"] = c.psum("pM", [128, TP])
        block = st.enter_context(nc.Block())

        @block.gpsimd
        def _(_g):
            dvi = c.sem("lnV")
            c.v.memset(vb["onesb"][:, :], 1.0)
            c.v.memset(vb["onesbb"][:, :], 1.0)
            dvi.inc(c.v.memset(vb["ones1f"][:, :], 1.0))
            xld = c.sem("xld"); xst = c.sem("xst")
            for p in range(NPASS):
                c.sy.wait_ge(dvi.h, dvi.n)
                c.sy.wait_ge(xst.h, xst.n)
                xld.inc(c.sy.dma_start(
                    out=XF[:, :].rearrange("pp (cc t) -> pp cc t", cc=DC),
                    in_=xin.rearrange("pp (cc t) -> pp cc t", cc=DC)
                    [:, :, p * TP:(p + 1) * TP]), 16)
                c.s.wait_ge(xld.h, xld.n)
                lnV = emit_ln(c, vb, XF, vb["XHB"])
                c.sy.wait_ge(lnV.h, lnV.n)
                xst.inc(c.sy.dma_start(
                    out=xh.rearrange("pp (cc t) -> pp cc t", cc=DC)
                    [:, :, p * TP:(p + 1) * TP],
                    in_=vb["XHB"][:, :].rearrange("pp (cc t) -> pp cc t", cc=DC)), 16)
            c.g.wait_ge(xst.h, xst.n)
    return nc


def build_attn_launch():
    nc = bass.Bass("TRN2", target_bir_lowering=False)
    xh = nc.dram_tensor("xh", [128, DC * SEQ], BF16, kind="ExternalInput").ap()
    wqkv = nc.dram_tensor("wqkv", [128, 12 * DC * 128], F32, kind="ExternalInput").ap()
    bqkv = nc.dram_tensor("bqkv", [128, 12], F32, kind="ExternalInput").ap()
    cmask = nc.dram_tensor("cmask", [128, 4 * 512], BF16, kind="ExternalInput").ap()
    identb = nc.dram_tensor("identb", [128, 128], BF16, kind="ExternalInput").ap()
    oown = nc.dram_tensor("oown", [512, SEQ], BF16, kind="ExternalOutput").ap()
    NT = SEQ // TP
    NKT = SEQ // 128

    with ExitStack() as st:
        c = Ctx(nc, st)
        WB = c.sbuf("WB", [128, 12 * DC * 128], BF16)
        BQ = c.sbuf("BQ", [128, 12], F32)
        CM = c.sbuf("CM", [128, 4 * 512], BF16)
        IDB = c.sbuf("IDB", [128, 128], BF16)
        XQB = c.sbuf("XQB", [128, DC * TP], BF16)
        Q = c.sbuf("Q", [128, SEQ], BF16)
        Kt = c.sbuf("Kt", [128, SEQ], BF16)
        VT = c.sbuf("VT", [128, NKT * 129], BF16)
        VF = c.sbuf("VF", [128, TP], BF16)
        PT = [c.sbuf("PT0", [128, TP], BF16), c.sbuf("PT1", [128, TP], BF16)]
        OB = c.sbuf("OB", [128, 128], BF16)
        RC = c.sbuf("RC", [128, 1], F32)
        OST = c.sbuf("OST", [128, TP], BF16)
        pS = [c.psum("pSa", [128, TP]), c.psum("pSb", [128, TP])]
        pO = [c.psum("pO0", [128, TP]), c.psum("pO1", [128, TP]),
              c.psum("pO2", [128, TP]), c.psum("pO3", [128, TP])]
        pTR = c.psum("pTR", [128, 128], BF16)
        block = st.enter_context(nc.Block())

        @block.gpsimd
        def _(_g):
            cst = c.sem("cst")
            cstg = c.sem("cstg")
            cstg.inc(c.g.dma_start(out=WB[:, :], in_=wqkv[:, :]), 16)
            cst.inc(c.sy.dma_start(out=BQ[:, :], in_=bqkv[:, :]), 16)
            cst.inc(c.sy.dma_start(out=CM[:, :], in_=cmask[:, :]), 16)
            cst.inc(c.sy.dma_start(out=IDB[:, :], in_=identb[:, :]), 16)
            dvi = c.sem("dvi")
            dvi.inc(c.v.memset(VT[:, :], 1.0))
            xld = c.sem("xld"); peQ = c.sem("peQ"); qkv = c.sem("qkv")
            peV = c.sem("peV"); vtd = c.sem("vtd")
            peS = c.sem("peS"); scP = c.sem("scP"); msk = c.sem("msk")
            peA = c.sem("peA"); onrm = c.sem("onrm"); peOT = c.sem("peOT")
            ost = c.sem("ost"); fst = c.sem("fst")
            c.t.wait_ge(cst.h, cst.n)
            c.t.wait_ge(cstg.h, cstg.n)
            c.v.wait_ge(cst.h, cst.n)
            for hi in range(4):
                for tp in range(NT):
                    c.sy.wait_ge(peQ.h, peQ.n)
                    xld.inc(c.sy.dma_start(
                        out=XQB[:, :].rearrange("pp (cc t) -> pp cc t", cc=DC),
                        in_=xh.rearrange("pp (cc t) -> pp cc t", cc=DC)
                        [:, :, tp * TP:(tp + 1) * TP]), 16)
                    c.t.wait_ge(xld.h, xld.n)
                    for j in range(3):
                        pp = pS[j % 2]
                        wslice = WB[:, (hi * 3 + j) * DC * 128:(hi * 3 + j + 1) * DC * 128]
                        c.t.wait_ge(qkv.h, qkv.n)
                        c.t.wait_ge(scP.h, scP.n)
                        for ci in range(DC):
                            inst = c.t.matmul(pp[:, :TP],
                                              wslice[:, ci * 128:(ci + 1) * 128],
                                              XQB[:, ci * TP:(ci + 1) * TP],
                                              start=(ci == 0), stop=(ci == DC - 1))
                        peQ.inc(inst)
                        c.v.wait_ge(peQ.h, peQ.n)
                        c.v.wait_ge(peS.h, peS.n)
                        c.v.wait_ge(peA.h, peA.n)
                        bcol = BQ[:, hi * 3 + j:hi * 3 + j + 1]
                        if j == 0:
                            qkv.inc(c.v.tensor_scalar(Q[:, tp * TP:(tp + 1) * TP],
                                                      pp[:, :TP], bcol, None, ALU.add))
                        elif j == 1:
                            qkv.inc(c.v.tensor_scalar(Kt[:, tp * TP:(tp + 1) * TP],
                                                      pp[:, :TP], bcol, None, ALU.add))
                        else:
                            c.v.wait_ge(peV.h, peV.n)
                            qkv.inc(c.v.tensor_scalar(VF[:, :], pp[:, :TP],
                                                      bcol, None, ALU.add))
                            for u in range(4):
                                kt = tp * 4 + u
                                c.t.wait_ge(qkv.h, qkv.n)
                                c.t.wait_ge(vtd.h, vtd.n)
                                tr = c.t.transpose(pTR[:, :],
                                                   VF[:, u * 128:(u + 1) * 128],
                                                   IDB[:, :])
                                peV.inc(tr)
                                c.v.wait_ge(peV.h, peV.n)
                                vtd.inc(c.v.tensor_copy(
                                    VT[:, kt * 129:kt * 129 + 128], pTR[:, :]))
                for qs in range(NT):
                    nkc = 4 * qs + 4
                    for kc in range(nkc):
                        pp = pS[kc % 2]
                        pt = PT[kc % 2]
                        c.t.wait_ge(qkv.h, qkv.n)
                        c.t.wait_ge(scP.h, scP.n)
                        c.t.wait_ge(vtd.h, vtd.n)
                        inst = c.t.matmul(pp[:, :TP], Kt[:, kc * 128:(kc + 1) * 128],
                                          Q[:, qs * TP:(qs + 1) * TP],
                                          start=True, stop=True)
                        peS.inc(inst)
                        c.s.wait_ge(peS.h, peS.n)
                        c.s.wait_ge(peA.h, peA.n)
                        scP.inc(c.s.activation(pt[:, :], pp[:, :TP], AF.Exp,
                                               scale=f32(1.0 / np.sqrt(128.0)),
                                               bias=f32(-16.0)))
                        if kc >= 4 * qs:
                            m = kc - 4 * qs
                            c.v.wait_ge(scP.h, scP.n)
                            msk.inc(c.v.tensor_tensor(
                                pt[:, :], pt[:, :],
                                CM[:, m * 512:(m + 1) * 512], ALU.mult))
                        c.t.wait_ge(scP.h, scP.n)
                        if kc >= 4 * qs:
                            c.t.wait_ge(msk.h, msk.n)
                        if kc == 0:
                            c.t.wait_ge(onrm.h, onrm.n)
                        qc0 = max(0, kc - 4 * qs)
                        for qc in range(qc0, 4):
                            inst = c.t.matmul(pO[qc][:, 0:129],
                                              pt[:, qc * 128:(qc + 1) * 128],
                                              VT[:, kc * 129:(kc + 1) * 129],
                                              start=(kc == 0),
                                              stop=(kc == 4 * qs + qc))
                        peA.inc(inst)
                    for qc in range(4):
                        c.v.wait_ge(peA.h, peA.n)
                        c.v.reciprocal(RC[:, :], pO[qc][:, 128:129])
                        c.v.wait_ge(peOT.h, peOT.n)
                        onrm.inc(c.v.tensor_scalar(OB[:, :], pO[qc][:, 0:128],
                                                   RC[:, :], None, ALU.mult))
                        c.t.wait_ge(onrm.h, onrm.n)
                        c.t.wait_ge(ost.h, ost.n)
                        peOT.inc(c.t.transpose(pTR[:, :], OB[:, :], IDB[:, :]))
                        c.v.wait_ge(peOT.h, peOT.n)
                        c.v.wait_ge(fst.h, fst.n)
                        ost.inc(c.v.tensor_copy(OST[:, qc * 128:(qc + 1) * 128],
                                                pTR[:, :]))
                    c.sy.wait_ge(ost.h, ost.n)
                    fst.inc(c.sy.dma_start(
                        out=oown[hi * 128:(hi + 1) * 128, qs * TP:(qs + 1) * TP],
                        in_=OST[:, :]), 16)
            c.g.wait_ge(fst.h, fst.n)
    return nc


def build_outproj_launch():
    nc = bass.Bass("TRN2", target_bir_lowering=False)
    xin = nc.dram_tensor("xin", [128, DC * TOKS], F32, kind="ExternalInput").ap()
    oin = nc.dram_tensor("oin", [128, DC * TOKS], BF16, kind="ExternalInput").ap()
    wout = nc.dram_tensor("wout", [128, DC * DC * 128], F32, kind="ExternalInput").ap()
    bout = nc.dram_tensor("bout", [128, DC], F32, kind="ExternalInput").ap()
    xout = nc.dram_tensor("xout", [128, DC * TOKS], F32, kind="ExternalOutput").ap()
    with ExitStack() as st:
        c = Ctx(nc, st)
        vb = {}
        vb["wbufs"] = [c.sbuf("wb0", [128, DC * 128], BF16),
                       c.sbuf("wb1", [128, DC * 128], BF16)]
        vb["pAB"] = [c.psum("pA", [128, TP]), c.psum("pB", [128, TP])]
        XF = c.sbuf("XF", [128, DC * TP], F32)
        OBT = c.sbuf("OBT", [128, DC * TP], BF16)
        BO = c.sbuf("BO", [128, DC], F32)
        block = st.enter_context(nc.Block())

        @block.gpsimd
        def _(_g):
            cst = c.sem("cst")
            cst.inc(c.sy.dma_start(out=BO[:, :], in_=bout[:, :]), 16)
            xld = c.sem("xld"); xst = c.sem("xst"); yev = c.sem("evd")
            c.v.wait_ge(cst.h, cst.n)
            for p in range(NPASS):
                c.sy.wait_ge(yev.h, yev.n)
                c.sy.wait_ge(xst.h, xst.n)
                xld.inc(c.sy.dma_start(
                    out=XF[:, :].rearrange("pp (cc t) -> pp cc t", cc=DC),
                    in_=xin.rearrange("pp (cc t) -> pp cc t", cc=DC)
                    [:, :, p * TP:(p + 1) * TP]), 16)
                xld.inc(c.sy.dma_start(
                    out=OBT[:, :].rearrange("pp (cc t) -> pp cc t", cc=DC),
                    in_=oin.rearrange("pp (cc t) -> pp cc t", cc=DC)
                    [:, :, p * TP:(p + 1) * TP]), 16)
                xw = (c.sem("xld"), c.sem("xld").n)

                def evac(oc, pp, wfree, wfree0, evd):
                    c.v.wait_ge(wfree.h, wfree0 + oc + 1)
                    inst = c.v.scalar_tensor_tensor(
                        XF[:, oc * TP:(oc + 1) * TP], pp[:, :TP],
                        BO[:, oc:oc + 1], XF[:, oc * TP:(oc + 1) * TP],
                        ALU.add, ALU.add)
                    evd.inc(inst)

                emit_wstream_mm(c, vb, wout, DC, DC,
                                lambda ci: OBT[:, ci * TP:(ci + 1) * TP],
                                evac, rhs_wait=xw)
                c.sy.wait_ge(yev.h, yev.n)
                xst.inc(c.sy.dma_start(
                    out=xout.rearrange("pp (cc t) -> pp cc t", cc=DC)
                    [:, :, p * TP:(p + 1) * TP],
                    in_=XF[:, :].rearrange("pp (cc t) -> pp cc t", cc=DC)), 16)
            c.g.wait_ge(xst.h, xst.n)
    return nc


def build_fused():
    nc = bass.Bass("TRN2", target_bir_lowering=False)
    xin = nc.dram_tensor("xin", [TOKS, D], F32, kind="ExternalInput").ap()
    w1 = nc.dram_tensor("w1", [128, H0C * DC * 128], F32, kind="ExternalInput").ap()
    w2 = nc.dram_tensor("w2", [128, H0C * H0C * 128], F32, kind="ExternalInput").ap()
    w3 = nc.dram_tensor("w3", [128, DC * H0C * 128], F32, kind="ExternalInput").ap()
    vcat_in = nc.dram_tensor("vcat", [128, 80], F32, kind="ExternalInput").ap()
    wcat = nc.dram_tensor("wcat", [128, DC * 384], F32, kind="ExternalInput").ap()
    brow = nc.dram_tensor("brow", [1, 384], F32, kind="ExternalInput").ap()
    mcum = nc.dram_tensor("mcum", [128, 128], F32, kind="ExternalInput").ap()
    chi = nc.dram_tensor("chi", [128, 2], F32, kind="ExternalInput").ap()
    amask = nc.dram_tensor("amask", [128, 128], U8, kind="ExternalInput").ap()
    identb = nc.dram_tensor("identb", [128, 128], BF16, kind="ExternalInput").ap()
    identf = nc.dram_tensor("identf", [128, 128], F32, kind="ExternalInput").ap()
    wo0 = nc.dram_tensor("wo0", [128, D], F32, kind="ExternalInput").ap()
    wup = nc.dram_tensor("wup", [2048, 1024], F32, kind="ExternalInput").ap()
    wga = nc.dram_tensor("wga", [2048, 1024], F32, kind="ExternalInput").ap()
    wdo = nc.dram_tensor("wdo", [8192, 256], F32, kind="ExternalInput").ap()
    h128u = nc.dram_tensor("h128u", [128, 128], F32, kind="ExternalInput").ap()
    h128d = nc.dram_tensor("h128d", [128, 128], F32, kind="ExternalInput").ap()
    suug = nc.dram_tensor("suug", [128, 16], F32, kind="ExternalInput").ap()
    sudp = nc.dram_tensor("sudp", [128, 64], F32, kind="ExternalInput").ap()
    wsup = nc.dram_tensor("wsup", [1, 1024], F32, kind="ExternalInput").ap()
    wsga = nc.dram_tensor("wsga", [1, 1024], F32, kind="ExternalInput").ap()
    wsdo = nc.dram_tensor("wsdo", [1, 256], F32, kind="ExternalInput").ap()
    wqkv5 = nc.dram_tensor("wqkv5", [128, 12 * DC * 128], F32, kind="ExternalInput").ap()
    bqkv5 = nc.dram_tensor("bqkv5", [128, 12], F32, kind="ExternalInput").ap()
    cmask5 = nc.dram_tensor("cmask5", [128, 4 * 512], BF16, kind="ExternalInput").ap()
    wout5 = nc.dram_tensor("wout5", [128, DC * DC * 128], F32, kind="ExternalInput").ap()
    bout5 = nc.dram_tensor("bout5", [128, DC], F32, kind="ExternalInput").ap()
    qoh = nc.dram_tensor("qoh", [128, 4], F32, kind="ExternalInput").ap()
    boh = nc.dram_tensor("boh", [128, 2], F32, kind="ExternalInput").ap()
    xout = nc.dram_tensor("xout", [TOKS, D], F32, kind="ExternalOutput").ap()
    xdram = nc.dram_tensor("xdram", [128, DC * TOKS], F32).ap()
    uown2 = nc.dram_tensor("uown2", [2048, 128], F32)
    uall2 = nc.dram_tensor("uall2", [4 * 2048, 128], F32)
    pcown2 = nc.dram_tensor("pcown2", [16, 128], F32)
    pcall2 = nc.dram_tensor("pcall2", [4 * 16, 128], F32)
    wpown = nc.dram_tensor("wpown", [128, 2 * 8 * DC * 128 + 2 * 64 * 128], BF16)
    wpall = nc.dram_tensor("wpall", [NCORES * 128, 2 * 8 * DC * 128 + 2 * 64 * 128], BF16)
    xhownp = [nc.dram_tensor("xhown%d" % p, [128, DC * TP], BF16)
              for p in range(NPASS)]
    xhallp = [nc.dram_tensor("xhall%d" % p, [4 * 128, DC * TP], BF16)
              for p in range(NPASS)]
    o5own = nc.dram_tensor("o5own", [NCORES * 512, TOKS], BF16)
    o5all = nc.dram_tensor("o5all", [NCORES * 512, TOKS], BF16)
    RG = [list(range(NCORES))]
    RG4 = [[0, 1, 2, 3], [4, 5, 6, 7]]

    with ExitStack() as st:
        c = Ctx(nc, st)
        vb = {}
        vb["XB2"] = c.sbuf("XB2", [128, DC * TP], BF16)
        vb["XQr"] = c.sbuf("XQr", [128, 2 * TP], BF16)
        vb["XFR"] = c.sbuf("XFR", [128, 2 * TP], F32)
        vb["XFS"] = c.sbuf("XFS", [128, TP], F32)
        vb["XHB"] = c.sbuf("XHB", [128, DC * TP], BF16)
        vb["SROW"] = c.sbuf("SROW", [1, 384], F32)
        vb["onesb"] = c.sbuf("onesb", [128, 1], BF16)
        vb["ones1f"] = c.sbuf("ones1f", [1, 128], F32)
        vb["onesbb"] = c.sbuf("onesbb", [128, 128], BF16)
        vb["LROW"] = c.sbuf("LROW", [128, 4 * TP], F32)
        vb["vcat"] = c.sbuf("vcat_sb", [128, 80], F32)
        vb["hb3"] = c.sbuf("hb3", [128, DC], F32)
        vb["B1C"], vb["B2C"], B3C = 0, 32, 64
        XF = c.sbuf("XF", [128, DC * TP], F32)
        IDBs = c.sbuf("IDBs", [128, 128], BF16)
        IDFs = c.sbuf("IDFs", [128, 128], F32)
        QOH = c.sbuf("QOH", [128, 4], F32)
        BOH = c.sbuf("BOH", [128, 2], F32)
        vb["pS1"] = c.psum("pS1", [128, TP])
        vb["pS2"] = c.psum("pS2", [128, TP])
        vb["pR"] = c.psum("pR", [128, TP])
        vb["pM"] = c.psum("pM", [128, TP])
        ccs = c.sem("ccs")
        xst = c.sem("xst")
        xld = c.sem("xld")
        block = st.enter_context(nc.Block())

        @block.gpsimd
        def _(_g):
            cst = c.sem("cst")
            cst.inc(c.sy.dma_start(out=vb["vcat"][:, :], in_=vcat_in[:, :]), 16)
            cst.inc(c.sy.dma_start(out=IDBs[:, :], in_=identb[:, :]), 16)
            cst.inc(c.sy.dma_start(out=IDFs[:, :], in_=identf[:, :]), 16)
            cst.inc(c.sy.dma_start(out=QOH[:, :], in_=qoh[:, :]), 16)
            cst.inc(c.sy.dma_start(out=BOH[:, :], in_=boh[:, :]), 16)
            c.v.wait_ge(cst.h, cst.n)
            c.s.wait_ge(cst.h, cst.n)
            c.v.memset(vb["onesb"][:, :], 1.0)
            c.v.memset(vb["onesbb"][:, :], 1.0)
            ini0 = c.v.memset(vb["ones1f"][:, :], 1.0)
            ini = c.v.tensor_scalar(vb["hb3"][:, :], vb["vcat"][:, B3C:B3C + DC],
                                    0.5, None, ALU.mult)
            c.sem("lnV").inc(ini)

            with ExitStack() as stp:
                cp = Ctx.__new__(Ctx)
                cp.__dict__.update(c.__dict__)
                pvb = {}
                pvb["ZA"] = stp.enter_context(nc.sbuf_tensor("ZA", [128, 8192], F32))
                pvb["ZB"] = stp.enter_context(nc.sbuf_tensor("ZB", [128, 8192], F32))
                pvb["WOUT"] = stp.enter_context(nc.sbuf_tensor("WOUT", [128, 8192], BF16))
                pvb["ones1f"] = vb["ones1f"]
                HU = stp.enter_context(nc.sbuf_tensor("HU", [128, 128], F32))
                HDN = stp.enter_context(nc.sbuf_tensor("HDN", [128, 128], F32))
                SUU = stp.enter_context(nc.sbuf_tensor("SUU", [128, 16], F32))
                SUD = stp.enter_context(nc.sbuf_tensor("SUD", [128, 64], F32))
                WSU = stp.enter_context(nc.sbuf_tensor("WSU", [1, 1024], F32))
                WSG = stp.enter_context(nc.sbuf_tensor("WSG", [1, 1024], F32))
                WSD = stp.enter_context(nc.sbuf_tensor("WSD", [1, 256], F32))
                pvb["pH"] = [stp.enter_context(nc.psum_tensor("pH0", [128, 512], F32)),
                             stp.enter_context(nc.psum_tensor("pH1", [128, 512], F32))]
                pvb["pW"] = stp.enter_context(nc.psum_tensor("pWb", [128, 512], F32))
                for dst, srcap in [(HU, h128u), (HDN, h128d), (SUU, suug),
                                   (SUD, sudp), (WSU, wsup), (WSG, wsga), (WSD, wsdo)]:
                    cst.inc(c.sy.dma_start(out=dst[:, :], in_=srcap[:, :]), 16)
                c.t.wait_ge(cst.h, cst.n)
                c.v.wait_ge(cst.h, cst.n)
                c.sy.wait_ge(c.sem("lnV").h, c.sem("lnV").n)
                UPSPAN = 8 * DC * 128
                emit_had_transform(c, pvb, wup, wpown.ap()[:, 0:UPSPAN],
                                   16, 1024, HU, SUU, WSU, "up")
                emit_had_transform(c, pvb, wga, wpown.ap()[:, UPSPAN:2 * UPSPAN],
                                   16, 1024, HU, SUU, WSG, "ga")
                so = emit_had_transform(c, pvb, wdo,
                                        wpown.ap()[:, 2 * UPSPAN:2 * UPSPAN + 2 * 64 * 128],
                                        64, 256, HDN, SUD, WSD, "do")
                hdv = c.sem("hdv"); hpe = c.sem("hpe")
                for eng in (c.t, c.v, c.s, c.sy, c.g):
                    eng.wait_ge(so.h, so.n)
                    eng.wait_ge(hdv.h, hdv.n)
                    eng.wait_ge(hpe.h, hpe.n)

            with ExitStack() as stf:
                fvb = dict(vb)
                fvb["H1B"] = stf.enter_context(nc.sbuf_tensor("H1B", [128, H0C * TP], BF16))
                fvb["H2B"] = stf.enter_context(nc.sbuf_tensor("H2B", [128, H0C * TP], BF16))
                fvb["wbufs"] = [stf.enter_context(nc.sbuf_tensor("wb0", [128, H0C * 128], BF16)),
                                stf.enter_context(nc.sbuf_tensor("wb1", [128, H0C * 128], BF16))]
                fvb["TY"] = stf.enter_context(nc.sbuf_tensor("TY", [128, 2 * TP], F32))
                XT = stf.enter_context(nc.sbuf_tensor("XT", [128, 2 * D], F32))
                fvb["pAB"] = [stf.enter_context(nc.psum_tensor("pA", [128, TP], F32)),
                              stf.enter_context(nc.psum_tensor("pB", [128, TP], F32))]
                pT = [stf.enter_context(nc.psum_tensor("pT0", [128, 128], F32)),
                      stf.enter_context(nc.psum_tensor("pT1", [128, 128], F32))]
                tpe = c.sem("tpe"); tdv = c.sem("tdv"); evd = c.sem("evd")
                for p in range(NPASS):
                    tpe0, tdv0 = tpe.n, tdv.n
                    i = 0
                    for half in range(2):
                        c.sy.wait_ge(tpe.h, tpe0 + i)
                        xld.inc(c.sy.dma_start(
                            out=XT[:, :].rearrange("pp (s d) -> pp s d", s=2),
                            in_=xin.rearrange("(s pp) d -> pp s d", pp=128)
                            [:, p * NSUB + half * 2:p * NSUB + half * 2 + 2, :]), 16)
                        c.t.wait_ge(xld.h, xld.n)
                        if half == 0:
                            c.t.wait_ge(evd.h, evd.n)
                            c.t.wait_ge(xst.h, xst.n)
                        for sub2 in range(2):
                            sub = half * 2 + sub2
                            for ci in range(DC):
                                if i >= 2:
                                    c.t.wait_ge(tdv.h, tdv0 + i - 1)
                                tpe.inc(c.t.transpose(
                                    pT[i % 2][:, :],
                                    XT[:, sub2 * D + ci * 128:sub2 * D + (ci + 1) * 128],
                                    IDFs[:, :]))
                                c.v.wait_ge(tpe.h, tpe0 + i + 1)
                                tdv.inc(c.v.tensor_copy(
                                    XF[:, ci * TP + sub * 128:ci * TP + (sub + 1) * 128],
                                    pT[i % 2][:, :]))
                                i += 1
                    emit_ln_stats(c, fvb, None, p, p % 2, sbuf_xf=XF,
                                  pre=(tdv, tdv.n))
                    lnV = emit_ln_norm(c, fvb, None, p, p % 2, sbuf_xf=XF,
                                       xhb_free=(c.sem("evd"), c.sem("evd").n),
                                       pre=(tdv, tdv.n))
                    evd = emit_ffn0_pass(c, fvb, XF, fvb["XHB"], lnV, w1, w2, w3)
                    c.sy.wait_ge(evd.h, evd.n)
                    xst.inc(c.sy.dma_start(
                        out=xdram.rearrange("pp (cc t) -> pp cc t", cc=DC)
                        [:, :, p * TP:(p + 1) * TP],
                        in_=XF[:, :].rearrange("pp (cc t) -> pp cc t", cc=DC)), 16)
                for eng in (c.t, c.v, c.s, c.sy, c.g):
                    eng.wait_ge(evd.h, evd.n)
                    eng.wait_ge(c.sem("wfree").h, c.sem("wfree").n)
                    eng.wait_ge(xst.h, xst.n)

            ccs.inc(c.g.collective_compute(
                "AllGather", ALU.bypass, replica_groups=RG,
                ins=[wpown.ap().opt()], outs=[wpall.ap().opt()]), 1)
            AGW = ccs.n

            st2 = ExitStack()
            QTall = st2.enter_context(nc.sbuf_tensor("QTall", [128, TOKS], BF16))
            OIT = st2.enter_context(nc.sbuf_tensor("OIT", [128, TOKS], F32))
            with ExitStack() as sta:
                avb = dict(vb)
                for nm, shape, dt in [("WCB", [128, DC * 384], BF16),
                                      ("BBC", [128, 384], F32),
                                      ("MCUM", [128, 128], F32),
                                      ("CHI", [128, 2], F32),
                                      ("AMASK", [128, 128], U8),
                                      ("ZEROB", [128, 128], BF16),
                                      ("P384", [128, 384], F32),
                                      ("SC64", [128, 256], F32),
                                      ("NCC", [128, 68], F32),
                                      ("NLB", [128, 64], F32),
                                      ("DELTA", [128, 64], F32),
                                      ("EM", [128, 64], F32),
                                      ("EP", [128, 64], F32),
                                      ("QTM", [128, 128], BF16),
                                      ("KTM", [128, 128], BF16),
                                      ("TKF", [128, 64], F32),
                                      ("VB", [128, 128], BF16),
                                      ("KTP", [128, 128], BF16),
                                      ("AB", [128, 128], BF16),
                                      ("USTG", [64, 32 * 128], F32),
                                      ("OISTG", [128, DC * 128], F32),
                                      ("OIF", [128, 128], F32),
                                      ("PCST", [64, 32], F32)]:
                    avb[nm] = sta.enter_context(nc.sbuf_tensor(nm, shape, dt))
                avb["IDB"] = IDBs
                avb["IDF"] = IDFs
                avb["QTall"] = QTall
                avb["pPP"] = sta.enter_context(nc.psum_tensor("pPP", [128, 448], F32))
                avb["pOX"] = sta.enter_context(nc.psum_tensor("pOX", [128, 194], F32))
                avb["pU"] = sta.enter_context(nc.psum_tensor("pU", [64, 256], F32))
                avb["pTQ"] = sta.enter_context(nc.psum_tensor("pTQ", [128, 256], BF16))
                cst.inc(c.sy.dma_start(out=avb["MCUM"][:, :], in_=mcum[:, :]), 16)
                cst.inc(c.sy.dma_start(out=avb["CHI"][:, :], in_=chi[:, :]), 16)
                cst.inc(c.sy.dma_start(out=avb["AMASK"][:, :], in_=amask[:, :]), 16)
                cstg = c.sem("cstg")
                cstg.inc(c.g.dma_start(out=avb["WCB"][:, :], in_=wcat[:, :]), 16)
                cst.inc(c.sy.dma_start(out=vb["SROW"][0:1, 0:384], in_=brow[:, :]), 16)
                dvI = c.sem("dvI")
                c.v.wait_ge(cst.h, cst.n)
                c.v.wait_ge(cstg.h, cstg.n)
                c.v.memset(avb["ZEROB"][:, :], 0.0)
                c.v.memset(QTall[:, :], 0.0)
                c.v.memset(avb["QTM"][:, :], 0.0)
                dvI.inc(c.v.memset(avb["KTM"][:, :], 0.0))
                c.t.wait_ge(dvI.h, dvI.n)
                c.t.wait_ge(c.sem("lnP").h, c.sem("lnP").n)
                bb2 = c.t.matmul(vb["pR"][:, 0:384], vb["ones1f"][:, :],
                                 vb["SROW"][0:1, 0:384], start=True, stop=True)
                c.sem("lnP").inc(bb2)
                c.v.wait_ge(c.sem("lnP").h, c.sem("lnP").n)
                ini2 = c.v.tensor_copy(avb["BBC"][:, :], vb["pR"][:, 0:384])
                c.sem("lnV").inc(ini2)
                emit_ln_stats(c, avb, xdram, 0, 0)
                emit_ln_stats(c, avb, xdram, 1, 1)
                emit_ln_norm(c, avb, xdram, 0, 0, qe=c.sy)
                prjP = c.sem("prjP")
                for p in range(NPASS):
                    for s in range(NSUB):
                        emit_gla_a_subtile(c, avb, avb["XHB"], p, s, 0)
                    if p + 1 < NPASS:
                        emit_ln_norm(c, avb, xdram, p + 1, (p + 1) % 2, qe=c.sy,
                                     xhb_free=(prjP, prjP.n))
                    if p + 2 < NPASS:
                        emit_ln_stats(c, avb, xdram, p + 2, p % 2)
                fina = c.sem("fina")
                c.v.wait_ge(c.sem("otv").h, c.sem("otv").n)
                otc = c.v.tensor_copy(OIT[:, :], avb["OISTG"][:, :])
                fina.inc(otc)
                c.sy.wait_ge(c.sem("uev").h, c.sem("uev").n)
                c.sy.wait_ge(c.sem("dvQ").h, c.sem("dvQ").n)
                ust = c.sem("ust")
                ust.inc(c.sy.dma_start(
                    out=uown2.ap()[:, :].rearrange("(j k) v -> k j v", k=64),
                    in_=avb["USTG"][:, :].rearrange("k (j v) -> k j v", v=128)), 16)
                ust.inc(c.sy.dma_start(
                    out=pcown2.ap()[:, :].rearrange("q (a j) -> (q a) j", a=4),
                    in_=avb["PCST"][:, :]), 16)
                c.g.wait_ge(ust.h, ust.n)
                ccs.inc(c.g.collective_compute(
                    "AllGather", ALU.bypass, replica_groups=RG4,
                    ins=[uown2.ap().opt()], outs=[uall2.ap().opt()]), 1)
                ccs.inc(c.g.collective_compute(
                    "AllGather", ALU.bypass, replica_groups=RG4,
                    ins=[pcown2.ap().opt()], outs=[pcall2.ap().opt()]), 1)
                AGU = ccs.n
                for eng in (c.t, c.v, c.s, c.sy, c.g):
                    eng.wait_ge(fina.h, fina.n)
                    eng.wait_ge(c.sem("tqv").h, c.sem("tqv").n)
                    eng.wait_ge(c.sem("sel").h, c.sem("sel").n)
                    eng.wait_ge(ust.h, ust.n)

            with ExitStack() as stb:
                SCAN = stb.enter_context(nc.sbuf_tensor("SCAN", [128, 129 * 128], F32))
                PCsb = stb.enter_context(nc.sbuf_tensor("PCsb", [128, 128], F32))
                OTall = stb.enter_context(nc.sbuf_tensor("OTall", [128, TOKS], BF16))
                WOB = stb.enter_context(nc.sbuf_tensor("WOB", [128, D], BF16))
                SDALL = stb.enter_context(nc.sbuf_tensor("SDALL", [128, 32 * 128], BF16))
                pOT = stb.enter_context(nc.psum_tensor("pOT", [128, 64], F32))
                pA2 = stb.enter_context(nc.psum_tensor("pA2", [128, TP], F32))
                pB2 = stb.enter_context(nc.psum_tensor("pB2", [128, TP], F32))
                ld2 = c.sem("ld2")
                ld2.inc(c.g.dma_start(out=WOB[:, :], in_=wo0[:, :]), 16)
                c.g.wait_ge(ccs.h, AGU)
                for half in range(2):
                    ld2.inc(c.g.dma_start(
                        out=SCAN[half * 64:(half + 1) * 64, 128:129 * 128]
                        .rearrange("k (j2 v) -> k j2 v", v=128),
                        in_=uall2.ap().rearrange("(j2 k) v -> k j2 v", k=64)), 16)
                for half in range(2):
                    ld2.inc(c.g.dma_start(
                        out=PCsb[half * 64:(half + 1) * 64, :]
                        .rearrange("k (j2 j) -> k j2 j", j=32),
                        in_=bass.AP(pcall2.ap().tensor, 0,
                                    [[32, 64], [2048, 4], [1, 32]])), 16)
                dvI2 = c.sem("dvI2")
                dvI2.inc(c.v.memset(SCAN[:, 0:128], 0.0))
                c.v.wait_ge(ld2.h, ld2.n)
                scn = c.sem("scn")
                view = SCAN[:, 128:].rearrange("p (j v) -> p v j", v=128)
                for vi in range(128):
                    inst = c.v.tensor_tensor_scan(view[:, vi, :], view[:, vi, :],
                                                  PCsb[:, :], 0.0, ALU.add, ALU.mult)
                scn.inc(inst)
                peI = c.sem("peI"); otv2 = c.sem("otv2")
                peY = c.sem("peY"); yev = c.sem("yev")
                sdc = c.sem("sdc")
                c.v.wait_ge(scn.h, scn.n)
                c.v.tensor_scalar(SDALL[:, :], SCAN[:, 0:4096], QOH[:, 0:1],
                                  None, ALU.mult)
                for w in range(1, 4):
                    inst = c.v.scalar_tensor_tensor(
                        SDALL[:, :], SCAN[:, w * 4096:(w + 1) * 4096],
                        QOH[:, w:w + 1], SDALL[:, :], ALU.mult, ALU.add)
                sdc.inc(inst)
                c.t.wait_ge(scn.h, scn.n)
                c.t.wait_ge(dvI2.h, dvI2.n)
                c.t.wait_ge(ld2.h, ld2.n)
                c.t.wait_ge(sdc.h, sdc.n)
                c.v.wait_ge(dvI2.h, dvI2.n)
                if True:
                    for p in range(NPASS):
                        c.sy.wait_ge(yev.h, yev.n)
                        c.sy.wait_ge(xst.h, xst.n)
                        xld.inc(c.sy.dma_start(
                            out=XF[:, :].rearrange("pp (cc t) -> pp cc t", cc=DC),
                            in_=xdram.rearrange("pp (cc t) -> pp cc t", cc=DC)
                            [:, :, p * TP:(p + 1) * TP]), 16)
                        for s in range(NSUB):
                            si = p * NSUB + s
                            st_ = si * 128
                            for h in range(2):
                                jloc = 2 * si + h
                                c.t.wait_ge(otv2.h, otv2.n)
                                mm = c.t.matmul(pOT[:, 0:64],
                                                SDALL[:, jloc * 128:(jloc + 1) * 128],
                                                QTall[:, st_ + h * 64:st_ + (h + 1) * 64],
                                                start=True, stop=True)
                                peI.inc(mm)
                                c.v.wait_ge(peI.h, peI.n)
                                otv2.inc(c.v.tensor_tensor(
                                    OTall[:, st_ + h * 64:st_ + (h + 1) * 64],
                                    pOT[:, 0:64],
                                    OIT[:, st_ + h * 64:st_ + (h + 1) * 64], ALU.add))
                        c.t.wait_ge(otv2.h, otv2.n)
                        c.t.wait_ge(xld.h, xld.n)
                        for oc in range(DC):
                            pp = [pA2, pB2][oc % 2]
                            if oc >= 2:
                                c.t.wait_ge(yev.h, yev.n)
                            peY.inc(c.t.matmul(pp[:, :TP], WOB[:, oc * 128:(oc + 1) * 128],
                                               OTall[:, p * TP:(p + 1) * TP],
                                               start=True, stop=True))
                            c.v.wait_ge(peY.h, peY.n)
                            yev.inc(c.v.tensor_tensor(XF[:, oc * TP:(oc + 1) * TP],
                                                      pp[:, :TP],
                                                      XF[:, oc * TP:(oc + 1) * TP], ALU.add))
                        c.sy.wait_ge(yev.h, yev.n)
                        xst.inc(c.sy.dma_start(
                            out=xdram.rearrange("pp (cc t) -> pp cc t", cc=DC)
                            [:, :, p * TP:(p + 1) * TP],
                            in_=XF[:, :].rearrange("pp (cc t) -> pp cc t", cc=DC)), 16)
                for eng in (c.t, c.v, c.s, c.sy, c.g):
                    eng.wait_ge(yev.h, yev.n)
                    eng.wait_ge(xst.h, xst.n)
                    eng.wait_ge(otv2.h, otv2.n)
            st2.close()

            with ExitStack() as stf:
                fvb = dict(vb)
                fvb["H1B"] = stf.enter_context(nc.sbuf_tensor("H1B3", [128, H0C * TP], BF16))
                fvb["H2B"] = stf.enter_context(nc.sbuf_tensor("H2B3", [128, H0C * TP], BF16))
                fvb["wbufs"] = [stf.enter_context(nc.sbuf_tensor("wb03", [128, H0C * 128], BF16)),
                                stf.enter_context(nc.sbuf_tensor("wb13", [128, H0C * 128], BF16)),
                                stf.enter_context(nc.sbuf_tensor("wb23", [128, H0C * 128], BF16))]
                fvb["TY"] = stf.enter_context(nc.sbuf_tensor("TY3", [128, 2 * TP], F32))
                fvb["pAB"] = [stf.enter_context(nc.psum_tensor("pA3", [128, TP], F32)),
                              stf.enter_context(nc.psum_tensor("pB3", [128, TP], F32))]
                evd = c.sem("evd")
                emit_ln_stats(c, fvb, xdram, 0, 0)
                emit_ln_stats(c, fvb, xdram, 1, 1)
                lnV = emit_ln_norm(c, fvb, xdram, 0, 0, qe=c.sy)
                for p in range(NPASS):
                    c.sy.wait_ge(evd.h, evd.n)
                    c.sy.wait_ge(xst.h, xst.n)
                    xld.inc(c.sy.dma_start(
                        out=XF[:, :].rearrange("pp (cc t) -> pp cc t", cc=DC),
                        in_=xdram.rearrange("pp (cc t) -> pp cc t", cc=DC)
                        [:, :, p * TP:(p + 1) * TP]), 16)

                    def hook3(p=p):
                        if p + 1 < NPASS:
                            emit_ln_norm(c, fvb, xdram, p + 1, (p + 1) % 2,
                                         qe=c.sy,
                                         xhb_free=(c.sem("evd"), c.sem("evd").n))
                        if p + 2 < NPASS:
                            emit_ln_stats(c, fvb, xdram, p + 2, p % 2)
                    evd = emit_ffn0_pass(c, fvb, XF, fvb["XHB"], lnV, w1, w2, w3,
                                         hook=hook3, xf_wait=(xld, xld.n))
                    lnV = c.sem("lnV")
                    c.sy.wait_ge(evd.h, evd.n)
                    xst.inc(c.sy.dma_start(
                        out=xdram.rearrange("pp (cc t) -> pp cc t", cc=DC)
                        [:, :, p * TP:(p + 1) * TP],
                        in_=XF[:, :].rearrange("pp (cc t) -> pp cc t", cc=DC)), 16)
                for eng in (c.t, c.v, c.s, c.sy, c.g):
                    eng.wait_ge(evd.h, evd.n)
                    eng.wait_ge(c.sem("wfree").h, c.sem("wfree").n)
                    eng.wait_ge(xst.h, xst.n)

            def l1ffn_phase(tagsuffix):
                with ExitStack() as stl:
                    GB = stl.enter_context(nc.sbuf_tensor("GB" + tagsuffix, [128, 64 * TP], BF16))
                    SGT = stl.enter_context(nc.sbuf_tensor("SGT" + tagsuffix, [128, TP], BF16))
                    wbu = [stl.enter_context(nc.sbuf_tensor("wbu0" + tagsuffix, [128, 2 * DC * 128], BF16)),
                           stl.enter_context(nc.sbuf_tensor("wbu1" + tagsuffix, [128, 2 * DC * 128], BF16))]
                    wbg = [stl.enter_context(nc.sbuf_tensor("wbg0" + tagsuffix, [128, 2 * DC * 128], BF16)),
                           stl.enter_context(nc.sbuf_tensor("wbg1" + tagsuffix, [128, 2 * DC * 128], BF16))]
                    wbd = [stl.enter_context(nc.sbuf_tensor("wbd0" + tagsuffix, [128, 32 * 128], BF16)),
                           stl.enter_context(nc.sbuf_tensor("wbd1" + tagsuffix, [128, 32 * 128], BF16))]
                    pG1 = stl.enter_context(nc.psum_tensor("pG1" + tagsuffix, [128, TP], F32))
                    pG2 = stl.enter_context(nc.psum_tensor("pG2" + tagsuffix, [128, TP], F32))
                    pY = [stl.enter_context(nc.psum_tensor("pY0" + tagsuffix, [128, TP], F32)),
                          stl.enter_context(nc.psum_tensor("pY1" + tagsuffix, [128, TP], F32))]
                    IC = 64
                    UPSPAN = 8 * DC * 128
                    wfree = c.sem("wfree"); gvd = c.sem("gvd"); sil = c.sem("sil")
                    yev = c.sem("yevL"); wldu = c.sem("wldu"); wldg = c.sem("wldg")
                    wldd = c.sem("wldd")
                    gpair = [(pG1, pG2), (vb["pR"], vb["pM"])]
                    c.g.wait_ge(ccs.h, AGW)

                    def wsrc(kind, hc):
                        r = hc >> 3
                        ocl = hc & 7
                        if kind == 0:
                            cs = ocl * (DC * 128)
                        elif kind == 1:
                            cs = UPSPAN + ocl * (DC * 128)
                        else:
                            r = hc >> 1
                            ocl = hc & 1
                            cs = 2 * UPSPAN + ocl * (64 * 128)
                        return wpall.ap()[r * 128:(r + 1) * 128,
                                          cs:cs + (DC * 128 if kind < 2 else 64 * 128)]

                    def wsrc2(kind, hp):
                        hc = 2 * hp
                        r = hc >> 3
                        cs = (0 if kind == 0 else UPSPAN) + (hc & 7) * (DC * 128)
                        return wpall.ap()[r * 128:(r + 1) * 128, cs:cs + 2 * DC * 128]

                    emit_ln_stats(c, vb, xdram, 0, 0)
                    emit_ln_stats(c, vb, xdram, 1, 1)
                    emit_ln_norm(c, vb, xdram, 0, 0, qe=c.g)
                    for p in range(NPASS):
                        c.sy.wait_ge(yev.h, yev.n)
                        c.sy.wait_ge(xst.h, xst.n)
                        xld.inc(c.sy.dma_start(
                            out=XF[:, :].rearrange("pp (cc t) -> pp cc t", cc=DC),
                            in_=xdram.rearrange("pp (cc t) -> pp cc t", cc=DC)
                            [:, :, p * TP:(p + 1) * TP]), 16)
                        lnV = c.sem("lnV")
                        wfree0, gvd0, sil0 = wfree.n, gvd.n, sil.n
                        wldu0, wldg0 = wldu.n, wldg.n
                        if not hasattr(c, "wlast"):
                            c.wlast = {}
                        for hp in range(IC // 2):
                            bu = wbu[hp % 2]; bg = wbg[hp % 2]
                            if id(bu) in c.wlast:
                                c.sy.wait_ge(wfree.h, c.wlast[id(bu)])
                            wldu.inc(c.sy.dma_start(out=bu[:, :], in_=wsrc2(0, hp)), 16)
                            wldg.inc(c.sy.dma_start(out=bg[:, :], in_=wsrc2(1, hp)), 16)
                            for k in range(2):
                                hc = 2 * hp + k
                                ppu, ppg = gpair[hc % 2]
                                if k == 0:
                                    c.t.wait_ge(wldu.h, wldu0 + 16 * (hp + 1))
                                    c.t.wait_ge(wldg.h, wldg0 + 16 * (hp + 1))
                                if hc == 0:
                                    c.t.wait_ge(lnV.h, lnV.n)
                                if hc >= 2:
                                    c.t.wait_ge(gvd.h, gvd0 + hc - 1)
                                else:
                                    c.t.wait_ge(gvd.h, gvd0)
                                ko = k * DC * 128
                                for ci in range(DC):
                                    c.t.matmul(ppu[:, :TP],
                                               bu[:, ko + ci * 128:ko + (ci + 1) * 128],
                                               vb["XHB"][:, ci * TP:(ci + 1) * TP],
                                               start=(ci == 0), stop=(ci == DC - 1))
                                for ci in range(DC):
                                    inst = c.t.matmul(ppg[:, :TP],
                                                      bg[:, ko + ci * 128:ko + (ci + 1) * 128],
                                                      vb["XHB"][:, ci * TP:(ci + 1) * TP],
                                                      start=(ci == 0), stop=(ci == DC - 1))
                                wfree.inc(inst)
                                c.s.wait_ge(wfree.h, wfree0 + hc + 1)
                                c.s.wait_ge(gvd.h, gvd0 + hc)
                                sil.inc(c.s.activation(SGT[:, :], ppg[:, :TP], AF.Silu))
                                c.v.wait_ge(sil.h, sil0 + hc + 1)
                                gvd.inc(c.v.tensor_tensor(GB[:, hc * TP:(hc + 1) * TP],
                                                          ppu[:, :TP], SGT[:, :], ALU.mult))
                            c.wlast[id(bu)] = wfree.n
                        if p + 1 < NPASS:
                            emit_ln_norm(c, vb, xdram, p + 1, (p + 1) % 2, qe=c.g,
                                         xhb_free=(wfree, wfree.n))
                        if p + 2 < NPASS:
                            emit_ln_stats(c, vb, xdram, p + 2, p % 2)
                        wldds = c.sem("wldds")
                        wldd0, wldds0, yev0 = wldd.n, wldds.n, yev.n
                        dh = c.sem("dh")
                        dh0 = dh.n
                        wfree0d = wfree.n
                        for dc in range(DC):
                            ws = wsrc(2, dc)
                            QW = 16 * 128
                            for q in range(4):
                                qe_ = [c.sy, c.s][q % 2]
                                half = q >> 1
                                bd = wbd[half]
                                if dc >= 1:
                                    if half == 0:
                                        qe_.wait_ge(dh.h, dh0 + dc)
                                    else:
                                        qe_.wait_ge(wfree.h, wfree0d + dc)
                                sem_ = [wldd, wldds][q % 2]
                                sem_.inc(qe_.dma_start(
                                    out=bd[:, (q & 1) * QW:(q & 1) * QW + QW],
                                    in_=ws[:, q * QW:(q + 1) * QW]), 16)
                            pp = pY[dc % 2]
                            if dc == 0:
                                c.t.wait_ge(gvd.h, gvd.n)
                            if dc >= 2:
                                c.t.wait_ge(yev.h, yev0 + dc - 1)
                            for fc in range(64):
                                q = fc // 16
                                bd = wbd[q >> 1]
                                if fc % 16 == 0:
                                    sem_ = [wldd, wldds][q % 2]
                                    base_ = [wldd0, wldds0][q % 2]
                                    c.t.wait_ge(sem_.h,
                                                base_ + 16 * (2 * dc + (q >> 1) + 1))
                                inst = c.t.matmul(
                                    pp[:, :TP],
                                    bd[:, ((fc % 32) // 16) * QW
                                       + (fc % 16) * 128:((fc % 32) // 16) * QW
                                       + (fc % 16) * 128 + 128],
                                    GB[:, fc * TP:(fc + 1) * TP],
                                    start=(fc == 0), stop=(fc == 63))
                                if fc == 31:
                                    dh.inc(inst)
                            wfree.inc(inst)
                            c.v.wait_ge(wfree.h, wfree.n)
                            if dc == 0:
                                c.v.wait_ge(xld.h, xld.n)
                            yev.inc(c.v.tensor_tensor(XF[:, dc * TP:(dc + 1) * TP],
                                                      pp[:, :TP],
                                                      XF[:, dc * TP:(dc + 1) * TP], ALU.add))
                        c.sy.wait_ge(yev.h, yev.n)
                        xst.inc(c.sy.dma_start(
                            out=xdram.rearrange("pp (cc t) -> pp cc t", cc=DC)
                            [:, :, p * TP:(p + 1) * TP],
                            in_=XF[:, :].rearrange("pp (cc t) -> pp cc t", cc=DC)), 16)
                    for eng in (c.t, c.v, c.s, c.sy, c.g):
                        eng.wait_ge(yev.h, yev.n)
                        eng.wait_ge(c.sem("wfree").h, c.sem("wfree").n)
                        eng.wait_ge(c.sem("gvd").h, c.sem("gvd").n)
                        eng.wait_ge(xst.h, xst.n)

            l1ffn_phase("a")

            emit_ln_stats(c, vb, xdram, 0, 0)
            emit_ln_stats(c, vb, xdram, 1, 1)
            emit_ln_norm(c, vb, xdram, 0, 0, qe=c.sy)
            AGXp = []
            for p in range(NPASS):
                c.sy.wait_ge(c.sem("lnV").h, c.sem("lnV").n)
                xst.inc(c.sy.dma_start(
                    out=xhownp[p].ap()[:, :], in_=vb["XHB"][:, :]), 16)
                c.g.wait_ge(xst.h, xst.n)
                ccs.inc(c.g.collective_compute(
                    "AllGather", ALU.bypass, replica_groups=RG4,
                    ins=[xhownp[p].ap().opt()], outs=[xhallp[p].ap().opt()]), 1)
                AGXp.append(ccs.n)
                if p + 1 < NPASS:
                    emit_ln_norm(c, vb, xdram, p + 1, (p + 1) % 2, qe=c.sy,
                                 xhb_free=(xst, xst.n))
                if p + 2 < NPASS:
                    emit_ln_stats(c, vb, xdram, p + 2, p % 2)

            with ExitStack() as stt:
                WB5 = stt.enter_context(nc.sbuf_tensor("WB5", [128, 12 * DC * 128], BF16))
                BQ5 = stt.enter_context(nc.sbuf_tensor("BQ5", [128, 12], F32))
                CM5 = stt.enter_context(nc.sbuf_tensor("CM5", [128, 4 * 512], BF16))
                Q5 = stt.enter_context(nc.sbuf_tensor("Q5", [128, SEQ], BF16))
                Kt5 = stt.enter_context(nc.sbuf_tensor("Kt5", [128, SEQ], BF16))
                VT5 = stt.enter_context(nc.sbuf_tensor("VT5", [128, 64 * 129], BF16))
                VF5 = stt.enter_context(nc.sbuf_tensor("VF5", [128, TP], BF16))
                PT5 = [stt.enter_context(nc.sbuf_tensor("PT5a", [128, TP], BF16)),
                       stt.enter_context(nc.sbuf_tensor("PT5b", [128, TP], BF16)),
                       stt.enter_context(nc.sbuf_tensor("PT5c", [128, TP], BF16)),
                       stt.enter_context(nc.sbuf_tensor("PT5d", [128, TP], BF16))]
                OB5 = stt.enter_context(nc.sbuf_tensor("OB5", [128, 128], BF16))
                RC5 = stt.enter_context(nc.sbuf_tensor("RC5", [128, 1], F32))
                OST5 = stt.enter_context(nc.sbuf_tensor("OST5", [128, TP], BF16))
                XB5 = vb["XB2"]
                pS5 = [vb["pS1"], vb["pS2"],
                       stt.enter_context(nc.psum_tensor("pO5a", [128, TP], F32))]
                NS5 = len(pS5)
                pO5 = [vb["pR"][:, 0:129], vb["pM"][:, 0:129],
                       stt.enter_context(nc.psum_tensor("pO5b", [128, TP], F32))
                       [:, 0:129],
                       stt.enter_context(nc.psum_tensor("pO5c", [128, TP], F32))
                       [:, 0:129]]
                pTR5 = stt.enter_context(nc.psum_tensor("pTR5", [128, 128], BF16))
                cstg = c.sem("cstg")
                cstg.inc(c.g.dma_start(out=WB5[:, :], in_=wqkv5[:, :]), 16)
                cst.inc(c.sy.dma_start(out=BQ5[:, :], in_=bqkv5[:, :]), 16)
                cst.inc(c.sy.dma_start(out=CM5[:, :], in_=cmask5[:, :]), 16)
                dvi5 = c.sem("dvi5")
                dvi5.inc(c.v.memset(VT5[:, :], 1.0))
                peQ = c.sem("peQ"); qkv = c.sem("qkv"); peV = c.sem("peV")
                vtd = c.sem("vtd"); peS = c.sem("peS"); scP = c.sem("scP")
                msk = c.sem("msk"); peA = c.sem("peA"); onrm = c.sem("onrm")
                peOT = c.sem("peOT5"); ost5 = c.sem("ost5"); fst5 = c.sem("fst5")
                c.t.wait_ge(cst.h, cst.n)
                c.t.wait_ge(c.sem("cstg").h, c.sem("cstg").n)
                c.t.wait_ge(c.sem("lnS").h, c.sem("lnS").n)
                c.t.wait_ge(c.sem("lnV").h, c.sem("lnV").n)
                c.v.wait_ge(cst.h, cst.n)
                c.sy.wait_ge(c.sem("lnP").h, c.sem("lnP").n)
                NT = SEQ // TP
                TPORD = [k + 4 * u for k in range(NPASS) for u in range(4)]
                qkv5b0 = qkv.n
                jg = 0
                if True:
                    for hi in range(4):
                        for ki, tp in enumerate(TPORD):
                            ti = hi * NT + ki
                            xbuf = [vb["XHB"], XB5][ti % 2]
                            if ti >= 2:
                                c.sy.wait_ge(peQ.h, 3 * (ti - 1))
                            c.sy.wait_ge(ccs.h, AGXp[tp & 3])
                            xld.inc(c.sy.dma_start(
                                out=xbuf[:, :],
                                in_=xhallp[tp & 3].ap()
                                [(tp >> 2) * 128:(tp >> 2) * 128 + 128, :]), 16)
                            c.t.wait_ge(xld.h, xld.n)
                            for j in range(3):
                                pp = pS5[jg % NS5]
                                wslice = WB5[:, (hi * 3 + j) * DC * 128:(hi * 3 + j + 1) * DC * 128]
                                c.t.wait_ge(qkv.h, qkv5b0 + max(0, jg - NS5 + 1))
                                c.t.wait_ge(scP.h, scP.n)
                                jg += 1
                                for ci in range(DC):
                                    inst = c.t.matmul(pp[:, :TP],
                                                      wslice[:, ci * 128:(ci + 1) * 128],
                                                      xbuf[:, ci * TP:(ci + 1) * TP],
                                                      start=(ci == 0), stop=(ci == DC - 1))
                                peQ.inc(inst)
                                c.v.wait_ge(peQ.h, peQ.n)
                                c.v.wait_ge(peS.h, peS.n)
                                c.v.wait_ge(peA.h, peA.n)
                                bcol = BQ5[:, hi * 3 + j:hi * 3 + j + 1]
                                if j == 0:
                                    qkv.inc(c.v.tensor_scalar(Q5[:, tp * TP:(tp + 1) * TP],
                                                              pp[:, :TP], bcol, None, ALU.add))
                                elif j == 1:
                                    qkv.inc(c.v.tensor_scalar(Kt5[:, tp * TP:(tp + 1) * TP],
                                                              pp[:, :TP], bcol, None, ALU.add))
                                else:
                                    c.v.wait_ge(peV.h, peV.n)
                                    qkv.inc(c.v.tensor_scalar(VF5[:, :], pp[:, :TP],
                                                              bcol, None, ALU.add))
                                    for u in range(4):
                                        kt = tp * 4 + u
                                        c.t.wait_ge(qkv.h, qkv.n)
                                        c.t.wait_ge(vtd.h, vtd.n)
                                        peV.inc(c.t.transpose(pTR5[:, :],
                                                              VF5[:, u * 128:(u + 1) * 128],
                                                              IDBs[:, :]))
                                        c.v.wait_ge(peV.h, peV.n)
                                        vtd.inc(c.v.tensor_copy(
                                            VT5[:, kt * 129:kt * 129 + 128], pTR5[:, :]))
                        for qs in range(NT):
                            nkc = 4 * qs + 4
                            scP0, peA0, msk0 = scP.n, peA.n, msk.n
                            for kk in range(nkc + 1):
                                if kk < nkc:
                                    kc = kk
                                    pp = pS5[kc % NS5]
                                    pt = PT5[kc % len(PT5)]
                                    if kc == 0:
                                        c.t.wait_ge(qkv.h, qkv.n)
                                        c.t.wait_ge(vtd.h, vtd.n)
                                    if kc >= NS5:
                                        c.t.wait_ge(scP.h, scP0 + kc - NS5 + 1)
                                    else:
                                        c.t.wait_ge(scP.h, scP0)
                                    peS.inc(c.t.matmul(
                                        pp[:, :TP], Kt5[:, kc * 128:(kc + 1) * 128],
                                        Q5[:, qs * TP:(qs + 1) * TP],
                                        start=True, stop=True))
                                    c.s.wait_ge(peS.h, peS.n)
                                    if kc >= len(PT5):
                                        c.s.wait_ge(peA.h, peA0 + kc - len(PT5) + 1)
                                    else:
                                        c.s.wait_ge(peA.h, peA0)
                                    scP.inc(c.s.activation(pt[:, :], pp[:, :TP], AF.Exp,
                                                           scale=f32(1.0 / np.sqrt(128.0)),
                                                           bias=f32(-16.0)))
                                    if kc >= 4 * qs:
                                        m = kc - 4 * qs
                                        c.v.wait_ge(scP.h, scP.n)
                                        msk.inc(c.v.tensor_tensor(
                                            pt[:, :], pt[:, :],
                                            CM5[:, m * 512:(m + 1) * 512], ALU.mult))
                                if kk >= 1:
                                    kc = kk - 1
                                    pt = PT5[kc % len(PT5)]
                                    c.t.wait_ge(scP.h, scP0 + kc + 1)
                                    if kc >= 4 * qs:
                                        c.t.wait_ge(msk.h, msk0 + kc - 4 * qs + 1)
                                    if kc == 0:
                                        c.t.wait_ge(onrm.h, onrm.n)
                                    qc0 = max(0, kc - 4 * qs)
                                    for qc in range(qc0, 4):
                                        inst = c.t.matmul(pO5[qc][:, 0:129],
                                                          pt[:, qc * 128:(qc + 1) * 128],
                                                          VT5[:, kc * 129:(kc + 1) * 129],
                                                          start=(kc == 0),
                                                          stop=(kc == 4 * qs + qc))
                                    peA.inc(inst)
                            for qc in range(4):
                                c.v.wait_ge(peA.h, peA.n)
                                c.v.reciprocal(RC5[:, :], pO5[qc][:, 128:129])
                                c.v.wait_ge(peOT.h, peOT.n)
                                onrm.inc(c.v.tensor_scalar(OB5[:, :], pO5[qc][:, 0:128],
                                                           RC5[:, :], None, ALU.mult))
                                c.t.wait_ge(onrm.h, onrm.n)
                                c.t.wait_ge(ost5.h, ost5.n)
                                peOT.inc(c.t.transpose(pTR5[:, :], OB5[:, :], IDBs[:, :]))
                                c.v.wait_ge(peOT.h, peOT.n)
                                c.v.wait_ge(fst5.h, fst5.n)
                                ost5.inc(c.v.tensor_copy(OST5[:, qc * 128:(qc + 1) * 128],
                                                         pTR5[:, :]))
                            c.sy.wait_ge(ost5.h, ost5.n)
                            j5 = qs >> 2
                            for slot in (j5, j5 + 4):
                                r0 = slot * 512 + hi * 128
                                fst5.inc(c.sy.dma_start(
                                    out=o5own.ap()[r0:r0 + 128,
                                                   (qs & 3) * TP:((qs & 3) + 1) * TP],
                                    in_=OST5[:, :]), 16)
                c.g.wait_ge(fst5.h, fst5.n)
                with allow_ici_all_to_all():
                    ccs.inc(c.g.collective_compute(
                        "AllToAll", ALU.bypass, replica_groups=RG,
                        ins=[o5own.ap().opt()], outs=[o5all.ap().opt()]), 1)
                AGO = ccs.n
                for eng in (c.t, c.v, c.s, c.sy, c.g):
                    eng.wait_ge(ccs.h, AGO)
                    eng.wait_ge(ost5.h, ost5.n)
                    eng.wait_ge(c.sem("scP").h, c.sem("scP").n)
                    eng.wait_ge(c.sem("msk").h, c.sem("msk").n)
                    eng.wait_ge(c.sem("onrm").h, c.sem("onrm").n)
                    eng.wait_ge(c.sem("qkv").h, c.sem("qkv").n)

            with ExitStack() as stc:
                cvb = dict(vb)
                cvb["wbufs"] = [stc.enter_context(nc.sbuf_tensor("wb5c0", [128, DC * 128], BF16)),
                                stc.enter_context(nc.sbuf_tensor("wb5c1", [128, DC * 128], BF16)),
                                stc.enter_context(nc.sbuf_tensor("wb5c2", [128, DC * 128], BF16))]
                cvb["pAB"] = [stc.enter_context(nc.psum_tensor("pA5c", [128, TP], F32)),
                              stc.enter_context(nc.psum_tensor("pB5c", [128, TP], F32))]
                OBT = stc.enter_context(nc.sbuf_tensor("OBT", [128, DC * TP], BF16))
                OBT2 = stc.enter_context(nc.sbuf_tensor("OBT2", [128, NH * 1024], BF16))
                BO5 = stc.enter_context(nc.sbuf_tensor("BO5", [128, DC], F32))
                cst.inc(c.sy.dma_start(out=BO5[:, :], in_=bout5[:, :]), 16)
                c.v.wait_ge(cst.h, cst.n)
                yev = c.sem("evd")
                osel = c.sem("osel")
                if True:
                    for p in range(NPASS):
                        c.sy.wait_ge(yev.h, yev.n)
                        c.sy.wait_ge(xst.h, xst.n)
                        xld.inc(c.sy.dma_start(
                            out=XF[:, :].rearrange("pp (cc t) -> pp cc t", cc=DC),
                            in_=xdram.rearrange("pp (cc t) -> pp cc t", cc=DC)
                            [:, :, p * TP:(p + 1) * TP]), 16)
                        c.sy.wait_ge(osel.h, osel.n)
                        for cand in range(2):
                            xld.inc(c.sy.dma_start(
                                out=OBT2[:, :].rearrange(
                                    "pp (h cand t) -> pp h cand t", h=NH, cand=2)
                                [:, :, cand, :],
                                in_=bass.AP(o5all.ap().tensor,
                                            cand * 2048 * TOKS + p * TP,
                                            [[TOKS, 128], [128 * TOKS, 16],
                                             [1, TP]])), 16)
                        c.v.wait_ge(xld.h, xld.n)
                        c.v.wait_ge(c.sem("wfree").h, c.sem("wfree").n)
                        for h in range(NH):
                            c.v.tensor_scalar(
                                OBT[:, h * TP:(h + 1) * TP],
                                OBT2[:, h * 1024:h * 1024 + TP],
                                BOH[:, 0:1], None, ALU.mult)
                            inst = c.v.scalar_tensor_tensor(
                                OBT[:, h * TP:(h + 1) * TP],
                                OBT2[:, h * 1024 + TP:h * 1024 + 2 * TP],
                                BOH[:, 1:2], OBT[:, h * TP:(h + 1) * TP],
                                ALU.mult, ALU.add)
                        osel.inc(inst)
                        xw = (osel, osel.n)

                        def evac5(oc, pp, wfree, wfree0, evd):
                            c.v.wait_ge(wfree.h, wfree0 + oc + 1)
                            inst = c.v.scalar_tensor_tensor(
                                XF[:, oc * TP:(oc + 1) * TP], pp[:, :TP],
                                BO5[:, oc:oc + 1], XF[:, oc * TP:(oc + 1) * TP],
                                ALU.add, ALU.add)
                            evd.inc(inst)

                        emit_wstream_mm(c, cvb, wout5, DC, DC,
                                        lambda ci: OBT[:, ci * TP:(ci + 1) * TP],
                                        evac5, rhs_wait=xw)
                        c.sy.wait_ge(yev.h, yev.n)
                        xst.inc(c.sy.dma_start(
                            out=xdram.rearrange("pp (cc t) -> pp cc t", cc=DC)
                            [:, :, p * TP:(p + 1) * TP],
                            in_=XF[:, :].rearrange("pp (cc t) -> pp cc t", cc=DC)), 16)
                for eng in (c.t, c.v, c.s, c.sy, c.g):
                    eng.wait_ge(c.sem("evd").h, c.sem("evd").n)
                    eng.wait_ge(c.sem("wfree").h, c.sem("wfree").n)
                    eng.wait_ge(xst.h, xst.n)

            l1ffn_phase("b")

            with ExitStack() as sto:
                XO = sto.enter_context(nc.sbuf_tensor("XO", [128, NSUB * D], F32))
                pT6 = [sto.enter_context(nc.psum_tensor("pT6a", [128, 128], F32)),
                       sto.enter_context(nc.psum_tensor("pT6b", [128, 128], F32))]
                tpe6 = c.sem("tpe6"); tdv6 = c.sem("tdv6"); fin6 = c.sem("fin6")
                for p in range(NPASS):
                    c.sy.wait_ge(xst.h, xst.n)
                    xld.inc(c.sy.dma_start(
                        out=XF[:, :].rearrange("pp (cc t) -> pp cc t", cc=DC),
                        in_=xdram.rearrange("pp (cc t) -> pp cc t", cc=DC)
                        [:, :, p * TP:(p + 1) * TP]), 16)
                    c.t.wait_ge(xld.h, xld.n)
                    c.t.wait_ge(fin6.h, fin6.n)
                    i6 = 0
                    tpe60, tdv60 = tpe6.n, tdv6.n
                    for sub in range(NSUB):
                        for ci in range(DC):
                            if i6 >= 2:
                                c.t.wait_ge(tdv6.h, tdv60 + i6 - 1)
                            tpe6.inc(c.t.transpose(
                                pT6[i6 % 2][:, :],
                                XF[:, ci * TP + sub * 128:ci * TP + (sub + 1) * 128],
                                IDFs[:, :]))
                            c.v.wait_ge(tpe6.h, tpe60 + i6 + 1)
                            tdv6.inc(c.v.tensor_copy(
                                XO[:, sub * D + ci * 128:sub * D + (ci + 1) * 128],
                                pT6[i6 % 2][:, :]))
                            i6 += 1
                    c.sy.wait_ge(tdv6.h, tdv6.n)
                    fin6.inc(c.sy.dma_start(
                        out=xout.rearrange("(s pp) d -> pp s d", pp=128)
                        [:, p * NSUB:(p + 1) * NSUB, :],
                        in_=XO[:, :].rearrange("pp (s d) -> pp s d", s=NSUB)), 16)
                c.g.wait_ge(fin6.h, fin6.n)
                c.sy.wait_ge(fin6.h, fin6.n)

    return nc


def fused_inmaps(ins):
    x0 = np.ascontiguousarray(ins["L0_attn_x"].reshape(-1, D))
    eye_f = np.eye(128, dtype=np.float32)
    eye_b = np.eye(128).astype(BF)
    w1t = wtile(ins["L0_ffn_W1"]); w2t = wtile(ins["L0_ffn_W2"]); w3t = wtile(ins["L0_ffn_W3"])
    vcat_ffn = np.concatenate([colchunks(ins["L0_ffn_b1"]), colchunks(ins["L0_ffn_b2"]),
                               colchunks(ins["L0_ffn_b3"])], 1).astype(np.float32)
    WC = np.concatenate([ins["L0_attn_Wq"], ins["L0_attn_Wk"], ins["L0_attn_Wv"],
                         ins["L0_attn_Wb"], ins["L0_attn_Wd"]], 1)
    wcat = np.ascontiguousarray(np.transpose(WC.reshape(DC, 128, 384), (1, 0, 2)).reshape(128, -1))
    brow = np.concatenate([np.zeros(256, np.float32), ins["L0_attn_bb"],
                           ins["L0_attn_bd"]]).reshape(1, 384).astype(np.float32)
    tt = np.arange(128)
    mcum = ((tt[:, None] // 64 == tt[None, :] // 64) &
            (tt[:, None] % 64 <= tt[None, :] % 64)).astype(np.float32)
    chi = np.stack([(tt // 64 == 0), (tt // 64 == 1)], 1).astype(np.float32)
    amask = ((tt[:, None] // 64 == tt[None, :] // 64) &
             (tt[:, None] <= tt[None, :])).astype(np.uint8)
    H = _hadamard128()
    h128u = H / np.float32(np.sqrt(2048.0))
    h128d = H * np.float32(0.5 / np.sqrt(8192.0))
    suug_ = colchunks(ins["L1_ffn_SU_ug"])
    sudp_ = colchunks(ins["L1_ffn_SU_dp"])
    jj = np.arange(TP)
    cmask = np.concatenate(
        [((128 * m + tt[:, None]) <= jj[None, :]).astype(BF) for m in range(4)], 1)
    wout_t = wtile(ins["L1_attn_w_out"])
    bo = colchunks(ins["L1_attn_b_out"])
    ims = []
    for cix in range(NCORES):
        b = cix >> 2
        blk = cix & 3
        wblocks = []
        bcols = np.zeros((128, 12), np.float32)
        for l in range(4):
            h = blk * 4 + l
            for j in range(3):
                rows = ins["L1_attn_w_qkv"][j * D + h * 128:j * D + (h + 1) * 128, :]
                wblocks.append(wtile(rows))
                bcols[:, l * 3 + j] = ins["L1_attn_b_qkv"][j * D + h * 128:j * D + (h + 1) * 128]
        qoh_ = np.zeros((128, 4), np.float32)
        qoh_[:, blk] = 1.0
        boh_ = np.zeros((128, 2), np.float32)
        boh_[:, b] = 1.0
        ims.append(dict(
            xin=np.ascontiguousarray(x0[cix * TOKS:(cix + 1) * TOKS]),
            w1=w1t, w2=w2t, w3=w3t, vcat=vcat_ffn, wcat=wcat, brow=brow,
            mcum=mcum, chi=chi, amask=amask, identb=eye_b, identf=eye_f,
            wo0=ins["L0_attn_Wo"],
            wup=np.ascontiguousarray(ins["L1_ffn_W_up"][cix * 1024:(cix + 1) * 1024, :].T),
            wga=np.ascontiguousarray(ins["L1_ffn_W_gate"][cix * 1024:(cix + 1) * 1024, :].T),
            wdo=np.ascontiguousarray(ins["L1_ffn_W_down"][cix * 256:(cix + 1) * 256, :].T),
            h128u=h128u, h128d=h128d, suug=suug_, sudp=sudp_,
            wsup=np.ascontiguousarray(ins["L1_ffn_Wscale_ug"][cix * 1024:(cix + 1) * 1024].reshape(1, -1)),
            wsga=np.ascontiguousarray(ins["L1_ffn_Wscale_ug"][8192 + cix * 1024:8192 + (cix + 1) * 1024].reshape(1, -1)),
            wsdo=np.ascontiguousarray(ins["L1_ffn_Wscale_dp"][cix * 256:(cix + 1) * 256].reshape(1, -1)),
            wqkv5=np.concatenate(wblocks, 1), bqkv5=bcols, cmask5=cmask,
            wout5=wout_t, bout5=bo, qoh=qoh_, boh=boh_))
    return ims


def kernel(**inputs):
    ins = {k: np.asarray(v, dtype=np.float32) for k, v in inputs.items()}
    nc = build_fused()
    res = run_bass_kernel_spmd(nc, fused_inmaps(ins), core_ids=list(range(NCORES)))
    out = np.concatenate([np.asarray(res.results[i]["xout"]) for i in range(NCORES)], 0)
    return np.ascontiguousarray(out.reshape(2, SEQ, D)).astype(np.float32)


def _hadamard128():
    i = np.arange(128)
    pc = np.array([bin(a & b).count("1") for a in i for b in i]).reshape(128, 128)
    return ((-1.0) ** pc).astype(np.float32)


def kernel_unfused(**inputs):
    ins = {k: np.asarray(v, dtype=np.float32) for k, v in inputs.items()}
    x0 = np.ascontiguousarray(ins["L0_attn_x"].reshape(-1, D))
    eye_f = np.eye(128, dtype=np.float32)
    eye_b = np.eye(128).astype(BF)

    w1t = wtile(ins["L0_ffn_W1"])
    w2t = wtile(ins["L0_ffn_W2"])
    w3t = wtile(ins["L0_ffn_W3"])
    vcat_ffn = np.concatenate([colchunks(ins["L0_ffn_b1"]),
                               colchunks(ins["L0_ffn_b2"]),
                               colchunks(ins["L0_ffn_b3"])], 1).astype(np.float32)
    nc1 = build_ffn0_launch(True)
    im = [dict(xin=np.ascontiguousarray(x0[i * TOKS:(i + 1) * TOKS]), w1=w1t,
               w2=w2t, w3=w3t, vcat=vcat_ffn, identf=eye_f) for i in range(NCORES)]
    r1 = run_bass_kernel_spmd(nc1, im, core_ids=list(range(NCORES)))
    x1f = [np.asarray(r1.results[i]["xout"]) for i in range(NCORES)]

    WC = np.concatenate([ins["L0_attn_Wq"], ins["L0_attn_Wk"], ins["L0_attn_Wv"],
                         ins["L0_attn_Wb"], ins["L0_attn_Wd"]], 1)
    wcat = np.ascontiguousarray(
        np.transpose(WC.reshape(DC, 128, 384), (1, 0, 2)).reshape(128, -1))
    brow = np.concatenate([np.zeros(256, np.float32), ins["L0_attn_bb"],
                           ins["L0_attn_bd"]]).reshape(1, 384).astype(np.float32)
    tt = np.arange(128)
    mcum = ((tt[:, None] // 64 == tt[None, :] // 64) &
            (tt[:, None] % 64 <= tt[None, :] % 64)).astype(np.float32)
    chi = np.stack([(tt // 64 == 0), (tt // 64 == 1)], 1).astype(np.float32)
    amask = ((tt[:, None] // 64 == tt[None, :] // 64) &
             (tt[:, None] <= tt[None, :])).astype(np.uint8)
    nc2a = build_gla_a_launch()
    im = [dict(xin=x1f[i], wcat=wcat, brow=brow, mcum=mcum, chi=chi,
               amask=amask, identb=eye_b, identf=eye_f) for i in range(NCORES)]
    r2a = run_bass_kernel_spmd(nc2a, im, core_ids=list(range(NCORES)))
    nc2b = build_gla_b_launch()
    im = []
    for cix in range(NCORES):
        a = cix & 3
        b = cix >> 2
        u_in = np.zeros((128 * 128, 128), np.float32)
        pc_in = np.ones((128, 128), np.float32)
        nd = 32 * (a + 1)
        pad = 128 - nd
        ucat = np.concatenate(
            [np.asarray(r2a.results[4 * b + j]["uown"]) for j in range(a + 1)], 0)
        u_in[pad * 128:] = np.concatenate(
            [ucat.reshape(nd, 64, 128)] * 2, 1).reshape(nd * 128, 128)
        pcc = np.concatenate(
            [np.asarray(r2a.results[4 * b + j]["pcown"]) for j in range(a + 1)], 1)
        pc_in[0:64, pad:] = pcc
        pc_in[64:128, pad:] = pcc
        im.append(dict(xin=x1f[cix], uin=u_in, pcin=pc_in,
                       qtin=np.asarray(r2a.results[cix]["qtown"]),
                       oiin=np.asarray(r2a.results[cix]["oiown"]),
                       wo=ins["L0_attn_Wo"]))
    r2b = run_bass_kernel_spmd(nc2b, im, core_ids=list(range(NCORES)))
    x2f = [np.asarray(r2b.results[i]["xout"]) for i in range(NCORES)]

    nc3 = build_ffn0_launch(False)
    im = [dict(xin=x2f[i], w1=w1t, w2=w2t, w3=w3t, vcat=vcat_ffn,
               identf=eye_f) for i in range(NCORES)]
    r3 = run_bass_kernel_spmd(nc3, im, core_ids=list(range(NCORES)))
    x3f = [np.asarray(r3.results[i]["xout"]) for i in range(NCORES)]

    H = _hadamard128()
    h128u = H / np.float32(np.sqrt(2048.0))
    h128d = H * np.float32(0.5 / np.sqrt(8192.0))
    suug = colchunks(ins["L1_ffn_SU_ug"])
    sudp = colchunks(ins["L1_ffn_SU_dp"])
    ncp = build_prep_launch()
    im = []
    for cix in range(NCORES):
        im.append(dict(
            wup=np.ascontiguousarray(ins["L1_ffn_W_up"][cix * 1024:(cix + 1) * 1024, :].T),
            wga=np.ascontiguousarray(ins["L1_ffn_W_gate"][cix * 1024:(cix + 1) * 1024, :].T),
            wdo=np.ascontiguousarray(ins["L1_ffn_W_down"][cix * 256:(cix + 1) * 256, :].T),
            h128u=h128u, h128d=h128d, suug=suug, sudp=sudp,
            wsup=np.ascontiguousarray(ins["L1_ffn_Wscale_ug"][cix * 1024:(cix + 1) * 1024].reshape(1, -1)),
            wsga=np.ascontiguousarray(ins["L1_ffn_Wscale_ug"][8192 + cix * 1024:8192 + (cix + 1) * 1024].reshape(1, -1)),
            wsdo=np.ascontiguousarray(ins["L1_ffn_Wscale_dp"][cix * 256:(cix + 1) * 256].reshape(1, -1))))
    rp = run_bass_kernel_spmd(ncp, im, core_ids=list(range(NCORES)))
    wub = np.concatenate([np.asarray(rp.results[i]["oup"]) for i in range(NCORES)], 1)
    wgb = np.concatenate([np.asarray(rp.results[i]["oga"]) for i in range(NCORES)], 1)
    wdb = np.concatenate([np.asarray(rp.results[i]["odo"]) for i in range(NCORES)], 1)

    nc4 = build_l1ffn_launch()
    im = [dict(xin=x3f[i], wub=wub, wgb=wgb, wdb=wdb) for i in range(NCORES)]
    r4 = run_bass_kernel_spmd(nc4, im, core_ids=list(range(NCORES)))
    x4f = [np.asarray(r4.results[i]["xout"]) for i in range(NCORES)]

    nc5a = build_ln5_launch()
    im = [dict(xin=x4f[i]) for i in range(NCORES)]
    r5a = run_bass_kernel_spmd(nc5a, im, core_ids=list(range(NCORES)))
    xh_b = []
    for b in range(2):
        blocks = [np.asarray(r5a.results[4 * b + j]["xh"]).reshape(128, DC, TOKS)
                  for j in range(4)]
        xh_b.append(np.ascontiguousarray(
            np.stack(blocks, 2).reshape(128, DC * SEQ)))
    nc5b = build_attn_launch()
    jj = np.arange(TP)
    cmask = np.concatenate(
        [((128 * m + tt[:, None]) <= jj[None, :]).astype(BF) for m in range(4)], 1)
    im = []
    for cix in range(NCORES):
        b = cix >> 2
        hg = cix & 3
        wblocks = []
        bcols = np.zeros((128, 12), np.float32)
        for l in range(4):
            h = hg * 4 + l
            for j in range(3):
                rows = ins["L1_attn_w_qkv"][j * D + h * 128:j * D + (h + 1) * 128, :]
                wblocks.append(wtile(rows))
                bcols[:, l * 3 + j] = ins["L1_attn_b_qkv"][j * D + h * 128:j * D + (h + 1) * 128]
        im.append(dict(xh=xh_b[b], wqkv=np.concatenate(wblocks, 1), bqkv=bcols,
                       cmask=cmask, identb=eye_b))
    r5b = run_bass_kernel_spmd(nc5b, im, core_ids=list(range(NCORES)))
    nc5c = build_outproj_launch()
    wout_t = wtile(ins["L1_attn_w_out"])
    bo = colchunks(ins["L1_attn_b_out"])
    im = []
    for cix in range(NCORES):
        b = cix >> 2
        blk = cix & 3
        ob = np.zeros((128, DC, TOKS), dtype=BF)
        for h in range(NH):
            srcc = 4 * b + h // 4
            ob[:, h, :] = np.asarray(r5b.results[srcc]["oown"])[
                (h % 4) * 128:(h % 4 + 1) * 128, blk * TOKS:(blk + 1) * TOKS]
        im.append(dict(xin=x4f[cix], oin=np.ascontiguousarray(ob.reshape(128, -1)),
                       wout=wout_t, bout=bo))
    r5c = run_bass_kernel_spmd(nc5c, im, core_ids=list(range(NCORES)))
    x5f = [np.asarray(r5c.results[i]["xout"]) for i in range(NCORES)]

    im = [dict(xin=x5f[i], wub=wub, wgb=wgb, wdb=wdb) for i in range(NCORES)]
    r6 = run_bass_kernel_spmd(nc4, im, core_ids=list(range(NCORES)))

    out = np.concatenate([x_tok_of_fmaj(np.asarray(r6.results[i]["xout"]))
                          for i in range(NCORES)], 0)
    return np.ascontiguousarray(out.reshape(2, SEQ, D)).astype(np.float32)


# version 10
# speedup vs baseline: 1.0491x; 1.0491x over previous
import numpy as np
import ml_dtypes
from contextlib import ExitStack
import concourse.bass as bass
import concourse.mybir as mybir
from concourse.bass_utils import run_bass_kernel_spmd
from concourse.aot_env import allow_ici_all_to_all

F32 = mybir.dt.float32
BF16 = mybir.dt.bfloat16
U8 = mybir.dt.uint8
AF = mybir.ActivationFunctionType
ALU = mybir.AluOpType
AX = mybir.AxisListType
f32 = np.float32
BF = ml_dtypes.bfloat16

NCORES = 8
D = 2048
DC = 16
TOKS = 2048
TP = 512
NPASS = TOKS // TP
NSUB = TP // 128
H0 = 4096
H0C = H0 // 128
INTER = 8192
NH = 16
HD = 128
SEQ = 8192


def wtile(W):
    dout, din = W.shape
    t = W.reshape(dout // 128, 128, din // 128, 128)
    return np.ascontiguousarray(np.transpose(t, (3, 0, 2, 1)).reshape(128, -1))


def colchunks(v):
    return np.ascontiguousarray(np.asarray(v).reshape(-1, 128).T)


def x_fmaj(x_tok):
    t = x_tok.reshape(-1, DC, 128)
    return np.ascontiguousarray(np.transpose(t, (2, 1, 0)).reshape(128, -1))


def x_tok_of_fmaj(xf):
    t = xf.reshape(128, DC, -1)
    return np.ascontiguousarray(np.transpose(t, (2, 1, 0)).reshape(-1, D))


class Sem:
    def __init__(self, st, nc, name):
        self.h = st.enter_context(nc.semaphore(name))
        self.n = 0

    def inc(self, inst, k=1):
        inst.then_inc(self.h, k)
        self.n += k
        return inst


class Ctx:
    def __init__(self, nc, st):
        self.nc = nc
        self.st = st
        self.t = nc.tensor
        self.v = nc.vector
        self.s = nc.scalar
        self.g = nc.gpsimd
        self.sy = nc.sync
        self._sems = {}

    def sem(self, name):
        if name not in self._sems:
            self._sems[name] = Sem(self.st, self.nc, name)
        return self._sems[name]

    def sbuf(self, name, shape, dtype):
        return self.st.enter_context(self.nc.sbuf_tensor(name, shape, dtype))

    def psum(self, name, shape, dtype=F32):
        return self.st.enter_context(self.nc.psum_tensor(name, shape, dtype))


def emit_ln(c, vb, XF, XHB, extra_bcast_waits=()):
    XB = vb["XB"]; XQ = vb["XQ"]
    pS1 = vb["pS1"]; pS2 = vb["pS2"]
    onesbb = vb["onesbb"]; LROW = vb["LROW"]
    lnS = c.sem("lnS"); lnV = c.sem("lnV"); lnP = c.sem("lnP")

    c.s.wait_ge(lnV.h, lnV.n)
    for ci in range(DC):
        c.s.copy(XB[:, ci * TP:(ci + 1) * TP], XF[:, ci * TP:(ci + 1) * TP])
        inst = c.s.activation(XQ[:, ci * TP:(ci + 1) * TP],
                              XF[:, ci * TP:(ci + 1) * TP], AF.Square)
    lnS.inc(inst)
    c.t.wait_ge(lnS.h, lnS.n)
    for s_ in extra_bcast_waits:
        c.t.wait_ge(s_.h, s_.n)
    for ci in range(DC):
        c.t.matmul(pS1[:, :TP], onesbb[:, :], XB[:, ci * TP:(ci + 1) * TP],
                   start=(ci == 0), stop=(ci == DC - 1))
    for ci in range(DC):
        inst = c.t.matmul(pS2[:, :TP], onesbb[:, :], XQ[:, ci * TP:(ci + 1) * TP],
                          start=(ci == 0), stop=(ci == DC - 1))
    lnP.inc(inst)
    mean = LROW[:, 0 * TP:1 * TP]
    var = LROW[:, 1 * TP:2 * TP]
    rstd = LROW[:, 2 * TP:3 * TP]
    mr = LROW[:, 3 * TP:4 * TP]
    c.s.wait_ge(lnP.h, lnP.n)
    c.s.activation(mean, pS1[:, :TP], AF.Copy, scale=f32(1.0 / D))
    lnS.inc(c.s.activation(var, mean, AF.Square))
    c.v.wait_ge(lnS.h, lnS.n)
    lnV.inc(c.v.scalar_tensor_tensor(var, pS2[:, :TP], 1.0 / D, var,
                                     ALU.mult, ALU.subtract))
    c.s.wait_ge(lnV.h, lnV.n)
    c.s.activation(rstd, var, AF.Ln, bias=f32(1e-5))
    lnS.inc(c.s.activation(rstd, rstd, AF.Exp, scale=f32(-0.5)))
    c.v.wait_ge(lnS.h, lnS.n)
    c.v.wait_ge(c.sem("wfree").h, c.sem("wfree").n)
    lnV.inc(c.v.tensor_tensor(mr, mean, rstd, ALU.mult))
    for ci in range(DC):
        c.v.tensor_tensor(XB[:, ci * TP:(ci + 1) * TP],
                          XF[:, ci * TP:(ci + 1) * TP], rstd, ALU.mult)
        inst = c.v.tensor_tensor(XHB[:, ci * TP:(ci + 1) * TP],
                                 XB[:, ci * TP:(ci + 1) * TP], mr,
                                 ALU.subtract)
    lnV.inc(inst)
    return lnV


def emit_ln_stats(c, vb, xdram, p, parity, sbuf_xf=None, pre=None):
    XB2 = vb["XB2"]; XQr = vb["XQr"]
    pS1 = vb["pS1"]; pS2 = vb["pS2"]; onesbb = vb["onesbb"]; LROW = vb["LROW"]
    lnq = c.sem("lnq"); lnP = c.sem("lnP")
    lnS = c.sem("lnS"); lnV = c.sem("lnV")
    lnq0, lnP0 = lnq.n, lnP.n
    if sbuf_xf is None:
        lnc = c.sem("lnc")
        c.g.wait_ge(lnP.h, lnP.n)
        lnc.inc(c.g.dma_start(
            out=XB2[:, :].rearrange("pp (cc t) -> pp cc t", cc=DC),
            in_=xdram.rearrange("pp (cc t) -> pp cc t", cc=DC)
            [:, :, p * TP:(p + 1) * TP]), 16)
        rdy = (lnc, lnc.n)
    else:
        lncs = c.sem("lncs")
        c.s.wait_ge(lnP.h, lnP.n)
        if pre is not None:
            c.s.wait_ge(pre[0].h, pre[1])
        for ci in range(DC):
            inst = c.s.copy(XB2[:, ci * TP:(ci + 1) * TP],
                            sbuf_xf[:, ci * TP:(ci + 1) * TP])
        lncs.inc(inst)
        rdy = (lncs, lncs.n)
    for ci in range(DC):
        b = ci % 2
        if ci == 0:
            c.v.wait_ge(rdy[0].h, rdy[1])
            c.v.wait_ge(lnP.h, lnP0)
        if ci >= 2:
            c.v.wait_ge(lnP.h, lnP0 + 2 * (ci - 1))
        lnq.inc(c.v.tensor_tensor(XQr[:, b * TP:(b + 1) * TP],
                                  XB2[:, ci * TP:(ci + 1) * TP],
                                  XB2[:, ci * TP:(ci + 1) * TP], ALU.mult))
        c.t.wait_ge(lnq.h, lnq0 + ci + 1)
        if ci == 0:
            c.t.wait_ge(lnS.h, lnS.n)
        c.t.matmul(pS1[:, :TP], onesbb[:, :], XB2[:, ci * TP:(ci + 1) * TP],
                   start=(ci == 0), stop=(ci == DC - 1))
        inst = c.t.matmul(pS2[:, :TP], onesbb[:, :], XQr[:, b * TP:(b + 1) * TP],
                          start=(ci == 0), stop=(ci == DC - 1))
        lnP.inc(inst, 2)
    rstd = LROW[:, (2 * parity) * TP:(2 * parity + 1) * TP]
    mr = LROW[:, (2 * parity + 1) * TP:(2 * parity + 2) * TP]
    c.s.wait_ge(lnP.h, lnP.n)
    c.s.wait_ge(lnV.h, lnV.n)
    c.s.activation(mr, pS1[:, :TP], AF.Copy, scale=f32(1.0 / D))
    lnS.inc(c.s.activation(rstd, mr, AF.Square))
    c.v.wait_ge(lnS.h, lnS.n)
    lnV.inc(c.v.scalar_tensor_tensor(rstd, pS2[:, :TP], 1.0 / D, rstd,
                                     ALU.mult, ALU.subtract))
    c.s.wait_ge(lnV.h, lnV.n)
    c.s.activation(rstd, rstd, AF.Ln, bias=f32(1e-5))
    lnS.inc(c.s.activation(rstd, rstd, AF.Exp, scale=f32(-0.5)))
    c.v.wait_ge(lnS.h, lnS.n)
    lnV.inc(c.v.tensor_tensor(mr, mr, rstd, ALU.mult))
    return lnV


def emit_ln_norm(c, vb, xdram, p, parity, qe=None, xhb_free=None,
                 sbuf_xf=None, pre=None):
    XFR = vb["XFR"]; XFS = vb["XFS"]; XHB = vb["XHB"]; LROW = vb["LROW"]
    lnV = c.sem("lnV")
    rstd = LROW[:, (2 * parity) * TP:(2 * parity + 1) * TP]
    mr = LROW[:, (2 * parity + 1) * TP:(2 * parity + 2) * TP]
    lnV0 = lnV.n
    if sbuf_xf is None:
        lnf = c.sem("lnf" if qe is c.sy else
                    ("lnfs" if qe is c.s else "lnfg"))
        lnf0 = lnf.n
    for ci in range(DC):
        b = ci % 2
        src = XFR[:, b * TP:(b + 1) * TP]
        if sbuf_xf is None:
            if ci < 2:
                qe.wait_ge(lnV.h, lnV0)
            else:
                qe.wait_ge(lnV.h, lnV0 + 2 * (ci - 1))
            if ci > 0:
                qe.wait_ge(lnf.h, lnf0 + 16 * ci)
            lnf.inc(qe.dma_start(
                out=src,
                in_=xdram[:, ci * TOKS + p * TP:ci * TOKS + (p + 1) * TP]), 16)
            c.v.wait_ge(lnf.h, lnf0 + 16 * (ci + 1))
        if ci == 0:
            if xhb_free is not None:
                c.v.wait_ge(xhb_free[0].h, xhb_free[1])
            if pre is not None:
                c.v.wait_ge(pre[0].h, pre[1])
        if sbuf_xf is None:
            c.v.tensor_tensor(XFS[:, :], src, rstd, ALU.mult)
        else:
            c.v.tensor_tensor(XFS[:, :], sbuf_xf[:, ci * TP:(ci + 1) * TP],
                              rstd, ALU.mult)
        inst = c.v.tensor_tensor(XHB[:, ci * TP:(ci + 1) * TP], XFS[:, :], mr,
                                 ALU.subtract)
        lnV.inc(inst, 2)
    return lnV


def emit_wstream_mm(c, vb, w_dram, n_oc, cin, RHS, evac_fn, rhs_wait=None):
    wbufs = vb["wbufs"]; pAB = vb["pAB"]
    nb = len(wbufs)
    wld = c.sem("wld"); wfree = c.sem("wfree"); evd = c.sem("evd")
    if not hasattr(c, "wlast"):
        c.wlast = {}
    span = cin * 128
    wld0, wfree0, evd0 = wld.n, wfree.n, evd.n
    for oc in range(n_oc):
        buf = wbufs[oc % nb]
        pp = pAB[oc % 2]
        key = id(buf)
        if key in c.wlast:
            c.g.wait_ge(wfree.h, c.wlast[key])
        wld.inc(c.g.dma_start(out=buf[:, :span],
                              in_=w_dram[:, oc * span:(oc + 1) * span]), 16)
        c.t.wait_ge(wld.h, wld0 + 16 * (oc + 1))
        if oc < 2:
            c.t.wait_ge(evd.h, evd0)
            if rhs_wait is not None and oc == 0:
                c.t.wait_ge(rhs_wait[0].h, rhs_wait[1])
        else:
            c.t.wait_ge(evd.h, evd0 + oc - 1)
        for ci in range(cin):
            inst = c.t.matmul(pp[:, :TP], buf[:, ci * 128:(ci + 1) * 128],
                              RHS(ci), start=(ci == 0), stop=(ci == cin - 1))
        wfree.inc(inst)
        c.wlast[key] = wfree.n
        evac_fn(oc, pp, wfree, wfree0, evd)
    return wld, wfree, evd


def emit_ffn0_pass(c, vb, XF, XHB, lnV, w1, w2, w3, hook=None, xf_wait=None):
    H1B = vb["H1B"]; H2B = vb["H2B"]; vcat = vb["vcat"]; hb3 = vb["hb3"]
    TY = vb["TY"]
    B1C, B2C = vb["B1C"], vb["B2C"]
    evd = c.sem("evd")

    def evac_relu(OUT, bcol0):
        def f(oc, pp, wfree, wfree0, evd):
            c.s.wait_ge(wfree.h, wfree0 + oc + 1)
            inst = c.s.activation(OUT[:, oc * TP:(oc + 1) * TP], pp[:, :TP],
                                  AF.Relu, bias=vcat[:, bcol0 + oc:bcol0 + oc + 1])
            evd.inc(inst)
        return f

    def evac_res(oc, pp, wfree, wfree0, evd):
        c.v.wait_ge(wfree.h, wfree0 + oc + 1)
        if oc == 0 and xf_wait is not None:
            c.v.wait_ge(xf_wait[0].h, xf_wait[1])
        ty = TY[:, 0:TP]
        c.v.tensor_scalar(ty, pp[:, :TP], 0.5, hb3[:, oc:oc + 1],
                          ALU.mult, ALU.add)
        inst = c.v.tensor_tensor(XF[:, oc * TP:(oc + 1) * TP], ty,
                                 XF[:, oc * TP:(oc + 1) * TP], ALU.add)
        evd.inc(inst)

    emit_wstream_mm(c, vb, w1, H0C, DC, lambda ci: XHB[:, ci * TP:(ci + 1) * TP],
                    evac_relu(H1B, B1C), rhs_wait=(lnV, lnV.n))
    if hook is not None:
        hook()
    emit_wstream_mm(c, vb, w2, H0C, H0C, lambda ci: H1B[:, ci * TP:(ci + 1) * TP],
                    evac_relu(H2B, B2C))
    emit_wstream_mm(c, vb, w3, DC, H0C, lambda ci: H2B[:, ci * TP:(ci + 1) * TP],
                    evac_res)
    return evd


def mk_ffn0_vb(c):
    vb = {}
    vb["XB"] = c.sbuf("XB", [128, DC * TP], BF16)
    vb["XQ"] = c.sbuf("XQ", [128, DC * TP], BF16)
    vb["XHB"] = c.sbuf("XHB", [128, DC * TP], BF16)
    vb["H1B"] = c.sbuf("H1B", [128, H0C * TP], BF16)
    vb["H2B"] = c.sbuf("H2B", [128, H0C * TP], BF16)
    vb["wbufs"] = [c.sbuf("wb0", [128, H0C * 128], BF16),
                   c.sbuf("wb1", [128, H0C * 128], BF16),
                   c.sbuf("wb2", [128, H0C * 128], BF16)]
    vb["TY"] = c.sbuf("TY", [128, 2 * TP], F32)
    vb["SROW"] = c.sbuf("SROW", [1, 5 * TP], F32)
    vb["onesb"] = c.sbuf("onesb", [128, 1], BF16)
    vb["ones1f"] = c.sbuf("ones1f", [1, 128], F32)
    vb["onesbb"] = c.sbuf("onesbb", [128, 128], BF16)
    vb["LROW"] = c.sbuf("LROW", [128, 4 * TP], F32)
    vb["pS1"] = c.psum("pS1", [128, TP])
    vb["pS2"] = c.psum("pS2", [128, TP])
    vb["pR"] = c.psum("pR", [128, TP])
    vb["pM"] = c.psum("pM", [128, TP])
    vb["pAB"] = [c.psum("pA", [128, TP]), c.psum("pB", [128, TP])]
    return vb


def build_ffn0_launch(first):
    nc = bass.Bass("TRN2", target_bir_lowering=False)
    if first:
        xin = nc.dram_tensor("xin", [TOKS, D], F32, kind="ExternalInput").ap()
    else:
        xin = nc.dram_tensor("xin", [128, DC * TOKS], F32, kind="ExternalInput").ap()
    w1 = nc.dram_tensor("w1", [128, H0C * DC * 128], F32, kind="ExternalInput").ap()
    w2 = nc.dram_tensor("w2", [128, H0C * H0C * 128], F32, kind="ExternalInput").ap()
    w3 = nc.dram_tensor("w3", [128, DC * H0C * 128], F32, kind="ExternalInput").ap()
    vcat_in = nc.dram_tensor("vcat", [128, 80], F32, kind="ExternalInput").ap()
    identf = nc.dram_tensor("identf", [128, 128], F32, kind="ExternalInput").ap()
    xout = nc.dram_tensor("xout", [128, DC * TOKS], F32, kind="ExternalOutput").ap()

    with ExitStack() as st:
        c = Ctx(nc, st)
        vb = mk_ffn0_vb(c)
        vb["vcat"] = c.sbuf("vcat_sb", [128, 80], F32)
        vb["hb3"] = c.sbuf("hb3", [128, DC], F32)
        vb["B1C"], vb["B2C"], B3C = 0, 32, 64
        XF = c.sbuf("XF", [128, DC * TP], F32)
        if first:
            XT = c.sbuf("XT", [128, NSUB * D], F32)
            identsb = c.sbuf("identsb", [128, 128], F32)
            pT = [c.psum("pT0", [128, 128]), c.psum("pT1", [128, 128])]
        block = st.enter_context(nc.Block())

        @block.gpsimd
        def _(_g):
            cst = c.sem("cst")
            cst.inc(c.sy.dma_start(out=vb["vcat"][:, :], in_=vcat_in[:, :]), 16)
            if first:
                cst.inc(c.sy.dma_start(out=identsb[:, :], in_=identf[:, :]), 16)
            c.v.wait_ge(cst.h, cst.n)
            c.s.wait_ge(cst.h, cst.n)
            c.v.memset(vb["onesb"][:, :], 1.0)
            c.v.memset(vb["onesbb"][:, :], 1.0)
            c.v.memset(vb["ones1f"][:, :], 1.0)
            ini = c.v.tensor_scalar(vb["hb3"][:, :], vb["vcat"][:, B3C:B3C + DC],
                                    0.5, None, ALU.mult)
            c.sem("lnV").inc(ini)

            xld = c.sem("xld")
            xst = c.sem("xst")
            tpe = c.sem("tpe")
            tdv = c.sem("tdv")
            evd = c.sem("evd")
            for p in range(NPASS):
                if first:
                    c.sy.wait_ge(tpe.h, tpe.n)
                    xld.inc(c.sy.dma_start(
                        out=XT[:, :].rearrange("pp (s d) -> pp s d", s=NSUB),
                        in_=xin.rearrange("(s pp) d -> pp s d", pp=128)
                        [:, p * NSUB:(p + 1) * NSUB, :]), 16)
                    tpe0, tdv0 = tpe.n, tdv.n
                    c.t.wait_ge(xld.h, xld.n)
                    c.t.wait_ge(evd.h, evd.n)
                    c.t.wait_ge(xst.h, xst.n)
                    i = 0
                    for sub in range(NSUB):
                        for ci in range(DC):
                            if i >= 2:
                                c.t.wait_ge(tdv.h, tdv0 + i - 1)
                            tr = c.t.transpose(
                                pT[i % 2][:, :],
                                XT[:, sub * D + ci * 128:sub * D + (ci + 1) * 128],
                                identsb[:, :])
                            tpe.inc(tr)
                            c.v.wait_ge(tpe.h, tpe0 + i + 1)
                            cp = c.v.tensor_copy(
                                XF[:, ci * TP + sub * 128:ci * TP + (sub + 1) * 128],
                                pT[i % 2][:, :])
                            tdv.inc(cp)
                            i += 1
                    c.s.wait_ge(tdv.h, tdv.n)
                else:
                    c.sy.wait_ge(evd.h, evd.n)
                    c.sy.wait_ge(xst.h, xst.n)
                    xld.inc(c.sy.dma_start(
                        out=XF[:, :].rearrange("pp (cc t) -> pp cc t", cc=DC),
                        in_=xin.rearrange("pp (cc t) -> pp cc t", cc=DC)
                        [:, :, p * TP:(p + 1) * TP]), 16)
                    c.s.wait_ge(xld.h, xld.n)

                lnV = emit_ln(c, vb, XF, vb["XHB"])
                evd = emit_ffn0_pass(c, vb, XF, vb["XHB"], lnV, w1, w2, w3)
                c.sy.wait_ge(evd.h, evd.n)
                xst.inc(c.sy.dma_start(
                    out=xout.rearrange("pp (cc t) -> pp cc t", cc=DC)
                    [:, :, p * TP:(p + 1) * TP],
                    in_=XF[:, :].rearrange("pp (cc t) -> pp cc t", cc=DC)), 16)
            c.g.wait_ge(xst.h, xst.n)

    return nc



def emit_gla_a_subtile(c, vb, XHB, p, s, core_b, cut=99):
    si = p * NSUB + s
    WCB = vb["WCB"]; BBC = vb["BBC"]; MCUM = vb["MCUM"]; CHI = vb["CHI"]
    AMASK = vb["AMASK"]; IDB = vb["IDB"]; ZEROB = vb["ZEROB"]
    P384 = vb["P384"]; SC64 = vb["SC64"]; NLB = vb["NLB"]; DELTA = vb["DELTA"]
    NCC = vb["NCC"]
    EM = vb["EM"]; EP = vb["EP"]; QTM = vb["QTM"]; KTM = vb["KTM"]
    TKF = vb["TKF"]; VB = vb["VB"]; KTP = vb["KTP"]; QTall = vb["QTall"]
    USTG = vb["USTG"]; OISTG = vb["OISTG"]; PCST = vb["PCST"]
    pPP = vb["pPP"]; pU = vb["pU"]; pTQ = vb["pTQ"]
    pOXv = vb.get("pOX")
    if pOXv is not None:
        pNCa, pNCb = pOXv[:, 128:192], pOXv[0:64, 192:194]
    else:
        pNC = vb["pNC"]
        pNCa, pNCb = pNC[:, 0:64], pNC[0:64, 64:66]
    pA = vb["pR"]; pO = vb["pM"]
    st = si * 128

    prjP = c.sem("prjP"); prj = c.sem("prj"); scA = c.sem("scA")
    dvN = c.sem("dvN"); pePC = c.sem("pePC"); nclam = c.sem("nclam")
    scE = c.sem("scE"); dvQ = c.sem("dvQ"); peU = c.sem("peU")
    peT = c.sem("peT"); uev = c.sem("uev"); tqv = c.sem("tqv")
    peA = c.sem("peA"); sel = c.sem("sel"); peO = c.sem("peO")
    oev = c.sem("oev")

    c.t.wait_ge(c.sem("lnV").h, c.sem("lnV").n)
    c.t.wait_ge(prj.h, prj.n)
    if vb.get("pOX") is None:
        c.t.wait_ge(c.sem("otv").h, c.sem("otv").n)
    for ci in range(DC):
        inst = c.t.matmul(pPP[:, 0:384],
                          XHB[:, ci * TP + s * 128:ci * TP + (s + 1) * 128],
                          WCB[:, ci * 384:(ci + 1) * 384],
                          start=(ci == 0), stop=(ci == DC - 1))
    prjP.inc(inst)
    c.v.wait_ge(prjP.h, prjP.n)
    c.v.wait_ge(c.sem("scA").h, scA.n)
    prj.inc(c.v.tensor_tensor(P384[:, :], pPP[:, 0:384], BBC[:, :], ALU.add))
    if cut <= 1:
        return
    zb = P384[:, 256:320]; zd = P384[:, 320:384]
    t0 = SC64[:, 0:64]; t1 = SC64[:, 64:128]
    t2 = SC64[:, 128:192]; t3 = SC64[:, 192:256]
    
    c.s.wait_ge(prj.h, prj.n)
    c.s.wait_ge(dvN.h, dvN.n)
    c.s.activation(t0, zb, AF.Abs)
    c.s.activation(t0, t0, AF.Exp, scale=f32(-1.0))
    c.s.activation(t0, t0, AF.Ln, bias=f32(1.0))
    c.s.activation(t1, zb, AF.Relu, scale=f32(-1.0))
    c.s.activation(t2, zd, AF.Abs)
    c.s.activation(t2, t2, AF.Exp, scale=f32(-1.0))
    c.s.activation(t2, t2, AF.Ln, bias=f32(1.0))
    scA.inc(c.s.activation(t3, zd, AF.Relu))
    c.v.wait_ge(scA.h, scA.n)
    c.v.wait_ge(pePC.h, pePC.n)
    c.v.tensor_tensor(NLB[:, :], t1, t0, ALU.add)
    dvN.inc(c.v.tensor_tensor(DELTA[:, :], t3, t2, ALU.add))
    if cut <= 2:
        return
    c.t.wait_ge(dvN.h, dvN.n)
    c.t.wait_ge(nclam.h, nclam.n)
    if pOXv is not None:
        c.t.wait_ge(c.sem("otv").h, c.sem("otv").n)
    c.t.matmul(pNCa, MCUM[:, :], NLB[:, :], start=True, stop=True)
    pePC.inc(c.t.matmul(pNCb, NLB[:, :], CHI[:, :], start=True, stop=True))
    c.v.wait_ge(pePC.h, pePC.n)
    c.v.tensor_scalar(NCC[:, 0:64], pNCa, 85.0, None, ALU.min)
    nclam.inc(c.v.tensor_scalar(NCC[0:64, 64:66], pNCb, 85.0, None, ALU.min))
    c.s.wait_ge(nclam.h, nclam.n)
    c.s.activation(EM[:, :], NCC[:, 0:64], AF.Exp, scale=f32(-1.0))
    c.s.activation(EP[:, :], NCC[:, 0:64], AF.Exp)
    scE.inc(c.s.activation(NCC[0:64, 66:68], NCC[0:64, 64:66], AF.Exp, scale=f32(-1.0)))
    c.v.wait_ge(scE.h, scE.n)
    c.v.wait_ge(peU.h, peU.n)
    c.v.wait_ge(peT.h, peT.n)
    c.v.wait_ge(peO.h, peO.n)
    c.v.tensor_tensor(QTM[0:64, 0:64], P384[0:64, 0:64], EM[0:64, :], ALU.mult)
    c.v.tensor_tensor(QTM[64:128, 64:128], P384[64:128, 0:64], EM[64:128, :], ALU.mult)
    c.v.tensor_tensor(TKF[:, :], P384[:, 64:128], DELTA[:, :], ALU.mult)
    c.v.tensor_tensor(KTM[0:64, 0:64], TKF[0:64, :], EP[0:64, :], ALU.mult)
    c.v.tensor_tensor(KTM[64:128, 64:128], TKF[64:128, :], EP[64:128, :], ALU.mult)
    c.v.tensor_copy(VB[:, :], P384[:, 128:256])
    dvQ.inc(c.v.tensor_copy(PCST[:, 2 * si:2 * si + 2], NCC[0:64, 66:68]))
    if cut <= 3:
        return
    c.t.wait_ge(dvQ.h, dvQ.n)
    c.t.wait_ge(uev.h, uev.n)
    c.t.matmul(pU[:, 0:128], KTM[:, 0:64], VB[:, :], start=True, stop=True)
    peU.inc(c.t.matmul(pU[:, 128:256], KTM[:, 64:128], VB[:, :],
                       start=True, stop=True))
    c.t.wait_ge(tqv.h, tqv.n)
    c.t.transpose(pTQ[:, 0:128], QTM[:, :], IDB[:, :])
    peT.inc(c.t.transpose(pTQ[:, 128:256], KTM[:, :], IDB[:, :]))
    c.v.wait_ge(peU.h, peU.n)
    c.v.tensor_copy(USTG[:, (2 * si) * 128:(2 * si + 1) * 128], pU[:, 0:128])
    uev.inc(c.v.tensor_copy(USTG[:, (2 * si + 1) * 128:(2 * si + 2) * 128],
                            pU[:, 128:256]))
    c.v.wait_ge(peT.h, peT.n)
    c.v.wait_ge(peA.h, peA.n)
    c.v.tensor_copy(QTall[:, st:st + 128], pTQ[:, 0:128])
    tqv.inc(c.v.tensor_copy(KTP[:, :], pTQ[:, 128:256]))
    if cut <= 4:
        return
    c.t.wait_ge(tqv.h, tqv.n)
    c.t.wait_ge(sel.h, sel.n)
    peA.inc(c.t.matmul(pA[:, 0:128], KTP[:, :], QTall[:, st:st + 128],
                       start=True, stop=True))
    c.v.wait_ge(peA.h, peA.n)
    sel.inc(c.v.select(vb["AB"][:, :], AMASK[:, :], pA[:, 0:128], ZEROB[:, :]))
    if cut <= 5:
        return
    c.t.wait_ge(sel.h, sel.n)
    c.t.wait_ge(oev.h, oev.n)
    peO.inc(c.t.matmul(pO[:, 0:128], vb["AB"][:, :], VB[:, :],
                       start=True, stop=True))
    c.v.wait_ge(peO.h, peO.n)
    oev.inc(c.v.tensor_copy(vb["OIF"][:, :], pO[:, 0:128]))
    if cut <= 6:
        return
    peOT = c.sem("peOT"); otv = c.sem("otv")
    pOX = pOXv
    if pOX is None:
        pOX = pPP
        c.t.wait_ge(prj.h, prj.n)
    else:
        c.t.wait_ge(nclam.h, nclam.n)
    c.t.wait_ge(oev.h, oev.n)
    c.t.wait_ge(otv.h, otv.n)
    peOT.inc(c.t.transpose(pOX[:, 0:128], vb["OIF"][:, :], vb["IDF"][:, :]))
    c.v.wait_ge(peOT.h, peOT.n)
    otv.inc(c.v.tensor_copy(OISTG[:, si * 128:(si + 1) * 128], pOX[:, 0:128]))


def build_gla_a_launch(npass=NPASS, nsub=NSUB, cut=99):
    nc = bass.Bass("TRN2", target_bir_lowering=False)
    xin = nc.dram_tensor("xin", [128, DC * TOKS], F32, kind="ExternalInput").ap()
    wcat = nc.dram_tensor("wcat", [128, DC * 384], F32, kind="ExternalInput").ap()
    brow = nc.dram_tensor("brow", [1, 384], F32, kind="ExternalInput").ap()
    mcum = nc.dram_tensor("mcum", [128, 128], F32, kind="ExternalInput").ap()
    chi = nc.dram_tensor("chi", [128, 2], F32, kind="ExternalInput").ap()
    amask = nc.dram_tensor("amask", [128, 128], U8, kind="ExternalInput").ap()
    identb = nc.dram_tensor("identb", [128, 128], BF16, kind="ExternalInput").ap()
    identf = nc.dram_tensor("identf", [128, 128], F32, kind="ExternalInput").ap()
    uown = nc.dram_tensor("uown", [2048, 128], F32, kind="ExternalOutput").ap()
    pcown = nc.dram_tensor("pcown", [64, 32], F32, kind="ExternalOutput").ap()
    qtown = nc.dram_tensor("qtown", [128, TOKS], BF16, kind="ExternalOutput").ap()
    oiown = nc.dram_tensor("oiown", [128, TOKS], F32, kind="ExternalOutput").ap()

    with ExitStack() as st:
        c = Ctx(nc, st)
        vb = {}
        vb["XB"] = c.sbuf("XB", [128, DC * TP], BF16)
        vb["XQ"] = c.sbuf("XQ", [128, DC * TP], BF16)
        vb["XHB"] = c.sbuf("XHB", [128, DC * TP], BF16)
        vb["SROW"] = c.sbuf("SROW", [1, 5 * TP], F32)
        vb["onesb"] = c.sbuf("onesb", [128, 1], BF16)
        vb["ones1f"] = c.sbuf("ones1f", [1, 128], F32)
        vb["onesbb"] = c.sbuf("onesbb", [128, 128], BF16)
        vb["LROW"] = c.sbuf("LROW", [128, 4 * TP], F32)
        vb["WCB"] = c.sbuf("WCB", [128, DC * 384], BF16)
        vb["BBC"] = c.sbuf("BBC", [128, 384], F32)
        vb["MCUM"] = c.sbuf("MCUM", [128, 128], F32)
        vb["CHI"] = c.sbuf("CHI", [128, 2], F32)
        vb["AMASK"] = c.sbuf("AMASK", [128, 128], U8)
        vb["IDB"] = c.sbuf("IDB", [128, 128], BF16)
        vb["ZEROB"] = c.sbuf("ZEROB", [128, 128], BF16)
        vb["P384"] = c.sbuf("P384", [128, 384], F32)
        vb["SC64"] = c.sbuf("SC64", [128, 256], F32)
        vb["NCC"] = c.sbuf("NCC", [128, 68], F32)
        vb["NLB"] = c.sbuf("NLB", [128, 64], F32)
        vb["DELTA"] = c.sbuf("DELTA", [128, 64], F32)
        vb["EM"] = c.sbuf("EM", [128, 64], F32)
        vb["EP"] = c.sbuf("EP", [128, 64], F32)
        vb["QTM"] = c.sbuf("QTM", [128, 128], BF16)
        vb["KTM"] = c.sbuf("KTM", [128, 128], BF16)
        vb["TKF"] = c.sbuf("TKF", [128, 64], F32)
        vb["VB"] = c.sbuf("VB", [128, 128], BF16)
        vb["KTP"] = c.sbuf("KTP", [128, 128], BF16)
        vb["QTall"] = c.sbuf("QTall", [128, TOKS], BF16)
        vb["AB"] = c.sbuf("AB", [128, 128], BF16)
        vb["USTG"] = c.sbuf("USTG", [64, 32 * 128], F32)
        vb["OISTG"] = c.sbuf("OISTG", [128, DC * 128], F32)
        vb["OIF"] = c.sbuf("OIF", [128, 128], F32)
        vb["IDF"] = c.sbuf("IDF", [128, 128], F32)
        vb["PCST"] = c.sbuf("PCST", [64, 32], F32)
        XF = c.sbuf("XF", [128, DC * TP], F32)
        vb["pS1"] = c.psum("pS1", [128, TP])
        vb["pS2"] = c.psum("pS2", [128, TP])
        vb["pR"] = c.psum("pR", [128, TP])
        vb["pM"] = c.psum("pM", [128, TP])
        vb["pPP"] = c.psum("pPP", [128, 448])
        vb["pNC"] = c.psum("pNC", [128, 66])
        vb["pU"] = c.psum("pU", [64, 256])
        vb["pTQ"] = c.psum("pTQ", [128, 256], BF16)
        block = st.enter_context(nc.Block())

        @block.gpsimd
        def _(_g):
            cst = c.sem("cst")
            cst.inc(c.sy.dma_start(out=vb["MCUM"][:, :], in_=mcum[:, :]), 16)
            cst.inc(c.sy.dma_start(out=vb["CHI"][:, :], in_=chi[:, :]), 16)
            cst.inc(c.sy.dma_start(out=vb["AMASK"][:, :], in_=amask[:, :]), 16)
            cst.inc(c.sy.dma_start(out=vb["IDB"][:, :], in_=identb[:, :]), 16)
            cst.inc(c.sy.dma_start(out=vb["IDF"][:, :], in_=identf[:, :]), 16)
            cstg = c.sem("cstg")
            cstg.inc(c.g.dma_start(out=vb["WCB"][:, :], in_=wcat[:, :]), 16)
            brsb = vb["SROW"][0:1, 0:384]
            cst.inc(c.sy.dma_start(out=brsb, in_=brow[:, :]), 16)
            c.v.wait_ge(cst.h, cst.n)
            c.v.wait_ge(cstg.h, cstg.n)
            c.t.wait_ge(cstg.h, cstg.n)
            c.v.memset(vb["onesb"][:, :], 1.0)
            c.v.memset(vb["onesbb"][:, :], 1.0)
            c.v.memset(vb["ones1f"][:, :], 1.0)
            c.v.memset(vb["ZEROB"][:, :], 0.0)
            c.v.memset(vb["QTall"][:, :], 0.0)
            c.v.memset(vb["QTM"][:, :], 0.0)
            c.v.memset(vb["KTM"][:, :], 0.0)
            mz = c.v.memset(vb["KTP"][:, :], 0.0)
            c.sem("dvI").inc(mz)
            c.t.wait_ge(c.sem("dvI").h, c.sem("dvI").n)
            bb = c.t.matmul(vb["pR"][:, 0:384], vb["ones1f"][:, :], brsb,
                            start=True, stop=True)
            c.sem("lnP").inc(bb)
            c.v.wait_ge(c.sem("lnP").h, c.sem("lnP").n)
            ini = c.v.tensor_copy(vb["BBC"][:, :], vb["pR"][:, 0:384])
            c.sem("lnV").inc(ini)
            xld = c.sem("xld")
            for p in range(npass):
                c.sy.wait_ge(c.sem("lnV").h, c.sem("lnV").n)
                xld.inc(c.sy.dma_start(
                    out=XF[:, :].rearrange("pp (cc t) -> pp cc t", cc=DC),
                    in_=xin.rearrange("pp (cc t) -> pp cc t", cc=DC)
                    [:, :, p * TP:(p + 1) * TP]), 16)
                c.s.wait_ge(xld.h, xld.n)
                emit_ln(c, vb, XF, vb["XHB"],
                        extra_bcast_waits=(c.sem("sel"), c.sem("oev")))
                for s in range(nsub):
                    emit_gla_a_subtile(c, vb, vb["XHB"], p, s, 0, cut)
            fin = c.sem("fin")
            c.sy.wait_ge(c.sem("uev").h, c.sem("uev").n)
            c.sy.wait_ge(c.sem("oev").h, c.sem("oev").n)
            c.sy.wait_ge(c.sem("dvQ").h, c.sem("dvQ").n)
            c.sy.wait_ge(c.sem("tqv").h, c.sem("tqv").n)
            fin.inc(c.sy.dma_start(
                out=uown.rearrange("(j k) v -> k j v", k=64),
                in_=vb["USTG"][:, :].rearrange("k (j v) -> k j v", v=128)), 16)
            fin.inc(c.sy.dma_start(out=pcown[:, :], in_=vb["PCST"][:, :]), 16)
            fin.inc(c.sy.dma_start(out=qtown[:, :], in_=vb["QTall"][:, :]), 16)
            fin.inc(c.sy.dma_start(out=oiown[:, :], in_=vb["OISTG"][:, :]), 16)
            c.g.wait_ge(fin.h, fin.n)

    return nc




def build_gla_b_launch():
    nc = bass.Bass("TRN2", target_bir_lowering=False)
    xin = nc.dram_tensor("xin", [128, DC * TOKS], F32, kind="ExternalInput").ap()
    uin = nc.dram_tensor("uin", [128 * 128, 128], F32, kind="ExternalInput").ap()
    pcin = nc.dram_tensor("pcin", [128, 128], F32, kind="ExternalInput").ap()
    qtin = nc.dram_tensor("qtin", [128, TOKS], BF16, kind="ExternalInput").ap()
    oiin = nc.dram_tensor("oiin", [128, TOKS], F32, kind="ExternalInput").ap()
    wo = nc.dram_tensor("wo", [128, D], F32, kind="ExternalInput").ap()
    xout = nc.dram_tensor("xout", [128, DC * TOKS], F32, kind="ExternalOutput").ap()

    with ExitStack() as st:
        c = Ctx(nc, st)
        SCAN = c.sbuf("SCAN", [128, 129 * 128], F32)
        PCsb = c.sbuf("PCsb", [128, 128], F32)
        QTsb = c.sbuf("QTsb", [128, TOKS], F32)
        OIT = c.sbuf("OIT", [128, TOKS], F32)
        OTall = c.sbuf("OTall", [128, TOKS], BF16)
        WOB = c.sbuf("WOB", [128, D], BF16)
        XF = c.sbuf("XF", [128, DC * TP], F32)
        pOT = c.psum("pOT", [128, 64])
        pA = c.psum("pA", [128, TP])
        pB = c.psum("pB", [128, TP])
        block = st.enter_context(nc.Block())

        @block.gpsimd
        def _(_g):
            ld = c.sem("ld")
            ld.inc(c.sy.dma_start(
                out=SCAN[:, 128:].rearrange("k (j v) -> k j v", v=128),
                in_=uin.rearrange("(j k) v -> k j v", k=128)), 16)
            ld.inc(c.sy.dma_start(out=PCsb[:, :], in_=pcin[:, :]), 16)
            ld.inc(c.g.dma_start(out=QTsb[:, :], in_=qtin[:, :]), 16)
            ld.inc(c.sy.dma_start(out=OIT[:, :], in_=oiin[:, :]), 16)
            ld.inc(c.g.dma_start(out=WOB[:, :], in_=wo[:, :]), 16)
            dvI = c.sem("dvI")
            dvI.inc(c.v.memset(SCAN[:, 0:128], 0.0))
            c.v.wait_ge(ld.h, ld.n)
            scn = c.sem("scn")
            view = SCAN[:, 128:].rearrange("p (j v) -> p v j", v=128)
            for vi in range(128):
                inst = c.v.tensor_tensor_scan(view[:, vi, :], view[:, vi, :],
                                              PCsb[:, :], 0.0, ALU.add, ALU.mult)
            scn.inc(inst)

            xld = c.sem("xld"); xst = c.sem("xst")
            peI = c.sem("peI"); otv = c.sem("otv")
            peY = c.sem("peY"); yev = c.sem("yev")
            c.t.wait_ge(scn.h, scn.n)
            c.t.wait_ge(dvI.h, dvI.n)
            for p in range(NPASS):
                c.sy.wait_ge(yev.h, yev.n)
                c.sy.wait_ge(xst.h, xst.n)
                xld.inc(c.sy.dma_start(
                    out=XF[:, :].rearrange("pp (cc t) -> pp cc t", cc=DC),
                    in_=xin.rearrange("pp (cc t) -> pp cc t", cc=DC)
                    [:, :, p * TP:(p + 1) * TP]), 16)
                for s in range(NSUB):
                    si = p * NSUB + s
                    st_ = si * 128
                    for h in range(2):
                        jloc = 2 * si + h
                        slot = 96 + jloc
                        c.t.wait_ge(otv.h, otv.n)
                        mm = c.t.matmul(pOT[:, 0:64],
                                        SCAN[:, slot * 128:(slot + 1) * 128],
                                        QTsb[:, st_ + h * 64:st_ + (h + 1) * 64],
                                        start=True, stop=True)
                        peI.inc(mm)
                        c.v.wait_ge(peI.h, peI.n)
                        ot = c.v.tensor_tensor(
                            OTall[:, st_ + h * 64:st_ + (h + 1) * 64],
                            pOT[:, 0:64],
                            OIT[:, st_ + h * 64:st_ + (h + 1) * 64], ALU.add)
                        otv.inc(ot)
                c.t.wait_ge(otv.h, otv.n)
                c.t.wait_ge(xld.h, xld.n)
                for oc in range(DC):
                    pp = [pA, pB][oc % 2]
                    if oc >= 2:
                        c.t.wait_ge(yev.h, yev.n - (DC - oc) - 1 + 1
                                    if False else yev.n)
                    mm = c.t.matmul(pp[:, :TP], WOB[:, oc * 128:(oc + 1) * 128],
                                    OTall[:, p * TP:(p + 1) * TP],
                                    start=True, stop=True)
                    peY.inc(mm)
                    c.v.wait_ge(peY.h, peY.n)
                    yv = c.v.tensor_tensor(XF[:, oc * TP:(oc + 1) * TP],
                                           pp[:, :TP],
                                           XF[:, oc * TP:(oc + 1) * TP], ALU.add)
                    yev.inc(yv)
                c.sy.wait_ge(yev.h, yev.n)
                xst.inc(c.sy.dma_start(
                    out=xout.rearrange("pp (cc t) -> pp cc t", cc=DC)
                    [:, :, p * TP:(p + 1) * TP],
                    in_=XF[:, :].rearrange("pp (cc t) -> pp cc t", cc=DC)), 16)
            c.g.wait_ge(xst.h, xst.n)

    return nc


def emit_had_transform(c, vb, w_sl, out_sl, nch, wcols, hconst, su, wsrow, tag):
    ZA = vb["ZA"]; ZB = vb["ZB"]; WOUT = vb["WOUT"]
    pH = vb["pH"]; pW = vb["pW"]
    W = 512 if nch == 16 else 128
    njs = wcols // W
    ld = c.sem("hld"); pe = c.sem("hpe"); dv = c.sem("hdv"); so = c.sem("hso")
    for js in range(njs):
        c.sy.wait_ge(dv.h, dv.n)
        c.sy.wait_ge(so.h, so.n)
        ld.inc(c.sy.dma_start(
            out=ZA[:, 0:nch * W].rearrange("p (cc w) -> p cc w", cc=nch),
            in_=w_sl.rearrange("(cc p) j -> p cc j", p=128)
            [:, :, js * W:(js + 1) * W]), 16)
        c.t.wait_ge(ld.h, ld.n)
        c.t.wait_ge(dv.h, dv.n)
        dv0 = dv.n
        for cc in range(nch):
            pp = pH[cc % 2]
            if cc >= 2:
                c.t.wait_ge(dv.h, dv0 + cc - 1)
            inst = c.t.matmul(pp[:, 0:W], hconst[:, :],
                              ZA[:, cc * W:(cc + 1) * W], start=True, stop=True)
            pe.inc(inst)
            c.v.wait_ge(pe.h, pe.n)
            dv.inc(c.v.tensor_copy(ZB[:, cc * W:(cc + 1) * W], pp[:, 0:W]))
        bufs = [ZB, ZA]
        cur = 0
        d = 1
        while d < nch:
            src_, dst_ = bufs[cur], bufs[1 - cur]
            for cc in range(nch):
                partner = cc ^ d
                op = ALU.add if (cc & d) == 0 else ALU.subtract
                a = cc if (cc & d) == 0 else partner
                b = partner if (cc & d) == 0 else cc
                inst = c.v.tensor_tensor(dst_[:, cc * W:(cc + 1) * W],
                                         src_[:, a * W:(a + 1) * W],
                                         src_[:, b * W:(b + 1) * W], op)
            dv.inc(inst)
            cur = 1 - cur
            d *= 2
        fin = bufs[cur]
        c.t.wait_ge(dv.h, dv.n)
        wsmm = c.t.matmul(pW[:, 0:W], vb["ones1f"][:, :],
                          wsrow[0:1, js * W:(js + 1) * W], start=True, stop=True)
        pe.inc(wsmm)
        c.v.wait_ge(pe.h, pe.n)
        for cc in range(nch):
            inst = c.v.scalar_tensor_tensor(
                WOUT[:, cc * W:(cc + 1) * W], fin[:, cc * W:(cc + 1) * W],
                su[:, cc:cc + 1], pW[:, 0:W], ALU.mult, ALU.mult)
        dv.inc(inst)
        c.sy.wait_ge(dv.h, dv.n)
        ocl_per_js = W // 128
        outv = out_sl.rearrange("p (ocl cc m) -> p ocl cc m", cc=nch, m=128)
        winv = WOUT[:, 0:nch * W].rearrange("p (cc ocl m) -> p cc ocl m",
                                            ocl=ocl_per_js, m=128)
        for ol in range(ocl_per_js):
            so.inc(c.sy.dma_start(
                out=outv[:, js * ocl_per_js + ol, :, :],
                in_=winv[:, :, ol, :]), 16)
    return so


def build_prep_launch():
    nc = bass.Bass("TRN2", target_bir_lowering=False)
    wup = nc.dram_tensor("wup", [2048, 1024], F32, kind="ExternalInput").ap()
    wga = nc.dram_tensor("wga", [2048, 1024], F32, kind="ExternalInput").ap()
    wdo = nc.dram_tensor("wdo", [8192, 256], F32, kind="ExternalInput").ap()
    h128u = nc.dram_tensor("h128u", [128, 128], F32, kind="ExternalInput").ap()
    h128d = nc.dram_tensor("h128d", [128, 128], F32, kind="ExternalInput").ap()
    suug = nc.dram_tensor("suug", [128, 16], F32, kind="ExternalInput").ap()
    sudp = nc.dram_tensor("sudp", [128, 64], F32, kind="ExternalInput").ap()
    wsup = nc.dram_tensor("wsup", [1, 1024], F32, kind="ExternalInput").ap()
    wsga = nc.dram_tensor("wsga", [1, 1024], F32, kind="ExternalInput").ap()
    wsdo = nc.dram_tensor("wsdo", [1, 256], F32, kind="ExternalInput").ap()
    oup = nc.dram_tensor("oup", [128, 8 * 16 * 128], BF16, kind="ExternalOutput").ap()
    oga = nc.dram_tensor("oga", [128, 8 * 16 * 128], BF16, kind="ExternalOutput").ap()
    odo = nc.dram_tensor("odo", [128, 2 * 64 * 128], BF16, kind="ExternalOutput").ap()

    with ExitStack() as st:
        c = Ctx(nc, st)
        vb = {}
        vb["ZA"] = c.sbuf("ZA", [128, 8192], F32)
        vb["ZB"] = c.sbuf("ZB", [128, 8192], F32)
        vb["WOUT"] = c.sbuf("WOUT", [128, 8192], BF16)
        vb["ones1f"] = c.sbuf("ones1f", [1, 128], F32)
        HU = c.sbuf("HU", [128, 128], F32)
        HDN = c.sbuf("HDN", [128, 128], F32)
        SUU = c.sbuf("SUU", [128, 16], F32)
        SUD = c.sbuf("SUD", [128, 64], F32)
        WSU = c.sbuf("WSU", [1, 1024], F32)
        WSG = c.sbuf("WSG", [1, 1024], F32)
        WSD = c.sbuf("WSD", [1, 256], F32)
        vb["pH"] = [c.psum("pH0", [128, 512]), c.psum("pH1", [128, 512])]
        vb["pW"] = c.psum("pWb", [128, 512])
        block = st.enter_context(nc.Block())

        @block.gpsimd
        def _(_g):
            cst = c.sem("cst")
            for dst, srcap in [(HU, h128u), (HDN, h128d), (SUU, suug),
                               (SUD, sudp), (WSU, wsup), (WSG, wsga), (WSD, wsdo)]:
                cst.inc(c.sy.dma_start(out=dst[:, :], in_=srcap[:, :]), 16)
            c.v.wait_ge(cst.h, cst.n)
            dvi = c.sem("hdv")
            dvi.inc(c.v.memset(vb["ones1f"][:, :], 1.0))
            c.t.wait_ge(dvi.h, dvi.n)
            emit_had_transform(c, vb, wup, oup, 16, 1024, HU, SUU, WSU, "up")
            emit_had_transform(c, vb, wga, oga, 16, 1024, HU, SUU, WSG, "ga")
            so = emit_had_transform(c, vb, wdo, odo, 64, 256, HDN, SUD, WSD, "do")
            c.g.wait_ge(so.h, so.n)

    return nc


def build_l1ffn_launch():
    nc = bass.Bass("TRN2", target_bir_lowering=False)
    xin = nc.dram_tensor("xin", [128, DC * TOKS], F32, kind="ExternalInput").ap()
    wub = nc.dram_tensor("wub", [128, 64 * DC * 128], BF16, kind="ExternalInput").ap()
    wgb = nc.dram_tensor("wgb", [128, 64 * DC * 128], BF16, kind="ExternalInput").ap()
    wdb = nc.dram_tensor("wdb", [128, DC * 64 * 128], BF16, kind="ExternalInput").ap()
    xout = nc.dram_tensor("xout", [128, DC * TOKS], F32, kind="ExternalOutput").ap()
    IC = 64

    with ExitStack() as st:
        c = Ctx(nc, st)
        vb = {}
        vb["XB"] = c.sbuf("XB", [128, DC * TP], BF16)
        vb["XQ"] = c.sbuf("XQ", [128, DC * TP], BF16)
        vb["XHB"] = c.sbuf("XHB", [128, DC * TP], BF16)
        vb["SROW"] = c.sbuf("SROW", [1, 5 * TP], F32)
        vb["onesb"] = c.sbuf("onesb", [128, 1], BF16)
        vb["ones1f"] = c.sbuf("ones1f", [1, 128], F32)
        vb["onesbb"] = c.sbuf("onesbb", [128, 128], BF16)
        vb["LROW"] = c.sbuf("LROW", [128, 4 * TP], F32)
        XF = c.sbuf("XF", [128, DC * TP], F32)
        GB = c.sbuf("GB", [128, IC * TP], BF16)
        SGT = c.sbuf("SGT", [128, TP], BF16)
        wbu = [c.sbuf("wbu0", [128, DC * 128], BF16), c.sbuf("wbu1", [128, DC * 128], BF16)]
        wbg = [c.sbuf("wbg0", [128, DC * 128], BF16), c.sbuf("wbg1", [128, DC * 128], BF16)]
        wbd = [c.sbuf("wbd0", [128, IC * 128], BF16), c.sbuf("wbd1", [128, IC * 128], BF16)]
        vb["pS1"] = c.psum("pS1", [128, TP])
        vb["pS2"] = c.psum("pS2", [128, TP])
        vb["pR"] = c.psum("pR", [128, TP])
        vb["pM"] = c.psum("pM", [128, TP])
        pG1 = c.psum("pG1", [128, TP])
        pG2 = c.psum("pG2", [128, TP])
        pY = [c.psum("pY0", [128, TP]), c.psum("pY1", [128, TP])]
        block = st.enter_context(nc.Block())

        @block.gpsimd
        def _(_g):
            dvi = c.sem("lnV")
            c.v.memset(vb["onesb"][:, :], 1.0)
            c.v.memset(vb["onesbb"][:, :], 1.0)
            dvi.inc(c.v.memset(vb["ones1f"][:, :], 1.0))
            xld = c.sem("xld"); xst = c.sem("xst")
            wldu = c.sem("wldu"); wldg = c.sem("wldg"); wldd = c.sem("wldd")
            wfree = c.sem("wfree"); gvd = c.sem("gvd"); sil = c.sem("sil")
            yev = c.sem("yev")
            gpair = [(pG1, pG2), (vb["pR"], vb["pM"])]
            for p in range(NPASS):
                c.sy.wait_ge(yev.h, yev.n)
                c.sy.wait_ge(xst.h, xst.n)
                xld.inc(c.sy.dma_start(
                    out=XF[:, :].rearrange("pp (cc t) -> pp cc t", cc=DC),
                    in_=xin.rearrange("pp (cc t) -> pp cc t", cc=DC)
                    [:, :, p * TP:(p + 1) * TP]), 16)
                c.s.wait_ge(xld.h, xld.n)
                emit_ln(c, vb, XF, vb["XHB"], extra_bcast_waits=(gvd,))
                lnV = c.sem("lnV")
                wfree0, gvd0, sil0 = wfree.n, gvd.n, sil.n
                wldu0, wldg0 = wldu.n, wldg.n
                span = DC * 128
                for hc in range(IC):
                    bu = wbu[hc % 2]; bg = wbg[hc % 2]
                    if hc >= 2:
                        c.sy.wait_ge(wfree.h, wfree0 + hc - 1)
                        c.s.wait_ge(wfree.h, wfree0 + hc - 1)
                    wldu.inc(c.sy.dma_start(out=bu[:, :],
                                            in_=wub[:, hc * span:(hc + 1) * span]), 16)
                    wldg.inc(c.s.dma_start(out=bg[:, :],
                                           in_=wgb[:, hc * span:(hc + 1) * span]), 16)
                    ppu, ppg = gpair[hc % 2]
                    c.t.wait_ge(wldu.h, wldu0 + 16 * (hc + 1))
                    c.t.wait_ge(wldg.h, wldg0 + 16 * (hc + 1))
                    if hc == 0:
                        c.t.wait_ge(lnV.h, lnV.n)
                    if hc >= 2:
                        c.t.wait_ge(gvd.h, gvd0 + hc - 1)
                    elif p > 0 and hc < 2:
                        c.t.wait_ge(gvd.h, gvd0)
                    for ci in range(DC):
                        c.t.matmul(ppu[:, :TP], bu[:, ci * 128:(ci + 1) * 128],
                                   vb["XHB"][:, ci * TP:(ci + 1) * TP],
                                   start=(ci == 0), stop=(ci == DC - 1))
                    for ci in range(DC):
                        inst = c.t.matmul(ppg[:, :TP], bg[:, ci * 128:(ci + 1) * 128],
                                          vb["XHB"][:, ci * TP:(ci + 1) * TP],
                                          start=(ci == 0), stop=(ci == DC - 1))
                    wfree.inc(inst)
                    c.s.wait_ge(wfree.h, wfree0 + hc + 1)
                    c.s.wait_ge(gvd.h, gvd0 + hc)
                    sil.inc(c.s.activation(SGT[:, :], ppg[:, :TP], AF.Silu))
                    c.v.wait_ge(sil.h, sil0 + hc + 1)
                    gvd.inc(c.v.tensor_tensor(GB[:, hc * TP:(hc + 1) * TP],
                                              ppu[:, :TP], SGT[:, :], ALU.mult))
                wldd0, yev0 = wldd.n, yev.n
                dspan = IC * 128
                for dc in range(DC):
                    bd = wbd[dc % 2]
                    if dc >= 2:
                        c.sy.wait_ge(yev.h, yev0 + dc - 1)
                    wldd.inc(c.sy.dma_start(out=bd[:, :],
                                            in_=wdb[:, dc * dspan:(dc + 1) * dspan]), 16)
                    pp = pY[dc % 2]
                    c.t.wait_ge(wldd.h, wldd0 + 16 * (dc + 1))
                    if dc == 0:
                        c.t.wait_ge(gvd.h, gvd.n)
                    if dc >= 2:
                        c.t.wait_ge(yev.h, yev0 + dc - 1)
                    for fc in range(IC):
                        inst = c.t.matmul(pp[:, :TP], bd[:, fc * 128:(fc + 1) * 128],
                                          GB[:, fc * TP:(fc + 1) * TP],
                                          start=(fc == 0), stop=(fc == IC - 1))
                    wfree.inc(inst)
                    c.v.wait_ge(wfree.h, wfree.n)
                    yev.inc(c.v.tensor_tensor(XF[:, dc * TP:(dc + 1) * TP],
                                              pp[:, :TP],
                                              XF[:, dc * TP:(dc + 1) * TP], ALU.add))
                c.sy.wait_ge(yev.h, yev.n)
                xst.inc(c.sy.dma_start(
                    out=xout.rearrange("pp (cc t) -> pp cc t", cc=DC)
                    [:, :, p * TP:(p + 1) * TP],
                    in_=XF[:, :].rearrange("pp (cc t) -> pp cc t", cc=DC)), 16)
            c.g.wait_ge(xst.h, xst.n)

    return nc


def build_ln5_launch():
    nc = bass.Bass("TRN2", target_bir_lowering=False)
    xin = nc.dram_tensor("xin", [128, DC * TOKS], F32, kind="ExternalInput").ap()
    xh = nc.dram_tensor("xh", [128, DC * TOKS], BF16, kind="ExternalOutput").ap()
    with ExitStack() as st:
        c = Ctx(nc, st)
        vb = {}
        vb["XB"] = c.sbuf("XB", [128, DC * TP], BF16)
        vb["XQ"] = c.sbuf("XQ", [128, DC * TP], BF16)
        vb["XHB"] = c.sbuf("XHB", [128, DC * TP], BF16)
        vb["SROW"] = c.sbuf("SROW", [1, 5 * TP], F32)
        vb["onesb"] = c.sbuf("onesb", [128, 1], BF16)
        vb["ones1f"] = c.sbuf("ones1f", [1, 128], F32)
        vb["onesbb"] = c.sbuf("onesbb", [128, 128], BF16)
        vb["LROW"] = c.sbuf("LROW", [128, 4 * TP], F32)
        XF = c.sbuf("XF", [128, DC * TP], F32)
        vb["pS1"] = c.psum("pS1", [128, TP])
        vb["pS2"] = c.psum("pS2", [128, TP])
        vb["pR"] = c.psum("pR", [128, TP])
        vb["pM"] = c.psum("pM", [128, TP])
        block = st.enter_context(nc.Block())

        @block.gpsimd
        def _(_g):
            dvi = c.sem("lnV")
            c.v.memset(vb["onesb"][:, :], 1.0)
            c.v.memset(vb["onesbb"][:, :], 1.0)
            dvi.inc(c.v.memset(vb["ones1f"][:, :], 1.0))
            xld = c.sem("xld"); xst = c.sem("xst")
            for p in range(NPASS):
                c.sy.wait_ge(dvi.h, dvi.n)
                c.sy.wait_ge(xst.h, xst.n)
                xld.inc(c.sy.dma_start(
                    out=XF[:, :].rearrange("pp (cc t) -> pp cc t", cc=DC),
                    in_=xin.rearrange("pp (cc t) -> pp cc t", cc=DC)
                    [:, :, p * TP:(p + 1) * TP]), 16)
                c.s.wait_ge(xld.h, xld.n)
                lnV = emit_ln(c, vb, XF, vb["XHB"])
                c.sy.wait_ge(lnV.h, lnV.n)
                xst.inc(c.sy.dma_start(
                    out=xh.rearrange("pp (cc t) -> pp cc t", cc=DC)
                    [:, :, p * TP:(p + 1) * TP],
                    in_=vb["XHB"][:, :].rearrange("pp (cc t) -> pp cc t", cc=DC)), 16)
            c.g.wait_ge(xst.h, xst.n)
    return nc


def build_attn_launch():
    nc = bass.Bass("TRN2", target_bir_lowering=False)
    xh = nc.dram_tensor("xh", [128, DC * SEQ], BF16, kind="ExternalInput").ap()
    wqkv = nc.dram_tensor("wqkv", [128, 12 * DC * 128], F32, kind="ExternalInput").ap()
    bqkv = nc.dram_tensor("bqkv", [128, 12], F32, kind="ExternalInput").ap()
    cmask = nc.dram_tensor("cmask", [128, 4 * 512], BF16, kind="ExternalInput").ap()
    identb = nc.dram_tensor("identb", [128, 128], BF16, kind="ExternalInput").ap()
    oown = nc.dram_tensor("oown", [512, SEQ], BF16, kind="ExternalOutput").ap()
    NT = SEQ // TP
    NKT = SEQ // 128

    with ExitStack() as st:
        c = Ctx(nc, st)
        WB = c.sbuf("WB", [128, 12 * DC * 128], BF16)
        BQ = c.sbuf("BQ", [128, 12], F32)
        CM = c.sbuf("CM", [128, 4 * 512], BF16)
        IDB = c.sbuf("IDB", [128, 128], BF16)
        XQB = c.sbuf("XQB", [128, DC * TP], BF16)
        Q = c.sbuf("Q", [128, SEQ], BF16)
        Kt = c.sbuf("Kt", [128, SEQ], BF16)
        VT = c.sbuf("VT", [128, NKT * 129], BF16)
        VF = c.sbuf("VF", [128, TP], BF16)
        PT = [c.sbuf("PT0", [128, TP], BF16), c.sbuf("PT1", [128, TP], BF16)]
        OB = c.sbuf("OB", [128, 128], BF16)
        RC = c.sbuf("RC", [128, 1], F32)
        OST = c.sbuf("OST", [128, TP], BF16)
        pS = [c.psum("pSa", [128, TP]), c.psum("pSb", [128, TP])]
        pO = [c.psum("pO0", [128, TP]), c.psum("pO1", [128, TP]),
              c.psum("pO2", [128, TP]), c.psum("pO3", [128, TP])]
        pTR = c.psum("pTR", [128, 128], BF16)
        block = st.enter_context(nc.Block())

        @block.gpsimd
        def _(_g):
            cst = c.sem("cst")
            cstg = c.sem("cstg")
            cstg.inc(c.g.dma_start(out=WB[:, :], in_=wqkv[:, :]), 16)
            cst.inc(c.sy.dma_start(out=BQ[:, :], in_=bqkv[:, :]), 16)
            cst.inc(c.sy.dma_start(out=CM[:, :], in_=cmask[:, :]), 16)
            cst.inc(c.sy.dma_start(out=IDB[:, :], in_=identb[:, :]), 16)
            dvi = c.sem("dvi")
            dvi.inc(c.v.memset(VT[:, :], 1.0))
            xld = c.sem("xld"); peQ = c.sem("peQ"); qkv = c.sem("qkv")
            peV = c.sem("peV"); vtd = c.sem("vtd")
            peS = c.sem("peS"); scP = c.sem("scP"); msk = c.sem("msk")
            peA = c.sem("peA"); onrm = c.sem("onrm"); peOT = c.sem("peOT")
            ost = c.sem("ost"); fst = c.sem("fst")
            c.t.wait_ge(cst.h, cst.n)
            c.t.wait_ge(cstg.h, cstg.n)
            c.v.wait_ge(cst.h, cst.n)
            for hi in range(4):
                for tp in range(NT):
                    c.sy.wait_ge(peQ.h, peQ.n)
                    xld.inc(c.sy.dma_start(
                        out=XQB[:, :].rearrange("pp (cc t) -> pp cc t", cc=DC),
                        in_=xh.rearrange("pp (cc t) -> pp cc t", cc=DC)
                        [:, :, tp * TP:(tp + 1) * TP]), 16)
                    c.t.wait_ge(xld.h, xld.n)
                    for j in range(3):
                        pp = pS[j % 2]
                        wslice = WB[:, (hi * 3 + j) * DC * 128:(hi * 3 + j + 1) * DC * 128]
                        c.t.wait_ge(qkv.h, qkv.n)
                        c.t.wait_ge(scP.h, scP.n)
                        for ci in range(DC):
                            inst = c.t.matmul(pp[:, :TP],
                                              wslice[:, ci * 128:(ci + 1) * 128],
                                              XQB[:, ci * TP:(ci + 1) * TP],
                                              start=(ci == 0), stop=(ci == DC - 1))
                        peQ.inc(inst)
                        c.v.wait_ge(peQ.h, peQ.n)
                        c.v.wait_ge(peS.h, peS.n)
                        c.v.wait_ge(peA.h, peA.n)
                        bcol = BQ[:, hi * 3 + j:hi * 3 + j + 1]
                        if j == 0:
                            qkv.inc(c.v.tensor_scalar(Q[:, tp * TP:(tp + 1) * TP],
                                                      pp[:, :TP], bcol, None, ALU.add))
                        elif j == 1:
                            qkv.inc(c.v.tensor_scalar(Kt[:, tp * TP:(tp + 1) * TP],
                                                      pp[:, :TP], bcol, None, ALU.add))
                        else:
                            c.v.wait_ge(peV.h, peV.n)
                            qkv.inc(c.v.tensor_scalar(VF[:, :], pp[:, :TP],
                                                      bcol, None, ALU.add))
                            for u in range(4):
                                kt = tp * 4 + u
                                c.t.wait_ge(qkv.h, qkv.n)
                                c.t.wait_ge(vtd.h, vtd.n)
                                tr = c.t.transpose(pTR[:, :],
                                                   VF[:, u * 128:(u + 1) * 128],
                                                   IDB[:, :])
                                peV.inc(tr)
                                c.v.wait_ge(peV.h, peV.n)
                                vtd.inc(c.v.tensor_copy(
                                    VT[:, kt * 129:kt * 129 + 128], pTR[:, :]))
                for qs in range(NT):
                    nkc = 4 * qs + 4
                    for kc in range(nkc):
                        pp = pS[kc % 2]
                        pt = PT[kc % 2]
                        c.t.wait_ge(qkv.h, qkv.n)
                        c.t.wait_ge(scP.h, scP.n)
                        c.t.wait_ge(vtd.h, vtd.n)
                        inst = c.t.matmul(pp[:, :TP], Kt[:, kc * 128:(kc + 1) * 128],
                                          Q[:, qs * TP:(qs + 1) * TP],
                                          start=True, stop=True)
                        peS.inc(inst)
                        c.s.wait_ge(peS.h, peS.n)
                        c.s.wait_ge(peA.h, peA.n)
                        scP.inc(c.s.activation(pt[:, :], pp[:, :TP], AF.Exp,
                                               scale=f32(1.0 / np.sqrt(128.0)),
                                               bias=f32(-16.0)))
                        if kc >= 4 * qs:
                            m = kc - 4 * qs
                            c.v.wait_ge(scP.h, scP.n)
                            msk.inc(c.v.tensor_tensor(
                                pt[:, :], pt[:, :],
                                CM[:, m * 512:(m + 1) * 512], ALU.mult))
                        c.t.wait_ge(scP.h, scP.n)
                        if kc >= 4 * qs:
                            c.t.wait_ge(msk.h, msk.n)
                        if kc == 0:
                            c.t.wait_ge(onrm.h, onrm.n)
                        qc0 = max(0, kc - 4 * qs)
                        for qc in range(qc0, 4):
                            inst = c.t.matmul(pO[qc][:, 0:129],
                                              pt[:, qc * 128:(qc + 1) * 128],
                                              VT[:, kc * 129:(kc + 1) * 129],
                                              start=(kc == 0),
                                              stop=(kc == 4 * qs + qc))
                        peA.inc(inst)
                    for qc in range(4):
                        c.v.wait_ge(peA.h, peA.n)
                        c.v.reciprocal(RC[:, :], pO[qc][:, 128:129])
                        c.v.wait_ge(peOT.h, peOT.n)
                        onrm.inc(c.v.tensor_scalar(OB[:, :], pO[qc][:, 0:128],
                                                   RC[:, :], None, ALU.mult))
                        c.t.wait_ge(onrm.h, onrm.n)
                        c.t.wait_ge(ost.h, ost.n)
                        peOT.inc(c.t.transpose(pTR[:, :], OB[:, :], IDB[:, :]))
                        c.v.wait_ge(peOT.h, peOT.n)
                        c.v.wait_ge(fst.h, fst.n)
                        ost.inc(c.v.tensor_copy(OST[:, qc * 128:(qc + 1) * 128],
                                                pTR[:, :]))
                    c.sy.wait_ge(ost.h, ost.n)
                    fst.inc(c.sy.dma_start(
                        out=oown[hi * 128:(hi + 1) * 128, qs * TP:(qs + 1) * TP],
                        in_=OST[:, :]), 16)
            c.g.wait_ge(fst.h, fst.n)
    return nc


def build_outproj_launch():
    nc = bass.Bass("TRN2", target_bir_lowering=False)
    xin = nc.dram_tensor("xin", [128, DC * TOKS], F32, kind="ExternalInput").ap()
    oin = nc.dram_tensor("oin", [128, DC * TOKS], BF16, kind="ExternalInput").ap()
    wout = nc.dram_tensor("wout", [128, DC * DC * 128], F32, kind="ExternalInput").ap()
    bout = nc.dram_tensor("bout", [128, DC], F32, kind="ExternalInput").ap()
    xout = nc.dram_tensor("xout", [128, DC * TOKS], F32, kind="ExternalOutput").ap()
    with ExitStack() as st:
        c = Ctx(nc, st)
        vb = {}
        vb["wbufs"] = [c.sbuf("wb0", [128, DC * 128], BF16),
                       c.sbuf("wb1", [128, DC * 128], BF16)]
        vb["pAB"] = [c.psum("pA", [128, TP]), c.psum("pB", [128, TP])]
        XF = c.sbuf("XF", [128, DC * TP], F32)
        OBT = c.sbuf("OBT", [128, DC * TP], BF16)
        BO = c.sbuf("BO", [128, DC], F32)
        block = st.enter_context(nc.Block())

        @block.gpsimd
        def _(_g):
            cst = c.sem("cst")
            cst.inc(c.sy.dma_start(out=BO[:, :], in_=bout[:, :]), 16)
            xld = c.sem("xld"); xst = c.sem("xst"); yev = c.sem("evd")
            c.v.wait_ge(cst.h, cst.n)
            for p in range(NPASS):
                c.sy.wait_ge(yev.h, yev.n)
                c.sy.wait_ge(xst.h, xst.n)
                xld.inc(c.sy.dma_start(
                    out=XF[:, :].rearrange("pp (cc t) -> pp cc t", cc=DC),
                    in_=xin.rearrange("pp (cc t) -> pp cc t", cc=DC)
                    [:, :, p * TP:(p + 1) * TP]), 16)
                xld.inc(c.sy.dma_start(
                    out=OBT[:, :].rearrange("pp (cc t) -> pp cc t", cc=DC),
                    in_=oin.rearrange("pp (cc t) -> pp cc t", cc=DC)
                    [:, :, p * TP:(p + 1) * TP]), 16)
                xw = (c.sem("xld"), c.sem("xld").n)

                def evac(oc, pp, wfree, wfree0, evd):
                    c.v.wait_ge(wfree.h, wfree0 + oc + 1)
                    inst = c.v.scalar_tensor_tensor(
                        XF[:, oc * TP:(oc + 1) * TP], pp[:, :TP],
                        BO[:, oc:oc + 1], XF[:, oc * TP:(oc + 1) * TP],
                        ALU.add, ALU.add)
                    evd.inc(inst)

                emit_wstream_mm(c, vb, wout, DC, DC,
                                lambda ci: OBT[:, ci * TP:(ci + 1) * TP],
                                evac, rhs_wait=xw)
                c.sy.wait_ge(yev.h, yev.n)
                xst.inc(c.sy.dma_start(
                    out=xout.rearrange("pp (cc t) -> pp cc t", cc=DC)
                    [:, :, p * TP:(p + 1) * TP],
                    in_=XF[:, :].rearrange("pp (cc t) -> pp cc t", cc=DC)), 16)
            c.g.wait_ge(xst.h, xst.n)
    return nc


def build_fused():
    nc = bass.Bass("TRN2", target_bir_lowering=False)
    xin = nc.dram_tensor("xin", [TOKS, D], F32, kind="ExternalInput").ap()
    w1 = nc.dram_tensor("w1", [128, H0C * DC * 128], F32, kind="ExternalInput").ap()
    w2 = nc.dram_tensor("w2", [128, H0C * H0C * 128], F32, kind="ExternalInput").ap()
    w3 = nc.dram_tensor("w3", [128, DC * H0C * 128], F32, kind="ExternalInput").ap()
    vcat_in = nc.dram_tensor("vcat", [128, 80], F32, kind="ExternalInput").ap()
    wcat = nc.dram_tensor("wcat", [128, DC * 384], F32, kind="ExternalInput").ap()
    brow = nc.dram_tensor("brow", [1, 384], F32, kind="ExternalInput").ap()
    mcum = nc.dram_tensor("mcum", [128, 128], F32, kind="ExternalInput").ap()
    chi = nc.dram_tensor("chi", [128, 2], F32, kind="ExternalInput").ap()
    amask = nc.dram_tensor("amask", [128, 128], U8, kind="ExternalInput").ap()
    identb = nc.dram_tensor("identb", [128, 128], BF16, kind="ExternalInput").ap()
    identf = nc.dram_tensor("identf", [128, 128], F32, kind="ExternalInput").ap()
    wo0 = nc.dram_tensor("wo0", [128, D], F32, kind="ExternalInput").ap()
    wup = nc.dram_tensor("wup", [2048, 1024], F32, kind="ExternalInput").ap()
    wga = nc.dram_tensor("wga", [2048, 1024], F32, kind="ExternalInput").ap()
    wdo = nc.dram_tensor("wdo", [8192, 256], F32, kind="ExternalInput").ap()
    h128u = nc.dram_tensor("h128u", [128, 128], F32, kind="ExternalInput").ap()
    h128d = nc.dram_tensor("h128d", [128, 128], F32, kind="ExternalInput").ap()
    suug = nc.dram_tensor("suug", [128, 16], F32, kind="ExternalInput").ap()
    sudp = nc.dram_tensor("sudp", [128, 64], F32, kind="ExternalInput").ap()
    wsup = nc.dram_tensor("wsup", [1, 1024], F32, kind="ExternalInput").ap()
    wsga = nc.dram_tensor("wsga", [1, 1024], F32, kind="ExternalInput").ap()
    wsdo = nc.dram_tensor("wsdo", [1, 256], F32, kind="ExternalInput").ap()
    wqkv5 = nc.dram_tensor("wqkv5", [128, 12 * DC * 128], F32, kind="ExternalInput").ap()
    bqkv5 = nc.dram_tensor("bqkv5", [128, 12], F32, kind="ExternalInput").ap()
    cmask5 = nc.dram_tensor("cmask5", [128, 4 * 512], BF16, kind="ExternalInput").ap()
    wout5 = nc.dram_tensor("wout5", [128, DC * DC * 128], F32, kind="ExternalInput").ap()
    bout5 = nc.dram_tensor("bout5", [128, DC], F32, kind="ExternalInput").ap()
    qoh = nc.dram_tensor("qoh", [128, 4], F32, kind="ExternalInput").ap()
    boh = nc.dram_tensor("boh", [128, 2], F32, kind="ExternalInput").ap()
    xout = nc.dram_tensor("xout", [TOKS, D], F32, kind="ExternalOutput").ap()
    xdram = nc.dram_tensor("xdram", [128, DC * TOKS], F32).ap()
    uown2 = nc.dram_tensor("uown2", [2048, 128], F32)
    uall2 = nc.dram_tensor("uall2", [4 * 2048, 128], F32)
    pcown2 = nc.dram_tensor("pcown2", [16, 128], F32)
    pcall2 = nc.dram_tensor("pcall2", [4 * 16, 128], F32)
    wpown = nc.dram_tensor("wpown", [128, 2 * 8 * DC * 128 + 2 * 64 * 128], BF16)
    wpall = nc.dram_tensor("wpall", [NCORES * 128, 2 * 8 * DC * 128 + 2 * 64 * 128], BF16)
    xhownp = [nc.dram_tensor("xhown%d" % p, [128, DC * TP], BF16)
              for p in range(NPASS)]
    xhallp = [nc.dram_tensor("xhall%d" % p, [4 * 128, DC * TP], BF16)
              for p in range(NPASS)]
    o5own = nc.dram_tensor("o5own", [NCORES * 512, TOKS], BF16)
    o5all = nc.dram_tensor("o5all", [NCORES * 512, TOKS], BF16)
    RG = [list(range(NCORES))]
    RG4 = [[0, 1, 2, 3], [4, 5, 6, 7]]

    with ExitStack() as st:
        c = Ctx(nc, st)
        vb = {}
        vb["XB2"] = c.sbuf("XB2", [128, DC * TP], BF16)
        vb["XQr"] = c.sbuf("XQr", [128, 2 * TP], BF16)
        vb["XFR"] = c.sbuf("XFR", [128, 2 * TP], F32)
        vb["XFS"] = c.sbuf("XFS", [128, TP], F32)
        vb["XHB"] = c.sbuf("XHB", [128, DC * TP], BF16)
        vb["SROW"] = c.sbuf("SROW", [1, 384], F32)
        vb["onesb"] = c.sbuf("onesb", [128, 1], BF16)
        vb["ones1f"] = c.sbuf("ones1f", [1, 128], F32)
        vb["onesbb"] = c.sbuf("onesbb", [128, 128], BF16)
        vb["LROW"] = c.sbuf("LROW", [128, 4 * TP], F32)
        vb["vcat"] = c.sbuf("vcat_sb", [128, 80], F32)
        vb["hb3"] = c.sbuf("hb3", [128, DC], F32)
        vb["B1C"], vb["B2C"], B3C = 0, 32, 64
        XF = c.sbuf("XF", [128, DC * TP], F32)
        IDBs = c.sbuf("IDBs", [128, 128], BF16)
        IDFs = c.sbuf("IDFs", [128, 128], F32)
        QOH = c.sbuf("QOH", [128, 4], F32)
        BOH = c.sbuf("BOH", [128, 2], F32)
        vb["pS1"] = c.psum("pS1", [128, TP])
        vb["pS2"] = c.psum("pS2", [128, TP])
        vb["pR"] = c.psum("pR", [128, TP])
        vb["pM"] = c.psum("pM", [128, TP])
        ccs = c.sem("ccs")
        xst = c.sem("xst")
        xld = c.sem("xld")
        block = st.enter_context(nc.Block())

        @block.gpsimd
        def _(_g):
            cst = c.sem("cst")
            cst.inc(c.sy.dma_start(out=vb["vcat"][:, :], in_=vcat_in[:, :]), 16)
            cst.inc(c.sy.dma_start(out=IDBs[:, :], in_=identb[:, :]), 16)
            cst.inc(c.sy.dma_start(out=IDFs[:, :], in_=identf[:, :]), 16)
            cst.inc(c.sy.dma_start(out=QOH[:, :], in_=qoh[:, :]), 16)
            cst.inc(c.sy.dma_start(out=BOH[:, :], in_=boh[:, :]), 16)
            c.v.wait_ge(cst.h, cst.n)
            c.s.wait_ge(cst.h, cst.n)
            c.v.memset(vb["onesb"][:, :], 1.0)
            c.v.memset(vb["onesbb"][:, :], 1.0)
            ini0 = c.v.memset(vb["ones1f"][:, :], 1.0)
            ini = c.v.tensor_scalar(vb["hb3"][:, :], vb["vcat"][:, B3C:B3C + DC],
                                    0.5, None, ALU.mult)
            c.sem("lnV").inc(ini)

            with ExitStack() as stp:
                cp = Ctx.__new__(Ctx)
                cp.__dict__.update(c.__dict__)
                pvb = {}
                pvb["ZA"] = stp.enter_context(nc.sbuf_tensor("ZA", [128, 8192], F32))
                pvb["ZB"] = stp.enter_context(nc.sbuf_tensor("ZB", [128, 8192], F32))
                pvb["WOUT"] = stp.enter_context(nc.sbuf_tensor("WOUT", [128, 8192], BF16))
                pvb["ones1f"] = vb["ones1f"]
                HU = stp.enter_context(nc.sbuf_tensor("HU", [128, 128], F32))
                HDN = stp.enter_context(nc.sbuf_tensor("HDN", [128, 128], F32))
                SUU = stp.enter_context(nc.sbuf_tensor("SUU", [128, 16], F32))
                SUD = stp.enter_context(nc.sbuf_tensor("SUD", [128, 64], F32))
                WSU = stp.enter_context(nc.sbuf_tensor("WSU", [1, 1024], F32))
                WSG = stp.enter_context(nc.sbuf_tensor("WSG", [1, 1024], F32))
                WSD = stp.enter_context(nc.sbuf_tensor("WSD", [1, 256], F32))
                pvb["pH"] = [stp.enter_context(nc.psum_tensor("pH0", [128, 512], F32)),
                             stp.enter_context(nc.psum_tensor("pH1", [128, 512], F32))]
                pvb["pW"] = stp.enter_context(nc.psum_tensor("pWb", [128, 512], F32))
                for dst, srcap in [(HU, h128u), (HDN, h128d), (SUU, suug),
                                   (SUD, sudp), (WSU, wsup), (WSG, wsga), (WSD, wsdo)]:
                    cst.inc(c.sy.dma_start(out=dst[:, :], in_=srcap[:, :]), 16)
                c.t.wait_ge(cst.h, cst.n)
                c.v.wait_ge(cst.h, cst.n)
                c.sy.wait_ge(c.sem("lnV").h, c.sem("lnV").n)
                UPSPAN = 8 * DC * 128
                emit_had_transform(c, pvb, wup, wpown.ap()[:, 0:UPSPAN],
                                   16, 1024, HU, SUU, WSU, "up")
                emit_had_transform(c, pvb, wga, wpown.ap()[:, UPSPAN:2 * UPSPAN],
                                   16, 1024, HU, SUU, WSG, "ga")
                so = emit_had_transform(c, pvb, wdo,
                                        wpown.ap()[:, 2 * UPSPAN:2 * UPSPAN + 2 * 64 * 128],
                                        64, 256, HDN, SUD, WSD, "do")
                hdv = c.sem("hdv"); hpe = c.sem("hpe")
                for eng in (c.t, c.v, c.s, c.sy, c.g):
                    eng.wait_ge(so.h, so.n)
                    eng.wait_ge(hdv.h, hdv.n)
                    eng.wait_ge(hpe.h, hpe.n)

            with ExitStack() as stf:
                fvb = dict(vb)
                fvb["H1B"] = stf.enter_context(nc.sbuf_tensor("H1B", [128, H0C * TP], BF16))
                fvb["H2B"] = stf.enter_context(nc.sbuf_tensor("H2B", [128, H0C * TP], BF16))
                fvb["wbufs"] = [stf.enter_context(nc.sbuf_tensor("wb0", [128, H0C * 128], BF16)),
                                stf.enter_context(nc.sbuf_tensor("wb1", [128, H0C * 128], BF16))]
                fvb["TY"] = stf.enter_context(nc.sbuf_tensor("TY", [128, 2 * TP], F32))
                XT = stf.enter_context(nc.sbuf_tensor("XT", [128, 2 * D], F32))
                fvb["pAB"] = [stf.enter_context(nc.psum_tensor("pA", [128, TP], F32)),
                              stf.enter_context(nc.psum_tensor("pB", [128, TP], F32))]
                pT = [stf.enter_context(nc.psum_tensor("pT0", [128, 128], F32)),
                      stf.enter_context(nc.psum_tensor("pT1", [128, 128], F32))]
                tpe = c.sem("tpe"); tdv = c.sem("tdv"); evd = c.sem("evd")
                for p in range(NPASS):
                    tpe0, tdv0 = tpe.n, tdv.n
                    i = 0
                    for half in range(2):
                        c.sy.wait_ge(tpe.h, tpe0 + i)
                        xld.inc(c.sy.dma_start(
                            out=XT[:, :].rearrange("pp (s d) -> pp s d", s=2),
                            in_=xin.rearrange("(s pp) d -> pp s d", pp=128)
                            [:, p * NSUB + half * 2:p * NSUB + half * 2 + 2, :]), 16)
                        c.t.wait_ge(xld.h, xld.n)
                        if half == 0:
                            c.t.wait_ge(evd.h, evd.n)
                            c.t.wait_ge(xst.h, xst.n)
                        for sub2 in range(2):
                            sub = half * 2 + sub2
                            for ci in range(DC):
                                if i >= 2:
                                    c.t.wait_ge(tdv.h, tdv0 + i - 1)
                                tpe.inc(c.t.transpose(
                                    pT[i % 2][:, :],
                                    XT[:, sub2 * D + ci * 128:sub2 * D + (ci + 1) * 128],
                                    IDFs[:, :]))
                                c.v.wait_ge(tpe.h, tpe0 + i + 1)
                                tdv.inc(c.v.tensor_copy(
                                    XF[:, ci * TP + sub * 128:ci * TP + (sub + 1) * 128],
                                    pT[i % 2][:, :]))
                                i += 1
                    emit_ln_stats(c, fvb, None, p, p % 2, sbuf_xf=XF,
                                  pre=(tdv, tdv.n))
                    lnV = emit_ln_norm(c, fvb, None, p, p % 2, sbuf_xf=XF,
                                       xhb_free=(c.sem("evd"), c.sem("evd").n),
                                       pre=(tdv, tdv.n))
                    evd = emit_ffn0_pass(c, fvb, XF, fvb["XHB"], lnV, w1, w2, w3)
                    c.sy.wait_ge(evd.h, evd.n)
                    xst.inc(c.sy.dma_start(
                        out=xdram.rearrange("pp (cc t) -> pp cc t", cc=DC)
                        [:, :, p * TP:(p + 1) * TP],
                        in_=XF[:, :].rearrange("pp (cc t) -> pp cc t", cc=DC)), 16)
                for eng in (c.t, c.v, c.s, c.sy, c.g):
                    eng.wait_ge(evd.h, evd.n)
                    eng.wait_ge(c.sem("wfree").h, c.sem("wfree").n)
                    eng.wait_ge(xst.h, xst.n)


            st2 = ExitStack()
            QTall = st2.enter_context(nc.sbuf_tensor("QTall", [128, TOKS], BF16))
            OIT = st2.enter_context(nc.sbuf_tensor("OIT", [128, TOKS], F32))
            with ExitStack() as sta:
                avb = dict(vb)
                for nm, shape, dt in [("WCB", [128, DC * 384], BF16),
                                      ("BBC", [128, 384], F32),
                                      ("MCUM", [128, 128], F32),
                                      ("CHI", [128, 2], F32),
                                      ("AMASK", [128, 128], U8),
                                      ("ZEROB", [128, 128], BF16),
                                      ("P384", [128, 384], F32),
                                      ("SC64", [128, 256], F32),
                                      ("NCC", [128, 68], F32),
                                      ("NLB", [128, 64], F32),
                                      ("DELTA", [128, 64], F32),
                                      ("EM", [128, 64], F32),
                                      ("EP", [128, 64], F32),
                                      ("QTM", [128, 128], BF16),
                                      ("KTM", [128, 128], BF16),
                                      ("TKF", [128, 64], F32),
                                      ("VB", [128, 128], BF16),
                                      ("KTP", [128, 128], BF16),
                                      ("AB", [128, 128], BF16),
                                      ("USTG", [64, 32 * 128], F32),
                                      ("OISTG", [128, DC * 128], F32),
                                      ("OIF", [128, 128], F32),
                                      ("PCST", [64, 32], F32)]:
                    avb[nm] = sta.enter_context(nc.sbuf_tensor(nm, shape, dt))
                avb["IDB"] = IDBs
                avb["IDF"] = IDFs
                avb["QTall"] = QTall
                avb["pPP"] = sta.enter_context(nc.psum_tensor("pPP", [128, 448], F32))
                avb["pOX"] = sta.enter_context(nc.psum_tensor("pOX", [128, 194], F32))
                avb["pU"] = sta.enter_context(nc.psum_tensor("pU", [64, 256], F32))
                avb["pTQ"] = sta.enter_context(nc.psum_tensor("pTQ", [128, 256], BF16))
                cst.inc(c.sy.dma_start(out=avb["MCUM"][:, :], in_=mcum[:, :]), 16)
                cst.inc(c.sy.dma_start(out=avb["CHI"][:, :], in_=chi[:, :]), 16)
                cst.inc(c.sy.dma_start(out=avb["AMASK"][:, :], in_=amask[:, :]), 16)
                cstg = c.sem("cstg")
                cstg.inc(c.g.dma_start(out=avb["WCB"][:, :], in_=wcat[:, :]), 16)
                cst.inc(c.sy.dma_start(out=vb["SROW"][0:1, 0:384], in_=brow[:, :]), 16)
                dvI = c.sem("dvI")
                c.v.wait_ge(cst.h, cst.n)
                c.v.wait_ge(cstg.h, cstg.n)
                c.v.memset(avb["ZEROB"][:, :], 0.0)
                c.v.memset(QTall[:, :], 0.0)
                c.v.memset(avb["QTM"][:, :], 0.0)
                dvI.inc(c.v.memset(avb["KTM"][:, :], 0.0))
                c.t.wait_ge(dvI.h, dvI.n)
                c.t.wait_ge(c.sem("lnP").h, c.sem("lnP").n)
                bb2 = c.t.matmul(vb["pR"][:, 0:384], vb["ones1f"][:, :],
                                 vb["SROW"][0:1, 0:384], start=True, stop=True)
                c.sem("lnP").inc(bb2)
                c.v.wait_ge(c.sem("lnP").h, c.sem("lnP").n)
                ini2 = c.v.tensor_copy(avb["BBC"][:, :], vb["pR"][:, 0:384])
                c.sem("lnV").inc(ini2)
                emit_ln_stats(c, avb, xdram, 0, 0)
                emit_ln_stats(c, avb, xdram, 1, 1)
                emit_ln_norm(c, avb, xdram, 0, 0, qe=c.sy)
                prjP = c.sem("prjP")
                for p in range(NPASS):
                    for s in range(NSUB):
                        emit_gla_a_subtile(c, avb, avb["XHB"], p, s, 0)
                    if p + 1 < NPASS:
                        emit_ln_norm(c, avb, xdram, p + 1, (p + 1) % 2, qe=c.sy,
                                     xhb_free=(prjP, prjP.n))
                    if p + 2 < NPASS:
                        emit_ln_stats(c, avb, xdram, p + 2, p % 2)
                fina = c.sem("fina")
                c.v.wait_ge(c.sem("otv").h, c.sem("otv").n)
                otc = c.v.tensor_copy(OIT[:, :], avb["OISTG"][:, :])
                fina.inc(otc)
                c.sy.wait_ge(c.sem("uev").h, c.sem("uev").n)
                c.sy.wait_ge(c.sem("dvQ").h, c.sem("dvQ").n)
                ust = c.sem("ust")
                ust.inc(c.sy.dma_start(
                    out=uown2.ap()[:, :].rearrange("(j k) v -> k j v", k=64),
                    in_=avb["USTG"][:, :].rearrange("k (j v) -> k j v", v=128)), 16)
                ust.inc(c.sy.dma_start(
                    out=pcown2.ap()[:, :].rearrange("q (a j) -> (q a) j", a=4),
                    in_=avb["PCST"][:, :]), 16)
                c.g.wait_ge(ust.h, ust.n)
                ccs.inc(c.g.collective_compute(
                    "AllGather", ALU.bypass, replica_groups=RG4,
                    ins=[uown2.ap().opt()], outs=[uall2.ap().opt()]), 1)
                ccs.inc(c.g.collective_compute(
                    "AllGather", ALU.bypass, replica_groups=RG4,
                    ins=[pcown2.ap().opt()], outs=[pcall2.ap().opt()]), 1)
                AGU = ccs.n
                for eng in (c.t, c.v, c.s, c.sy, c.g):
                    eng.wait_ge(fina.h, fina.n)
                    eng.wait_ge(c.sem("tqv").h, c.sem("tqv").n)
                    eng.wait_ge(c.sem("sel").h, c.sem("sel").n)
                    eng.wait_ge(ust.h, ust.n)

            with ExitStack() as stb:
                SCAN = stb.enter_context(nc.sbuf_tensor("SCAN", [128, 129 * 128], F32))
                PCsb = stb.enter_context(nc.sbuf_tensor("PCsb", [128, 128], F32))
                OTall = stb.enter_context(nc.sbuf_tensor("OTall", [128, TOKS], BF16))
                WOB = stb.enter_context(nc.sbuf_tensor("WOB", [128, D], BF16))
                SDALL = stb.enter_context(nc.sbuf_tensor("SDALL", [128, 32 * 128], BF16))
                pOTs = [stb.enter_context(nc.psum_tensor("pOT", [128, 64], F32)),
                        stb.enter_context(nc.psum_tensor("pOT2", [128, 64], F32))]
                pA2 = stb.enter_context(nc.psum_tensor("pA2", [128, TP], F32))
                pB2 = stb.enter_context(nc.psum_tensor("pB2", [128, TP], F32))
                ld2 = c.sem("ld2")
                ld2.inc(c.g.dma_start(out=WOB[:, :], in_=wo0[:, :]), 16)
                c.g.wait_ge(ccs.h, AGU)
                for half in range(2):
                    ld2.inc(c.g.dma_start(
                        out=SCAN[half * 64:(half + 1) * 64, 128:129 * 128]
                        .rearrange("k (j2 v) -> k j2 v", v=128),
                        in_=uall2.ap().rearrange("(j2 k) v -> k j2 v", k=64)), 16)
                for half in range(2):
                    ld2.inc(c.g.dma_start(
                        out=PCsb[half * 64:(half + 1) * 64, :]
                        .rearrange("k (j2 j) -> k j2 j", j=32),
                        in_=bass.AP(pcall2.ap().tensor, 0,
                                    [[32, 64], [2048, 4], [1, 32]])), 16)
                ccs.inc(c.g.collective_compute(
                    "AllGather", ALU.bypass, replica_groups=RG,
                    ins=[wpown.ap().opt()], outs=[wpall.ap().opt()]), 1)
                AGW = ccs.n
                dvI2 = c.sem("dvI2")
                dvI2.inc(c.v.memset(SCAN[:, 0:128], 0.0))
                c.v.wait_ge(ld2.h, ld2.n)
                scn = c.sem("scn")
                view = SCAN[:, 128:].rearrange("p (j v) -> p v j", v=128)
                for vi in range(128):
                    inst = c.v.tensor_tensor_scan(view[:, vi, :], view[:, vi, :],
                                                  PCsb[:, :], 0.0, ALU.add, ALU.mult)
                scn.inc(inst)
                peI = c.sem("peI"); otv2 = c.sem("otv2")
                peY = c.sem("peY"); yev = c.sem("yev")
                sdc = c.sem("sdc")
                c.v.wait_ge(scn.h, scn.n)
                c.v.tensor_scalar(SDALL[:, :], SCAN[:, 0:4096], QOH[:, 0:1],
                                  None, ALU.mult)
                for w in range(1, 4):
                    inst = c.v.scalar_tensor_tensor(
                        SDALL[:, :], SCAN[:, w * 4096:(w + 1) * 4096],
                        QOH[:, w:w + 1], SDALL[:, :], ALU.mult, ALU.add)
                sdc.inc(inst)
                c.t.wait_ge(scn.h, scn.n)
                c.t.wait_ge(dvI2.h, dvI2.n)
                c.t.wait_ge(ld2.h, ld2.n)
                c.t.wait_ge(sdc.h, sdc.n)
                c.v.wait_ge(dvI2.h, dvI2.n)
                if True:
                    for p in range(NPASS):
                        c.sy.wait_ge(yev.h, yev.n)
                        c.sy.wait_ge(xst.h, xst.n)
                        xld.inc(c.sy.dma_start(
                            out=XF[:, :].rearrange("pp (cc t) -> pp cc t", cc=DC),
                            in_=xdram.rearrange("pp (cc t) -> pp cc t", cc=DC)
                            [:, :, p * TP:(p + 1) * TP]), 16)
                        otv20 = otv2.n
                        for s in range(NSUB):
                            si = p * NSUB + s
                            st_ = si * 128
                            for h in range(2):
                                jloc = 2 * si + h
                                k2 = 2 * s + h
                                pOT = pOTs[k2 % 2]
                                if k2 >= 2:
                                    c.t.wait_ge(otv2.h, otv20 + k2 - 1)
                                mm = c.t.matmul(pOT[:, 0:64],
                                                SDALL[:, jloc * 128:(jloc + 1) * 128],
                                                QTall[:, st_ + h * 64:st_ + (h + 1) * 64],
                                                start=True, stop=True)
                                peI.inc(mm)
                                c.v.wait_ge(peI.h, peI.n)
                                otv2.inc(c.v.tensor_tensor(
                                    OTall[:, st_ + h * 64:st_ + (h + 1) * 64],
                                    pOT[:, 0:64],
                                    OIT[:, st_ + h * 64:st_ + (h + 1) * 64], ALU.add))
                        c.t.wait_ge(otv2.h, otv2.n)
                        c.t.wait_ge(xld.h, xld.n)
                        for oc in range(DC):
                            pp = [pA2, pB2][oc % 2]
                            if oc >= 2:
                                c.t.wait_ge(yev.h, yev.n)
                            peY.inc(c.t.matmul(pp[:, :TP], WOB[:, oc * 128:(oc + 1) * 128],
                                               OTall[:, p * TP:(p + 1) * TP],
                                               start=True, stop=True))
                            c.v.wait_ge(peY.h, peY.n)
                            yev.inc(c.v.tensor_tensor(XF[:, oc * TP:(oc + 1) * TP],
                                                      pp[:, :TP],
                                                      XF[:, oc * TP:(oc + 1) * TP], ALU.add))
                        c.sy.wait_ge(yev.h, yev.n)
                        xst.inc(c.sy.dma_start(
                            out=xdram.rearrange("pp (cc t) -> pp cc t", cc=DC)
                            [:, :, p * TP:(p + 1) * TP],
                            in_=XF[:, :].rearrange("pp (cc t) -> pp cc t", cc=DC)), 16)
                for eng in (c.t, c.v, c.s, c.sy, c.g):
                    eng.wait_ge(yev.h, yev.n)
                    eng.wait_ge(xst.h, xst.n)
                    eng.wait_ge(otv2.h, otv2.n)
            st2.close()

            with ExitStack() as stf:
                fvb = dict(vb)
                fvb["H1B"] = stf.enter_context(nc.sbuf_tensor("H1B3", [128, H0C * TP], BF16))
                fvb["H2B"] = stf.enter_context(nc.sbuf_tensor("H2B3", [128, H0C * TP], BF16))
                fvb["wbufs"] = [stf.enter_context(nc.sbuf_tensor("wb03", [128, H0C * 128], BF16)),
                                stf.enter_context(nc.sbuf_tensor("wb13", [128, H0C * 128], BF16)),
                                stf.enter_context(nc.sbuf_tensor("wb23", [128, H0C * 128], BF16))]
                fvb["TY"] = stf.enter_context(nc.sbuf_tensor("TY3", [128, 2 * TP], F32))
                fvb["pAB"] = [stf.enter_context(nc.psum_tensor("pA3", [128, TP], F32)),
                              stf.enter_context(nc.psum_tensor("pB3", [128, TP], F32))]
                evd = c.sem("evd")
                emit_ln_stats(c, fvb, xdram, 0, 0)
                emit_ln_stats(c, fvb, xdram, 1, 1)
                lnV = emit_ln_norm(c, fvb, xdram, 0, 0, qe=c.sy)
                for p in range(NPASS):
                    c.sy.wait_ge(evd.h, evd.n)
                    c.sy.wait_ge(xst.h, xst.n)
                    xld.inc(c.sy.dma_start(
                        out=XF[:, :].rearrange("pp (cc t) -> pp cc t", cc=DC),
                        in_=xdram.rearrange("pp (cc t) -> pp cc t", cc=DC)
                        [:, :, p * TP:(p + 1) * TP]), 16)

                    def hook3(p=p):
                        if p + 1 < NPASS:
                            emit_ln_norm(c, fvb, xdram, p + 1, (p + 1) % 2,
                                         qe=c.sy,
                                         xhb_free=(c.sem("evd"), c.sem("evd").n))
                        if p + 2 < NPASS:
                            emit_ln_stats(c, fvb, xdram, p + 2, p % 2)
                    evd = emit_ffn0_pass(c, fvb, XF, fvb["XHB"], lnV, w1, w2, w3,
                                         hook=hook3, xf_wait=(xld, xld.n))
                    lnV = c.sem("lnV")
                    c.sy.wait_ge(evd.h, evd.n)
                    xst.inc(c.sy.dma_start(
                        out=xdram.rearrange("pp (cc t) -> pp cc t", cc=DC)
                        [:, :, p * TP:(p + 1) * TP],
                        in_=XF[:, :].rearrange("pp (cc t) -> pp cc t", cc=DC)), 16)
                for eng in (c.t, c.v, c.s, c.sy, c.g):
                    eng.wait_ge(evd.h, evd.n)
                    eng.wait_ge(c.sem("wfree").h, c.sem("wfree").n)
                    eng.wait_ge(xst.h, xst.n)

            def l1ffn_phase(tagsuffix):
                with ExitStack() as stl:
                    GB = stl.enter_context(nc.sbuf_tensor("GB" + tagsuffix, [128, 64 * TP], BF16))
                    SGT = stl.enter_context(nc.sbuf_tensor("SGT" + tagsuffix, [128, TP], BF16))
                    wbu = [stl.enter_context(nc.sbuf_tensor("wbu0" + tagsuffix, [128, 2 * DC * 128], BF16)),
                           stl.enter_context(nc.sbuf_tensor("wbu1" + tagsuffix, [128, 2 * DC * 128], BF16))]
                    wbg = [stl.enter_context(nc.sbuf_tensor("wbg0" + tagsuffix, [128, 2 * DC * 128], BF16)),
                           stl.enter_context(nc.sbuf_tensor("wbg1" + tagsuffix, [128, 2 * DC * 128], BF16))]
                    wbd = [stl.enter_context(nc.sbuf_tensor("wbd0" + tagsuffix, [128, 32 * 128], BF16)),
                           stl.enter_context(nc.sbuf_tensor("wbd1" + tagsuffix, [128, 32 * 128], BF16))]
                    pG1 = stl.enter_context(nc.psum_tensor("pG1" + tagsuffix, [128, TP], F32))
                    pG2 = stl.enter_context(nc.psum_tensor("pG2" + tagsuffix, [128, TP], F32))
                    pY = [stl.enter_context(nc.psum_tensor("pY0" + tagsuffix, [128, TP], F32)),
                          stl.enter_context(nc.psum_tensor("pY1" + tagsuffix, [128, TP], F32))]
                    IC = 64
                    UPSPAN = 8 * DC * 128
                    wfree = c.sem("wfree"); gvd = c.sem("gvd"); sil = c.sem("sil")
                    yev = c.sem("yevL"); wldu = c.sem("wldu"); wldg = c.sem("wldg")
                    wldd = c.sem("wldd")
                    gpair = [(pG1, pG2), (vb["pR"], vb["pM"])]
                    c.g.wait_ge(ccs.h, AGW)

                    def wsrc(kind, hc):
                        r = hc >> 3
                        ocl = hc & 7
                        if kind == 0:
                            cs = ocl * (DC * 128)
                        elif kind == 1:
                            cs = UPSPAN + ocl * (DC * 128)
                        else:
                            r = hc >> 1
                            ocl = hc & 1
                            cs = 2 * UPSPAN + ocl * (64 * 128)
                        return wpall.ap()[r * 128:(r + 1) * 128,
                                          cs:cs + (DC * 128 if kind < 2 else 64 * 128)]

                    def wsrc2(kind, hp):
                        hc = 2 * hp
                        r = hc >> 3
                        cs = (0 if kind == 0 else UPSPAN) + (hc & 7) * (DC * 128)
                        return wpall.ap()[r * 128:(r + 1) * 128, cs:cs + 2 * DC * 128]

                    emit_ln_stats(c, vb, xdram, 0, 0)
                    emit_ln_stats(c, vb, xdram, 1, 1)
                    emit_ln_norm(c, vb, xdram, 0, 0, qe=c.g)
                    for p in range(NPASS):
                        c.sy.wait_ge(yev.h, yev.n)
                        c.sy.wait_ge(xst.h, xst.n)
                        xld.inc(c.sy.dma_start(
                            out=XF[:, :].rearrange("pp (cc t) -> pp cc t", cc=DC),
                            in_=xdram.rearrange("pp (cc t) -> pp cc t", cc=DC)
                            [:, :, p * TP:(p + 1) * TP]), 16)
                        lnV = c.sem("lnV")
                        wfree0, gvd0, sil0 = wfree.n, gvd.n, sil.n
                        wldu0, wldg0 = wldu.n, wldg.n
                        if not hasattr(c, "wlast"):
                            c.wlast = {}
                        for hp in range(IC // 2):
                            bu = wbu[hp % 2]; bg = wbg[hp % 2]
                            if id(bu) in c.wlast:
                                c.sy.wait_ge(wfree.h, c.wlast[id(bu)])
                            wldu.inc(c.sy.dma_start(out=bu[:, :], in_=wsrc2(0, hp)), 16)
                            wldg.inc(c.sy.dma_start(out=bg[:, :], in_=wsrc2(1, hp)), 16)
                            for k in range(2):
                                hc = 2 * hp + k
                                ppu, ppg = gpair[hc % 2]
                                if k == 0:
                                    c.t.wait_ge(wldu.h, wldu0 + 16 * (hp + 1))
                                    c.t.wait_ge(wldg.h, wldg0 + 16 * (hp + 1))
                                if hc == 0:
                                    c.t.wait_ge(lnV.h, lnV.n)
                                if hc >= 2:
                                    c.t.wait_ge(gvd.h, gvd0 + hc - 1)
                                else:
                                    c.t.wait_ge(gvd.h, gvd0)
                                ko = k * DC * 128
                                for ci in range(DC):
                                    c.t.matmul(ppu[:, :TP],
                                               bu[:, ko + ci * 128:ko + (ci + 1) * 128],
                                               vb["XHB"][:, ci * TP:(ci + 1) * TP],
                                               start=(ci == 0), stop=(ci == DC - 1))
                                for ci in range(DC):
                                    inst = c.t.matmul(ppg[:, :TP],
                                                      bg[:, ko + ci * 128:ko + (ci + 1) * 128],
                                                      vb["XHB"][:, ci * TP:(ci + 1) * TP],
                                                      start=(ci == 0), stop=(ci == DC - 1))
                                wfree.inc(inst)
                                c.s.wait_ge(wfree.h, wfree0 + hc + 1)
                                c.s.wait_ge(gvd.h, gvd0 + hc)
                                sil.inc(c.s.activation(SGT[:, :], ppg[:, :TP], AF.Silu))
                                c.v.wait_ge(sil.h, sil0 + hc + 1)
                                gvd.inc(c.v.tensor_tensor(GB[:, hc * TP:(hc + 1) * TP],
                                                          ppu[:, :TP], SGT[:, :], ALU.mult))
                            c.wlast[id(bu)] = wfree.n
                        if p + 1 < NPASS:
                            emit_ln_norm(c, vb, xdram, p + 1, (p + 1) % 2, qe=c.g,
                                         xhb_free=(wfree, wfree.n))
                        if p + 2 < NPASS:
                            emit_ln_stats(c, vb, xdram, p + 2, p % 2)
                        wldds = c.sem("wldds")
                        wldd0, wldds0, yev0 = wldd.n, wldds.n, yev.n
                        dh = c.sem("dh")
                        dh0 = dh.n
                        wfree0d = wfree.n
                        for dc in range(DC):
                            ws = wsrc(2, dc)
                            QW = 16 * 128
                            for q in range(4):
                                qe_ = [c.sy, c.s][q % 2]
                                half = q >> 1
                                bd = wbd[half]
                                if dc >= 1:
                                    if half == 0:
                                        qe_.wait_ge(dh.h, dh0 + dc)
                                    else:
                                        qe_.wait_ge(wfree.h, wfree0d + dc)
                                sem_ = [wldd, wldds][q % 2]
                                sem_.inc(qe_.dma_start(
                                    out=bd[:, (q & 1) * QW:(q & 1) * QW + QW],
                                    in_=ws[:, q * QW:(q + 1) * QW]), 16)
                            pp = pY[dc % 2]
                            if dc == 0:
                                c.t.wait_ge(gvd.h, gvd.n)
                            if dc >= 2:
                                c.t.wait_ge(yev.h, yev0 + dc - 1)
                            for fc in range(64):
                                q = fc // 16
                                bd = wbd[q >> 1]
                                if fc % 16 == 0:
                                    sem_ = [wldd, wldds][q % 2]
                                    base_ = [wldd0, wldds0][q % 2]
                                    c.t.wait_ge(sem_.h,
                                                base_ + 16 * (2 * dc + (q >> 1) + 1))
                                inst = c.t.matmul(
                                    pp[:, :TP],
                                    bd[:, ((fc % 32) // 16) * QW
                                       + (fc % 16) * 128:((fc % 32) // 16) * QW
                                       + (fc % 16) * 128 + 128],
                                    GB[:, fc * TP:(fc + 1) * TP],
                                    start=(fc == 0), stop=(fc == 63))
                                if fc == 31:
                                    dh.inc(inst)
                            wfree.inc(inst)
                            c.v.wait_ge(wfree.h, wfree.n)
                            if dc == 0:
                                c.v.wait_ge(xld.h, xld.n)
                            yev.inc(c.v.tensor_tensor(XF[:, dc * TP:(dc + 1) * TP],
                                                      pp[:, :TP],
                                                      XF[:, dc * TP:(dc + 1) * TP], ALU.add))
                        c.sy.wait_ge(yev.h, yev.n)
                        xst.inc(c.sy.dma_start(
                            out=xdram.rearrange("pp (cc t) -> pp cc t", cc=DC)
                            [:, :, p * TP:(p + 1) * TP],
                            in_=XF[:, :].rearrange("pp (cc t) -> pp cc t", cc=DC)), 16)
                    for eng in (c.t, c.v, c.s, c.sy, c.g):
                        eng.wait_ge(yev.h, yev.n)
                        eng.wait_ge(c.sem("wfree").h, c.sem("wfree").n)
                        eng.wait_ge(c.sem("gvd").h, c.sem("gvd").n)
                        eng.wait_ge(xst.h, xst.n)

            l1ffn_phase("a")

            emit_ln_stats(c, vb, xdram, 0, 0)
            emit_ln_stats(c, vb, xdram, 1, 1)
            emit_ln_norm(c, vb, xdram, 0, 0, qe=c.sy)
            AGXp = []
            for p in range(NPASS):
                c.sy.wait_ge(c.sem("lnV").h, c.sem("lnV").n)
                xst.inc(c.sy.dma_start(
                    out=xhownp[p].ap()[:, :], in_=vb["XHB"][:, :]), 16)
                c.g.wait_ge(xst.h, xst.n)
                ccs.inc(c.g.collective_compute(
                    "AllGather", ALU.bypass, replica_groups=RG4,
                    ins=[xhownp[p].ap().opt()], outs=[xhallp[p].ap().opt()]), 1)
                AGXp.append(ccs.n)
                if p + 1 < NPASS:
                    emit_ln_norm(c, vb, xdram, p + 1, (p + 1) % 2, qe=c.sy,
                                 xhb_free=(xst, xst.n))
                if p + 2 < NPASS:
                    emit_ln_stats(c, vb, xdram, p + 2, p % 2)

            with ExitStack() as stt:
                WB5 = stt.enter_context(nc.sbuf_tensor("WB5", [128, 12 * DC * 128], BF16))
                BQ5 = stt.enter_context(nc.sbuf_tensor("BQ5", [128, 12], F32))
                CM5 = stt.enter_context(nc.sbuf_tensor("CM5", [128, 4 * 512], BF16))
                Q5 = stt.enter_context(nc.sbuf_tensor("Q5", [128, SEQ], BF16))
                Kt5 = stt.enter_context(nc.sbuf_tensor("Kt5", [128, SEQ], BF16))
                VT5 = stt.enter_context(nc.sbuf_tensor("VT5", [128, 64 * 129], BF16))
                VF5 = stt.enter_context(nc.sbuf_tensor("VF5", [128, TP], BF16))
                PT5 = [stt.enter_context(nc.sbuf_tensor("PT5a", [128, TP], BF16)),
                       stt.enter_context(nc.sbuf_tensor("PT5b", [128, TP], BF16)),
                       stt.enter_context(nc.sbuf_tensor("PT5c", [128, TP], BF16)),
                       stt.enter_context(nc.sbuf_tensor("PT5d", [128, TP], BF16))]
                OB5 = stt.enter_context(nc.sbuf_tensor("OB5", [128, 128], BF16))
                RC5 = stt.enter_context(nc.sbuf_tensor("RC5", [128, 1], F32))
                OST5 = stt.enter_context(nc.sbuf_tensor("OST5", [128, TP], BF16))
                XB5 = vb["XB2"]
                pS5 = [vb["pS1"], vb["pS2"],
                       stt.enter_context(nc.psum_tensor("pO5a", [128, TP], F32))]
                NS5 = len(pS5)
                pO5 = [vb["pR"][:, 0:129], vb["pM"][:, 0:129],
                       stt.enter_context(nc.psum_tensor("pO5b", [128, TP], F32))
                       [:, 0:129],
                       stt.enter_context(nc.psum_tensor("pO5c", [128, TP], F32))
                       [:, 0:129]]
                pTR5 = stt.enter_context(nc.psum_tensor("pTR5", [128, 128], BF16))
                cstg = c.sem("cstg")
                cstg.inc(c.g.dma_start(out=WB5[:, :], in_=wqkv5[:, :]), 16)
                cst.inc(c.sy.dma_start(out=BQ5[:, :], in_=bqkv5[:, :]), 16)
                cst.inc(c.sy.dma_start(out=CM5[:, :], in_=cmask5[:, :]), 16)
                dvi5 = c.sem("dvi5")
                dvi5.inc(c.v.memset(VT5[:, :], 1.0))
                peQ = c.sem("peQ"); qkv = c.sem("qkv"); peV = c.sem("peV")
                vtd = c.sem("vtd"); peS = c.sem("peS"); scP = c.sem("scP")
                msk = c.sem("msk"); peA = c.sem("peA"); onrm = c.sem("onrm")
                peOT = c.sem("peOT5"); ost5 = c.sem("ost5"); fst5 = c.sem("fst5")
                c.t.wait_ge(cst.h, cst.n)
                c.t.wait_ge(c.sem("cstg").h, c.sem("cstg").n)
                c.t.wait_ge(c.sem("lnS").h, c.sem("lnS").n)
                c.t.wait_ge(c.sem("lnV").h, c.sem("lnV").n)
                c.v.wait_ge(cst.h, cst.n)
                c.sy.wait_ge(c.sem("lnP").h, c.sem("lnP").n)
                NT = SEQ // TP
                TPORD = [k + 4 * u for k in range(NPASS) for u in range(4)]
                qkv5b0 = qkv.n
                jg = 0
                if True:
                    for hi in range(4):
                        for ki, tp in enumerate(TPORD):
                            ti = hi * NT + ki
                            xbuf = [vb["XHB"], XB5][ti % 2]
                            if ti >= 2:
                                c.sy.wait_ge(peQ.h, 3 * (ti - 1))
                            c.sy.wait_ge(ccs.h, AGXp[tp & 3])
                            xld.inc(c.sy.dma_start(
                                out=xbuf[:, :],
                                in_=xhallp[tp & 3].ap()
                                [(tp >> 2) * 128:(tp >> 2) * 128 + 128, :]), 16)
                            c.t.wait_ge(xld.h, xld.n)
                            for j in range(3):
                                pp = pS5[jg % NS5]
                                wslice = WB5[:, (hi * 3 + j) * DC * 128:(hi * 3 + j + 1) * DC * 128]
                                c.t.wait_ge(qkv.h, qkv5b0 + max(0, jg - NS5 + 1))
                                c.t.wait_ge(scP.h, scP.n)
                                jg += 1
                                for ci in range(DC):
                                    inst = c.t.matmul(pp[:, :TP],
                                                      wslice[:, ci * 128:(ci + 1) * 128],
                                                      xbuf[:, ci * TP:(ci + 1) * TP],
                                                      start=(ci == 0), stop=(ci == DC - 1))
                                peQ.inc(inst)
                                c.v.wait_ge(peQ.h, peQ.n)
                                c.v.wait_ge(peS.h, peS.n)
                                c.v.wait_ge(peA.h, peA.n)
                                bcol = BQ5[:, hi * 3 + j:hi * 3 + j + 1]
                                if j == 0:
                                    qkv.inc(c.v.tensor_scalar(Q5[:, tp * TP:(tp + 1) * TP],
                                                              pp[:, :TP], bcol, None, ALU.add))
                                elif j == 1:
                                    qkv.inc(c.v.tensor_scalar(Kt5[:, tp * TP:(tp + 1) * TP],
                                                              pp[:, :TP], bcol, None, ALU.add))
                                else:
                                    c.v.wait_ge(peV.h, peV.n)
                                    qkv.inc(c.v.tensor_scalar(VF5[:, :], pp[:, :TP],
                                                              bcol, None, ALU.add))
                                    for u in range(4):
                                        kt = tp * 4 + u
                                        c.t.wait_ge(qkv.h, qkv.n)
                                        c.t.wait_ge(vtd.h, vtd.n)
                                        peV.inc(c.t.transpose(pTR5[:, :],
                                                              VF5[:, u * 128:(u + 1) * 128],
                                                              IDBs[:, :]))
                                        c.v.wait_ge(peV.h, peV.n)
                                        vtd.inc(c.v.tensor_copy(
                                            VT5[:, kt * 129:kt * 129 + 128], pTR5[:, :]))
                        for qs in range(NT):
                            nkc = 4 * qs + 4
                            scP0, peA0, msk0 = scP.n, peA.n, msk.n
                            for kk in range(nkc + 1):
                                if kk < nkc:
                                    kc = kk
                                    pp = pS5[kc % NS5]
                                    pt = PT5[kc % len(PT5)]
                                    if kc == 0:
                                        c.t.wait_ge(qkv.h, qkv.n)
                                        c.t.wait_ge(vtd.h, vtd.n)
                                    if kc >= NS5:
                                        c.t.wait_ge(scP.h, scP0 + kc - NS5 + 1)
                                    else:
                                        c.t.wait_ge(scP.h, scP0)
                                    peS.inc(c.t.matmul(
                                        pp[:, :TP], Kt5[:, kc * 128:(kc + 1) * 128],
                                        Q5[:, qs * TP:(qs + 1) * TP],
                                        start=True, stop=True))
                                    c.s.wait_ge(peS.h, peS.n)
                                    if kc >= len(PT5):
                                        c.s.wait_ge(peA.h, peA0 + kc - len(PT5) + 1)
                                    else:
                                        c.s.wait_ge(peA.h, peA0)
                                    scP.inc(c.s.activation(pt[:, :], pp[:, :TP], AF.Exp,
                                                           scale=f32(1.0 / np.sqrt(128.0)),
                                                           bias=f32(-16.0)))
                                    if kc >= 4 * qs:
                                        m = kc - 4 * qs
                                        c.v.wait_ge(scP.h, scP.n)
                                        msk.inc(c.v.tensor_tensor(
                                            pt[:, :], pt[:, :],
                                            CM5[:, m * 512:(m + 1) * 512], ALU.mult))
                                if kk >= 1:
                                    kc = kk - 1
                                    pt = PT5[kc % len(PT5)]
                                    c.t.wait_ge(scP.h, scP0 + kc + 1)
                                    if kc >= 4 * qs:
                                        c.t.wait_ge(msk.h, msk0 + kc - 4 * qs + 1)
                                    if kc == 0:
                                        c.t.wait_ge(onrm.h, onrm.n)
                                    qc0 = max(0, kc - 4 * qs)
                                    for qc in range(qc0, 4):
                                        inst = c.t.matmul(pO5[qc][:, 0:129],
                                                          pt[:, qc * 128:(qc + 1) * 128],
                                                          VT5[:, kc * 129:(kc + 1) * 129],
                                                          start=(kc == 0),
                                                          stop=(kc == 4 * qs + qc))
                                    peA.inc(inst)
                            for qc in range(4):
                                c.v.wait_ge(peA.h, peA.n)
                                c.v.reciprocal(RC5[:, :], pO5[qc][:, 128:129])
                                c.v.wait_ge(peOT.h, peOT.n)
                                onrm.inc(c.v.tensor_scalar(OB5[:, :], pO5[qc][:, 0:128],
                                                           RC5[:, :], None, ALU.mult))
                                c.t.wait_ge(onrm.h, onrm.n)
                                c.t.wait_ge(ost5.h, ost5.n)
                                peOT.inc(c.t.transpose(pTR5[:, :], OB5[:, :], IDBs[:, :]))
                                c.v.wait_ge(peOT.h, peOT.n)
                                c.v.wait_ge(fst5.h, fst5.n)
                                ost5.inc(c.v.tensor_copy(OST5[:, qc * 128:(qc + 1) * 128],
                                                         pTR5[:, :]))
                            c.sy.wait_ge(ost5.h, ost5.n)
                            j5 = qs >> 2
                            for slot in (j5, j5 + 4):
                                r0 = slot * 512 + hi * 128
                                fst5.inc(c.sy.dma_start(
                                    out=o5own.ap()[r0:r0 + 128,
                                                   (qs & 3) * TP:((qs & 3) + 1) * TP],
                                    in_=OST5[:, :]), 16)
                c.g.wait_ge(fst5.h, fst5.n)
                with allow_ici_all_to_all():
                    ccs.inc(c.g.collective_compute(
                        "AllToAll", ALU.bypass, replica_groups=RG,
                        ins=[o5own.ap().opt()], outs=[o5all.ap().opt()]), 1)
                AGO = ccs.n
                for eng in (c.t, c.v, c.s, c.sy, c.g):
                    eng.wait_ge(ccs.h, AGO)
                    eng.wait_ge(ost5.h, ost5.n)
                    eng.wait_ge(c.sem("scP").h, c.sem("scP").n)
                    eng.wait_ge(c.sem("msk").h, c.sem("msk").n)
                    eng.wait_ge(c.sem("onrm").h, c.sem("onrm").n)
                    eng.wait_ge(c.sem("qkv").h, c.sem("qkv").n)

            with ExitStack() as stc:
                cvb = dict(vb)
                cvb["wbufs"] = [stc.enter_context(nc.sbuf_tensor("wb5c0", [128, DC * 128], BF16)),
                                stc.enter_context(nc.sbuf_tensor("wb5c1", [128, DC * 128], BF16)),
                                stc.enter_context(nc.sbuf_tensor("wb5c2", [128, DC * 128], BF16))]
                cvb["pAB"] = [stc.enter_context(nc.psum_tensor("pA5c", [128, TP], F32)),
                              stc.enter_context(nc.psum_tensor("pB5c", [128, TP], F32))]
                OBT = stc.enter_context(nc.sbuf_tensor("OBT", [128, DC * TP], BF16))
                OBT2 = stc.enter_context(nc.sbuf_tensor("OBT2", [128, NH * 1024], BF16))
                BO5 = stc.enter_context(nc.sbuf_tensor("BO5", [128, DC], F32))
                cst.inc(c.sy.dma_start(out=BO5[:, :], in_=bout5[:, :]), 16)
                c.v.wait_ge(cst.h, cst.n)
                yev = c.sem("evd")
                osel = c.sem("osel")
                if True:
                    for p in range(NPASS):
                        c.sy.wait_ge(yev.h, yev.n)
                        c.sy.wait_ge(xst.h, xst.n)
                        xld.inc(c.sy.dma_start(
                            out=XF[:, :].rearrange("pp (cc t) -> pp cc t", cc=DC),
                            in_=xdram.rearrange("pp (cc t) -> pp cc t", cc=DC)
                            [:, :, p * TP:(p + 1) * TP]), 16)
                        c.sy.wait_ge(osel.h, osel.n)
                        for cand in range(2):
                            xld.inc(c.sy.dma_start(
                                out=OBT2[:, :].rearrange(
                                    "pp (h cand t) -> pp h cand t", h=NH, cand=2)
                                [:, :, cand, :],
                                in_=bass.AP(o5all.ap().tensor,
                                            cand * 2048 * TOKS + p * TP,
                                            [[TOKS, 128], [128 * TOKS, 16],
                                             [1, TP]])), 16)
                        c.v.wait_ge(xld.h, xld.n)
                        c.v.wait_ge(c.sem("wfree").h, c.sem("wfree").n)
                        for h in range(NH):
                            c.v.tensor_scalar(
                                OBT[:, h * TP:(h + 1) * TP],
                                OBT2[:, h * 1024:h * 1024 + TP],
                                BOH[:, 0:1], None, ALU.mult)
                            inst = c.v.scalar_tensor_tensor(
                                OBT[:, h * TP:(h + 1) * TP],
                                OBT2[:, h * 1024 + TP:h * 1024 + 2 * TP],
                                BOH[:, 1:2], OBT[:, h * TP:(h + 1) * TP],
                                ALU.mult, ALU.add)
                        osel.inc(inst)
                        xw = (osel, osel.n)

                        def evac5(oc, pp, wfree, wfree0, evd):
                            c.v.wait_ge(wfree.h, wfree0 + oc + 1)
                            inst = c.v.scalar_tensor_tensor(
                                XF[:, oc * TP:(oc + 1) * TP], pp[:, :TP],
                                BO5[:, oc:oc + 1], XF[:, oc * TP:(oc + 1) * TP],
                                ALU.add, ALU.add)
                            evd.inc(inst)

                        emit_wstream_mm(c, cvb, wout5, DC, DC,
                                        lambda ci: OBT[:, ci * TP:(ci + 1) * TP],
                                        evac5, rhs_wait=xw)
                        c.sy.wait_ge(yev.h, yev.n)
                        xst.inc(c.sy.dma_start(
                            out=xdram.rearrange("pp (cc t) -> pp cc t", cc=DC)
                            [:, :, p * TP:(p + 1) * TP],
                            in_=XF[:, :].rearrange("pp (cc t) -> pp cc t", cc=DC)), 16)
                for eng in (c.t, c.v, c.s, c.sy, c.g):
                    eng.wait_ge(c.sem("evd").h, c.sem("evd").n)
                    eng.wait_ge(c.sem("wfree").h, c.sem("wfree").n)
                    eng.wait_ge(xst.h, xst.n)

            l1ffn_phase("b")

            with ExitStack() as sto:
                XO = sto.enter_context(nc.sbuf_tensor("XO", [128, NSUB * D], F32))
                pT6 = [sto.enter_context(nc.psum_tensor("pT6a", [128, 128], F32)),
                       sto.enter_context(nc.psum_tensor("pT6b", [128, 128], F32))]
                tpe6 = c.sem("tpe6"); tdv6 = c.sem("tdv6"); fin6 = c.sem("fin6")
                for p in range(NPASS):
                    c.sy.wait_ge(xst.h, xst.n)
                    xld.inc(c.sy.dma_start(
                        out=XF[:, :].rearrange("pp (cc t) -> pp cc t", cc=DC),
                        in_=xdram.rearrange("pp (cc t) -> pp cc t", cc=DC)
                        [:, :, p * TP:(p + 1) * TP]), 16)
                    c.t.wait_ge(xld.h, xld.n)
                    c.t.wait_ge(fin6.h, fin6.n)
                    i6 = 0
                    tpe60, tdv60 = tpe6.n, tdv6.n
                    for sub in range(NSUB):
                        for ci in range(DC):
                            if i6 >= 2:
                                c.t.wait_ge(tdv6.h, tdv60 + i6 - 1)
                            tpe6.inc(c.t.transpose(
                                pT6[i6 % 2][:, :],
                                XF[:, ci * TP + sub * 128:ci * TP + (sub + 1) * 128],
                                IDFs[:, :]))
                            c.v.wait_ge(tpe6.h, tpe60 + i6 + 1)
                            tdv6.inc(c.v.tensor_copy(
                                XO[:, sub * D + ci * 128:sub * D + (ci + 1) * 128],
                                pT6[i6 % 2][:, :]))
                            i6 += 1
                    c.sy.wait_ge(tdv6.h, tdv6.n)
                    fin6.inc(c.sy.dma_start(
                        out=xout.rearrange("(s pp) d -> pp s d", pp=128)
                        [:, p * NSUB:(p + 1) * NSUB, :],
                        in_=XO[:, :].rearrange("pp (s d) -> pp s d", s=NSUB)), 16)
                c.g.wait_ge(fin6.h, fin6.n)
                c.sy.wait_ge(fin6.h, fin6.n)

    return nc


def fused_inmaps(ins):
    x0 = np.ascontiguousarray(ins["L0_attn_x"].reshape(-1, D))
    eye_f = np.eye(128, dtype=np.float32)
    eye_b = np.eye(128).astype(BF)
    w1t = wtile(ins["L0_ffn_W1"]); w2t = wtile(ins["L0_ffn_W2"]); w3t = wtile(ins["L0_ffn_W3"])
    vcat_ffn = np.concatenate([colchunks(ins["L0_ffn_b1"]), colchunks(ins["L0_ffn_b2"]),
                               colchunks(ins["L0_ffn_b3"])], 1).astype(np.float32)
    WC = np.concatenate([ins["L0_attn_Wq"], ins["L0_attn_Wk"], ins["L0_attn_Wv"],
                         ins["L0_attn_Wb"], ins["L0_attn_Wd"]], 1)
    wcat = np.ascontiguousarray(np.transpose(WC.reshape(DC, 128, 384), (1, 0, 2)).reshape(128, -1))
    brow = np.concatenate([np.zeros(256, np.float32), ins["L0_attn_bb"],
                           ins["L0_attn_bd"]]).reshape(1, 384).astype(np.float32)
    tt = np.arange(128)
    mcum = ((tt[:, None] // 64 == tt[None, :] // 64) &
            (tt[:, None] % 64 <= tt[None, :] % 64)).astype(np.float32)
    chi = np.stack([(tt // 64 == 0), (tt // 64 == 1)], 1).astype(np.float32)
    amask = ((tt[:, None] // 64 == tt[None, :] // 64) &
             (tt[:, None] <= tt[None, :])).astype(np.uint8)
    H = _hadamard128()
    h128u = H / np.float32(np.sqrt(2048.0))
    h128d = H * np.float32(0.5 / np.sqrt(8192.0))
    suug_ = colchunks(ins["L1_ffn_SU_ug"])
    sudp_ = colchunks(ins["L1_ffn_SU_dp"])
    jj = np.arange(TP)
    cmask = np.concatenate(
        [((128 * m + tt[:, None]) <= jj[None, :]).astype(BF) for m in range(4)], 1)
    wout_t = wtile(ins["L1_attn_w_out"])
    bo = colchunks(ins["L1_attn_b_out"])
    ims = []
    for cix in range(NCORES):
        b = cix >> 2
        blk = cix & 3
        wblocks = []
        bcols = np.zeros((128, 12), np.float32)
        for l in range(4):
            h = blk * 4 + l
            for j in range(3):
                rows = ins["L1_attn_w_qkv"][j * D + h * 128:j * D + (h + 1) * 128, :]
                wblocks.append(wtile(rows))
                bcols[:, l * 3 + j] = ins["L1_attn_b_qkv"][j * D + h * 128:j * D + (h + 1) * 128]
        qoh_ = np.zeros((128, 4), np.float32)
        qoh_[:, blk] = 1.0
        boh_ = np.zeros((128, 2), np.float32)
        boh_[:, b] = 1.0
        ims.append(dict(
            xin=np.ascontiguousarray(x0[cix * TOKS:(cix + 1) * TOKS]),
            w1=w1t, w2=w2t, w3=w3t, vcat=vcat_ffn, wcat=wcat, brow=brow,
            mcum=mcum, chi=chi, amask=amask, identb=eye_b, identf=eye_f,
            wo0=ins["L0_attn_Wo"],
            wup=np.ascontiguousarray(ins["L1_ffn_W_up"][cix * 1024:(cix + 1) * 1024, :].T),
            wga=np.ascontiguousarray(ins["L1_ffn_W_gate"][cix * 1024:(cix + 1) * 1024, :].T),
            wdo=np.ascontiguousarray(ins["L1_ffn_W_down"][cix * 256:(cix + 1) * 256, :].T),
            h128u=h128u, h128d=h128d, suug=suug_, sudp=sudp_,
            wsup=np.ascontiguousarray(ins["L1_ffn_Wscale_ug"][cix * 1024:(cix + 1) * 1024].reshape(1, -1)),
            wsga=np.ascontiguousarray(ins["L1_ffn_Wscale_ug"][8192 + cix * 1024:8192 + (cix + 1) * 1024].reshape(1, -1)),
            wsdo=np.ascontiguousarray(ins["L1_ffn_Wscale_dp"][cix * 256:(cix + 1) * 256].reshape(1, -1)),
            wqkv5=np.concatenate(wblocks, 1), bqkv5=bcols, cmask5=cmask,
            wout5=wout_t, bout5=bo, qoh=qoh_, boh=boh_))
    return ims


def kernel(**inputs):
    ins = {k: np.asarray(v, dtype=np.float32) for k, v in inputs.items()}
    nc = build_fused()
    res = run_bass_kernel_spmd(nc, fused_inmaps(ins), core_ids=list(range(NCORES)))
    out = np.concatenate([np.asarray(res.results[i]["xout"]) for i in range(NCORES)], 0)
    return np.ascontiguousarray(out.reshape(2, SEQ, D)).astype(np.float32)


def _hadamard128():
    i = np.arange(128)
    pc = np.array([bin(a & b).count("1") for a in i for b in i]).reshape(128, 128)
    return ((-1.0) ** pc).astype(np.float32)


def kernel_unfused(**inputs):
    ins = {k: np.asarray(v, dtype=np.float32) for k, v in inputs.items()}
    x0 = np.ascontiguousarray(ins["L0_attn_x"].reshape(-1, D))
    eye_f = np.eye(128, dtype=np.float32)
    eye_b = np.eye(128).astype(BF)

    w1t = wtile(ins["L0_ffn_W1"])
    w2t = wtile(ins["L0_ffn_W2"])
    w3t = wtile(ins["L0_ffn_W3"])
    vcat_ffn = np.concatenate([colchunks(ins["L0_ffn_b1"]),
                               colchunks(ins["L0_ffn_b2"]),
                               colchunks(ins["L0_ffn_b3"])], 1).astype(np.float32)
    nc1 = build_ffn0_launch(True)
    im = [dict(xin=np.ascontiguousarray(x0[i * TOKS:(i + 1) * TOKS]), w1=w1t,
               w2=w2t, w3=w3t, vcat=vcat_ffn, identf=eye_f) for i in range(NCORES)]
    r1 = run_bass_kernel_spmd(nc1, im, core_ids=list(range(NCORES)))
    x1f = [np.asarray(r1.results[i]["xout"]) for i in range(NCORES)]

    WC = np.concatenate([ins["L0_attn_Wq"], ins["L0_attn_Wk"], ins["L0_attn_Wv"],
                         ins["L0_attn_Wb"], ins["L0_attn_Wd"]], 1)
    wcat = np.ascontiguousarray(
        np.transpose(WC.reshape(DC, 128, 384), (1, 0, 2)).reshape(128, -1))
    brow = np.concatenate([np.zeros(256, np.float32), ins["L0_attn_bb"],
                           ins["L0_attn_bd"]]).reshape(1, 384).astype(np.float32)
    tt = np.arange(128)
    mcum = ((tt[:, None] // 64 == tt[None, :] // 64) &
            (tt[:, None] % 64 <= tt[None, :] % 64)).astype(np.float32)
    chi = np.stack([(tt // 64 == 0), (tt // 64 == 1)], 1).astype(np.float32)
    amask = ((tt[:, None] // 64 == tt[None, :] // 64) &
             (tt[:, None] <= tt[None, :])).astype(np.uint8)
    nc2a = build_gla_a_launch()
    im = [dict(xin=x1f[i], wcat=wcat, brow=brow, mcum=mcum, chi=chi,
               amask=amask, identb=eye_b, identf=eye_f) for i in range(NCORES)]
    r2a = run_bass_kernel_spmd(nc2a, im, core_ids=list(range(NCORES)))
    nc2b = build_gla_b_launch()
    im = []
    for cix in range(NCORES):
        a = cix & 3
        b = cix >> 2
        u_in = np.zeros((128 * 128, 128), np.float32)
        pc_in = np.ones((128, 128), np.float32)
        nd = 32 * (a + 1)
        pad = 128 - nd
        ucat = np.concatenate(
            [np.asarray(r2a.results[4 * b + j]["uown"]) for j in range(a + 1)], 0)
        u_in[pad * 128:] = np.concatenate(
            [ucat.reshape(nd, 64, 128)] * 2, 1).reshape(nd * 128, 128)
        pcc = np.concatenate(
            [np.asarray(r2a.results[4 * b + j]["pcown"]) for j in range(a + 1)], 1)
        pc_in[0:64, pad:] = pcc
        pc_in[64:128, pad:] = pcc
        im.append(dict(xin=x1f[cix], uin=u_in, pcin=pc_in,
                       qtin=np.asarray(r2a.results[cix]["qtown"]),
                       oiin=np.asarray(r2a.results[cix]["oiown"]),
                       wo=ins["L0_attn_Wo"]))
    r2b = run_bass_kernel_spmd(nc2b, im, core_ids=list(range(NCORES)))
    x2f = [np.asarray(r2b.results[i]["xout"]) for i in range(NCORES)]

    nc3 = build_ffn0_launch(False)
    im = [dict(xin=x2f[i], w1=w1t, w2=w2t, w3=w3t, vcat=vcat_ffn,
               identf=eye_f) for i in range(NCORES)]
    r3 = run_bass_kernel_spmd(nc3, im, core_ids=list(range(NCORES)))
    x3f = [np.asarray(r3.results[i]["xout"]) for i in range(NCORES)]

    H = _hadamard128()
    h128u = H / np.float32(np.sqrt(2048.0))
    h128d = H * np.float32(0.5 / np.sqrt(8192.0))
    suug = colchunks(ins["L1_ffn_SU_ug"])
    sudp = colchunks(ins["L1_ffn_SU_dp"])
    ncp = build_prep_launch()
    im = []
    for cix in range(NCORES):
        im.append(dict(
            wup=np.ascontiguousarray(ins["L1_ffn_W_up"][cix * 1024:(cix + 1) * 1024, :].T),
            wga=np.ascontiguousarray(ins["L1_ffn_W_gate"][cix * 1024:(cix + 1) * 1024, :].T),
            wdo=np.ascontiguousarray(ins["L1_ffn_W_down"][cix * 256:(cix + 1) * 256, :].T),
            h128u=h128u, h128d=h128d, suug=suug, sudp=sudp,
            wsup=np.ascontiguousarray(ins["L1_ffn_Wscale_ug"][cix * 1024:(cix + 1) * 1024].reshape(1, -1)),
            wsga=np.ascontiguousarray(ins["L1_ffn_Wscale_ug"][8192 + cix * 1024:8192 + (cix + 1) * 1024].reshape(1, -1)),
            wsdo=np.ascontiguousarray(ins["L1_ffn_Wscale_dp"][cix * 256:(cix + 1) * 256].reshape(1, -1))))
    rp = run_bass_kernel_spmd(ncp, im, core_ids=list(range(NCORES)))
    wub = np.concatenate([np.asarray(rp.results[i]["oup"]) for i in range(NCORES)], 1)
    wgb = np.concatenate([np.asarray(rp.results[i]["oga"]) for i in range(NCORES)], 1)
    wdb = np.concatenate([np.asarray(rp.results[i]["odo"]) for i in range(NCORES)], 1)

    nc4 = build_l1ffn_launch()
    im = [dict(xin=x3f[i], wub=wub, wgb=wgb, wdb=wdb) for i in range(NCORES)]
    r4 = run_bass_kernel_spmd(nc4, im, core_ids=list(range(NCORES)))
    x4f = [np.asarray(r4.results[i]["xout"]) for i in range(NCORES)]

    nc5a = build_ln5_launch()
    im = [dict(xin=x4f[i]) for i in range(NCORES)]
    r5a = run_bass_kernel_spmd(nc5a, im, core_ids=list(range(NCORES)))
    xh_b = []
    for b in range(2):
        blocks = [np.asarray(r5a.results[4 * b + j]["xh"]).reshape(128, DC, TOKS)
                  for j in range(4)]
        xh_b.append(np.ascontiguousarray(
            np.stack(blocks, 2).reshape(128, DC * SEQ)))
    nc5b = build_attn_launch()
    jj = np.arange(TP)
    cmask = np.concatenate(
        [((128 * m + tt[:, None]) <= jj[None, :]).astype(BF) for m in range(4)], 1)
    im = []
    for cix in range(NCORES):
        b = cix >> 2
        hg = cix & 3
        wblocks = []
        bcols = np.zeros((128, 12), np.float32)
        for l in range(4):
            h = hg * 4 + l
            for j in range(3):
                rows = ins["L1_attn_w_qkv"][j * D + h * 128:j * D + (h + 1) * 128, :]
                wblocks.append(wtile(rows))
                bcols[:, l * 3 + j] = ins["L1_attn_b_qkv"][j * D + h * 128:j * D + (h + 1) * 128]
        im.append(dict(xh=xh_b[b], wqkv=np.concatenate(wblocks, 1), bqkv=bcols,
                       cmask=cmask, identb=eye_b))
    r5b = run_bass_kernel_spmd(nc5b, im, core_ids=list(range(NCORES)))
    nc5c = build_outproj_launch()
    wout_t = wtile(ins["L1_attn_w_out"])
    bo = colchunks(ins["L1_attn_b_out"])
    im = []
    for cix in range(NCORES):
        b = cix >> 2
        blk = cix & 3
        ob = np.zeros((128, DC, TOKS), dtype=BF)
        for h in range(NH):
            srcc = 4 * b + h // 4
            ob[:, h, :] = np.asarray(r5b.results[srcc]["oown"])[
                (h % 4) * 128:(h % 4 + 1) * 128, blk * TOKS:(blk + 1) * TOKS]
        im.append(dict(xin=x4f[cix], oin=np.ascontiguousarray(ob.reshape(128, -1)),
                       wout=wout_t, bout=bo))
    r5c = run_bass_kernel_spmd(nc5c, im, core_ids=list(range(NCORES)))
    x5f = [np.asarray(r5c.results[i]["xout"]) for i in range(NCORES)]

    im = [dict(xin=x5f[i], wub=wub, wgb=wgb, wdb=wdb) for i in range(NCORES)]
    r6 = run_bass_kernel_spmd(nc4, im, core_ids=list(range(NCORES)))

    out = np.concatenate([x_tok_of_fmaj(np.asarray(r6.results[i]["xout"]))
                          for i in range(NCORES)], 0)
    return np.ascontiguousarray(out.reshape(2, SEQ, D)).astype(np.float32)


# version 12
# speedup vs baseline: 1.0517x; 1.0025x over previous
import numpy as np
import ml_dtypes
from contextlib import ExitStack
import concourse.bass as bass
import concourse.mybir as mybir
from concourse.bass_utils import run_bass_kernel_spmd
from concourse.aot_env import allow_ici_all_to_all

F32 = mybir.dt.float32
BF16 = mybir.dt.bfloat16
U8 = mybir.dt.uint8
AF = mybir.ActivationFunctionType
ALU = mybir.AluOpType
AX = mybir.AxisListType
f32 = np.float32
BF = ml_dtypes.bfloat16

NCORES = 8
D = 2048
DC = 16
TOKS = 2048
TP = 512
NPASS = TOKS // TP
NSUB = TP // 128
H0 = 4096
H0C = H0 // 128
INTER = 8192
NH = 16
HD = 128
SEQ = 8192


def wtile(W):
    dout, din = W.shape
    t = W.reshape(dout // 128, 128, din // 128, 128)
    return np.ascontiguousarray(np.transpose(t, (3, 0, 2, 1)).reshape(128, -1))


def colchunks(v):
    return np.ascontiguousarray(np.asarray(v).reshape(-1, 128).T)


def x_fmaj(x_tok):
    t = x_tok.reshape(-1, DC, 128)
    return np.ascontiguousarray(np.transpose(t, (2, 1, 0)).reshape(128, -1))


def x_tok_of_fmaj(xf):
    t = xf.reshape(128, DC, -1)
    return np.ascontiguousarray(np.transpose(t, (2, 1, 0)).reshape(-1, D))


class Sem:
    def __init__(self, st, nc, name):
        self.h = st.enter_context(nc.semaphore(name))
        self.n = 0

    def inc(self, inst, k=1):
        inst.then_inc(self.h, k)
        self.n += k
        return inst


class Ctx:
    def __init__(self, nc, st):
        self.nc = nc
        self.st = st
        self.t = nc.tensor
        self.v = nc.vector
        self.s = nc.scalar
        self.g = nc.gpsimd
        self.sy = nc.sync
        self._sems = {}

    def sem(self, name):
        if name not in self._sems:
            self._sems[name] = Sem(self.st, self.nc, name)
        return self._sems[name]

    def sbuf(self, name, shape, dtype):
        return self.st.enter_context(self.nc.sbuf_tensor(name, shape, dtype))

    def psum(self, name, shape, dtype=F32):
        return self.st.enter_context(self.nc.psum_tensor(name, shape, dtype))


def emit_ln(c, vb, XF, XHB, extra_bcast_waits=()):
    XB = vb["XB"]; XQ = vb["XQ"]
    pS1 = vb["pS1"]; pS2 = vb["pS2"]
    onesbb = vb["onesbb"]; LROW = vb["LROW"]
    lnS = c.sem("lnS"); lnV = c.sem("lnV"); lnP = c.sem("lnP")

    c.s.wait_ge(lnV.h, lnV.n)
    for ci in range(DC):
        c.s.copy(XB[:, ci * TP:(ci + 1) * TP], XF[:, ci * TP:(ci + 1) * TP])
        inst = c.s.activation(XQ[:, ci * TP:(ci + 1) * TP],
                              XF[:, ci * TP:(ci + 1) * TP], AF.Square)
    lnS.inc(inst)
    c.t.wait_ge(lnS.h, lnS.n)
    for s_ in extra_bcast_waits:
        c.t.wait_ge(s_.h, s_.n)
    for ci in range(DC):
        c.t.matmul(pS1[:, :TP], onesbb[:, :], XB[:, ci * TP:(ci + 1) * TP],
                   start=(ci == 0), stop=(ci == DC - 1))
    for ci in range(DC):
        inst = c.t.matmul(pS2[:, :TP], onesbb[:, :], XQ[:, ci * TP:(ci + 1) * TP],
                          start=(ci == 0), stop=(ci == DC - 1))
    lnP.inc(inst)
    mean = LROW[:, 0 * TP:1 * TP]
    var = LROW[:, 1 * TP:2 * TP]
    rstd = LROW[:, 2 * TP:3 * TP]
    mr = LROW[:, 3 * TP:4 * TP]
    c.s.wait_ge(lnP.h, lnP.n)
    c.s.activation(mean, pS1[:, :TP], AF.Copy, scale=f32(1.0 / D))
    lnS.inc(c.s.activation(var, mean, AF.Square))
    c.v.wait_ge(lnS.h, lnS.n)
    lnV.inc(c.v.scalar_tensor_tensor(var, pS2[:, :TP], 1.0 / D, var,
                                     ALU.mult, ALU.subtract))
    c.s.wait_ge(lnV.h, lnV.n)
    c.s.activation(rstd, var, AF.Ln, bias=f32(1e-5))
    lnS.inc(c.s.activation(rstd, rstd, AF.Exp, scale=f32(-0.5)))
    c.v.wait_ge(lnS.h, lnS.n)
    c.v.wait_ge(c.sem("wfree").h, c.sem("wfree").n)
    lnV.inc(c.v.tensor_tensor(mr, mean, rstd, ALU.mult))
    for ci in range(DC):
        c.v.tensor_tensor(XB[:, ci * TP:(ci + 1) * TP],
                          XF[:, ci * TP:(ci + 1) * TP], rstd, ALU.mult)
        inst = c.v.tensor_tensor(XHB[:, ci * TP:(ci + 1) * TP],
                                 XB[:, ci * TP:(ci + 1) * TP], mr,
                                 ALU.subtract)
    lnV.inc(inst)
    return lnV


def emit_ln_stats(c, vb, xdram, p, parity, sbuf_xf=None, pre=None):
    XB2 = vb["XB2"]; XQr = vb["XQr"]
    pS1 = vb["pS1"]; pS2 = vb["pS2"]; onesbb = vb["onesbb"]; LROW = vb["LROW"]
    lnq = c.sem("lnq"); lnP = c.sem("lnP")
    lnS = c.sem("lnS"); lnV = c.sem("lnV")
    lnq0, lnP0 = lnq.n, lnP.n
    if sbuf_xf is None:
        lnc = c.sem("lnc")
        c.g.wait_ge(lnP.h, lnP.n)
        lnc.inc(c.g.dma_start(
            out=XB2[:, :].rearrange("pp (cc t) -> pp cc t", cc=DC),
            in_=xdram.rearrange("pp (cc t) -> pp cc t", cc=DC)
            [:, :, p * TP:(p + 1) * TP]), 16)
        rdy = (lnc, lnc.n)
    else:
        lncs = c.sem("lncs")
        c.s.wait_ge(lnP.h, lnP.n)
        if pre is not None:
            c.s.wait_ge(pre[0].h, pre[1])
        for ci in range(DC):
            inst = c.s.copy(XB2[:, ci * TP:(ci + 1) * TP],
                            sbuf_xf[:, ci * TP:(ci + 1) * TP])
        lncs.inc(inst)
        rdy = (lncs, lncs.n)
    for ci in range(DC):
        b = ci % 2
        if ci == 0:
            c.v.wait_ge(rdy[0].h, rdy[1])
            c.v.wait_ge(lnP.h, lnP0)
        if ci >= 2:
            c.v.wait_ge(lnP.h, lnP0 + 2 * (ci - 1))
        lnq.inc(c.v.tensor_tensor(XQr[:, b * TP:(b + 1) * TP],
                                  XB2[:, ci * TP:(ci + 1) * TP],
                                  XB2[:, ci * TP:(ci + 1) * TP], ALU.mult))
        c.t.wait_ge(lnq.h, lnq0 + ci + 1)
        if ci == 0:
            c.t.wait_ge(lnS.h, lnS.n)
        c.t.matmul(pS1[:, :TP], onesbb[:, :], XB2[:, ci * TP:(ci + 1) * TP],
                   start=(ci == 0), stop=(ci == DC - 1))
        inst = c.t.matmul(pS2[:, :TP], onesbb[:, :], XQr[:, b * TP:(b + 1) * TP],
                          start=(ci == 0), stop=(ci == DC - 1))
        lnP.inc(inst, 2)
    rstd = LROW[:, (2 * parity) * TP:(2 * parity + 1) * TP]
    mr = LROW[:, (2 * parity + 1) * TP:(2 * parity + 2) * TP]
    c.s.wait_ge(lnP.h, lnP.n)
    c.s.wait_ge(lnV.h, lnV.n)
    c.s.activation(mr, pS1[:, :TP], AF.Copy, scale=f32(1.0 / D))
    lnS.inc(c.s.activation(rstd, mr, AF.Square))
    c.v.wait_ge(lnS.h, lnS.n)
    lnV.inc(c.v.scalar_tensor_tensor(rstd, pS2[:, :TP], 1.0 / D, rstd,
                                     ALU.mult, ALU.subtract))
    c.s.wait_ge(lnV.h, lnV.n)
    c.s.activation(rstd, rstd, AF.Ln, bias=f32(1e-5))
    lnS.inc(c.s.activation(rstd, rstd, AF.Exp, scale=f32(-0.5)))
    c.v.wait_ge(lnS.h, lnS.n)
    lnV.inc(c.v.tensor_tensor(mr, mr, rstd, ALU.mult))
    return lnV


def emit_ln_norm(c, vb, xdram, p, parity, qe=None, xhb_free=None,
                 sbuf_xf=None, pre=None):
    XFR = vb["XFR"]; XFS = vb["XFS"]; XHB = vb["XHB"]; LROW = vb["LROW"]
    lnV = c.sem("lnV")
    rstd = LROW[:, (2 * parity) * TP:(2 * parity + 1) * TP]
    mr = LROW[:, (2 * parity + 1) * TP:(2 * parity + 2) * TP]
    lnV0 = lnV.n
    if sbuf_xf is None:
        lnf = c.sem("lnf" if qe is c.sy else
                    ("lnfs" if qe is c.s else "lnfg"))
        lnf0 = lnf.n
    for ci in range(DC):
        b = ci % 2
        src = XFR[:, b * TP:(b + 1) * TP]
        if sbuf_xf is None:
            if ci < 2:
                qe.wait_ge(lnV.h, lnV0)
            else:
                qe.wait_ge(lnV.h, lnV0 + 2 * (ci - 1))
            if ci > 0:
                qe.wait_ge(lnf.h, lnf0 + 16 * ci)
            lnf.inc(qe.dma_start(
                out=src,
                in_=xdram[:, ci * TOKS + p * TP:ci * TOKS + (p + 1) * TP]), 16)
            c.v.wait_ge(lnf.h, lnf0 + 16 * (ci + 1))
        if ci == 0:
            if xhb_free is not None:
                c.v.wait_ge(xhb_free[0].h, xhb_free[1])
            if pre is not None:
                c.v.wait_ge(pre[0].h, pre[1])
        if sbuf_xf is None:
            c.v.tensor_tensor(XFS[:, :], src, rstd, ALU.mult)
        else:
            c.v.tensor_tensor(XFS[:, :], sbuf_xf[:, ci * TP:(ci + 1) * TP],
                              rstd, ALU.mult)
        inst = c.v.tensor_tensor(XHB[:, ci * TP:(ci + 1) * TP], XFS[:, :], mr,
                                 ALU.subtract)
        lnV.inc(inst, 2)
    return lnV


def emit_wstream_mm(c, vb, w_dram, n_oc, cin, RHS, evac_fn, rhs_wait=None):
    wbufs = vb["wbufs"]; pAB = vb["pAB"]
    nb = len(wbufs)
    wld = c.sem("wld"); wfree = c.sem("wfree"); evd = c.sem("evd")
    if not hasattr(c, "wlast"):
        c.wlast = {}
    span = cin * 128
    wld0, wfree0, evd0 = wld.n, wfree.n, evd.n
    for oc in range(n_oc):
        buf = wbufs[oc % nb]
        pp = pAB[oc % 2]
        key = id(buf)
        if key in c.wlast:
            c.g.wait_ge(wfree.h, c.wlast[key])
        wld.inc(c.g.dma_start(out=buf[:, :span],
                              in_=w_dram[:, oc * span:(oc + 1) * span]), 16)
        c.t.wait_ge(wld.h, wld0 + 16 * (oc + 1))
        if oc < 2:
            c.t.wait_ge(evd.h, evd0)
            if rhs_wait is not None and oc == 0:
                c.t.wait_ge(rhs_wait[0].h, rhs_wait[1])
        else:
            c.t.wait_ge(evd.h, evd0 + oc - 1)
        for ci in range(cin):
            inst = c.t.matmul(pp[:, :TP], buf[:, ci * 128:(ci + 1) * 128],
                              RHS(ci), start=(ci == 0), stop=(ci == cin - 1))
        wfree.inc(inst)
        c.wlast[key] = wfree.n
        evac_fn(oc, pp, wfree, wfree0, evd)
    return wld, wfree, evd


def emit_ffn0_pass(c, vb, XF, XHB, lnV, w1, w2, w3, hook=None, xf_wait=None):
    H1B = vb["H1B"]; H2B = vb["H2B"]; vcat = vb["vcat"]; hb3 = vb["hb3"]
    TY = vb["TY"]
    B1C, B2C = vb["B1C"], vb["B2C"]
    evd = c.sem("evd")

    def evac_relu(OUT, bcol0):
        def f(oc, pp, wfree, wfree0, evd):
            c.s.wait_ge(wfree.h, wfree0 + oc + 1)
            inst = c.s.activation(OUT[:, oc * TP:(oc + 1) * TP], pp[:, :TP],
                                  AF.Relu, bias=vcat[:, bcol0 + oc:bcol0 + oc + 1])
            evd.inc(inst)
        return f

    def evac_res(oc, pp, wfree, wfree0, evd):
        c.v.wait_ge(wfree.h, wfree0 + oc + 1)
        if oc == 0 and xf_wait is not None:
            c.v.wait_ge(xf_wait[0].h, xf_wait[1])
        ty = TY[:, 0:TP]
        c.v.tensor_scalar(ty, pp[:, :TP], 0.5, hb3[:, oc:oc + 1],
                          ALU.mult, ALU.add)
        inst = c.v.tensor_tensor(XF[:, oc * TP:(oc + 1) * TP], ty,
                                 XF[:, oc * TP:(oc + 1) * TP], ALU.add)
        evd.inc(inst)

    emit_wstream_mm(c, vb, w1, H0C, DC, lambda ci: XHB[:, ci * TP:(ci + 1) * TP],
                    evac_relu(H1B, B1C), rhs_wait=(lnV, lnV.n))
    if hook is not None:
        hook()
    emit_wstream_mm(c, vb, w2, H0C, H0C, lambda ci: H1B[:, ci * TP:(ci + 1) * TP],
                    evac_relu(H2B, B2C))
    emit_wstream_mm(c, vb, w3, DC, H0C, lambda ci: H2B[:, ci * TP:(ci + 1) * TP],
                    evac_res)
    return evd


def mk_ffn0_vb(c):
    vb = {}
    vb["XB"] = c.sbuf("XB", [128, DC * TP], BF16)
    vb["XQ"] = c.sbuf("XQ", [128, DC * TP], BF16)
    vb["XHB"] = c.sbuf("XHB", [128, DC * TP], BF16)
    vb["H1B"] = c.sbuf("H1B", [128, H0C * TP], BF16)
    vb["H2B"] = c.sbuf("H2B", [128, H0C * TP], BF16)
    vb["wbufs"] = [c.sbuf("wb0", [128, H0C * 128], BF16),
                   c.sbuf("wb1", [128, H0C * 128], BF16),
                   c.sbuf("wb2", [128, H0C * 128], BF16)]
    vb["TY"] = c.sbuf("TY", [128, 2 * TP], F32)
    vb["SROW"] = c.sbuf("SROW", [1, 5 * TP], F32)
    vb["onesb"] = c.sbuf("onesb", [128, 1], BF16)
    vb["ones1f"] = c.sbuf("ones1f", [1, 128], F32)
    vb["onesbb"] = c.sbuf("onesbb", [128, 128], BF16)
    vb["LROW"] = c.sbuf("LROW", [128, 4 * TP], F32)
    vb["pS1"] = c.psum("pS1", [128, TP])
    vb["pS2"] = c.psum("pS2", [128, TP])
    vb["pR"] = c.psum("pR", [128, TP])
    vb["pM"] = c.psum("pM", [128, TP])
    vb["pAB"] = [c.psum("pA", [128, TP]), c.psum("pB", [128, TP])]
    return vb


def build_ffn0_launch(first):
    nc = bass.Bass("TRN2", target_bir_lowering=False)
    if first:
        xin = nc.dram_tensor("xin", [TOKS, D], F32, kind="ExternalInput").ap()
    else:
        xin = nc.dram_tensor("xin", [128, DC * TOKS], F32, kind="ExternalInput").ap()
    w1 = nc.dram_tensor("w1", [128, H0C * DC * 128], F32, kind="ExternalInput").ap()
    w2 = nc.dram_tensor("w2", [128, H0C * H0C * 128], F32, kind="ExternalInput").ap()
    w3 = nc.dram_tensor("w3", [128, DC * H0C * 128], F32, kind="ExternalInput").ap()
    vcat_in = nc.dram_tensor("vcat", [128, 80], F32, kind="ExternalInput").ap()
    identf = nc.dram_tensor("identf", [128, 128], F32, kind="ExternalInput").ap()
    xout = nc.dram_tensor("xout", [128, DC * TOKS], F32, kind="ExternalOutput").ap()

    with ExitStack() as st:
        c = Ctx(nc, st)
        vb = mk_ffn0_vb(c)
        vb["vcat"] = c.sbuf("vcat_sb", [128, 80], F32)
        vb["hb3"] = c.sbuf("hb3", [128, DC], F32)
        vb["B1C"], vb["B2C"], B3C = 0, 32, 64
        XF = c.sbuf("XF", [128, DC * TP], F32)
        if first:
            XT = c.sbuf("XT", [128, NSUB * D], F32)
            identsb = c.sbuf("identsb", [128, 128], F32)
            pT = [c.psum("pT0", [128, 128]), c.psum("pT1", [128, 128])]
        block = st.enter_context(nc.Block())

        @block.gpsimd
        def _(_g):
            cst = c.sem("cst")
            cst.inc(c.sy.dma_start(out=vb["vcat"][:, :], in_=vcat_in[:, :]), 16)
            if first:
                cst.inc(c.sy.dma_start(out=identsb[:, :], in_=identf[:, :]), 16)
            c.v.wait_ge(cst.h, cst.n)
            c.s.wait_ge(cst.h, cst.n)
            c.v.memset(vb["onesb"][:, :], 1.0)
            c.v.memset(vb["onesbb"][:, :], 1.0)
            c.v.memset(vb["ones1f"][:, :], 1.0)
            ini = c.v.tensor_scalar(vb["hb3"][:, :], vb["vcat"][:, B3C:B3C + DC],
                                    0.5, None, ALU.mult)
            c.sem("lnV").inc(ini)

            xld = c.sem("xld")
            xst = c.sem("xst")
            tpe = c.sem("tpe")
            tdv = c.sem("tdv")
            evd = c.sem("evd")
            for p in range(NPASS):
                if first:
                    c.sy.wait_ge(tpe.h, tpe.n)
                    xld.inc(c.sy.dma_start(
                        out=XT[:, :].rearrange("pp (s d) -> pp s d", s=NSUB),
                        in_=xin.rearrange("(s pp) d -> pp s d", pp=128)
                        [:, p * NSUB:(p + 1) * NSUB, :]), 16)
                    tpe0, tdv0 = tpe.n, tdv.n
                    c.t.wait_ge(xld.h, xld.n)
                    c.t.wait_ge(evd.h, evd.n)
                    c.t.wait_ge(xst.h, xst.n)
                    i = 0
                    for sub in range(NSUB):
                        for ci in range(DC):
                            if i >= 2:
                                c.t.wait_ge(tdv.h, tdv0 + i - 1)
                            tr = c.t.transpose(
                                pT[i % 2][:, :],
                                XT[:, sub * D + ci * 128:sub * D + (ci + 1) * 128],
                                identsb[:, :])
                            tpe.inc(tr)
                            c.v.wait_ge(tpe.h, tpe0 + i + 1)
                            cp = c.v.tensor_copy(
                                XF[:, ci * TP + sub * 128:ci * TP + (sub + 1) * 128],
                                pT[i % 2][:, :])
                            tdv.inc(cp)
                            i += 1
                    c.s.wait_ge(tdv.h, tdv.n)
                else:
                    c.sy.wait_ge(evd.h, evd.n)
                    c.sy.wait_ge(xst.h, xst.n)
                    xld.inc(c.sy.dma_start(
                        out=XF[:, :].rearrange("pp (cc t) -> pp cc t", cc=DC),
                        in_=xin.rearrange("pp (cc t) -> pp cc t", cc=DC)
                        [:, :, p * TP:(p + 1) * TP]), 16)
                    c.s.wait_ge(xld.h, xld.n)

                lnV = emit_ln(c, vb, XF, vb["XHB"])
                evd = emit_ffn0_pass(c, vb, XF, vb["XHB"], lnV, w1, w2, w3)
                c.sy.wait_ge(evd.h, evd.n)
                xst.inc(c.sy.dma_start(
                    out=xout.rearrange("pp (cc t) -> pp cc t", cc=DC)
                    [:, :, p * TP:(p + 1) * TP],
                    in_=XF[:, :].rearrange("pp (cc t) -> pp cc t", cc=DC)), 16)
            c.g.wait_ge(xst.h, xst.n)

    return nc



def emit_gla_a_subtile(c, vb, XHB, p, s, core_b, cut=99):
    si = p * NSUB + s
    WCB = vb["WCB"]; BBC = vb["BBC"]; MCUM = vb["MCUM"]; CHI = vb["CHI"]
    AMASK = vb["AMASK"]; IDB = vb["IDB"]; ZEROB = vb["ZEROB"]
    P384 = vb["P384"]; SC64 = vb["SC64"]; NLB = vb["NLB"]; DELTA = vb["DELTA"]
    NCC = vb["NCC"]
    EM = vb["EM"]; EP = vb["EP"]; QTM = vb["QTM"]; KTM = vb["KTM"]
    TKF = vb["TKF"]; VB = vb["VB"]; KTP = vb["KTP"]; QTall = vb["QTall"]
    USTG = vb["USTG"]; OISTG = vb["OISTG"]; PCST = vb["PCST"]
    pPP = vb["pPP"]; pU = vb["pU"]; pTQ = vb["pTQ"]
    pOXv = vb.get("pOX")
    if pOXv is not None:
        pNCa, pNCb = pOXv[:, 128:192], pOXv[0:64, 192:194]
    else:
        pNC = vb["pNC"]
        pNCa, pNCb = pNC[:, 0:64], pNC[0:64, 64:66]
    pA = vb["pR"]; pO = vb["pM"]
    st = si * 128

    prjP = c.sem("prjP"); prj = c.sem("prj"); scA = c.sem("scA")
    dvN = c.sem("dvN"); pePC = c.sem("pePC"); nclam = c.sem("nclam")
    scE = c.sem("scE"); dvQ = c.sem("dvQ"); peU = c.sem("peU")
    peT = c.sem("peT"); uev = c.sem("uev"); tqv = c.sem("tqv")
    peA = c.sem("peA"); sel = c.sem("sel"); peO = c.sem("peO")
    oev = c.sem("oev")

    c.t.wait_ge(c.sem("lnV").h, c.sem("lnV").n)
    c.t.wait_ge(prj.h, prj.n)
    if vb.get("pOX") is None:
        c.t.wait_ge(c.sem("otv").h, c.sem("otv").n)
    for ci in range(DC):
        inst = c.t.matmul(pPP[:, 0:384],
                          XHB[:, ci * TP + s * 128:ci * TP + (s + 1) * 128],
                          WCB[:, ci * 384:(ci + 1) * 384],
                          start=(ci == 0), stop=(ci == DC - 1))
    prjP.inc(inst)
    c.v.wait_ge(prjP.h, prjP.n)
    c.v.wait_ge(c.sem("scA").h, scA.n)
    prj.inc(c.v.tensor_tensor(P384[:, :], pPP[:, 0:384], BBC[:, :], ALU.add))
    if cut <= 1:
        return
    zb = P384[:, 256:320]; zd = P384[:, 320:384]
    t0 = SC64[:, 0:64]; t1 = SC64[:, 64:128]
    t2 = SC64[:, 128:192]; t3 = SC64[:, 192:256]
    
    c.s.wait_ge(prj.h, prj.n)
    c.s.wait_ge(dvN.h, dvN.n)
    c.s.activation(t0, zb, AF.Abs)
    c.s.activation(t0, t0, AF.Exp, scale=f32(-1.0))
    c.s.activation(t0, t0, AF.Ln, bias=f32(1.0))
    c.s.activation(t1, zb, AF.Relu, scale=f32(-1.0))
    c.s.activation(t2, zd, AF.Abs)
    c.s.activation(t2, t2, AF.Exp, scale=f32(-1.0))
    c.s.activation(t2, t2, AF.Ln, bias=f32(1.0))
    scA.inc(c.s.activation(t3, zd, AF.Relu))
    c.v.wait_ge(scA.h, scA.n)
    c.v.wait_ge(pePC.h, pePC.n)
    c.v.tensor_tensor(NLB[:, :], t1, t0, ALU.add)
    dvN.inc(c.v.tensor_tensor(DELTA[:, :], t3, t2, ALU.add))
    if cut <= 2:
        return
    c.t.wait_ge(dvN.h, dvN.n)
    c.t.wait_ge(nclam.h, nclam.n)
    if pOXv is not None:
        c.t.wait_ge(c.sem("otv").h, c.sem("otv").n)
    c.t.matmul(pNCa, MCUM[:, :], NLB[:, :], start=True, stop=True)
    pePC.inc(c.t.matmul(pNCb, NLB[:, :], CHI[:, :], start=True, stop=True))
    c.v.wait_ge(pePC.h, pePC.n)
    c.v.tensor_scalar(NCC[:, 0:64], pNCa, 85.0, None, ALU.min)
    nclam.inc(c.v.tensor_scalar(NCC[0:64, 64:66], pNCb, 85.0, None, ALU.min))
    c.s.wait_ge(nclam.h, nclam.n)
    c.s.activation(EM[:, :], NCC[:, 0:64], AF.Exp, scale=f32(-1.0))
    c.s.activation(EP[:, :], NCC[:, 0:64], AF.Exp)
    scE.inc(c.s.activation(NCC[0:64, 66:68], NCC[0:64, 64:66], AF.Exp, scale=f32(-1.0)))
    c.v.wait_ge(scE.h, scE.n)
    c.v.wait_ge(peU.h, peU.n)
    c.v.wait_ge(peT.h, peT.n)
    c.v.wait_ge(peO.h, peO.n)
    c.v.tensor_tensor(QTM[0:64, 0:64], P384[0:64, 0:64], EM[0:64, :], ALU.mult)
    c.v.tensor_tensor(QTM[64:128, 64:128], P384[64:128, 0:64], EM[64:128, :], ALU.mult)
    c.v.tensor_tensor(TKF[:, :], P384[:, 64:128], DELTA[:, :], ALU.mult)
    c.v.tensor_tensor(KTM[0:64, 0:64], TKF[0:64, :], EP[0:64, :], ALU.mult)
    c.v.tensor_tensor(KTM[64:128, 64:128], TKF[64:128, :], EP[64:128, :], ALU.mult)
    c.v.tensor_copy(VB[:, :], P384[:, 128:256])
    dvQ.inc(c.v.tensor_copy(PCST[:, 2 * si:2 * si + 2], NCC[0:64, 66:68]))
    if cut <= 3:
        return
    c.t.wait_ge(dvQ.h, dvQ.n)
    c.t.wait_ge(uev.h, uev.n)
    c.t.matmul(pU[:, 0:128], KTM[:, 0:64], VB[:, :], start=True, stop=True)
    peU.inc(c.t.matmul(pU[:, 128:256], KTM[:, 64:128], VB[:, :],
                       start=True, stop=True))
    c.t.wait_ge(tqv.h, tqv.n)
    c.t.transpose(pTQ[:, 0:128], QTM[:, :], IDB[:, :])
    peT.inc(c.t.transpose(pTQ[:, 128:256], KTM[:, :], IDB[:, :]))
    c.v.wait_ge(peU.h, peU.n)
    c.v.tensor_copy(USTG[:, (2 * si) * 128:(2 * si + 1) * 128], pU[:, 0:128])
    uev.inc(c.v.tensor_copy(USTG[:, (2 * si + 1) * 128:(2 * si + 2) * 128],
                            pU[:, 128:256]))
    c.v.wait_ge(peT.h, peT.n)
    c.v.wait_ge(peA.h, peA.n)
    c.v.tensor_copy(QTall[:, st:st + 128], pTQ[:, 0:128])
    tqv.inc(c.v.tensor_copy(KTP[:, :], pTQ[:, 128:256]))
    if cut <= 4:
        return
    c.t.wait_ge(tqv.h, tqv.n)
    c.t.wait_ge(sel.h, sel.n)
    peA.inc(c.t.matmul(pA[:, 0:128], KTP[:, :], QTall[:, st:st + 128],
                       start=True, stop=True))
    c.v.wait_ge(peA.h, peA.n)
    sel.inc(c.v.select(vb["AB"][:, :], AMASK[:, :], pA[:, 0:128], ZEROB[:, :]))
    if cut <= 5:
        return
    c.t.wait_ge(sel.h, sel.n)
    c.t.wait_ge(oev.h, oev.n)
    peO.inc(c.t.matmul(pO[:, 0:128], vb["AB"][:, :], VB[:, :],
                       start=True, stop=True))
    c.v.wait_ge(peO.h, peO.n)
    oev.inc(c.v.tensor_copy(vb["OIF"][:, :], pO[:, 0:128]))
    if cut <= 6:
        return
    peOT = c.sem("peOT"); otv = c.sem("otv")
    pOX = pOXv
    if pOX is None:
        pOX = pPP
        c.t.wait_ge(prj.h, prj.n)
    else:
        c.t.wait_ge(nclam.h, nclam.n)
    c.t.wait_ge(oev.h, oev.n)
    c.t.wait_ge(otv.h, otv.n)
    peOT.inc(c.t.transpose(pOX[:, 0:128], vb["OIF"][:, :], vb["IDF"][:, :]))
    c.v.wait_ge(peOT.h, peOT.n)
    otv.inc(c.v.tensor_copy(OISTG[:, si * 128:(si + 1) * 128], pOX[:, 0:128]))


def build_gla_a_launch(npass=NPASS, nsub=NSUB, cut=99):
    nc = bass.Bass("TRN2", target_bir_lowering=False)
    xin = nc.dram_tensor("xin", [128, DC * TOKS], F32, kind="ExternalInput").ap()
    wcat = nc.dram_tensor("wcat", [128, DC * 384], F32, kind="ExternalInput").ap()
    brow = nc.dram_tensor("brow", [1, 384], F32, kind="ExternalInput").ap()
    mcum = nc.dram_tensor("mcum", [128, 128], F32, kind="ExternalInput").ap()
    chi = nc.dram_tensor("chi", [128, 2], F32, kind="ExternalInput").ap()
    amask = nc.dram_tensor("amask", [128, 128], U8, kind="ExternalInput").ap()
    identb = nc.dram_tensor("identb", [128, 128], BF16, kind="ExternalInput").ap()
    identf = nc.dram_tensor("identf", [128, 128], F32, kind="ExternalInput").ap()
    uown = nc.dram_tensor("uown", [2048, 128], F32, kind="ExternalOutput").ap()
    pcown = nc.dram_tensor("pcown", [64, 32], F32, kind="ExternalOutput").ap()
    qtown = nc.dram_tensor("qtown", [128, TOKS], BF16, kind="ExternalOutput").ap()
    oiown = nc.dram_tensor("oiown", [128, TOKS], F32, kind="ExternalOutput").ap()

    with ExitStack() as st:
        c = Ctx(nc, st)
        vb = {}
        vb["XB"] = c.sbuf("XB", [128, DC * TP], BF16)
        vb["XQ"] = c.sbuf("XQ", [128, DC * TP], BF16)
        vb["XHB"] = c.sbuf("XHB", [128, DC * TP], BF16)
        vb["SROW"] = c.sbuf("SROW", [1, 5 * TP], F32)
        vb["onesb"] = c.sbuf("onesb", [128, 1], BF16)
        vb["ones1f"] = c.sbuf("ones1f", [1, 128], F32)
        vb["onesbb"] = c.sbuf("onesbb", [128, 128], BF16)
        vb["LROW"] = c.sbuf("LROW", [128, 4 * TP], F32)
        vb["WCB"] = c.sbuf("WCB", [128, DC * 384], BF16)
        vb["BBC"] = c.sbuf("BBC", [128, 384], F32)
        vb["MCUM"] = c.sbuf("MCUM", [128, 128], F32)
        vb["CHI"] = c.sbuf("CHI", [128, 2], F32)
        vb["AMASK"] = c.sbuf("AMASK", [128, 128], U8)
        vb["IDB"] = c.sbuf("IDB", [128, 128], BF16)
        vb["ZEROB"] = c.sbuf("ZEROB", [128, 128], BF16)
        vb["P384"] = c.sbuf("P384", [128, 384], F32)
        vb["SC64"] = c.sbuf("SC64", [128, 256], F32)
        vb["NCC"] = c.sbuf("NCC", [128, 68], F32)
        vb["NLB"] = c.sbuf("NLB", [128, 64], F32)
        vb["DELTA"] = c.sbuf("DELTA", [128, 64], F32)
        vb["EM"] = c.sbuf("EM", [128, 64], F32)
        vb["EP"] = c.sbuf("EP", [128, 64], F32)
        vb["QTM"] = c.sbuf("QTM", [128, 128], BF16)
        vb["KTM"] = c.sbuf("KTM", [128, 128], BF16)
        vb["TKF"] = c.sbuf("TKF", [128, 64], F32)
        vb["VB"] = c.sbuf("VB", [128, 128], BF16)
        vb["KTP"] = c.sbuf("KTP", [128, 128], BF16)
        vb["QTall"] = c.sbuf("QTall", [128, TOKS], BF16)
        vb["AB"] = c.sbuf("AB", [128, 128], BF16)
        vb["USTG"] = c.sbuf("USTG", [64, 32 * 128], F32)
        vb["OISTG"] = c.sbuf("OISTG", [128, DC * 128], F32)
        vb["OIF"] = c.sbuf("OIF", [128, 128], F32)
        vb["IDF"] = c.sbuf("IDF", [128, 128], F32)
        vb["PCST"] = c.sbuf("PCST", [64, 32], F32)
        XF = c.sbuf("XF", [128, DC * TP], F32)
        vb["pS1"] = c.psum("pS1", [128, TP])
        vb["pS2"] = c.psum("pS2", [128, TP])
        vb["pR"] = c.psum("pR", [128, TP])
        vb["pM"] = c.psum("pM", [128, TP])
        vb["pPP"] = c.psum("pPP", [128, 448])
        vb["pNC"] = c.psum("pNC", [128, 66])
        vb["pU"] = c.psum("pU", [64, 256])
        vb["pTQ"] = c.psum("pTQ", [128, 256], BF16)
        block = st.enter_context(nc.Block())

        @block.gpsimd
        def _(_g):
            cst = c.sem("cst")
            cst.inc(c.sy.dma_start(out=vb["MCUM"][:, :], in_=mcum[:, :]), 16)
            cst.inc(c.sy.dma_start(out=vb["CHI"][:, :], in_=chi[:, :]), 16)
            cst.inc(c.sy.dma_start(out=vb["AMASK"][:, :], in_=amask[:, :]), 16)
            cst.inc(c.sy.dma_start(out=vb["IDB"][:, :], in_=identb[:, :]), 16)
            cst.inc(c.sy.dma_start(out=vb["IDF"][:, :], in_=identf[:, :]), 16)
            cstg = c.sem("cstg")
            cstg.inc(c.g.dma_start(out=vb["WCB"][:, :], in_=wcat[:, :]), 16)
            brsb = vb["SROW"][0:1, 0:384]
            cst.inc(c.sy.dma_start(out=brsb, in_=brow[:, :]), 16)
            c.v.wait_ge(cst.h, cst.n)
            c.v.wait_ge(cstg.h, cstg.n)
            c.t.wait_ge(cstg.h, cstg.n)
            c.v.memset(vb["onesb"][:, :], 1.0)
            c.v.memset(vb["onesbb"][:, :], 1.0)
            c.v.memset(vb["ones1f"][:, :], 1.0)
            c.v.memset(vb["ZEROB"][:, :], 0.0)
            c.v.memset(vb["QTall"][:, :], 0.0)
            c.v.memset(vb["QTM"][:, :], 0.0)
            c.v.memset(vb["KTM"][:, :], 0.0)
            mz = c.v.memset(vb["KTP"][:, :], 0.0)
            c.sem("dvI").inc(mz)
            c.t.wait_ge(c.sem("dvI").h, c.sem("dvI").n)
            bb = c.t.matmul(vb["pR"][:, 0:384], vb["ones1f"][:, :], brsb,
                            start=True, stop=True)
            c.sem("lnP").inc(bb)
            c.v.wait_ge(c.sem("lnP").h, c.sem("lnP").n)
            ini = c.v.tensor_copy(vb["BBC"][:, :], vb["pR"][:, 0:384])
            c.sem("lnV").inc(ini)
            xld = c.sem("xld")
            for p in range(npass):
                c.sy.wait_ge(c.sem("lnV").h, c.sem("lnV").n)
                xld.inc(c.sy.dma_start(
                    out=XF[:, :].rearrange("pp (cc t) -> pp cc t", cc=DC),
                    in_=xin.rearrange("pp (cc t) -> pp cc t", cc=DC)
                    [:, :, p * TP:(p + 1) * TP]), 16)
                c.s.wait_ge(xld.h, xld.n)
                emit_ln(c, vb, XF, vb["XHB"],
                        extra_bcast_waits=(c.sem("sel"), c.sem("oev")))
                for s in range(nsub):
                    emit_gla_a_subtile(c, vb, vb["XHB"], p, s, 0, cut)
            fin = c.sem("fin")
            c.sy.wait_ge(c.sem("uev").h, c.sem("uev").n)
            c.sy.wait_ge(c.sem("oev").h, c.sem("oev").n)
            c.sy.wait_ge(c.sem("dvQ").h, c.sem("dvQ").n)
            c.sy.wait_ge(c.sem("tqv").h, c.sem("tqv").n)
            fin.inc(c.sy.dma_start(
                out=uown.rearrange("(j k) v -> k j v", k=64),
                in_=vb["USTG"][:, :].rearrange("k (j v) -> k j v", v=128)), 16)
            fin.inc(c.sy.dma_start(out=pcown[:, :], in_=vb["PCST"][:, :]), 16)
            fin.inc(c.sy.dma_start(out=qtown[:, :], in_=vb["QTall"][:, :]), 16)
            fin.inc(c.sy.dma_start(out=oiown[:, :], in_=vb["OISTG"][:, :]), 16)
            c.g.wait_ge(fin.h, fin.n)

    return nc




def build_gla_b_launch():
    nc = bass.Bass("TRN2", target_bir_lowering=False)
    xin = nc.dram_tensor("xin", [128, DC * TOKS], F32, kind="ExternalInput").ap()
    uin = nc.dram_tensor("uin", [128 * 128, 128], F32, kind="ExternalInput").ap()
    pcin = nc.dram_tensor("pcin", [128, 128], F32, kind="ExternalInput").ap()
    qtin = nc.dram_tensor("qtin", [128, TOKS], BF16, kind="ExternalInput").ap()
    oiin = nc.dram_tensor("oiin", [128, TOKS], F32, kind="ExternalInput").ap()
    wo = nc.dram_tensor("wo", [128, D], F32, kind="ExternalInput").ap()
    xout = nc.dram_tensor("xout", [128, DC * TOKS], F32, kind="ExternalOutput").ap()

    with ExitStack() as st:
        c = Ctx(nc, st)
        SCAN = c.sbuf("SCAN", [128, 129 * 128], F32)
        PCsb = c.sbuf("PCsb", [128, 128], F32)
        QTsb = c.sbuf("QTsb", [128, TOKS], F32)
        OIT = c.sbuf("OIT", [128, TOKS], F32)
        OTall = c.sbuf("OTall", [128, TOKS], BF16)
        WOB = c.sbuf("WOB", [128, D], BF16)
        XF = c.sbuf("XF", [128, DC * TP], F32)
        pOT = c.psum("pOT", [128, 64])
        pA = c.psum("pA", [128, TP])
        pB = c.psum("pB", [128, TP])
        block = st.enter_context(nc.Block())

        @block.gpsimd
        def _(_g):
            ld = c.sem("ld")
            ld.inc(c.sy.dma_start(
                out=SCAN[:, 128:].rearrange("k (j v) -> k j v", v=128),
                in_=uin.rearrange("(j k) v -> k j v", k=128)), 16)
            ld.inc(c.sy.dma_start(out=PCsb[:, :], in_=pcin[:, :]), 16)
            ld.inc(c.g.dma_start(out=QTsb[:, :], in_=qtin[:, :]), 16)
            ld.inc(c.sy.dma_start(out=OIT[:, :], in_=oiin[:, :]), 16)
            ld.inc(c.g.dma_start(out=WOB[:, :], in_=wo[:, :]), 16)
            dvI = c.sem("dvI")
            dvI.inc(c.v.memset(SCAN[:, 0:128], 0.0))
            c.v.wait_ge(ld.h, ld.n)
            scn = c.sem("scn")
            view = SCAN[:, 128:].rearrange("p (j v) -> p v j", v=128)
            for vi in range(128):
                inst = c.v.tensor_tensor_scan(view[:, vi, :], view[:, vi, :],
                                              PCsb[:, :], 0.0, ALU.add, ALU.mult)
            scn.inc(inst)

            xld = c.sem("xld"); xst = c.sem("xst")
            peI = c.sem("peI"); otv = c.sem("otv")
            peY = c.sem("peY"); yev = c.sem("yev")
            c.t.wait_ge(scn.h, scn.n)
            c.t.wait_ge(dvI.h, dvI.n)
            for p in range(NPASS):
                c.sy.wait_ge(yev.h, yev.n)
                c.sy.wait_ge(xst.h, xst.n)
                xld.inc(c.sy.dma_start(
                    out=XF[:, :].rearrange("pp (cc t) -> pp cc t", cc=DC),
                    in_=xin.rearrange("pp (cc t) -> pp cc t", cc=DC)
                    [:, :, p * TP:(p + 1) * TP]), 16)
                for s in range(NSUB):
                    si = p * NSUB + s
                    st_ = si * 128
                    for h in range(2):
                        jloc = 2 * si + h
                        slot = 96 + jloc
                        c.t.wait_ge(otv.h, otv.n)
                        mm = c.t.matmul(pOT[:, 0:64],
                                        SCAN[:, slot * 128:(slot + 1) * 128],
                                        QTsb[:, st_ + h * 64:st_ + (h + 1) * 64],
                                        start=True, stop=True)
                        peI.inc(mm)
                        c.v.wait_ge(peI.h, peI.n)
                        ot = c.v.tensor_tensor(
                            OTall[:, st_ + h * 64:st_ + (h + 1) * 64],
                            pOT[:, 0:64],
                            OIT[:, st_ + h * 64:st_ + (h + 1) * 64], ALU.add)
                        otv.inc(ot)
                c.t.wait_ge(otv.h, otv.n)
                c.t.wait_ge(xld.h, xld.n)
                for oc in range(DC):
                    pp = [pA, pB][oc % 2]
                    if oc >= 2:
                        c.t.wait_ge(yev.h, yev.n - (DC - oc) - 1 + 1
                                    if False else yev.n)
                    mm = c.t.matmul(pp[:, :TP], WOB[:, oc * 128:(oc + 1) * 128],
                                    OTall[:, p * TP:(p + 1) * TP],
                                    start=True, stop=True)
                    peY.inc(mm)
                    c.v.wait_ge(peY.h, peY.n)
                    yv = c.v.tensor_tensor(XF[:, oc * TP:(oc + 1) * TP],
                                           pp[:, :TP],
                                           XF[:, oc * TP:(oc + 1) * TP], ALU.add)
                    yev.inc(yv)
                c.sy.wait_ge(yev.h, yev.n)
                xst.inc(c.sy.dma_start(
                    out=xout.rearrange("pp (cc t) -> pp cc t", cc=DC)
                    [:, :, p * TP:(p + 1) * TP],
                    in_=XF[:, :].rearrange("pp (cc t) -> pp cc t", cc=DC)), 16)
            c.g.wait_ge(xst.h, xst.n)

    return nc


def emit_had_transform(c, vb, w_sl, out_sl, nch, wcols, hconst, su, wsrow, tag):
    ZA = vb["ZA"]; ZB = vb["ZB"]; WOUT = vb["WOUT"]
    pH = vb["pH"]; pW = vb["pW"]
    W = 512 if nch == 16 else 128
    njs = wcols // W
    ld = c.sem("hld"); pe = c.sem("hpe"); dv = c.sem("hdv"); so = c.sem("hso")
    for js in range(njs):
        c.sy.wait_ge(dv.h, dv.n)
        c.sy.wait_ge(so.h, so.n)
        ld.inc(c.sy.dma_start(
            out=ZA[:, 0:nch * W].rearrange("p (cc w) -> p cc w", cc=nch),
            in_=w_sl.rearrange("(cc p) j -> p cc j", p=128)
            [:, :, js * W:(js + 1) * W]), 16)
        c.t.wait_ge(ld.h, ld.n)
        c.t.wait_ge(dv.h, dv.n)
        dv0 = dv.n
        for cc in range(nch):
            pp = pH[cc % 2]
            if cc >= 2:
                c.t.wait_ge(dv.h, dv0 + cc - 1)
            inst = c.t.matmul(pp[:, 0:W], hconst[:, :],
                              ZA[:, cc * W:(cc + 1) * W], start=True, stop=True)
            pe.inc(inst)
            c.v.wait_ge(pe.h, pe.n)
            dv.inc(c.v.tensor_copy(ZB[:, cc * W:(cc + 1) * W], pp[:, 0:W]))
        bufs = [ZB, ZA]
        cur = 0
        d = 1
        while d < nch:
            src_, dst_ = bufs[cur], bufs[1 - cur]
            for cc in range(nch):
                partner = cc ^ d
                op = ALU.add if (cc & d) == 0 else ALU.subtract
                a = cc if (cc & d) == 0 else partner
                b = partner if (cc & d) == 0 else cc
                inst = c.v.tensor_tensor(dst_[:, cc * W:(cc + 1) * W],
                                         src_[:, a * W:(a + 1) * W],
                                         src_[:, b * W:(b + 1) * W], op)
            dv.inc(inst)
            cur = 1 - cur
            d *= 2
        fin = bufs[cur]
        c.t.wait_ge(dv.h, dv.n)
        wsmm = c.t.matmul(pW[:, 0:W], vb["ones1f"][:, :],
                          wsrow[0:1, js * W:(js + 1) * W], start=True, stop=True)
        pe.inc(wsmm)
        c.v.wait_ge(pe.h, pe.n)
        for cc in range(nch):
            inst = c.v.scalar_tensor_tensor(
                WOUT[:, cc * W:(cc + 1) * W], fin[:, cc * W:(cc + 1) * W],
                su[:, cc:cc + 1], pW[:, 0:W], ALU.mult, ALU.mult)
        dv.inc(inst)
        c.sy.wait_ge(dv.h, dv.n)
        ocl_per_js = W // 128
        outv = out_sl.rearrange("p (ocl cc m) -> p ocl cc m", cc=nch, m=128)
        winv = WOUT[:, 0:nch * W].rearrange("p (cc ocl m) -> p cc ocl m",
                                            ocl=ocl_per_js, m=128)
        for ol in range(ocl_per_js):
            so.inc(c.sy.dma_start(
                out=outv[:, js * ocl_per_js + ol, :, :],
                in_=winv[:, :, ol, :]), 16)
    return so


def build_prep_launch():
    nc = bass.Bass("TRN2", target_bir_lowering=False)
    wup = nc.dram_tensor("wup", [2048, 1024], F32, kind="ExternalInput").ap()
    wga = nc.dram_tensor("wga", [2048, 1024], F32, kind="ExternalInput").ap()
    wdo = nc.dram_tensor("wdo", [8192, 256], F32, kind="ExternalInput").ap()
    h128u = nc.dram_tensor("h128u", [128, 128], F32, kind="ExternalInput").ap()
    h128d = nc.dram_tensor("h128d", [128, 128], F32, kind="ExternalInput").ap()
    suug = nc.dram_tensor("suug", [128, 16], F32, kind="ExternalInput").ap()
    sudp = nc.dram_tensor("sudp", [128, 64], F32, kind="ExternalInput").ap()
    wsup = nc.dram_tensor("wsup", [1, 1024], F32, kind="ExternalInput").ap()
    wsga = nc.dram_tensor("wsga", [1, 1024], F32, kind="ExternalInput").ap()
    wsdo = nc.dram_tensor("wsdo", [1, 256], F32, kind="ExternalInput").ap()
    oup = nc.dram_tensor("oup", [128, 8 * 16 * 128], BF16, kind="ExternalOutput").ap()
    oga = nc.dram_tensor("oga", [128, 8 * 16 * 128], BF16, kind="ExternalOutput").ap()
    odo = nc.dram_tensor("odo", [128, 2 * 64 * 128], BF16, kind="ExternalOutput").ap()

    with ExitStack() as st:
        c = Ctx(nc, st)
        vb = {}
        vb["ZA"] = c.sbuf("ZA", [128, 8192], F32)
        vb["ZB"] = c.sbuf("ZB", [128, 8192], F32)
        vb["WOUT"] = c.sbuf("WOUT", [128, 8192], BF16)
        vb["ones1f"] = c.sbuf("ones1f", [1, 128], F32)
        HU = c.sbuf("HU", [128, 128], F32)
        HDN = c.sbuf("HDN", [128, 128], F32)
        SUU = c.sbuf("SUU", [128, 16], F32)
        SUD = c.sbuf("SUD", [128, 64], F32)
        WSU = c.sbuf("WSU", [1, 1024], F32)
        WSG = c.sbuf("WSG", [1, 1024], F32)
        WSD = c.sbuf("WSD", [1, 256], F32)
        vb["pH"] = [c.psum("pH0", [128, 512]), c.psum("pH1", [128, 512])]
        vb["pW"] = c.psum("pWb", [128, 512])
        block = st.enter_context(nc.Block())

        @block.gpsimd
        def _(_g):
            cst = c.sem("cst")
            for dst, srcap in [(HU, h128u), (HDN, h128d), (SUU, suug),
                               (SUD, sudp), (WSU, wsup), (WSG, wsga), (WSD, wsdo)]:
                cst.inc(c.sy.dma_start(out=dst[:, :], in_=srcap[:, :]), 16)
            c.v.wait_ge(cst.h, cst.n)
            dvi = c.sem("hdv")
            dvi.inc(c.v.memset(vb["ones1f"][:, :], 1.0))
            c.t.wait_ge(dvi.h, dvi.n)
            emit_had_transform(c, vb, wup, oup, 16, 1024, HU, SUU, WSU, "up")
            emit_had_transform(c, vb, wga, oga, 16, 1024, HU, SUU, WSG, "ga")
            so = emit_had_transform(c, vb, wdo, odo, 64, 256, HDN, SUD, WSD, "do")
            c.g.wait_ge(so.h, so.n)

    return nc


def build_l1ffn_launch():
    nc = bass.Bass("TRN2", target_bir_lowering=False)
    xin = nc.dram_tensor("xin", [128, DC * TOKS], F32, kind="ExternalInput").ap()
    wub = nc.dram_tensor("wub", [128, 64 * DC * 128], BF16, kind="ExternalInput").ap()
    wgb = nc.dram_tensor("wgb", [128, 64 * DC * 128], BF16, kind="ExternalInput").ap()
    wdb = nc.dram_tensor("wdb", [128, DC * 64 * 128], BF16, kind="ExternalInput").ap()
    xout = nc.dram_tensor("xout", [128, DC * TOKS], F32, kind="ExternalOutput").ap()
    IC = 64

    with ExitStack() as st:
        c = Ctx(nc, st)
        vb = {}
        vb["XB"] = c.sbuf("XB", [128, DC * TP], BF16)
        vb["XQ"] = c.sbuf("XQ", [128, DC * TP], BF16)
        vb["XHB"] = c.sbuf("XHB", [128, DC * TP], BF16)
        vb["SROW"] = c.sbuf("SROW", [1, 5 * TP], F32)
        vb["onesb"] = c.sbuf("onesb", [128, 1], BF16)
        vb["ones1f"] = c.sbuf("ones1f", [1, 128], F32)
        vb["onesbb"] = c.sbuf("onesbb", [128, 128], BF16)
        vb["LROW"] = c.sbuf("LROW", [128, 4 * TP], F32)
        XF = c.sbuf("XF", [128, DC * TP], F32)
        GB = c.sbuf("GB", [128, IC * TP], BF16)
        SGT = c.sbuf("SGT", [128, TP], BF16)
        wbu = [c.sbuf("wbu0", [128, DC * 128], BF16), c.sbuf("wbu1", [128, DC * 128], BF16)]
        wbg = [c.sbuf("wbg0", [128, DC * 128], BF16), c.sbuf("wbg1", [128, DC * 128], BF16)]
        wbd = [c.sbuf("wbd0", [128, IC * 128], BF16), c.sbuf("wbd1", [128, IC * 128], BF16)]
        vb["pS1"] = c.psum("pS1", [128, TP])
        vb["pS2"] = c.psum("pS2", [128, TP])
        vb["pR"] = c.psum("pR", [128, TP])
        vb["pM"] = c.psum("pM", [128, TP])
        pG1 = c.psum("pG1", [128, TP])
        pG2 = c.psum("pG2", [128, TP])
        pY = [c.psum("pY0", [128, TP]), c.psum("pY1", [128, TP])]
        block = st.enter_context(nc.Block())

        @block.gpsimd
        def _(_g):
            dvi = c.sem("lnV")
            c.v.memset(vb["onesb"][:, :], 1.0)
            c.v.memset(vb["onesbb"][:, :], 1.0)
            dvi.inc(c.v.memset(vb["ones1f"][:, :], 1.0))
            xld = c.sem("xld"); xst = c.sem("xst")
            wldu = c.sem("wldu"); wldg = c.sem("wldg"); wldd = c.sem("wldd")
            wfree = c.sem("wfree"); gvd = c.sem("gvd"); sil = c.sem("sil")
            yev = c.sem("yev")
            gpair = [(pG1, pG2), (vb["pR"], vb["pM"])]
            for p in range(NPASS):
                c.sy.wait_ge(yev.h, yev.n)
                c.sy.wait_ge(xst.h, xst.n)
                xld.inc(c.sy.dma_start(
                    out=XF[:, :].rearrange("pp (cc t) -> pp cc t", cc=DC),
                    in_=xin.rearrange("pp (cc t) -> pp cc t", cc=DC)
                    [:, :, p * TP:(p + 1) * TP]), 16)
                c.s.wait_ge(xld.h, xld.n)
                emit_ln(c, vb, XF, vb["XHB"], extra_bcast_waits=(gvd,))
                lnV = c.sem("lnV")
                wfree0, gvd0, sil0 = wfree.n, gvd.n, sil.n
                wldu0, wldg0 = wldu.n, wldg.n
                span = DC * 128
                for hc in range(IC):
                    bu = wbu[hc % 2]; bg = wbg[hc % 2]
                    if hc >= 2:
                        c.sy.wait_ge(wfree.h, wfree0 + hc - 1)
                        c.s.wait_ge(wfree.h, wfree0 + hc - 1)
                    wldu.inc(c.sy.dma_start(out=bu[:, :],
                                            in_=wub[:, hc * span:(hc + 1) * span]), 16)
                    wldg.inc(c.s.dma_start(out=bg[:, :],
                                           in_=wgb[:, hc * span:(hc + 1) * span]), 16)
                    ppu, ppg = gpair[hc % 2]
                    c.t.wait_ge(wldu.h, wldu0 + 16 * (hc + 1))
                    c.t.wait_ge(wldg.h, wldg0 + 16 * (hc + 1))
                    if hc == 0:
                        c.t.wait_ge(lnV.h, lnV.n)
                    if hc >= 2:
                        c.t.wait_ge(gvd.h, gvd0 + hc - 1)
                    elif p > 0 and hc < 2:
                        c.t.wait_ge(gvd.h, gvd0)
                    for ci in range(DC):
                        c.t.matmul(ppu[:, :TP], bu[:, ci * 128:(ci + 1) * 128],
                                   vb["XHB"][:, ci * TP:(ci + 1) * TP],
                                   start=(ci == 0), stop=(ci == DC - 1))
                    for ci in range(DC):
                        inst = c.t.matmul(ppg[:, :TP], bg[:, ci * 128:(ci + 1) * 128],
                                          vb["XHB"][:, ci * TP:(ci + 1) * TP],
                                          start=(ci == 0), stop=(ci == DC - 1))
                    wfree.inc(inst)
                    c.s.wait_ge(wfree.h, wfree0 + hc + 1)
                    c.s.wait_ge(gvd.h, gvd0 + hc)
                    sil.inc(c.s.activation(SGT[:, :], ppg[:, :TP], AF.Silu))
                    c.v.wait_ge(sil.h, sil0 + hc + 1)
                    gvd.inc(c.v.tensor_tensor(GB[:, hc * TP:(hc + 1) * TP],
                                              ppu[:, :TP], SGT[:, :], ALU.mult))
                wldd0, yev0 = wldd.n, yev.n
                dspan = IC * 128
                for dc in range(DC):
                    bd = wbd[dc % 2]
                    if dc >= 2:
                        c.sy.wait_ge(yev.h, yev0 + dc - 1)
                    wldd.inc(c.sy.dma_start(out=bd[:, :],
                                            in_=wdb[:, dc * dspan:(dc + 1) * dspan]), 16)
                    pp = pY[dc % 2]
                    c.t.wait_ge(wldd.h, wldd0 + 16 * (dc + 1))
                    if dc == 0:
                        c.t.wait_ge(gvd.h, gvd.n)
                    if dc >= 2:
                        c.t.wait_ge(yev.h, yev0 + dc - 1)
                    for fc in range(IC):
                        inst = c.t.matmul(pp[:, :TP], bd[:, fc * 128:(fc + 1) * 128],
                                          GB[:, fc * TP:(fc + 1) * TP],
                                          start=(fc == 0), stop=(fc == IC - 1))
                    wfree.inc(inst)
                    c.v.wait_ge(wfree.h, wfree.n)
                    yev.inc(c.v.tensor_tensor(XF[:, dc * TP:(dc + 1) * TP],
                                              pp[:, :TP],
                                              XF[:, dc * TP:(dc + 1) * TP], ALU.add))
                c.sy.wait_ge(yev.h, yev.n)
                xst.inc(c.sy.dma_start(
                    out=xout.rearrange("pp (cc t) -> pp cc t", cc=DC)
                    [:, :, p * TP:(p + 1) * TP],
                    in_=XF[:, :].rearrange("pp (cc t) -> pp cc t", cc=DC)), 16)
            c.g.wait_ge(xst.h, xst.n)

    return nc


def build_ln5_launch():
    nc = bass.Bass("TRN2", target_bir_lowering=False)
    xin = nc.dram_tensor("xin", [128, DC * TOKS], F32, kind="ExternalInput").ap()
    xh = nc.dram_tensor("xh", [128, DC * TOKS], BF16, kind="ExternalOutput").ap()
    with ExitStack() as st:
        c = Ctx(nc, st)
        vb = {}
        vb["XB"] = c.sbuf("XB", [128, DC * TP], BF16)
        vb["XQ"] = c.sbuf("XQ", [128, DC * TP], BF16)
        vb["XHB"] = c.sbuf("XHB", [128, DC * TP], BF16)
        vb["SROW"] = c.sbuf("SROW", [1, 5 * TP], F32)
        vb["onesb"] = c.sbuf("onesb", [128, 1], BF16)
        vb["ones1f"] = c.sbuf("ones1f", [1, 128], F32)
        vb["onesbb"] = c.sbuf("onesbb", [128, 128], BF16)
        vb["LROW"] = c.sbuf("LROW", [128, 4 * TP], F32)
        XF = c.sbuf("XF", [128, DC * TP], F32)
        vb["pS1"] = c.psum("pS1", [128, TP])
        vb["pS2"] = c.psum("pS2", [128, TP])
        vb["pR"] = c.psum("pR", [128, TP])
        vb["pM"] = c.psum("pM", [128, TP])
        block = st.enter_context(nc.Block())

        @block.gpsimd
        def _(_g):
            dvi = c.sem("lnV")
            c.v.memset(vb["onesb"][:, :], 1.0)
            c.v.memset(vb["onesbb"][:, :], 1.0)
            dvi.inc(c.v.memset(vb["ones1f"][:, :], 1.0))
            xld = c.sem("xld"); xst = c.sem("xst")
            for p in range(NPASS):
                c.sy.wait_ge(dvi.h, dvi.n)
                c.sy.wait_ge(xst.h, xst.n)
                xld.inc(c.sy.dma_start(
                    out=XF[:, :].rearrange("pp (cc t) -> pp cc t", cc=DC),
                    in_=xin.rearrange("pp (cc t) -> pp cc t", cc=DC)
                    [:, :, p * TP:(p + 1) * TP]), 16)
                c.s.wait_ge(xld.h, xld.n)
                lnV = emit_ln(c, vb, XF, vb["XHB"])
                c.sy.wait_ge(lnV.h, lnV.n)
                xst.inc(c.sy.dma_start(
                    out=xh.rearrange("pp (cc t) -> pp cc t", cc=DC)
                    [:, :, p * TP:(p + 1) * TP],
                    in_=vb["XHB"][:, :].rearrange("pp (cc t) -> pp cc t", cc=DC)), 16)
            c.g.wait_ge(xst.h, xst.n)
    return nc


def build_attn_launch():
    nc = bass.Bass("TRN2", target_bir_lowering=False)
    xh = nc.dram_tensor("xh", [128, DC * SEQ], BF16, kind="ExternalInput").ap()
    wqkv = nc.dram_tensor("wqkv", [128, 12 * DC * 128], F32, kind="ExternalInput").ap()
    bqkv = nc.dram_tensor("bqkv", [128, 12], F32, kind="ExternalInput").ap()
    cmask = nc.dram_tensor("cmask", [128, 4 * 512], BF16, kind="ExternalInput").ap()
    identb = nc.dram_tensor("identb", [128, 128], BF16, kind="ExternalInput").ap()
    oown = nc.dram_tensor("oown", [512, SEQ], BF16, kind="ExternalOutput").ap()
    NT = SEQ // TP
    NKT = SEQ // 128

    with ExitStack() as st:
        c = Ctx(nc, st)
        WB = c.sbuf("WB", [128, 12 * DC * 128], BF16)
        BQ = c.sbuf("BQ", [128, 12], F32)
        CM = c.sbuf("CM", [128, 4 * 512], BF16)
        IDB = c.sbuf("IDB", [128, 128], BF16)
        XQB = c.sbuf("XQB", [128, DC * TP], BF16)
        Q = c.sbuf("Q", [128, SEQ], BF16)
        Kt = c.sbuf("Kt", [128, SEQ], BF16)
        VT = c.sbuf("VT", [128, NKT * 129], BF16)
        VF = c.sbuf("VF", [128, TP], BF16)
        PT = [c.sbuf("PT0", [128, TP], BF16), c.sbuf("PT1", [128, TP], BF16)]
        OB = c.sbuf("OB", [128, 128], BF16)
        RC = c.sbuf("RC", [128, 1], F32)
        OST = c.sbuf("OST", [128, TP], BF16)
        pS = [c.psum("pSa", [128, TP]), c.psum("pSb", [128, TP])]
        pO = [c.psum("pO0", [128, TP]), c.psum("pO1", [128, TP]),
              c.psum("pO2", [128, TP]), c.psum("pO3", [128, TP])]
        pTR = c.psum("pTR", [128, 128], BF16)
        block = st.enter_context(nc.Block())

        @block.gpsimd
        def _(_g):
            cst = c.sem("cst")
            cstg = c.sem("cstg")
            cstg.inc(c.g.dma_start(out=WB[:, :], in_=wqkv[:, :]), 16)
            cst.inc(c.sy.dma_start(out=BQ[:, :], in_=bqkv[:, :]), 16)
            cst.inc(c.sy.dma_start(out=CM[:, :], in_=cmask[:, :]), 16)
            cst.inc(c.sy.dma_start(out=IDB[:, :], in_=identb[:, :]), 16)
            dvi = c.sem("dvi")
            dvi.inc(c.v.memset(VT[:, :], 1.0))
            xld = c.sem("xld"); peQ = c.sem("peQ"); qkv = c.sem("qkv")
            peV = c.sem("peV"); vtd = c.sem("vtd")
            peS = c.sem("peS"); scP = c.sem("scP"); msk = c.sem("msk")
            peA = c.sem("peA"); onrm = c.sem("onrm"); peOT = c.sem("peOT")
            ost = c.sem("ost"); fst = c.sem("fst")
            c.t.wait_ge(cst.h, cst.n)
            c.t.wait_ge(cstg.h, cstg.n)
            c.v.wait_ge(cst.h, cst.n)
            for hi in range(4):
                for tp in range(NT):
                    c.sy.wait_ge(peQ.h, peQ.n)
                    xld.inc(c.sy.dma_start(
                        out=XQB[:, :].rearrange("pp (cc t) -> pp cc t", cc=DC),
                        in_=xh.rearrange("pp (cc t) -> pp cc t", cc=DC)
                        [:, :, tp * TP:(tp + 1) * TP]), 16)
                    c.t.wait_ge(xld.h, xld.n)
                    for j in range(3):
                        pp = pS[j % 2]
                        wslice = WB[:, (hi * 3 + j) * DC * 128:(hi * 3 + j + 1) * DC * 128]
                        c.t.wait_ge(qkv.h, qkv.n)
                        c.t.wait_ge(scP.h, scP.n)
                        for ci in range(DC):
                            inst = c.t.matmul(pp[:, :TP],
                                              wslice[:, ci * 128:(ci + 1) * 128],
                                              XQB[:, ci * TP:(ci + 1) * TP],
                                              start=(ci == 0), stop=(ci == DC - 1))
                        peQ.inc(inst)
                        c.v.wait_ge(peQ.h, peQ.n)
                        c.v.wait_ge(peS.h, peS.n)
                        c.v.wait_ge(peA.h, peA.n)
                        bcol = BQ[:, hi * 3 + j:hi * 3 + j + 1]
                        if j == 0:
                            qkv.inc(c.v.tensor_scalar(Q[:, tp * TP:(tp + 1) * TP],
                                                      pp[:, :TP], bcol, None, ALU.add))
                        elif j == 1:
                            qkv.inc(c.v.tensor_scalar(Kt[:, tp * TP:(tp + 1) * TP],
                                                      pp[:, :TP], bcol, None, ALU.add))
                        else:
                            c.v.wait_ge(peV.h, peV.n)
                            qkv.inc(c.v.tensor_scalar(VF[:, :], pp[:, :TP],
                                                      bcol, None, ALU.add))
                            for u in range(4):
                                kt = tp * 4 + u
                                c.t.wait_ge(qkv.h, qkv.n)
                                c.t.wait_ge(vtd.h, vtd.n)
                                tr = c.t.transpose(pTR[:, :],
                                                   VF[:, u * 128:(u + 1) * 128],
                                                   IDB[:, :])
                                peV.inc(tr)
                                c.v.wait_ge(peV.h, peV.n)
                                vtd.inc(c.v.tensor_copy(
                                    VT[:, kt * 129:kt * 129 + 128], pTR[:, :]))
                for qs in range(NT):
                    nkc = 4 * qs + 4
                    for kc in range(nkc):
                        pp = pS[kc % 2]
                        pt = PT[kc % 2]
                        c.t.wait_ge(qkv.h, qkv.n)
                        c.t.wait_ge(scP.h, scP.n)
                        c.t.wait_ge(vtd.h, vtd.n)
                        inst = c.t.matmul(pp[:, :TP], Kt[:, kc * 128:(kc + 1) * 128],
                                          Q[:, qs * TP:(qs + 1) * TP],
                                          start=True, stop=True)
                        peS.inc(inst)
                        c.s.wait_ge(peS.h, peS.n)
                        c.s.wait_ge(peA.h, peA.n)
                        scP.inc(c.s.activation(pt[:, :], pp[:, :TP], AF.Exp,
                                               scale=f32(1.0 / np.sqrt(128.0)),
                                               bias=f32(-16.0)))
                        if kc >= 4 * qs:
                            m = kc - 4 * qs
                            c.v.wait_ge(scP.h, scP.n)
                            msk.inc(c.v.tensor_tensor(
                                pt[:, :], pt[:, :],
                                CM[:, m * 512:(m + 1) * 512], ALU.mult))
                        c.t.wait_ge(scP.h, scP.n)
                        if kc >= 4 * qs:
                            c.t.wait_ge(msk.h, msk.n)
                        if kc == 0:
                            c.t.wait_ge(onrm.h, onrm.n)
                        qc0 = max(0, kc - 4 * qs)
                        for qc in range(qc0, 4):
                            inst = c.t.matmul(pO[qc][:, 0:129],
                                              pt[:, qc * 128:(qc + 1) * 128],
                                              VT[:, kc * 129:(kc + 1) * 129],
                                              start=(kc == 0),
                                              stop=(kc == 4 * qs + qc))
                        peA.inc(inst)
                    for qc in range(4):
                        c.v.wait_ge(peA.h, peA.n)
                        c.v.reciprocal(RC[:, :], pO[qc][:, 128:129])
                        c.v.wait_ge(peOT.h, peOT.n)
                        onrm.inc(c.v.tensor_scalar(OB[:, :], pO[qc][:, 0:128],
                                                   RC[:, :], None, ALU.mult))
                        c.t.wait_ge(onrm.h, onrm.n)
                        c.t.wait_ge(ost.h, ost.n)
                        peOT.inc(c.t.transpose(pTR[:, :], OB[:, :], IDB[:, :]))
                        c.v.wait_ge(peOT.h, peOT.n)
                        c.v.wait_ge(fst.h, fst.n)
                        ost.inc(c.v.tensor_copy(OST[:, qc * 128:(qc + 1) * 128],
                                                pTR[:, :]))
                    c.sy.wait_ge(ost.h, ost.n)
                    fst.inc(c.sy.dma_start(
                        out=oown[hi * 128:(hi + 1) * 128, qs * TP:(qs + 1) * TP],
                        in_=OST[:, :]), 16)
            c.g.wait_ge(fst.h, fst.n)
    return nc


def build_outproj_launch():
    nc = bass.Bass("TRN2", target_bir_lowering=False)
    xin = nc.dram_tensor("xin", [128, DC * TOKS], F32, kind="ExternalInput").ap()
    oin = nc.dram_tensor("oin", [128, DC * TOKS], BF16, kind="ExternalInput").ap()
    wout = nc.dram_tensor("wout", [128, DC * DC * 128], F32, kind="ExternalInput").ap()
    bout = nc.dram_tensor("bout", [128, DC], F32, kind="ExternalInput").ap()
    xout = nc.dram_tensor("xout", [128, DC * TOKS], F32, kind="ExternalOutput").ap()
    with ExitStack() as st:
        c = Ctx(nc, st)
        vb = {}
        vb["wbufs"] = [c.sbuf("wb0", [128, DC * 128], BF16),
                       c.sbuf("wb1", [128, DC * 128], BF16)]
        vb["pAB"] = [c.psum("pA", [128, TP]), c.psum("pB", [128, TP])]
        XF = c.sbuf("XF", [128, DC * TP], F32)
        OBT = c.sbuf("OBT", [128, DC * TP], BF16)
        BO = c.sbuf("BO", [128, DC], F32)
        block = st.enter_context(nc.Block())

        @block.gpsimd
        def _(_g):
            cst = c.sem("cst")
            cst.inc(c.sy.dma_start(out=BO[:, :], in_=bout[:, :]), 16)
            xld = c.sem("xld"); xst = c.sem("xst"); yev = c.sem("evd")
            c.v.wait_ge(cst.h, cst.n)
            for p in range(NPASS):
                c.sy.wait_ge(yev.h, yev.n)
                c.sy.wait_ge(xst.h, xst.n)
                xld.inc(c.sy.dma_start(
                    out=XF[:, :].rearrange("pp (cc t) -> pp cc t", cc=DC),
                    in_=xin.rearrange("pp (cc t) -> pp cc t", cc=DC)
                    [:, :, p * TP:(p + 1) * TP]), 16)
                xld.inc(c.sy.dma_start(
                    out=OBT[:, :].rearrange("pp (cc t) -> pp cc t", cc=DC),
                    in_=oin.rearrange("pp (cc t) -> pp cc t", cc=DC)
                    [:, :, p * TP:(p + 1) * TP]), 16)
                xw = (c.sem("xld"), c.sem("xld").n)

                def evac(oc, pp, wfree, wfree0, evd):
                    c.v.wait_ge(wfree.h, wfree0 + oc + 1)
                    inst = c.v.scalar_tensor_tensor(
                        XF[:, oc * TP:(oc + 1) * TP], pp[:, :TP],
                        BO[:, oc:oc + 1], XF[:, oc * TP:(oc + 1) * TP],
                        ALU.add, ALU.add)
                    evd.inc(inst)

                emit_wstream_mm(c, vb, wout, DC, DC,
                                lambda ci: OBT[:, ci * TP:(ci + 1) * TP],
                                evac, rhs_wait=xw)
                c.sy.wait_ge(yev.h, yev.n)
                xst.inc(c.sy.dma_start(
                    out=xout.rearrange("pp (cc t) -> pp cc t", cc=DC)
                    [:, :, p * TP:(p + 1) * TP],
                    in_=XF[:, :].rearrange("pp (cc t) -> pp cc t", cc=DC)), 16)
            c.g.wait_ge(xst.h, xst.n)
    return nc


def build_fused():
    nc = bass.Bass("TRN2", target_bir_lowering=False)
    xin = nc.dram_tensor("xin", [TOKS, D], F32, kind="ExternalInput").ap()
    w1 = nc.dram_tensor("w1", [128, H0C * DC * 128], F32, kind="ExternalInput").ap()
    w2 = nc.dram_tensor("w2", [128, H0C * H0C * 128], F32, kind="ExternalInput").ap()
    w3 = nc.dram_tensor("w3", [128, DC * H0C * 128], F32, kind="ExternalInput").ap()
    vcat_in = nc.dram_tensor("vcat", [128, 80], F32, kind="ExternalInput").ap()
    wcat = nc.dram_tensor("wcat", [128, DC * 384], F32, kind="ExternalInput").ap()
    brow = nc.dram_tensor("brow", [1, 384], F32, kind="ExternalInput").ap()
    mcum = nc.dram_tensor("mcum", [128, 128], F32, kind="ExternalInput").ap()
    chi = nc.dram_tensor("chi", [128, 2], F32, kind="ExternalInput").ap()
    amask = nc.dram_tensor("amask", [128, 128], U8, kind="ExternalInput").ap()
    identb = nc.dram_tensor("identb", [128, 128], BF16, kind="ExternalInput").ap()
    identf = nc.dram_tensor("identf", [128, 128], F32, kind="ExternalInput").ap()
    wo0 = nc.dram_tensor("wo0", [128, D], F32, kind="ExternalInput").ap()
    wup = nc.dram_tensor("wup", [2048, 1024], F32, kind="ExternalInput").ap()
    wga = nc.dram_tensor("wga", [2048, 1024], F32, kind="ExternalInput").ap()
    wdo = nc.dram_tensor("wdo", [8192, 256], F32, kind="ExternalInput").ap()
    h128u = nc.dram_tensor("h128u", [128, 128], F32, kind="ExternalInput").ap()
    h128d = nc.dram_tensor("h128d", [128, 128], F32, kind="ExternalInput").ap()
    suug = nc.dram_tensor("suug", [128, 16], F32, kind="ExternalInput").ap()
    sudp = nc.dram_tensor("sudp", [128, 64], F32, kind="ExternalInput").ap()
    wsup = nc.dram_tensor("wsup", [1, 1024], F32, kind="ExternalInput").ap()
    wsga = nc.dram_tensor("wsga", [1, 1024], F32, kind="ExternalInput").ap()
    wsdo = nc.dram_tensor("wsdo", [1, 256], F32, kind="ExternalInput").ap()
    wqkv5 = nc.dram_tensor("wqkv5", [128, 12 * DC * 128], F32, kind="ExternalInput").ap()
    bqkv5 = nc.dram_tensor("bqkv5", [128, 12], F32, kind="ExternalInput").ap()
    cmask5 = nc.dram_tensor("cmask5", [128, 4 * 512], BF16, kind="ExternalInput").ap()
    wout5 = nc.dram_tensor("wout5", [128, DC * DC * 128], F32, kind="ExternalInput").ap()
    bout5 = nc.dram_tensor("bout5", [128, DC], F32, kind="ExternalInput").ap()
    qoh = nc.dram_tensor("qoh", [128, 4], F32, kind="ExternalInput").ap()
    boh = nc.dram_tensor("boh", [128, 2], F32, kind="ExternalInput").ap()
    xout = nc.dram_tensor("xout", [TOKS, D], F32, kind="ExternalOutput").ap()
    xdram = nc.dram_tensor("xdram", [128, DC * TOKS], F32).ap()
    uown2 = nc.dram_tensor("uown2", [2048, 128], F32)
    uall2 = nc.dram_tensor("uall2", [4 * 2048, 128], F32)
    pcown2 = nc.dram_tensor("pcown2", [16, 128], F32)
    pcall2 = nc.dram_tensor("pcall2", [4 * 16, 128], F32)
    wpown = nc.dram_tensor("wpown", [128, 2 * 8 * DC * 128 + 2 * 64 * 128], BF16)
    wpall = nc.dram_tensor("wpall", [NCORES * 128, 2 * 8 * DC * 128 + 2 * 64 * 128], BF16)
    xhownp = [nc.dram_tensor("xhown%d" % p, [128, DC * TP], BF16)
              for p in range(NPASS)]
    xhallp = [nc.dram_tensor("xhall%d" % p, [4 * 128, DC * TP], BF16)
              for p in range(NPASS)]
    o5own = nc.dram_tensor("o5own", [NCORES * 512, TOKS], BF16)
    o5all = nc.dram_tensor("o5all", [NCORES * 512, TOKS], BF16)
    RG = [list(range(NCORES))]
    RG4 = [[0, 1, 2, 3], [4, 5, 6, 7]]

    with ExitStack() as st:
        c = Ctx(nc, st)
        vb = {}
        vb["XB2"] = c.sbuf("XB2", [128, DC * TP], BF16)
        vb["XQr"] = c.sbuf("XQr", [128, 2 * TP], BF16)
        vb["XFR"] = c.sbuf("XFR", [128, 2 * TP], F32)
        vb["XFS"] = c.sbuf("XFS", [128, TP], F32)
        vb["XHB"] = c.sbuf("XHB", [128, DC * TP], BF16)
        vb["SROW"] = c.sbuf("SROW", [1, 384], F32)
        vb["onesb"] = c.sbuf("onesb", [128, 1], BF16)
        vb["ones1f"] = c.sbuf("ones1f", [1, 128], F32)
        vb["onesbb"] = c.sbuf("onesbb", [128, 128], BF16)
        vb["LROW"] = c.sbuf("LROW", [128, 4 * TP], F32)
        vb["vcat"] = c.sbuf("vcat_sb", [128, 80], F32)
        vb["hb3"] = c.sbuf("hb3", [128, DC], F32)
        vb["B1C"], vb["B2C"], B3C = 0, 32, 64
        XF = c.sbuf("XF", [128, DC * TP], F32)
        IDBs = c.sbuf("IDBs", [128, 128], BF16)
        IDFs = c.sbuf("IDFs", [128, 128], F32)
        QOH = c.sbuf("QOH", [128, 4], F32)
        BOH = c.sbuf("BOH", [128, 2], F32)
        vb["pS1"] = c.psum("pS1", [128, TP])
        vb["pS2"] = c.psum("pS2", [128, TP])
        vb["pR"] = c.psum("pR", [128, TP])
        vb["pM"] = c.psum("pM", [128, TP])
        ccs = c.sem("ccs")
        xst = c.sem("xst")
        xld = c.sem("xld")
        block = st.enter_context(nc.Block())

        @block.gpsimd
        def _(_g):
            cst = c.sem("cst")
            cst.inc(c.sy.dma_start(out=vb["vcat"][:, :], in_=vcat_in[:, :]), 16)
            cst.inc(c.sy.dma_start(out=IDBs[:, :], in_=identb[:, :]), 16)
            cst.inc(c.sy.dma_start(out=IDFs[:, :], in_=identf[:, :]), 16)
            cst.inc(c.sy.dma_start(out=QOH[:, :], in_=qoh[:, :]), 16)
            cst.inc(c.sy.dma_start(out=BOH[:, :], in_=boh[:, :]), 16)
            c.v.wait_ge(cst.h, cst.n)
            c.s.wait_ge(cst.h, cst.n)
            c.v.memset(vb["onesb"][:, :], 1.0)
            c.v.memset(vb["onesbb"][:, :], 1.0)
            ini0 = c.v.memset(vb["ones1f"][:, :], 1.0)
            ini = c.v.tensor_scalar(vb["hb3"][:, :], vb["vcat"][:, B3C:B3C + DC],
                                    0.5, None, ALU.mult)
            c.sem("lnV").inc(ini)

            with ExitStack() as stp:
                cp = Ctx.__new__(Ctx)
                cp.__dict__.update(c.__dict__)
                pvb = {}
                pvb["ZA"] = stp.enter_context(nc.sbuf_tensor("ZA", [128, 8192], F32))
                pvb["ZB"] = stp.enter_context(nc.sbuf_tensor("ZB", [128, 8192], F32))
                pvb["WOUT"] = stp.enter_context(nc.sbuf_tensor("WOUT", [128, 8192], BF16))
                pvb["ones1f"] = vb["ones1f"]
                HU = stp.enter_context(nc.sbuf_tensor("HU", [128, 128], F32))
                HDN = stp.enter_context(nc.sbuf_tensor("HDN", [128, 128], F32))
                SUU = stp.enter_context(nc.sbuf_tensor("SUU", [128, 16], F32))
                SUD = stp.enter_context(nc.sbuf_tensor("SUD", [128, 64], F32))
                WSU = stp.enter_context(nc.sbuf_tensor("WSU", [1, 1024], F32))
                WSG = stp.enter_context(nc.sbuf_tensor("WSG", [1, 1024], F32))
                WSD = stp.enter_context(nc.sbuf_tensor("WSD", [1, 256], F32))
                pvb["pH"] = [stp.enter_context(nc.psum_tensor("pH0", [128, 512], F32)),
                             stp.enter_context(nc.psum_tensor("pH1", [128, 512], F32))]
                pvb["pW"] = stp.enter_context(nc.psum_tensor("pWb", [128, 512], F32))
                for dst, srcap in [(HU, h128u), (HDN, h128d), (SUU, suug),
                                   (SUD, sudp), (WSU, wsup), (WSG, wsga), (WSD, wsdo)]:
                    cst.inc(c.sy.dma_start(out=dst[:, :], in_=srcap[:, :]), 16)
                c.t.wait_ge(cst.h, cst.n)
                c.v.wait_ge(cst.h, cst.n)
                c.sy.wait_ge(c.sem("lnV").h, c.sem("lnV").n)
                UPSPAN = 8 * DC * 128
                emit_had_transform(c, pvb, wup, wpown.ap()[:, 0:UPSPAN],
                                   16, 1024, HU, SUU, WSU, "up")
                emit_had_transform(c, pvb, wga, wpown.ap()[:, UPSPAN:2 * UPSPAN],
                                   16, 1024, HU, SUU, WSG, "ga")
                so = emit_had_transform(c, pvb, wdo,
                                        wpown.ap()[:, 2 * UPSPAN:2 * UPSPAN + 2 * 64 * 128],
                                        64, 256, HDN, SUD, WSD, "do")
                hdv = c.sem("hdv"); hpe = c.sem("hpe")
                for eng in (c.t, c.v, c.s, c.sy, c.g):
                    eng.wait_ge(so.h, so.n)
                    eng.wait_ge(hdv.h, hdv.n)
                    eng.wait_ge(hpe.h, hpe.n)

            with ExitStack() as stf:
                fvb = dict(vb)
                fvb["H1B"] = stf.enter_context(nc.sbuf_tensor("H1B", [128, H0C * TP], BF16))
                fvb["H2B"] = stf.enter_context(nc.sbuf_tensor("H2B", [128, H0C * TP], BF16))
                fvb["wbufs"] = [stf.enter_context(nc.sbuf_tensor("wb0", [128, H0C * 128], BF16)),
                                stf.enter_context(nc.sbuf_tensor("wb1", [128, H0C * 128], BF16))]
                fvb["TY"] = stf.enter_context(nc.sbuf_tensor("TY", [128, 2 * TP], F32))
                XT = stf.enter_context(nc.sbuf_tensor("XT", [128, 2 * D], F32))
                fvb["pAB"] = [stf.enter_context(nc.psum_tensor("pA", [128, TP], F32)),
                              stf.enter_context(nc.psum_tensor("pB", [128, TP], F32))]
                pT = [stf.enter_context(nc.psum_tensor("pT0", [128, 128], F32)),
                      stf.enter_context(nc.psum_tensor("pT1", [128, 128], F32))]
                tpe = c.sem("tpe"); tdv = c.sem("tdv"); evd = c.sem("evd")
                for p in range(NPASS):
                    tpe0, tdv0 = tpe.n, tdv.n
                    i = 0
                    for half in range(2):
                        c.sy.wait_ge(tpe.h, tpe0 + i)
                        xld.inc(c.sy.dma_start(
                            out=XT[:, :].rearrange("pp (s d) -> pp s d", s=2),
                            in_=xin.rearrange("(s pp) d -> pp s d", pp=128)
                            [:, p * NSUB + half * 2:p * NSUB + half * 2 + 2, :]), 16)
                        c.t.wait_ge(xld.h, xld.n)
                        if half == 0:
                            c.t.wait_ge(evd.h, evd.n)
                            c.t.wait_ge(xst.h, xst.n)
                        for sub2 in range(2):
                            sub = half * 2 + sub2
                            for ci in range(DC):
                                if i >= 2:
                                    c.t.wait_ge(tdv.h, tdv0 + i - 1)
                                tpe.inc(c.t.transpose(
                                    pT[i % 2][:, :],
                                    XT[:, sub2 * D + ci * 128:sub2 * D + (ci + 1) * 128],
                                    IDFs[:, :]))
                                c.v.wait_ge(tpe.h, tpe0 + i + 1)
                                tdv.inc(c.v.tensor_copy(
                                    XF[:, ci * TP + sub * 128:ci * TP + (sub + 1) * 128],
                                    pT[i % 2][:, :]))
                                i += 1
                    emit_ln_stats(c, fvb, None, p, p % 2, sbuf_xf=XF,
                                  pre=(tdv, tdv.n))
                    lnV = emit_ln_norm(c, fvb, None, p, p % 2, sbuf_xf=XF,
                                       xhb_free=(c.sem("evd"), c.sem("evd").n),
                                       pre=(tdv, tdv.n))
                    evd = emit_ffn0_pass(c, fvb, XF, fvb["XHB"], lnV, w1, w2, w3)
                    c.sy.wait_ge(evd.h, evd.n)
                    xst.inc(c.sy.dma_start(
                        out=xdram.rearrange("pp (cc t) -> pp cc t", cc=DC)
                        [:, :, p * TP:(p + 1) * TP],
                        in_=XF[:, :].rearrange("pp (cc t) -> pp cc t", cc=DC)), 16)
                for eng in (c.t, c.v, c.s, c.sy, c.g):
                    eng.wait_ge(evd.h, evd.n)
                    eng.wait_ge(c.sem("wfree").h, c.sem("wfree").n)
                    eng.wait_ge(xst.h, xst.n)


            st2 = ExitStack()
            QTall = st2.enter_context(nc.sbuf_tensor("QTall", [128, TOKS], BF16))
            OIT = st2.enter_context(nc.sbuf_tensor("OIT", [128, TOKS], F32))
            with ExitStack() as sta:
                avb = dict(vb)
                for nm, shape, dt in [("WCB", [128, DC * 384], BF16),
                                      ("BBC", [128, 384], F32),
                                      ("MCUM", [128, 128], F32),
                                      ("CHI", [128, 2], F32),
                                      ("AMASK", [128, 128], U8),
                                      ("ZEROB", [128, 128], BF16),
                                      ("P384", [128, 384], F32),
                                      ("SC64", [128, 256], F32),
                                      ("NCC", [128, 68], F32),
                                      ("NLB", [128, 64], F32),
                                      ("DELTA", [128, 64], F32),
                                      ("EM", [128, 64], F32),
                                      ("EP", [128, 64], F32),
                                      ("QTM", [128, 128], BF16),
                                      ("KTM", [128, 128], BF16),
                                      ("TKF", [128, 64], F32),
                                      ("VB", [128, 128], BF16),
                                      ("KTP", [128, 128], BF16),
                                      ("AB", [128, 128], BF16),
                                      ("USTG", [64, 32 * 128], F32),
                                      ("OISTG", [128, DC * 128], F32),
                                      ("OIF", [128, 128], F32),
                                      ("PCST", [64, 32], F32)]:
                    avb[nm] = sta.enter_context(nc.sbuf_tensor(nm, shape, dt))
                avb["IDB"] = IDBs
                avb["IDF"] = IDFs
                avb["QTall"] = QTall
                avb["pPP"] = sta.enter_context(nc.psum_tensor("pPP", [128, 448], F32))
                avb["pOX"] = sta.enter_context(nc.psum_tensor("pOX", [128, 194], F32))
                avb["pU"] = sta.enter_context(nc.psum_tensor("pU", [64, 256], F32))
                avb["pTQ"] = sta.enter_context(nc.psum_tensor("pTQ", [128, 256], BF16))
                cst.inc(c.sy.dma_start(out=avb["MCUM"][:, :], in_=mcum[:, :]), 16)
                cst.inc(c.sy.dma_start(out=avb["CHI"][:, :], in_=chi[:, :]), 16)
                cst.inc(c.sy.dma_start(out=avb["AMASK"][:, :], in_=amask[:, :]), 16)
                cstg = c.sem("cstg")
                cstg.inc(c.g.dma_start(out=avb["WCB"][:, :], in_=wcat[:, :]), 16)
                cst.inc(c.sy.dma_start(out=vb["SROW"][0:1, 0:384], in_=brow[:, :]), 16)
                dvI = c.sem("dvI")
                c.v.wait_ge(cst.h, cst.n)
                c.v.wait_ge(cstg.h, cstg.n)
                c.v.memset(avb["ZEROB"][:, :], 0.0)
                c.v.memset(QTall[:, :], 0.0)
                c.v.memset(avb["QTM"][:, :], 0.0)
                dvI.inc(c.v.memset(avb["KTM"][:, :], 0.0))
                c.t.wait_ge(dvI.h, dvI.n)
                c.t.wait_ge(c.sem("lnP").h, c.sem("lnP").n)
                bb2 = c.t.matmul(vb["pR"][:, 0:384], vb["ones1f"][:, :],
                                 vb["SROW"][0:1, 0:384], start=True, stop=True)
                c.sem("lnP").inc(bb2)
                c.v.wait_ge(c.sem("lnP").h, c.sem("lnP").n)
                ini2 = c.v.tensor_copy(avb["BBC"][:, :], vb["pR"][:, 0:384])
                c.sem("lnV").inc(ini2)
                emit_ln_stats(c, avb, xdram, 0, 0)
                emit_ln_stats(c, avb, xdram, 1, 1)
                emit_ln_norm(c, avb, xdram, 0, 0, qe=c.sy)
                prjP = c.sem("prjP")
                for p in range(NPASS):
                    for s in range(NSUB):
                        emit_gla_a_subtile(c, avb, avb["XHB"], p, s, 0)
                    if p + 1 < NPASS:
                        emit_ln_norm(c, avb, xdram, p + 1, (p + 1) % 2, qe=c.sy,
                                     xhb_free=(prjP, prjP.n))
                    if p + 2 < NPASS:
                        emit_ln_stats(c, avb, xdram, p + 2, p % 2)
                fina = c.sem("fina")
                c.v.wait_ge(c.sem("otv").h, c.sem("otv").n)
                otc = c.v.tensor_copy(OIT[:, :], avb["OISTG"][:, :])
                fina.inc(otc)
                c.sy.wait_ge(c.sem("uev").h, c.sem("uev").n)
                c.sy.wait_ge(c.sem("dvQ").h, c.sem("dvQ").n)
                ust = c.sem("ust")
                ust.inc(c.sy.dma_start(
                    out=uown2.ap()[:, :].rearrange("(j k) v -> k j v", k=64),
                    in_=avb["USTG"][:, :].rearrange("k (j v) -> k j v", v=128)), 16)
                ust.inc(c.sy.dma_start(
                    out=pcown2.ap()[:, :].rearrange("q (a j) -> (q a) j", a=4),
                    in_=avb["PCST"][:, :]), 16)
                c.g.wait_ge(ust.h, ust.n)
                ccs.inc(c.g.collective_compute(
                    "AllGather", ALU.bypass, replica_groups=RG4,
                    ins=[uown2.ap().opt()], outs=[uall2.ap().opt()]), 1)
                ccs.inc(c.g.collective_compute(
                    "AllGather", ALU.bypass, replica_groups=RG4,
                    ins=[pcown2.ap().opt()], outs=[pcall2.ap().opt()]), 1)
                AGU = ccs.n
                for eng in (c.t, c.v, c.s, c.sy, c.g):
                    eng.wait_ge(fina.h, fina.n)
                    eng.wait_ge(c.sem("tqv").h, c.sem("tqv").n)
                    eng.wait_ge(c.sem("sel").h, c.sem("sel").n)
                    eng.wait_ge(ust.h, ust.n)

            with ExitStack() as stb:
                SCAN = stb.enter_context(nc.sbuf_tensor("SCAN", [128, 129 * 128], F32))
                PCsb = stb.enter_context(nc.sbuf_tensor("PCsb", [128, 128], F32))
                OTall = stb.enter_context(nc.sbuf_tensor("OTall", [128, TOKS], BF16))
                WOB = stb.enter_context(nc.sbuf_tensor("WOB", [128, D], BF16))
                SDALL = stb.enter_context(nc.sbuf_tensor("SDALL", [128, 32 * 128], BF16))
                pOTs = [stb.enter_context(nc.psum_tensor("pOT", [128, 64], F32)),
                        stb.enter_context(nc.psum_tensor("pOT2", [128, 64], F32))]
                pA2 = stb.enter_context(nc.psum_tensor("pA2", [128, TP], F32))
                pB2 = stb.enter_context(nc.psum_tensor("pB2", [128, TP], F32))
                ld2 = c.sem("ld2")
                ld2.inc(c.g.dma_start(out=WOB[:, :], in_=wo0[:, :]), 16)
                c.g.wait_ge(ccs.h, AGU)
                for half in range(2):
                    ld2.inc(c.g.dma_start(
                        out=SCAN[half * 64:(half + 1) * 64, 128:129 * 128]
                        .rearrange("k (j2 v) -> k j2 v", v=128),
                        in_=uall2.ap().rearrange("(j2 k) v -> k j2 v", k=64)), 16)
                for half in range(2):
                    ld2.inc(c.g.dma_start(
                        out=PCsb[half * 64:(half + 1) * 64, :]
                        .rearrange("k (j2 j) -> k j2 j", j=32),
                        in_=bass.AP(pcall2.ap().tensor, 0,
                                    [[32, 64], [2048, 4], [1, 32]])), 16)
                ccs.inc(c.g.collective_compute(
                    "AllGather", ALU.bypass, replica_groups=RG,
                    ins=[wpown.ap().opt()], outs=[wpall.ap().opt()]), 1)
                AGW = ccs.n
                dvI2 = c.sem("dvI2")
                dvI2.inc(c.v.memset(SCAN[:, 0:128], 0.0))
                c.v.wait_ge(ld2.h, ld2.n)
                scn = c.sem("scn")
                view = SCAN[:, 128:].rearrange("p (j v) -> p v j", v=128)
                for vi in range(128):
                    inst = c.v.tensor_tensor_scan(view[:, vi, :], view[:, vi, :],
                                                  PCsb[:, :], 0.0, ALU.add, ALU.mult)
                scn.inc(inst)
                peI = c.sem("peI"); otv2 = c.sem("otv2")
                peY = c.sem("peY"); yev = c.sem("yev")
                sdc = c.sem("sdc")
                c.v.wait_ge(scn.h, scn.n)
                c.v.tensor_scalar(SDALL[:, :], SCAN[:, 0:4096], QOH[:, 0:1],
                                  None, ALU.mult)
                for w in range(1, 4):
                    inst = c.v.scalar_tensor_tensor(
                        SDALL[:, :], SCAN[:, w * 4096:(w + 1) * 4096],
                        QOH[:, w:w + 1], SDALL[:, :], ALU.mult, ALU.add)
                sdc.inc(inst)
                c.t.wait_ge(scn.h, scn.n)
                c.t.wait_ge(dvI2.h, dvI2.n)
                c.t.wait_ge(ld2.h, ld2.n)
                c.t.wait_ge(sdc.h, sdc.n)
                c.v.wait_ge(dvI2.h, dvI2.n)
                if True:
                    for p in range(NPASS):
                        c.sy.wait_ge(yev.h, yev.n)
                        c.sy.wait_ge(xst.h, xst.n)
                        xld.inc(c.sy.dma_start(
                            out=XF[:, :].rearrange("pp (cc t) -> pp cc t", cc=DC),
                            in_=xdram.rearrange("pp (cc t) -> pp cc t", cc=DC)
                            [:, :, p * TP:(p + 1) * TP]), 16)
                        otv20 = otv2.n
                        for s in range(NSUB):
                            si = p * NSUB + s
                            st_ = si * 128
                            for h in range(2):
                                jloc = 2 * si + h
                                k2 = 2 * s + h
                                pOT = pOTs[k2 % 2]
                                if k2 >= 2:
                                    c.t.wait_ge(otv2.h, otv20 + k2 - 1)
                                mm = c.t.matmul(pOT[:, 0:64],
                                                SDALL[:, jloc * 128:(jloc + 1) * 128],
                                                QTall[:, st_ + h * 64:st_ + (h + 1) * 64],
                                                start=True, stop=True)
                                peI.inc(mm)
                                c.v.wait_ge(peI.h, peI.n)
                                otv2.inc(c.v.tensor_tensor(
                                    OTall[:, st_ + h * 64:st_ + (h + 1) * 64],
                                    pOT[:, 0:64],
                                    OIT[:, st_ + h * 64:st_ + (h + 1) * 64], ALU.add))
                        c.t.wait_ge(otv2.h, otv2.n)
                        c.t.wait_ge(xld.h, xld.n)
                        for oc in range(DC):
                            pp = [pA2, pB2][oc % 2]
                            if oc >= 2:
                                c.t.wait_ge(yev.h, yev.n)
                            peY.inc(c.t.matmul(pp[:, :TP], WOB[:, oc * 128:(oc + 1) * 128],
                                               OTall[:, p * TP:(p + 1) * TP],
                                               start=True, stop=True))
                            c.v.wait_ge(peY.h, peY.n)
                            yev.inc(c.v.tensor_tensor(XF[:, oc * TP:(oc + 1) * TP],
                                                      pp[:, :TP],
                                                      XF[:, oc * TP:(oc + 1) * TP], ALU.add))
                        c.sy.wait_ge(yev.h, yev.n)
                        xst.inc(c.sy.dma_start(
                            out=xdram.rearrange("pp (cc t) -> pp cc t", cc=DC)
                            [:, :, p * TP:(p + 1) * TP],
                            in_=XF[:, :].rearrange("pp (cc t) -> pp cc t", cc=DC)), 16)
                        if p == 0:
                            c.g.wait_ge(xst.h, xst.n)
                            emit_ln_stats(c, vb, xdram, 0, 0)
                            emit_ln_norm(c, vb, xdram, 0, 0, qe=c.sy)
                        if p == 1:
                            c.g.wait_ge(xst.h, xst.n)
                            emit_ln_stats(c, vb, xdram, 1, 1)
                for eng in (c.t, c.v, c.s, c.sy, c.g):
                    eng.wait_ge(yev.h, yev.n)
                    eng.wait_ge(xst.h, xst.n)
                    eng.wait_ge(otv2.h, otv2.n)
            st2.close()

            with ExitStack() as stf:
                fvb = dict(vb)
                fvb["H1B"] = stf.enter_context(nc.sbuf_tensor("H1B3", [128, H0C * TP], BF16))
                fvb["H2B"] = stf.enter_context(nc.sbuf_tensor("H2B3", [128, H0C * TP], BF16))
                fvb["wbufs"] = [stf.enter_context(nc.sbuf_tensor("wb03", [128, H0C * 128], BF16)),
                                stf.enter_context(nc.sbuf_tensor("wb13", [128, H0C * 128], BF16)),
                                stf.enter_context(nc.sbuf_tensor("wb23", [128, H0C * 128], BF16))]
                fvb["TY"] = stf.enter_context(nc.sbuf_tensor("TY3", [128, 2 * TP], F32))
                fvb["pAB"] = [stf.enter_context(nc.psum_tensor("pA3", [128, TP], F32)),
                              stf.enter_context(nc.psum_tensor("pB3", [128, TP], F32))]
                evd = c.sem("evd")
                lnV = c.sem("lnV")
                for p in range(NPASS):
                    c.sy.wait_ge(evd.h, evd.n)
                    c.sy.wait_ge(xst.h, xst.n)
                    xld.inc(c.sy.dma_start(
                        out=XF[:, :].rearrange("pp (cc t) -> pp cc t", cc=DC),
                        in_=xdram.rearrange("pp (cc t) -> pp cc t", cc=DC)
                        [:, :, p * TP:(p + 1) * TP]), 16)

                    def hook3(p=p):
                        if p + 1 < NPASS:
                            emit_ln_norm(c, fvb, xdram, p + 1, (p + 1) % 2,
                                         qe=c.sy,
                                         xhb_free=(c.sem("evd"), c.sem("evd").n))
                        else:
                            emit_ln_norm(c, fvb, xdram, 0, 0, qe=c.sy,
                                         xhb_free=(c.sem("evd"), c.sem("evd").n))
                        if p + 2 < NPASS:
                            emit_ln_stats(c, fvb, xdram, p + 2, p % 2)
                    evd = emit_ffn0_pass(c, fvb, XF, fvb["XHB"], lnV, w1, w2, w3,
                                         hook=hook3, xf_wait=(xld, xld.n))
                    lnV = c.sem("lnV")
                    c.sy.wait_ge(evd.h, evd.n)
                    xst.inc(c.sy.dma_start(
                        out=xdram.rearrange("pp (cc t) -> pp cc t", cc=DC)
                        [:, :, p * TP:(p + 1) * TP],
                        in_=XF[:, :].rearrange("pp (cc t) -> pp cc t", cc=DC)), 16)
                    if p == 1:
                        c.g.wait_ge(xst.h, xst.n)
                        emit_ln_stats(c, fvb, xdram, 0, 0)
                    if p == 2:
                        c.g.wait_ge(xst.h, xst.n)
                        emit_ln_stats(c, fvb, xdram, 1, 1)
                for eng in (c.t, c.v, c.s, c.sy, c.g):
                    eng.wait_ge(evd.h, evd.n)
                    eng.wait_ge(c.sem("wfree").h, c.sem("wfree").n)
                    eng.wait_ge(xst.h, xst.n)

            def l1ffn_phase(tagsuffix, pre_ln=False, next_ln=False):
                with ExitStack() as stl:
                    GB = stl.enter_context(nc.sbuf_tensor("GB" + tagsuffix, [128, 64 * TP], BF16))
                    SGT = stl.enter_context(nc.sbuf_tensor("SGT" + tagsuffix, [128, TP], BF16))
                    wbu = [stl.enter_context(nc.sbuf_tensor("wbu0" + tagsuffix, [128, 2 * DC * 128], BF16)),
                           stl.enter_context(nc.sbuf_tensor("wbu1" + tagsuffix, [128, 2 * DC * 128], BF16))]
                    wbg = [stl.enter_context(nc.sbuf_tensor("wbg0" + tagsuffix, [128, 2 * DC * 128], BF16)),
                           stl.enter_context(nc.sbuf_tensor("wbg1" + tagsuffix, [128, 2 * DC * 128], BF16))]
                    wbd = [stl.enter_context(nc.sbuf_tensor("wbd0" + tagsuffix, [128, 32 * 128], BF16)),
                           stl.enter_context(nc.sbuf_tensor("wbd1" + tagsuffix, [128, 32 * 128], BF16))]
                    pG1 = stl.enter_context(nc.psum_tensor("pG1" + tagsuffix, [128, TP], F32))
                    pG2 = stl.enter_context(nc.psum_tensor("pG2" + tagsuffix, [128, TP], F32))
                    pY = [stl.enter_context(nc.psum_tensor("pY0" + tagsuffix, [128, TP], F32)),
                          stl.enter_context(nc.psum_tensor("pY1" + tagsuffix, [128, TP], F32))]
                    IC = 64
                    UPSPAN = 8 * DC * 128
                    wfree = c.sem("wfree"); gvd = c.sem("gvd"); sil = c.sem("sil")
                    yev = c.sem("yevL"); wldu = c.sem("wldu"); wldg = c.sem("wldg")
                    wldd = c.sem("wldd")
                    gpair = [(pG1, pG2), (vb["pR"], vb["pM"])]
                    c.g.wait_ge(ccs.h, AGW)

                    def wsrc(kind, hc):
                        r = hc >> 3
                        ocl = hc & 7
                        if kind == 0:
                            cs = ocl * (DC * 128)
                        elif kind == 1:
                            cs = UPSPAN + ocl * (DC * 128)
                        else:
                            r = hc >> 1
                            ocl = hc & 1
                            cs = 2 * UPSPAN + ocl * (64 * 128)
                        return wpall.ap()[r * 128:(r + 1) * 128,
                                          cs:cs + (DC * 128 if kind < 2 else 64 * 128)]

                    def wsrc2(kind, hp):
                        hc = 2 * hp
                        r = hc >> 3
                        cs = (0 if kind == 0 else UPSPAN) + (hc & 7) * (DC * 128)
                        return wpall.ap()[r * 128:(r + 1) * 128, cs:cs + 2 * DC * 128]

                    if not pre_ln:
                        emit_ln_stats(c, vb, xdram, 0, 0)
                        emit_ln_stats(c, vb, xdram, 1, 1)
                        emit_ln_norm(c, vb, xdram, 0, 0, qe=c.g)
                    for p in range(NPASS):
                        c.sy.wait_ge(yev.h, yev.n)
                        c.sy.wait_ge(xst.h, xst.n)
                        xld.inc(c.sy.dma_start(
                            out=XF[:, :].rearrange("pp (cc t) -> pp cc t", cc=DC),
                            in_=xdram.rearrange("pp (cc t) -> pp cc t", cc=DC)
                            [:, :, p * TP:(p + 1) * TP]), 16)
                        lnV = c.sem("lnV")
                        wfree0, gvd0, sil0 = wfree.n, gvd.n, sil.n
                        wldu0, wldg0 = wldu.n, wldg.n
                        if not hasattr(c, "wlast"):
                            c.wlast = {}
                        for hp in range(IC // 2):
                            bu = wbu[hp % 2]; bg = wbg[hp % 2]
                            if id(bu) in c.wlast:
                                c.sy.wait_ge(wfree.h, c.wlast[id(bu)])
                            wldu.inc(c.sy.dma_start(out=bu[:, :], in_=wsrc2(0, hp)), 16)
                            wldg.inc(c.sy.dma_start(out=bg[:, :], in_=wsrc2(1, hp)), 16)
                            for k in range(2):
                                hc = 2 * hp + k
                                ppu, ppg = gpair[hc % 2]
                                if k == 0:
                                    c.t.wait_ge(wldu.h, wldu0 + 16 * (hp + 1))
                                    c.t.wait_ge(wldg.h, wldg0 + 16 * (hp + 1))
                                if hc == 0:
                                    c.t.wait_ge(lnV.h, lnV.n)
                                if hc >= 2:
                                    c.t.wait_ge(gvd.h, gvd0 + hc - 1)
                                else:
                                    c.t.wait_ge(gvd.h, gvd0)
                                ko = k * DC * 128
                                for ci in range(DC):
                                    c.t.matmul(ppu[:, :TP],
                                               bu[:, ko + ci * 128:ko + (ci + 1) * 128],
                                               vb["XHB"][:, ci * TP:(ci + 1) * TP],
                                               start=(ci == 0), stop=(ci == DC - 1))
                                for ci in range(DC):
                                    inst = c.t.matmul(ppg[:, :TP],
                                                      bg[:, ko + ci * 128:ko + (ci + 1) * 128],
                                                      vb["XHB"][:, ci * TP:(ci + 1) * TP],
                                                      start=(ci == 0), stop=(ci == DC - 1))
                                wfree.inc(inst)
                                c.s.wait_ge(wfree.h, wfree0 + hc + 1)
                                c.s.wait_ge(gvd.h, gvd0 + hc)
                                sil.inc(c.s.activation(SGT[:, :], ppg[:, :TP], AF.Silu))
                                c.v.wait_ge(sil.h, sil0 + hc + 1)
                                gvd.inc(c.v.tensor_tensor(GB[:, hc * TP:(hc + 1) * TP],
                                                          ppu[:, :TP], SGT[:, :], ALU.mult))
                            c.wlast[id(bu)] = wfree.n
                        if p + 1 < NPASS:
                            emit_ln_norm(c, vb, xdram, p + 1, (p + 1) % 2, qe=c.g,
                                         xhb_free=(wfree, wfree.n))
                        elif next_ln:
                            emit_ln_norm(c, vb, xdram, 0, 0, qe=c.g,
                                         xhb_free=(wfree, wfree.n))
                        if p + 2 < NPASS:
                            emit_ln_stats(c, vb, xdram, p + 2, p % 2)
                        wldds = c.sem("wldds")
                        wldd0, wldds0, yev0 = wldd.n, wldds.n, yev.n
                        dh = c.sem("dh")
                        dh0 = dh.n
                        wfree0d = wfree.n
                        for dc in range(DC):
                            ws = wsrc(2, dc)
                            QW = 16 * 128
                            for q in range(4):
                                qe_ = [c.sy, c.s][q % 2]
                                half = q >> 1
                                bd = wbd[half]
                                if dc >= 1:
                                    if half == 0:
                                        qe_.wait_ge(dh.h, dh0 + dc)
                                    else:
                                        qe_.wait_ge(wfree.h, wfree0d + dc)
                                sem_ = [wldd, wldds][q % 2]
                                sem_.inc(qe_.dma_start(
                                    out=bd[:, (q & 1) * QW:(q & 1) * QW + QW],
                                    in_=ws[:, q * QW:(q + 1) * QW]), 16)
                            pp = pY[dc % 2]
                            if dc == 0:
                                c.t.wait_ge(gvd.h, gvd.n)
                            if dc >= 2:
                                c.t.wait_ge(yev.h, yev0 + dc - 1)
                            for fc in range(64):
                                q = fc // 16
                                bd = wbd[q >> 1]
                                if fc % 16 == 0:
                                    sem_ = [wldd, wldds][q % 2]
                                    base_ = [wldd0, wldds0][q % 2]
                                    c.t.wait_ge(sem_.h,
                                                base_ + 16 * (2 * dc + (q >> 1) + 1))
                                inst = c.t.matmul(
                                    pp[:, :TP],
                                    bd[:, ((fc % 32) // 16) * QW
                                       + (fc % 16) * 128:((fc % 32) // 16) * QW
                                       + (fc % 16) * 128 + 128],
                                    GB[:, fc * TP:(fc + 1) * TP],
                                    start=(fc == 0), stop=(fc == 63))
                                if fc == 31:
                                    dh.inc(inst)
                            wfree.inc(inst)
                            c.v.wait_ge(wfree.h, wfree.n)
                            if dc == 0:
                                c.v.wait_ge(xld.h, xld.n)
                            yev.inc(c.v.tensor_tensor(XF[:, dc * TP:(dc + 1) * TP],
                                                      pp[:, :TP],
                                                      XF[:, dc * TP:(dc + 1) * TP], ALU.add))
                        c.sy.wait_ge(yev.h, yev.n)
                        xst.inc(c.sy.dma_start(
                            out=xdram.rearrange("pp (cc t) -> pp cc t", cc=DC)
                            [:, :, p * TP:(p + 1) * TP],
                            in_=XF[:, :].rearrange("pp (cc t) -> pp cc t", cc=DC)), 16)
                        if next_ln and p == 1:
                            c.g.wait_ge(xst.h, xst.n)
                            emit_ln_stats(c, vb, xdram, 0, 0)
                        if next_ln and p == 2:
                            c.g.wait_ge(xst.h, xst.n)
                            emit_ln_stats(c, vb, xdram, 1, 1)
                    for eng in (c.t, c.v, c.s, c.sy, c.g):
                        eng.wait_ge(yev.h, yev.n)
                        eng.wait_ge(c.sem("wfree").h, c.sem("wfree").n)
                        eng.wait_ge(c.sem("gvd").h, c.sem("gvd").n)
                        eng.wait_ge(xst.h, xst.n)

            l1ffn_phase("a", pre_ln=True, next_ln=True)

            AGXp = []
            for p in range(NPASS):
                c.sy.wait_ge(c.sem("lnV").h, c.sem("lnV").n)
                xst.inc(c.sy.dma_start(
                    out=xhownp[p].ap()[:, :], in_=vb["XHB"][:, :]), 16)
                c.g.wait_ge(xst.h, xst.n)
                ccs.inc(c.g.collective_compute(
                    "AllGather", ALU.bypass, replica_groups=RG4,
                    ins=[xhownp[p].ap().opt()], outs=[xhallp[p].ap().opt()]), 1)
                AGXp.append(ccs.n)
                if p + 1 < NPASS:
                    emit_ln_norm(c, vb, xdram, p + 1, (p + 1) % 2, qe=c.sy,
                                 xhb_free=(xst, xst.n))
                if p + 2 < NPASS:
                    emit_ln_stats(c, vb, xdram, p + 2, p % 2)

            with ExitStack() as stt:
                WB5 = stt.enter_context(nc.sbuf_tensor("WB5", [128, 12 * DC * 128], BF16))
                BQ5 = stt.enter_context(nc.sbuf_tensor("BQ5", [128, 12], F32))
                CM5 = stt.enter_context(nc.sbuf_tensor("CM5", [128, 4 * 512], BF16))
                Q5 = stt.enter_context(nc.sbuf_tensor("Q5", [128, SEQ], BF16))
                Kt5 = stt.enter_context(nc.sbuf_tensor("Kt5", [128, SEQ], BF16))
                VT5 = stt.enter_context(nc.sbuf_tensor("VT5", [128, 64 * 129], BF16))
                VF5 = stt.enter_context(nc.sbuf_tensor("VF5", [128, TP], BF16))
                PT5 = [stt.enter_context(nc.sbuf_tensor("PT5a", [128, TP], BF16)),
                       stt.enter_context(nc.sbuf_tensor("PT5b", [128, TP], BF16)),
                       stt.enter_context(nc.sbuf_tensor("PT5c", [128, TP], BF16)),
                       stt.enter_context(nc.sbuf_tensor("PT5d", [128, TP], BF16))]
                OB5 = stt.enter_context(nc.sbuf_tensor("OB5", [128, 128], BF16))
                RC5 = stt.enter_context(nc.sbuf_tensor("RC5", [128, 1], F32))
                OST5 = stt.enter_context(nc.sbuf_tensor("OST5", [128, TP], BF16))
                XB5 = vb["XB2"]
                pS5 = [vb["pS1"], vb["pS2"],
                       stt.enter_context(nc.psum_tensor("pO5a", [128, TP], F32))]
                NS5 = len(pS5)
                pO5 = [vb["pR"][:, 0:129], vb["pM"][:, 0:129],
                       stt.enter_context(nc.psum_tensor("pO5b", [128, TP], F32))
                       [:, 0:129],
                       stt.enter_context(nc.psum_tensor("pO5c", [128, TP], F32))
                       [:, 0:129]]
                pTR5 = stt.enter_context(nc.psum_tensor("pTR5", [128, 128], BF16))
                cstg = c.sem("cstg")
                cstg.inc(c.g.dma_start(out=WB5[:, :], in_=wqkv5[:, :]), 16)
                cst.inc(c.sy.dma_start(out=BQ5[:, :], in_=bqkv5[:, :]), 16)
                cst.inc(c.sy.dma_start(out=CM5[:, :], in_=cmask5[:, :]), 16)
                dvi5 = c.sem("dvi5")
                dvi5.inc(c.v.memset(VT5[:, :], 1.0))
                peQ = c.sem("peQ"); qkv = c.sem("qkv"); peV = c.sem("peV")
                vtd = c.sem("vtd"); peS = c.sem("peS"); scP = c.sem("scP")
                msk = c.sem("msk"); peA = c.sem("peA"); onrm = c.sem("onrm")
                peOT = c.sem("peOT5"); ost5 = c.sem("ost5"); fst5 = c.sem("fst5")
                c.t.wait_ge(cst.h, cst.n)
                c.t.wait_ge(c.sem("cstg").h, c.sem("cstg").n)
                c.t.wait_ge(c.sem("lnS").h, c.sem("lnS").n)
                c.t.wait_ge(c.sem("lnV").h, c.sem("lnV").n)
                c.v.wait_ge(cst.h, cst.n)
                c.sy.wait_ge(c.sem("lnP").h, c.sem("lnP").n)
                NT = SEQ // TP
                TPORD = [k + 4 * u for k in range(NPASS) for u in range(4)]
                qkv5b0 = qkv.n
                jg = 0
                if True:
                    for hi in range(4):
                        for ki, tp in enumerate(TPORD):
                            ti = hi * NT + ki
                            xbuf = [vb["XHB"], XB5][ti % 2]
                            if ti >= 2:
                                c.sy.wait_ge(peQ.h, 3 * (ti - 1))
                            c.sy.wait_ge(ccs.h, AGXp[tp & 3])
                            xld.inc(c.sy.dma_start(
                                out=xbuf[:, :],
                                in_=xhallp[tp & 3].ap()
                                [(tp >> 2) * 128:(tp >> 2) * 128 + 128, :]), 16)
                            c.t.wait_ge(xld.h, xld.n)
                            for j in range(3):
                                pp = pS5[jg % NS5]
                                wslice = WB5[:, (hi * 3 + j) * DC * 128:(hi * 3 + j + 1) * DC * 128]
                                c.t.wait_ge(qkv.h, qkv5b0 + max(0, jg - NS5 + 1))
                                c.t.wait_ge(scP.h, scP.n)
                                jg += 1
                                for ci in range(DC):
                                    inst = c.t.matmul(pp[:, :TP],
                                                      wslice[:, ci * 128:(ci + 1) * 128],
                                                      xbuf[:, ci * TP:(ci + 1) * TP],
                                                      start=(ci == 0), stop=(ci == DC - 1))
                                peQ.inc(inst)
                                c.v.wait_ge(peQ.h, peQ.n)
                                c.v.wait_ge(peS.h, peS.n)
                                c.v.wait_ge(peA.h, peA.n)
                                bcol = BQ5[:, hi * 3 + j:hi * 3 + j + 1]
                                if j == 0:
                                    qkv.inc(c.v.tensor_scalar(Q5[:, tp * TP:(tp + 1) * TP],
                                                              pp[:, :TP], bcol, None, ALU.add))
                                elif j == 1:
                                    qkv.inc(c.v.tensor_scalar(Kt5[:, tp * TP:(tp + 1) * TP],
                                                              pp[:, :TP], bcol, None, ALU.add))
                                else:
                                    c.v.wait_ge(peV.h, peV.n)
                                    qkv.inc(c.v.tensor_scalar(VF5[:, :], pp[:, :TP],
                                                              bcol, None, ALU.add))
                                    for u in range(4):
                                        kt = tp * 4 + u
                                        c.t.wait_ge(qkv.h, qkv.n)
                                        c.t.wait_ge(vtd.h, vtd.n)
                                        peV.inc(c.t.transpose(pTR5[:, :],
                                                              VF5[:, u * 128:(u + 1) * 128],
                                                              IDBs[:, :]))
                                        c.v.wait_ge(peV.h, peV.n)
                                        vtd.inc(c.v.tensor_copy(
                                            VT5[:, kt * 129:kt * 129 + 128], pTR5[:, :]))
                        for qs in range(NT):
                            nkc = 4 * qs + 4
                            scP0, peA0, msk0 = scP.n, peA.n, msk.n
                            for kk in range(nkc + 1):
                                if kk < nkc:
                                    kc = kk
                                    pp = pS5[kc % NS5]
                                    pt = PT5[kc % len(PT5)]
                                    if kc == 0:
                                        c.t.wait_ge(qkv.h, qkv.n)
                                        c.t.wait_ge(vtd.h, vtd.n)
                                    if kc >= NS5:
                                        c.t.wait_ge(scP.h, scP0 + kc - NS5 + 1)
                                    else:
                                        c.t.wait_ge(scP.h, scP0)
                                    peS.inc(c.t.matmul(
                                        pp[:, :TP], Kt5[:, kc * 128:(kc + 1) * 128],
                                        Q5[:, qs * TP:(qs + 1) * TP],
                                        start=True, stop=True))
                                    c.s.wait_ge(peS.h, peS.n)
                                    if kc >= len(PT5):
                                        c.s.wait_ge(peA.h, peA0 + kc - len(PT5) + 1)
                                    else:
                                        c.s.wait_ge(peA.h, peA0)
                                    scP.inc(c.s.activation(pt[:, :], pp[:, :TP], AF.Exp,
                                                           scale=f32(1.0 / np.sqrt(128.0)),
                                                           bias=f32(-16.0)))
                                    if kc >= 4 * qs:
                                        m = kc - 4 * qs
                                        c.v.wait_ge(scP.h, scP.n)
                                        msk.inc(c.v.tensor_tensor(
                                            pt[:, :], pt[:, :],
                                            CM5[:, m * 512:(m + 1) * 512], ALU.mult))
                                if kk >= 1:
                                    kc = kk - 1
                                    pt = PT5[kc % len(PT5)]
                                    c.t.wait_ge(scP.h, scP0 + kc + 1)
                                    if kc >= 4 * qs:
                                        c.t.wait_ge(msk.h, msk0 + kc - 4 * qs + 1)
                                    if kc == 0:
                                        c.t.wait_ge(onrm.h, onrm.n)
                                    qc0 = max(0, kc - 4 * qs)
                                    for qc in range(qc0, 4):
                                        inst = c.t.matmul(pO5[qc][:, 0:129],
                                                          pt[:, qc * 128:(qc + 1) * 128],
                                                          VT5[:, kc * 129:(kc + 1) * 129],
                                                          start=(kc == 0),
                                                          stop=(kc == 4 * qs + qc))
                                    peA.inc(inst)
                            for qc in range(4):
                                c.v.wait_ge(peA.h, peA.n)
                                c.v.reciprocal(RC5[:, :], pO5[qc][:, 128:129])
                                c.v.wait_ge(peOT.h, peOT.n)
                                onrm.inc(c.v.tensor_scalar(OB5[:, :], pO5[qc][:, 0:128],
                                                           RC5[:, :], None, ALU.mult))
                                c.t.wait_ge(onrm.h, onrm.n)
                                c.t.wait_ge(ost5.h, ost5.n)
                                peOT.inc(c.t.transpose(pTR5[:, :], OB5[:, :], IDBs[:, :]))
                                c.v.wait_ge(peOT.h, peOT.n)
                                c.v.wait_ge(fst5.h, fst5.n)
                                ost5.inc(c.v.tensor_copy(OST5[:, qc * 128:(qc + 1) * 128],
                                                         pTR5[:, :]))
                            c.sy.wait_ge(ost5.h, ost5.n)
                            j5 = qs >> 2
                            for slot in (j5, j5 + 4):
                                r0 = slot * 512 + hi * 128
                                fst5.inc(c.sy.dma_start(
                                    out=o5own.ap()[r0:r0 + 128,
                                                   (qs & 3) * TP:((qs & 3) + 1) * TP],
                                    in_=OST5[:, :]), 16)
                c.g.wait_ge(fst5.h, fst5.n)
                with allow_ici_all_to_all():
                    ccs.inc(c.g.collective_compute(
                        "AllToAll", ALU.bypass, replica_groups=RG,
                        ins=[o5own.ap().opt()], outs=[o5all.ap().opt()]), 1)
                AGO = ccs.n
                for eng in (c.t, c.v, c.s, c.sy, c.g):
                    eng.wait_ge(ccs.h, AGO)
                    eng.wait_ge(ost5.h, ost5.n)
                    eng.wait_ge(c.sem("scP").h, c.sem("scP").n)
                    eng.wait_ge(c.sem("msk").h, c.sem("msk").n)
                    eng.wait_ge(c.sem("onrm").h, c.sem("onrm").n)
                    eng.wait_ge(c.sem("qkv").h, c.sem("qkv").n)

            with ExitStack() as stc:
                cvb = dict(vb)
                cvb["wbufs"] = [stc.enter_context(nc.sbuf_tensor("wb5c0", [128, DC * 128], BF16)),
                                stc.enter_context(nc.sbuf_tensor("wb5c1", [128, DC * 128], BF16)),
                                stc.enter_context(nc.sbuf_tensor("wb5c2", [128, DC * 128], BF16))]
                cvb["pAB"] = [stc.enter_context(nc.psum_tensor("pA5c", [128, TP], F32)),
                              stc.enter_context(nc.psum_tensor("pB5c", [128, TP], F32))]
                OBT = stc.enter_context(nc.sbuf_tensor("OBT", [128, DC * TP], BF16))
                OBT2 = stc.enter_context(nc.sbuf_tensor("OBT2", [128, NH * 1024], BF16))
                BO5 = stc.enter_context(nc.sbuf_tensor("BO5", [128, DC], F32))
                cst.inc(c.sy.dma_start(out=BO5[:, :], in_=bout5[:, :]), 16)
                c.v.wait_ge(cst.h, cst.n)
                yev = c.sem("evd")
                osel = c.sem("osel")
                if True:
                    for p in range(NPASS):
                        c.sy.wait_ge(yev.h, yev.n)
                        c.sy.wait_ge(xst.h, xst.n)
                        xld.inc(c.sy.dma_start(
                            out=XF[:, :].rearrange("pp (cc t) -> pp cc t", cc=DC),
                            in_=xdram.rearrange("pp (cc t) -> pp cc t", cc=DC)
                            [:, :, p * TP:(p + 1) * TP]), 16)
                        c.sy.wait_ge(osel.h, osel.n)
                        for cand in range(2):
                            xld.inc(c.sy.dma_start(
                                out=OBT2[:, :].rearrange(
                                    "pp (h cand t) -> pp h cand t", h=NH, cand=2)
                                [:, :, cand, :],
                                in_=bass.AP(o5all.ap().tensor,
                                            cand * 2048 * TOKS + p * TP,
                                            [[TOKS, 128], [128 * TOKS, 16],
                                             [1, TP]])), 16)
                        c.v.wait_ge(xld.h, xld.n)
                        c.v.wait_ge(c.sem("wfree").h, c.sem("wfree").n)
                        for h in range(NH):
                            c.v.tensor_scalar(
                                OBT[:, h * TP:(h + 1) * TP],
                                OBT2[:, h * 1024:h * 1024 + TP],
                                BOH[:, 0:1], None, ALU.mult)
                            inst = c.v.scalar_tensor_tensor(
                                OBT[:, h * TP:(h + 1) * TP],
                                OBT2[:, h * 1024 + TP:h * 1024 + 2 * TP],
                                BOH[:, 1:2], OBT[:, h * TP:(h + 1) * TP],
                                ALU.mult, ALU.add)
                        osel.inc(inst)
                        xw = (osel, osel.n)

                        def evac5(oc, pp, wfree, wfree0, evd):
                            c.v.wait_ge(wfree.h, wfree0 + oc + 1)
                            inst = c.v.scalar_tensor_tensor(
                                XF[:, oc * TP:(oc + 1) * TP], pp[:, :TP],
                                BO5[:, oc:oc + 1], XF[:, oc * TP:(oc + 1) * TP],
                                ALU.add, ALU.add)
                            evd.inc(inst)

                        emit_wstream_mm(c, cvb, wout5, DC, DC,
                                        lambda ci: OBT[:, ci * TP:(ci + 1) * TP],
                                        evac5, rhs_wait=xw)
                        c.sy.wait_ge(yev.h, yev.n)
                        xst.inc(c.sy.dma_start(
                            out=xdram.rearrange("pp (cc t) -> pp cc t", cc=DC)
                            [:, :, p * TP:(p + 1) * TP],
                            in_=XF[:, :].rearrange("pp (cc t) -> pp cc t", cc=DC)), 16)
                        if p == 0:
                            c.g.wait_ge(xst.h, xst.n)
                            emit_ln_stats(c, vb, xdram, 0, 0)
                        if p == 1:
                            c.g.wait_ge(xst.h, xst.n)
                            emit_ln_stats(c, vb, xdram, 1, 1)
                            emit_ln_norm(c, vb, xdram, 0, 0, qe=c.s)
                for eng in (c.t, c.v, c.s, c.sy, c.g):
                    eng.wait_ge(c.sem("evd").h, c.sem("evd").n)
                    eng.wait_ge(c.sem("wfree").h, c.sem("wfree").n)
                    eng.wait_ge(xst.h, xst.n)

            l1ffn_phase("b", pre_ln=True)

            with ExitStack() as sto:
                XO = sto.enter_context(nc.sbuf_tensor("XO", [128, NSUB * D], F32))
                pT6 = [sto.enter_context(nc.psum_tensor("pT6a", [128, 128], F32)),
                       sto.enter_context(nc.psum_tensor("pT6b", [128, 128], F32))]
                tpe6 = c.sem("tpe6"); tdv6 = c.sem("tdv6"); fin6 = c.sem("fin6")
                for p in range(NPASS):
                    c.sy.wait_ge(xst.h, xst.n)
                    xld.inc(c.sy.dma_start(
                        out=XF[:, :].rearrange("pp (cc t) -> pp cc t", cc=DC),
                        in_=xdram.rearrange("pp (cc t) -> pp cc t", cc=DC)
                        [:, :, p * TP:(p + 1) * TP]), 16)
                    c.t.wait_ge(xld.h, xld.n)
                    c.t.wait_ge(fin6.h, fin6.n)
                    i6 = 0
                    tpe60, tdv60 = tpe6.n, tdv6.n
                    for sub in range(NSUB):
                        for ci in range(DC):
                            if i6 >= 2:
                                c.t.wait_ge(tdv6.h, tdv60 + i6 - 1)
                            tpe6.inc(c.t.transpose(
                                pT6[i6 % 2][:, :],
                                XF[:, ci * TP + sub * 128:ci * TP + (sub + 1) * 128],
                                IDFs[:, :]))
                            c.v.wait_ge(tpe6.h, tpe60 + i6 + 1)
                            tdv6.inc(c.v.tensor_copy(
                                XO[:, sub * D + ci * 128:sub * D + (ci + 1) * 128],
                                pT6[i6 % 2][:, :]))
                            i6 += 1
                    c.sy.wait_ge(tdv6.h, tdv6.n)
                    fin6.inc(c.sy.dma_start(
                        out=xout.rearrange("(s pp) d -> pp s d", pp=128)
                        [:, p * NSUB:(p + 1) * NSUB, :],
                        in_=XO[:, :].rearrange("pp (s d) -> pp s d", s=NSUB)), 16)
                c.g.wait_ge(fin6.h, fin6.n)
                c.sy.wait_ge(fin6.h, fin6.n)

    return nc


def fused_inmaps(ins):
    x0 = np.ascontiguousarray(ins["L0_attn_x"].reshape(-1, D))
    eye_f = np.eye(128, dtype=np.float32)
    eye_b = np.eye(128).astype(BF)
    w1t = wtile(ins["L0_ffn_W1"]); w2t = wtile(ins["L0_ffn_W2"]); w3t = wtile(ins["L0_ffn_W3"])
    vcat_ffn = np.concatenate([colchunks(ins["L0_ffn_b1"]), colchunks(ins["L0_ffn_b2"]),
                               colchunks(ins["L0_ffn_b3"])], 1).astype(np.float32)
    WC = np.concatenate([ins["L0_attn_Wq"], ins["L0_attn_Wk"], ins["L0_attn_Wv"],
                         ins["L0_attn_Wb"], ins["L0_attn_Wd"]], 1)
    wcat = np.ascontiguousarray(np.transpose(WC.reshape(DC, 128, 384), (1, 0, 2)).reshape(128, -1))
    brow = np.concatenate([np.zeros(256, np.float32), ins["L0_attn_bb"],
                           ins["L0_attn_bd"]]).reshape(1, 384).astype(np.float32)
    tt = np.arange(128)
    mcum = ((tt[:, None] // 64 == tt[None, :] // 64) &
            (tt[:, None] % 64 <= tt[None, :] % 64)).astype(np.float32)
    chi = np.stack([(tt // 64 == 0), (tt // 64 == 1)], 1).astype(np.float32)
    amask = ((tt[:, None] // 64 == tt[None, :] // 64) &
             (tt[:, None] <= tt[None, :])).astype(np.uint8)
    H = _hadamard128()
    h128u = H / np.float32(np.sqrt(2048.0))
    h128d = H * np.float32(0.5 / np.sqrt(8192.0))
    suug_ = colchunks(ins["L1_ffn_SU_ug"])
    sudp_ = colchunks(ins["L1_ffn_SU_dp"])
    jj = np.arange(TP)
    cmask = np.concatenate(
        [((128 * m + tt[:, None]) <= jj[None, :]).astype(BF) for m in range(4)], 1)
    wout_t = wtile(ins["L1_attn_w_out"])
    bo = colchunks(ins["L1_attn_b_out"])
    ims = []
    for cix in range(NCORES):
        b = cix >> 2
        blk = cix & 3
        wblocks = []
        bcols = np.zeros((128, 12), np.float32)
        for l in range(4):
            h = blk * 4 + l
            for j in range(3):
                rows = ins["L1_attn_w_qkv"][j * D + h * 128:j * D + (h + 1) * 128, :]
                wblocks.append(wtile(rows))
                bcols[:, l * 3 + j] = ins["L1_attn_b_qkv"][j * D + h * 128:j * D + (h + 1) * 128]
        qoh_ = np.zeros((128, 4), np.float32)
        qoh_[:, blk] = 1.0
        boh_ = np.zeros((128, 2), np.float32)
        boh_[:, b] = 1.0
        ims.append(dict(
            xin=np.ascontiguousarray(x0[cix * TOKS:(cix + 1) * TOKS]),
            w1=w1t, w2=w2t, w3=w3t, vcat=vcat_ffn, wcat=wcat, brow=brow,
            mcum=mcum, chi=chi, amask=amask, identb=eye_b, identf=eye_f,
            wo0=ins["L0_attn_Wo"],
            wup=np.ascontiguousarray(ins["L1_ffn_W_up"][cix * 1024:(cix + 1) * 1024, :].T),
            wga=np.ascontiguousarray(ins["L1_ffn_W_gate"][cix * 1024:(cix + 1) * 1024, :].T),
            wdo=np.ascontiguousarray(ins["L1_ffn_W_down"][cix * 256:(cix + 1) * 256, :].T),
            h128u=h128u, h128d=h128d, suug=suug_, sudp=sudp_,
            wsup=np.ascontiguousarray(ins["L1_ffn_Wscale_ug"][cix * 1024:(cix + 1) * 1024].reshape(1, -1)),
            wsga=np.ascontiguousarray(ins["L1_ffn_Wscale_ug"][8192 + cix * 1024:8192 + (cix + 1) * 1024].reshape(1, -1)),
            wsdo=np.ascontiguousarray(ins["L1_ffn_Wscale_dp"][cix * 256:(cix + 1) * 256].reshape(1, -1)),
            wqkv5=np.concatenate(wblocks, 1), bqkv5=bcols, cmask5=cmask,
            wout5=wout_t, bout5=bo, qoh=qoh_, boh=boh_))
    return ims


def kernel(**inputs):
    ins = {k: np.asarray(v, dtype=np.float32) for k, v in inputs.items()}
    nc = build_fused()
    res = run_bass_kernel_spmd(nc, fused_inmaps(ins), core_ids=list(range(NCORES)))
    out = np.concatenate([np.asarray(res.results[i]["xout"]) for i in range(NCORES)], 0)
    return np.ascontiguousarray(out.reshape(2, SEQ, D)).astype(np.float32)


def _hadamard128():
    i = np.arange(128)
    pc = np.array([bin(a & b).count("1") for a in i for b in i]).reshape(128, 128)
    return ((-1.0) ** pc).astype(np.float32)


def kernel_unfused(**inputs):
    ins = {k: np.asarray(v, dtype=np.float32) for k, v in inputs.items()}
    x0 = np.ascontiguousarray(ins["L0_attn_x"].reshape(-1, D))
    eye_f = np.eye(128, dtype=np.float32)
    eye_b = np.eye(128).astype(BF)

    w1t = wtile(ins["L0_ffn_W1"])
    w2t = wtile(ins["L0_ffn_W2"])
    w3t = wtile(ins["L0_ffn_W3"])
    vcat_ffn = np.concatenate([colchunks(ins["L0_ffn_b1"]),
                               colchunks(ins["L0_ffn_b2"]),
                               colchunks(ins["L0_ffn_b3"])], 1).astype(np.float32)
    nc1 = build_ffn0_launch(True)
    im = [dict(xin=np.ascontiguousarray(x0[i * TOKS:(i + 1) * TOKS]), w1=w1t,
               w2=w2t, w3=w3t, vcat=vcat_ffn, identf=eye_f) for i in range(NCORES)]
    r1 = run_bass_kernel_spmd(nc1, im, core_ids=list(range(NCORES)))
    x1f = [np.asarray(r1.results[i]["xout"]) for i in range(NCORES)]

    WC = np.concatenate([ins["L0_attn_Wq"], ins["L0_attn_Wk"], ins["L0_attn_Wv"],
                         ins["L0_attn_Wb"], ins["L0_attn_Wd"]], 1)
    wcat = np.ascontiguousarray(
        np.transpose(WC.reshape(DC, 128, 384), (1, 0, 2)).reshape(128, -1))
    brow = np.concatenate([np.zeros(256, np.float32), ins["L0_attn_bb"],
                           ins["L0_attn_bd"]]).reshape(1, 384).astype(np.float32)
    tt = np.arange(128)
    mcum = ((tt[:, None] // 64 == tt[None, :] // 64) &
            (tt[:, None] % 64 <= tt[None, :] % 64)).astype(np.float32)
    chi = np.stack([(tt // 64 == 0), (tt // 64 == 1)], 1).astype(np.float32)
    amask = ((tt[:, None] // 64 == tt[None, :] // 64) &
             (tt[:, None] <= tt[None, :])).astype(np.uint8)
    nc2a = build_gla_a_launch()
    im = [dict(xin=x1f[i], wcat=wcat, brow=brow, mcum=mcum, chi=chi,
               amask=amask, identb=eye_b, identf=eye_f) for i in range(NCORES)]
    r2a = run_bass_kernel_spmd(nc2a, im, core_ids=list(range(NCORES)))
    nc2b = build_gla_b_launch()
    im = []
    for cix in range(NCORES):
        a = cix & 3
        b = cix >> 2
        u_in = np.zeros((128 * 128, 128), np.float32)
        pc_in = np.ones((128, 128), np.float32)
        nd = 32 * (a + 1)
        pad = 128 - nd
        ucat = np.concatenate(
            [np.asarray(r2a.results[4 * b + j]["uown"]) for j in range(a + 1)], 0)
        u_in[pad * 128:] = np.concatenate(
            [ucat.reshape(nd, 64, 128)] * 2, 1).reshape(nd * 128, 128)
        pcc = np.concatenate(
            [np.asarray(r2a.results[4 * b + j]["pcown"]) for j in range(a + 1)], 1)
        pc_in[0:64, pad:] = pcc
        pc_in[64:128, pad:] = pcc
        im.append(dict(xin=x1f[cix], uin=u_in, pcin=pc_in,
                       qtin=np.asarray(r2a.results[cix]["qtown"]),
                       oiin=np.asarray(r2a.results[cix]["oiown"]),
                       wo=ins["L0_attn_Wo"]))
    r2b = run_bass_kernel_spmd(nc2b, im, core_ids=list(range(NCORES)))
    x2f = [np.asarray(r2b.results[i]["xout"]) for i in range(NCORES)]

    nc3 = build_ffn0_launch(False)
    im = [dict(xin=x2f[i], w1=w1t, w2=w2t, w3=w3t, vcat=vcat_ffn,
               identf=eye_f) for i in range(NCORES)]
    r3 = run_bass_kernel_spmd(nc3, im, core_ids=list(range(NCORES)))
    x3f = [np.asarray(r3.results[i]["xout"]) for i in range(NCORES)]

    H = _hadamard128()
    h128u = H / np.float32(np.sqrt(2048.0))
    h128d = H * np.float32(0.5 / np.sqrt(8192.0))
    suug = colchunks(ins["L1_ffn_SU_ug"])
    sudp = colchunks(ins["L1_ffn_SU_dp"])
    ncp = build_prep_launch()
    im = []
    for cix in range(NCORES):
        im.append(dict(
            wup=np.ascontiguousarray(ins["L1_ffn_W_up"][cix * 1024:(cix + 1) * 1024, :].T),
            wga=np.ascontiguousarray(ins["L1_ffn_W_gate"][cix * 1024:(cix + 1) * 1024, :].T),
            wdo=np.ascontiguousarray(ins["L1_ffn_W_down"][cix * 256:(cix + 1) * 256, :].T),
            h128u=h128u, h128d=h128d, suug=suug, sudp=sudp,
            wsup=np.ascontiguousarray(ins["L1_ffn_Wscale_ug"][cix * 1024:(cix + 1) * 1024].reshape(1, -1)),
            wsga=np.ascontiguousarray(ins["L1_ffn_Wscale_ug"][8192 + cix * 1024:8192 + (cix + 1) * 1024].reshape(1, -1)),
            wsdo=np.ascontiguousarray(ins["L1_ffn_Wscale_dp"][cix * 256:(cix + 1) * 256].reshape(1, -1))))
    rp = run_bass_kernel_spmd(ncp, im, core_ids=list(range(NCORES)))
    wub = np.concatenate([np.asarray(rp.results[i]["oup"]) for i in range(NCORES)], 1)
    wgb = np.concatenate([np.asarray(rp.results[i]["oga"]) for i in range(NCORES)], 1)
    wdb = np.concatenate([np.asarray(rp.results[i]["odo"]) for i in range(NCORES)], 1)

    nc4 = build_l1ffn_launch()
    im = [dict(xin=x3f[i], wub=wub, wgb=wgb, wdb=wdb) for i in range(NCORES)]
    r4 = run_bass_kernel_spmd(nc4, im, core_ids=list(range(NCORES)))
    x4f = [np.asarray(r4.results[i]["xout"]) for i in range(NCORES)]

    nc5a = build_ln5_launch()
    im = [dict(xin=x4f[i]) for i in range(NCORES)]
    r5a = run_bass_kernel_spmd(nc5a, im, core_ids=list(range(NCORES)))
    xh_b = []
    for b in range(2):
        blocks = [np.asarray(r5a.results[4 * b + j]["xh"]).reshape(128, DC, TOKS)
                  for j in range(4)]
        xh_b.append(np.ascontiguousarray(
            np.stack(blocks, 2).reshape(128, DC * SEQ)))
    nc5b = build_attn_launch()
    jj = np.arange(TP)
    cmask = np.concatenate(
        [((128 * m + tt[:, None]) <= jj[None, :]).astype(BF) for m in range(4)], 1)
    im = []
    for cix in range(NCORES):
        b = cix >> 2
        hg = cix & 3
        wblocks = []
        bcols = np.zeros((128, 12), np.float32)
        for l in range(4):
            h = hg * 4 + l
            for j in range(3):
                rows = ins["L1_attn_w_qkv"][j * D + h * 128:j * D + (h + 1) * 128, :]
                wblocks.append(wtile(rows))
                bcols[:, l * 3 + j] = ins["L1_attn_b_qkv"][j * D + h * 128:j * D + (h + 1) * 128]
        im.append(dict(xh=xh_b[b], wqkv=np.concatenate(wblocks, 1), bqkv=bcols,
                       cmask=cmask, identb=eye_b))
    r5b = run_bass_kernel_spmd(nc5b, im, core_ids=list(range(NCORES)))
    nc5c = build_outproj_launch()
    wout_t = wtile(ins["L1_attn_w_out"])
    bo = colchunks(ins["L1_attn_b_out"])
    im = []
    for cix in range(NCORES):
        b = cix >> 2
        blk = cix & 3
        ob = np.zeros((128, DC, TOKS), dtype=BF)
        for h in range(NH):
            srcc = 4 * b + h // 4
            ob[:, h, :] = np.asarray(r5b.results[srcc]["oown"])[
                (h % 4) * 128:(h % 4 + 1) * 128, blk * TOKS:(blk + 1) * TOKS]
        im.append(dict(xin=x4f[cix], oin=np.ascontiguousarray(ob.reshape(128, -1)),
                       wout=wout_t, bout=bo))
    r5c = run_bass_kernel_spmd(nc5c, im, core_ids=list(range(NCORES)))
    x5f = [np.asarray(r5c.results[i]["xout"]) for i in range(NCORES)]

    im = [dict(xin=x5f[i], wub=wub, wgb=wgb, wdb=wdb) for i in range(NCORES)]
    r6 = run_bass_kernel_spmd(nc4, im, core_ids=list(range(NCORES)))

    out = np.concatenate([x_tok_of_fmaj(np.asarray(r6.results[i]["xout"]))
                          for i in range(NCORES)], 0)
    return np.ascontiguousarray(out.reshape(2, SEQ, D)).astype(np.float32)
